# version 02
# baseline (speedup 1.0000x reference)
import numpy as np
import ml_dtypes
from contextlib import ExitStack

import concourse.bass as bass
import concourse.mybir as mybir
from concourse.bass_utils import run_bass_kernel_spmd
from concourse.bass_core_rust import IndirectOffsetOnAxis

F32 = mybir.dt.float32
BF16 = mybir.dt.bfloat16
I32 = mybir.dt.int32
AF = mybir.ActivationFunctionType
OP = mybir.AluOpType
AX = mybir.AxisListType.X

S, D, BATCH = 8192, 2048, 2
T = 2048
NT = 16
NKV, NH, DQK = 4, 16, 128
WIN, FRONT = 512, 256
XKV = FRONT + WIN + T
NE, TOPK, INTER = 32, 8, 768
CAP = 640
ROWS = NE * CAP
PAYW = 2176
WCOL = 2048
NEG = -1e30
BIGC = 1e9
EPS = 1e-5

ENGS = ("sync", "scalar", "vector", "tensor", "gpsimd")
SEMS = ("T", "BV", "BS", "BG",
        "SG0", "SG1", "IOT", "PP0", "PP1", "PP2", "RP", "PKB0", "PKB1",
        "SMK", "EXP", "PNM", "PTE", "OHE", "YB0", "YB1",
        "LSQ", "LRC", "LSR", "LXA", "LXB", "LV1",
        "GLG", "GL2", "RNG", "RXP", "RWV", "RCA", "RS8", "RPW",
        "SIL", "HMU", "WCL", "YC0", "YC1", "GSM", "OF", "KTD", "VTD", "VFN", "SFN")
DMA_CHANNELS = ("MISC", "XKLa", "XKLb", "XKSa", "XKSb", "WSTa", "WSTb", "TL",
                "PLDa", "PLDb", "QST", "XSTa", "XSTb", "YSTa", "YSTb", "XAT",
                "OUT", "TRXa", "TRXb", "TRXG", "TRX4a", "TRX4b", "SCTa", "SCTb",
                "GTH", "DBG")
BAR_SEM = {"tensor": "T", "vector": "BV", "scalar": "BS", "gpsimd": "BG"}


FENCE_ALL = True


class Bld:
    def __init__(self):
        self.q = {e: [] for e in ENGS}
        self.cnt = {s: 0 for s in SEMS}
        self.cnt.update({c: 0 for c in DMA_CHANNELS})
        self.cnt.update({c: 0 for c in DMA_CHANNELS})
        self.sem = {}
        self.last = {e: None for e in ENGS}

    def fop(self, eng, fn, inc=None):
        if FENCE_ALL and eng in ("vector", "scalar"):
            return self.op(eng, fn, inc=inc)
        tok = self.op(eng, fn, inc=inc)
        if tok is not None:
            self.q[eng].append(lambda E, k=tok[0], v=tok[1]: E.wait_ge(self.sem[k], v))
            return tok
        fch = "VFN" if eng == "vector" else "SFN"
        self.cnt[fch] += 1
        v = self.cnt[fch]
        real = self.q[eng].pop()
        self.q[eng].append(lambda E, real=real, fch=fch: real(E).then_inc(self.sem[fch], 1))
        self.q[eng].append(lambda E, fch=fch, v=v: E.wait_ge(self.sem[fch], v))
        return tok

    def op(self, eng, fn, inc=None, nop=True):
        if FENCE_ALL and eng in ("vector", "scalar"):
            tok = self._op_raw(eng, fn, inc=inc, nop=False)
            if tok is not None:
                self.q[eng].append(lambda E, k=tok[0], v=tok[1]: E.wait_ge(self.sem[k], v))
                return tok
            fch = "VFN" if eng == "vector" else "SFN"
            self.cnt[fch] += 1
            v = self.cnt[fch]
            real = self.q[eng].pop()
            self.q[eng].append(lambda E, real=real, fch=fch: real(E).then_inc(self.sem[fch], 1))
            self.q[eng].append(lambda E, fch=fch, v=v: E.wait_ge(self.sem[fch], v))
            return None
        return self._op_raw(eng, fn, inc=inc, nop=nop)

    def _op_raw(self, eng, fn, inc=None, nop=True):
        if nop and eng == "vector":
            self.q[eng].append(lambda E: E.engine_nop())
        elif nop and eng in ("scalar", "gpsimd"):
            self.q[eng].append(lambda E: E.nop())
        if inc:
            k = "T" if (inc is True and eng == "tensor") else inc
            assert k is not True, "non-tensor inc needs a channel name"
            self.cnt[k] += 1
            v = self.cnt[k]
            self.q[eng].append(lambda E, fn=fn, k=k: fn(E).then_inc(self.sem[k], 1))
            tok = (k, v)
            self.last[eng] = tok
            return tok
        self.q[eng].append(fn)
        return None

    def mark(self, eng):
        k = BAR_SEM[eng]
        self.cnt[k] += 1
        v = self.cnt[k]
        if eng == "vector":
            self.q[eng].append(lambda E, k=k: E.engine_nop().then_inc(self.sem[k], 1))
        else:
            self.q[eng].append(lambda E, k=k: E.nop().then_inc(self.sem[k], 1))
        tok = (k, v)
        self.last[eng] = tok
        return tok

    def dma(self, eng, fn, ch="MISC"):
        k = ch
        self.cnt[k] += 16
        v = self.cnt[k]
        self.q[eng].append(lambda E, fn=fn, k=k: fn(E).then_inc(self.sem[k], 16))
        tok = (k, v)
        self.last[eng] = tok
        return tok

    def wait(self, eng, tok):
        if tok is None:
            return
        k, v = tok
        self.q[eng].append(lambda E, k=k, v=v: E.wait_ge(self.sem[k], v))

    def waits(self, eng, toks):
        best = {}
        for t in toks:
            if t is None:
                continue
            k, v = t
            best[k] = max(best.get(k, 0), v)
        for k, v in best.items():
            self.q[eng].append(lambda E, k=k, v=v: E.wait_ge(self.sem[k], v))

    def barrier(self, engines=ENGS):
        toks = []
        for ch in SEMS + DMA_CHANNELS:
            if self.cnt[ch] > 0:
                toks.append((ch, self.cnt[ch]))
        for e in engines:
            self.waits(e, toks)


def build_nc(stage="full", races=True, dbg=False, rparts=4):
    nc = bass.Bass("TRN2", target_bir_lowering=False,
                   detect_race_conditions=races)
    top = ExitStack()

    dt = nc.dram_tensor
    xkv_in = dt("xkv_in", [XKV, D], F32, kind="ExternalInput").ap()
    wq_in = dt("wq_in", [D, D], F32, kind="ExternalInput").ap()
    wk_in = dt("wk_in", [D, 512], F32, kind="ExternalInput").ap()
    wv_in = dt("wv_in", [D, 512], F32, kind="ExternalInput").ap()
    wo_in = dt("wo_in", [D, D], F32, kind="ExternalInput").ap()
    gate_in = dt("gate_in", [D, NE], F32, kind="ExternalInput").ap()
    wg_in = dt("wg_in", [NE, D, INTER], F32, kind="ExternalInput").ap()
    wu_in = dt("wu_in", [NE, D, INTER], F32, kind="ExternalInput").ap()
    wd_in = dt("wd_in", [NE, INTER, D], F32, kind="ExternalInput").ap()
    cq_in = dt("cq_in", [T, 64], F32, kind="ExternalInput").ap()
    sq_in = dt("sq_in", [T, 64], F32, kind="ExternalInput").ap()
    ckv_in = dt("ckv_in", [XKV, 64], F32, kind="ExternalInput").ap()
    skv_in = dt("skv_in", [XKV, 64], F32, kind="ExternalInput").ap()
    mask_in = dt("mask_in", [NT, 128, 896], F32, kind="ExternalInput").ap()
    idf_in = dt("idf_in", [128, 128], F32, kind="ExternalInput").ap()
    idb_in = dt("idb_in", [128, 128], BF16, kind="ExternalInput").ap()

    y_out = dt("y_out", [T, D], F32, kind="ExternalOutput").ap()
    dbg_out = dt("dbg_out", [8, 128, 2048], F32, kind="ExternalOutput").ap() if dbg else None
    sdump = dt("sdump", [128, 128], I32, kind="ExternalOutput").ap() if dbg else None
    wdump = dt("wdump", [ROWS + 1, 64], BF16, kind="ExternalOutput").ap() if dbg else None
    ydump = dt("ydump", [ROWS + 1, 8], BF16, kind="ExternalOutput").ap() if dbg else None

    xkv_bf = dt("xkv_bf", [XKV, D], BF16, kind="Internal").ap()
    qt_dram = dt("qt_dram", [NT, 16, 128, 128], BF16, kind="Internal").ap()
    xa_dram = dt("xa_dram", [T, D], F32, kind="Internal").ap()
    xab_dram = dt("xab_dram", [T, D], BF16, kind="Internal").ap()
    xsorted = dt("xsorted", [ROWS + 1, PAYW], BF16, kind="Internal").ap()
    ysorted = dt("ysorted", [ROWS + 1, D], BF16, kind="Internal").ap()

    def SB(stk, name, shape, dtype):
        return stk.enter_context(nc.sbuf_tensor(name, shape, dtype))

    def PS(stk, name, shape, dtype=F32):
        return stk.enter_context(nc.psum_tensor(name, shape, dtype))

    idf = SB(top, "idf", [128, 128], F32)
    idb = SB(top, "idb", [128, 128], BF16)
    stage_f = [SB(top, f"stage{j}", [128, 2048], F32) for j in range(2)]
    small = SB(top, "small", [128, 16], F32)
    x1 = SB(top, "x1", [128, D], F32)
    logits = SB(top, "logits", [128, NT * NE], F32)
    rdbg = [SB(top, "rdbg0", [32, 128], F32), SB(top, "rdbg1", [32, 128], F32),
            SB(top, "rdbg2", [128, 32], F32), SB(top, "rdbg3", [128, 32], F32),
            SB(top, "rdbg4", [128, 8], F32), SB(top, "rdbg5", [128, 8], F32)]
    slot8i = SB(top, "slot8i", [128, NT * 8], I32)
    gateb = SB(top, "gateb", [128, 16 * NE], BF16)
    iota32 = SB(top, "iota32", [128, NE], F32)
    iota2048 = SB(top, "iota2048", [128, NE], F32)
    Yp = [PS(top, f"Yp{j}", [128, 512]) for j in range(2)]

    sems = {}
    for sname in SEMS + DMA_CHANNELS:
        sems[sname] = top.enter_context(nc.semaphore(sname))
    b = Bld()
    b.sem = sems

    cast_state = {"i": 0, "hist": {}}

    def cast_chunk(src_ap, dst_ap, cols, eng, extra_waits=()):
        par = cast_state["i"] % 2
        stg = stage_f[par]
        cast_state["i"] += 1
        prev = cast_state["hist"].get(id(stg))
        if prev is not None:
            b.wait("sync", prev)
        tl = b.dma("sync", lambda E, stg=stg, cols=cols, src_ap=src_ap: E.dma_start(
            out=stg[:, 0:cols], in_=src_ap), ch="WSTa" if par == 0 else "WSTb")
        b.wait(eng, tl)
        for w in extra_waits:
            b.wait(eng, w)
        sg = "SG0" if par == 0 else "SG1"
        if eng == "scalar":
            tc = b.op(eng, lambda E, stg=stg, cols=cols, dst_ap=dst_ap: E.copy(
                out=dst_ap, in_=stg[:, 0:cols]), inc=sg)
        else:
            tc = b.op(eng, lambda E, stg=stg, cols=cols, dst_ap=dst_ap: E.tensor_copy(
                out=dst_ap, in_=stg[:, 0:cols]), inc=sg)
        cast_state["hist"][id(stg)] = tc
        return tc

    pa = ExitStack()
    pab = ExitStack()
    kt = SB(pab, "kt", [128, NKV * XKV], BF16)
    vt = SB(pab, "vt", [128, 22 * 512], BF16)
    xsl = [SB(pab, "xsl0", [128, 16 * 512], BF16),
           SB(pa, "xsl1", [128, 16 * 512], BF16)]
    cqtm = SB(pa, "cqtm", [128, 16 * 64], F32)
    sqtm = SB(pa, "sqtm", [128, 16 * 64], F32)
    rtmp2 = SB(pa, "rtmp2", [128, 16 * 64], F32)
    qtm = SB(pa, "qtm", [128, 2048], F32)
    qrope = SB(pa, "qrope", [128, 2048], F32)
    qrope_b = SB(pa, "qrope_b", [128, 2048], BF16)
    qtts = SB(pa, "qtts", [128, 2048], BF16)
    pp = [PS(pa, f"pp{j}", [128, 512]) for j in range(3)]
    pkb = [PS(pa, f"pkb{j}", [128, 128], BF16) for j in range(2)]
    pa0 = ExitStack()
    ckvtm = SB(pa0, "ckvtm", [128, 22 * 64], F32)
    skvtm = SB(pa0, "skvtm", [128, 22 * 64], F32)
    ktm = SB(pa0, "ktm", [128, 512], F32)
    krope = SB(pa0, "krope", [128, 512], F32)
    krope_b = SB(pa0, "krope_b", [128, 512], BF16)
    wkb = SB(pa0, "wkb", [128, 16 * 512], BF16)
    wvb = SB(pa0, "wvb", [128, 16 * 512], BF16)

    b.dma("sync", lambda E: E.dma_start(out=idf[:], in_=idf_in[:]))
    b.dma("sync", lambda E: E.dma_start(out=idb[:], in_=idb_in[:]))
    for tl_ in range(16):
        b.dma("sync", lambda E, tl_=tl_: E.dma_start(
            out=cqtm[:, tl_ * 64:(tl_ + 1) * 64], in_=cq_in[tl_ * 128:(tl_ + 1) * 128, :]))
        b.dma("sync", lambda E, tl_=tl_: E.dma_start(
            out=sqtm[:, tl_ * 64:(tl_ + 1) * 64], in_=sq_in[tl_ * 128:(tl_ + 1) * 128, :]))
    for tl_ in range(22):
        b.dma("sync", lambda E, tl_=tl_: E.dma_start(
            out=ckvtm[:, tl_ * 64:(tl_ + 1) * 64], in_=ckv_in[tl_ * 128:(tl_ + 1) * 128, :]))
        b.dma("sync", lambda E, tl_=tl_: E.dma_start(
            out=skvtm[:, tl_ * 64:(tl_ + 1) * 64], in_=skv_in[tl_ * 128:(tl_ + 1) * 128, :]))

    xkv_done = []
    for chk in range(22):
        buf = xsl[chk % 2]
        dst = buf[:, 0:2048]
        if chk >= 2:
            b.wait("gpsimd", xkv_done[chk - 2])
        tl = b.dma("gpsimd", lambda E, chk=chk, dst=dst: E.dma_start(
            out=dst, in_=xkv_in[chk * 128:(chk + 1) * 128, :]),
            ch="XKLa" if chk % 2 == 0 else "XKLb")
        b.wait("sync", tl)
        ts_ = b.dma("sync", lambda E, chk=chk, dst=dst: E.dma_start(
            out=xkv_bf[chk * 128:(chk + 1) * 128, :], in_=dst),
            ch="XKSa" if chk % 2 == 0 else "XKSb")
        xkv_done.append(ts_)
    t_xkvbf = [xkv_done[-1], xkv_done[-2]]

    b.op("gpsimd", lambda E: E.iota(iota32[:], pattern=[[1, NE]], base=0,
                                    channel_multiplier=0,
                                    allow_small_or_imprecise_dtypes=True))
    b.op("gpsimd", lambda E: E.iota(iota2048[:], pattern=[[2048, NE]], base=0,
                                    channel_multiplier=0,
                                    allow_small_or_imprecise_dtypes=True), inc="IOT")

    for dmc in range(16):
        cast_chunk(wk_in[dmc * 128:(dmc + 1) * 128, :], wkb[:, dmc * 512:(dmc + 1) * 512],
                   512, "vector")
        cast_chunk(wv_in[dmc * 128:(dmc + 1) * 128, :], wvb[:, dmc * 512:(dmc + 1) * 512],
                   512, "scalar")
    for dmc in range(16):
        cast_chunk(gate_in[dmc * 128:(dmc + 1) * 128, :], gateb[:, dmc * NE:(dmc + 1) * NE],
                   NE, "vector")
    b.barrier(("sync", "vector", "scalar", "gpsimd"))

    def emit_rope_tm(dst, srct, ctab, stab, nh, tmp):
        sr = srct.rearrange("p (h half i) -> p h half i", h=nh, half=2)
        dr = dst.rearrange("p (h half i) -> p h half i", h=nh, half=2)
        tr = tmp.rearrange("p (h i) -> p h i", h=nh)
        cb = ctab.unsqueeze(1).to_broadcast([128, nh, 64])
        sb_ = stab.unsqueeze(1).to_broadcast([128, nh, 64])
        b.op("vector", lambda E: E.tensor_tensor(dr[:, :, 0, :], sr[:, :, 0, :], cb, op=OP.mult))
        b.op("vector", lambda E: E.tensor_tensor(tr[:], sr[:, :, 1, :], sb_, op=OP.mult))
        b.op("vector", lambda E: E.tensor_tensor(dr[:, :, 0, :], dr[:, :, 0, :], tr[:], op=OP.subtract))
        b.op("vector", lambda E: E.tensor_tensor(dr[:, :, 1, :], sr[:, :, 1, :], cb, op=OP.mult))
        b.op("vector", lambda E: E.tensor_tensor(tr[:], sr[:, :, 0, :], sb_, op=OP.mult))
        b.op("vector", lambda E: E.tensor_tensor(
            dr[:, :, 1, :], dr[:, :, 1, :], tr[:], op=OP.add))
        return None

    if stage == "p0":
        b.barrier(ENGS)
        tlast = None
        for i in range(NT):
            tlast = b.dma("gpsimd", lambda E, i=i: E.dma_start(
                out=y_out[i * 128:(i + 1) * 128, :],
                in_=xkv_bf[768 + i * 128:768 + (i + 1) * 128, :]), ch="DBG")
        b.wait("sync", tlast)
        finish(nc, b)
        return nc

    pptok = {}

    def pwait(j, eng):
        t = pptok.get(j)
        if t is not None:
            b.wait(eng, t)

    pidx = 0
    kv_slices = [(i * 512, min(512, XKV - i * 512)) for i in range(6)]
    slice_done = {}
    for si, (t0, w) in enumerate(kv_slices):
        buf = xsl[si % 2]
        b.waits("scalar", t_xkvbf)
        if si >= 2:
            b.wait("scalar", slice_done[si - 2])
        ttr = None
        for dmc in range(16):
            ttr = b.dma("scalar", lambda E, t0=t0, w=w, dmc=dmc, buf=buf: E.dma_start_transpose(
                out=buf[:, dmc * 512:dmc * 512 + w],
                in_=xkv_bf[t0:t0 + w, dmc * 128:(dmc + 1) * 128]),
                ch="TRXa" if si % 2 == 0 else "TRXb")
        b.waits("tensor", [ttr])
        last_use = None
        pkb_tok = [None, None]
        for tt in range(w // 128):
            blk = t0 // 128 + tt
            j = pidx % 3
            pidx += 1
            pwait(j, "tensor")
            tmm = None
            for dmc in range(16):
                tmm = b.op("tensor", lambda E, j=j, dmc=dmc, tt=tt, buf=buf: E.matmul(
                    pp[j][:, 0:512],
                    buf[:, dmc * 512 + tt * 128:dmc * 512 + (tt + 1) * 128],
                    wkb[:, dmc * 512:(dmc + 1) * 512],
                    start=(dmc == 0), stop=(dmc == 15)), nop=False,
                    inc=(True if dmc == 15 else None))
            b.wait("vector", tmm)
            tevk = b.op("vector", lambda E, j=j: E.tensor_copy(out=ktm[:], in_=pp[j][:, 0:512]),
                        inc=f"PP{j}")
            pptok[j] = tevk
            emit_rope_tm(krope[:], ktm[:], ckvtm[:, blk * 64:(blk + 1) * 64],
                         skvtm[:, blk * 64:(blk + 1) * 64], NKV, rtmp2[:, 0:256])
            trope = b.op("vector", lambda E: E.tensor_copy(out=krope_b[:], in_=krope[:]),
                         inc="RP")
            b.waits("tensor", [trope])
            for f in range(NKV):
                pb_ = f % 2
                b.waits("tensor", [pkb_tok[pb_]])
                ttp = b.op("tensor", lambda E, f=f, pb_=pb_: E.transpose(
                    pkb[pb_][:], krope_b[:, f * 128:(f + 1) * 128], idb[:]), inc=True, nop=False)
                b.wait("vector", ttp)
                tevt = b.op("vector", lambda E, f=f, blk=blk, pb_=pb_: E.tensor_copy(
                    out=kt[:, f * XKV + blk * 128:f * XKV + (blk + 1) * 128],
                    in_=pkb[pb_][:]), inc=f"PKB{pb_}")
                pkb_tok[pb_] = tevt
                last_use = tevt
        ntt = w // 128
        for tt in range(ntt):
            j = pidx % 3
            pidx += 1
            pwait(j, "tensor")
            tmm = None
            for dmc in range(16):
                tmm = b.op("tensor", lambda E, j=j, dmc=dmc, tt=tt, buf=buf: E.matmul(
                    pp[j][:, 0:512],
                    buf[:, dmc * 512 + tt * 128:dmc * 512 + (tt + 1) * 128],
                    wvb[:, dmc * 512:(dmc + 1) * 512],
                    start=(dmc == 0), stop=(dmc == 15)), nop=False,
                    inc=(True if dmc == 15 else None))
            blk = t0 // 128 + tt
            b.wait("scalar", tmm)
            tev = b.op("scalar", lambda E, j=j, blk=blk: E.copy(
                out=vt[:, blk * 512:(blk + 1) * 512], in_=pp[j][:, 0:512]), inc=f"PP{j}")
            pptok[j] = tev
            last_use = tev
        slice_done[si] = last_use

    if stage == "p1":
        b.barrier(ENGS)
        tl1 = b.dma("gpsimd", lambda E: E.dma_start(out=y_out[0:128, :], in_=kt[:, 0:2048]), ch="DBG")
        tl2 = b.dma("gpsimd", lambda E: E.dma_start(out=y_out[128:256, :], in_=vt[:, 0:2048]), ch="DBG")
        b.wait("sync", tl2)
        finish(nc, b)
        return nc

    pa0.close()
    wqb = SB(pa, "wqb", [128, 16 * 2048], BF16)
    for dmc in range(16):
        cast_chunk(wq_in[dmc * 128:(dmc + 1) * 128, :],
                   wqb[:, dmc * 2048:(dmc + 1) * 2048], 2048,
                   "vector" if dmc % 2 == 0 else "scalar")
    b.barrier(("sync", "vector", "scalar", "tensor"))

    qst_tok = None
    p2_slfree = {}
    for si in range(4):
        t0 = 768 + si * 512
        buf = xsl[si % 2]
        b.waits("scalar", [p2_slfree.get(si % 2)])
        ttr = None
        for dmc in range(16):
            ttr = b.dma("scalar", lambda E, t0=t0, dmc=dmc, buf=buf: E.dma_start_transpose(
                out=buf[:, dmc * 512:(dmc + 1) * 512],
                in_=xkv_bf[t0:t0 + 512, dmc * 128:(dmc + 1) * 128]),
                ch="TRXa" if si % 2 == 0 else "TRXb")
        b.wait("tensor", ttr)
        pkb_tok = [None, None]
        for tt in range(4):
            tile = si * 4 + tt
            evtoks = []
            for nq in range(4):
                j = pidx % 3
                pidx += 1
                pwait(j, "tensor")
                tmm = None
                for dmc in range(16):
                    tmm = b.op("tensor", lambda E, j=j, dmc=dmc, nq=nq, tt=tt, buf=buf: E.matmul(
                        pp[j][:, 0:512],
                        buf[:, dmc * 512 + tt * 128:dmc * 512 + (tt + 1) * 128],
                        wqb[:, dmc * 2048 + nq * 512:dmc * 2048 + (nq + 1) * 512],
                        start=(dmc == 0), stop=(dmc == 15)), nop=False,
                        inc=(True if dmc == 15 else None))
                b.wait("vector", tmm)
                tcp = b.op("vector", lambda E, j=j, nq=nq: E.tensor_copy(
                    out=qtm[:, nq * 512:(nq + 1) * 512], in_=pp[j][:, 0:512]), inc=f"PP{j}")
                pptok[j] = tcp
                evtoks.append(tcp)
            emit_rope_tm(qrope[:], qtm[:], cqtm[:, tile * 64:(tile + 1) * 64],
                         sqtm[:, tile * 64:(tile + 1) * 64], NH, rtmp2[:])
            trope = b.op("vector", lambda E: E.tensor_copy(out=qrope_b[:], in_=qrope[:]),
                         inc="RP")
            b.waits("tensor", [trope])
            b.waits("vector", [qst_tok])
            for f in range(16):
                pb_ = f % 2
                b.waits("tensor", [pkb_tok[pb_]])
                ttp = b.op("tensor", lambda E, f=f, pb_=pb_: E.transpose(
                    pkb[pb_][:], qrope_b[:, f * 128:(f + 1) * 128], idb[:]), inc=True, nop=False)
                b.wait("vector", ttp)
                tevt = b.op("vector", lambda E, f=f, pb_=pb_: E.tensor_copy(
                    out=qtts[:, f * 128:(f + 1) * 128], in_=pkb[pb_][:]), inc=f"PKB{pb_}")
                pkb_tok[pb_] = tevt
            b.waits("sync", pkb_tok)
            qst_tok = b.dma("sync", lambda E, tile=tile: E.dma_start(
                out=qt_dram[tile].rearrange("f p c -> p f c"),
                in_=qtts[:].rearrange("p (f c) -> p f c", f=16)), ch="QST")
        p2_slfree[si % 2] = tmm

    b.barrier(ENGS)
    if stage == "p2":
        tl1 = None
        for i in range(4):
            tl1 = b.dma("gpsimd", lambda E, i=i: E.dma_start(
                out=y_out[i * 128:(i + 1) * 128, :].rearrange("p (f c) -> p f c", f=16),
                in_=qt_dram[i].rearrange("f p c -> p f c")), ch="DBG")
        b.wait("sync", tl1)
        finish(nc, b)
        return nc

    pa.close()

    pb = ExitStack()
    wob = SB(pb, "wob", [128, 16 * 2048], BF16)
    qt_t = SB(pb, "qt_t", [128, 16 * 128], BF16)
    mask_t = SB(pb, "mask_t", [128, 896], F32)
    x_t = SB(pb, "x_t", [128, D], F32)
    Pt = SB(pb, "Pt", [128, 896], BF16)
    PTs = SB(pb, "PTs", [128, 896], BF16)
    sm = SB(pb, "sm", [128, 896], F32)
    outT = SB(pb, "outT", [128, 16 * 128], BF16)
    xa_f = SB(pb, "xa_f", [128, D], F32)
    xab_t = SB(pb, "xab_t", [128, D], BF16)
    m8 = SB(pb, "m8", [128, 8], F32)
    zapt = SB(pb, "zapt", [128, NE], F32)
    Mm = SB(pb, "Mm", [128, NE], F32)
    ew = SB(pb, "ew", [128, NE], F32)
    Wd = SB(pb, "Wd", [128, NE], F32)
    Whi = SB(pb, "Whi", [128, NE], BF16)
    Wlo = SB(pb, "Wlo", [128, NE], F32)
    Wlob = SB(pb, "Wlob", [128, NE], BF16)
    rbT = SB(pb, "rbT", [32, 128], F32)
    cums = SB(pb, "cums", [32, 128], F32)
    zer32 = SB(pb, "zer32", [32, 128], F32)
    carry = SB(pb, "carry", [32, 1], F32)
    posT = SB(pb, "posT", [32, 128], F32)
    pos_tm = SB(pb, "pos_tm", [128, NE], F32)
    kva = SB(pb, "kva", [128, NE], F32)
    kvm = SB(pb, "kvm", [128, NE], F32)
    kv8 = SB(pb, "kv8", [128, 8], F32)
    e8 = SB(pb, "e8", [128, 8], F32)
    Em = SB(pb, "Em", [128, NE], F32)
    pos8 = SB(pb, "pos8", [128, 8], F32)
    slot8f = SB(pb, "slot8f", [128, 8], F32)
    ovf = SB(pb, "ovf", [128, 8], F32)
    logT_sb = SB(pb, "logT_sb", [32, 512], F32)
    payload = [SB(pb, f"pay{j}", [128, PAYW], BF16) for j in range(2)]
    S1 = PS(pb, "S1", [128, 512])
    S2 = PS(pb, "S2", [128, 384])
    PTp = PS(pb, "PTp", [128, 896], BF16)
    OHT = PS(pb, "OHT", [128, 128])
    misc1 = PS(pb, "misc1", [32, 512])
    misc2 = PS(pb, "misc2", [128, 32])

    b.op("vector", lambda E: E.memset(zer32[:], 0.0))
    b.op("vector", lambda E: E.memset(carry[:], 0.0))
    b.op("vector", lambda E: E.memset(payload[0][:, 2112:2176], 0.0))
    b.op("vector", lambda E: E.memset(payload[1][:, 2112:2176], 0.0))

    for f in range(16):
        cast_chunk(wo_in[f * 128:(f + 1) * 128, :], wob[:, f * 2048:(f + 1) * 2048],
                   2048, "vector" if f % 2 == 0 else "scalar")
    b.barrier(("sync", "vector", "scalar", "tensor"))

    tile_tok = {}
    pay_tok = {}
    xab_store = {}
    xa_store = {}
    state = {
        "s_bank": None,
        "sm_free": None,
        "pt_bank": None,
        "pts_free": None,
        "pt_sb": None,
        "oht_bank": None,
        "outT_free": None,
        "y_bank": None,
        "xaf_free": None,
        "xab_free": None,
        "pay_free": {},
        "route_in": {},
    }

    def emit_tile_loads(i, first=False):
        toks = []
        if not first:
            b.waits("sync", [state["y_bank"]])
        toks.append(b.dma("sync", lambda E, i=i: E.dma_start(
            out=qt_t[:].rearrange("p (f c) -> p f c", f=16),
            in_=qt_dram[i].rearrange("f p c -> p f c")), ch="TL"))
        toks.append(b.dma("sync", lambda E, i=i: E.dma_start(
            out=mask_t[:], in_=mask_in[i]), ch="TL"))
        toks.append(b.dma("sync", lambda E, i=i: E.dma_start(
            out=x_t[:], in_=xkv_in[768 + i * 128:768 + (i + 1) * 128, :]), ch="TL"))
        tile_tok[i] = toks

    def emit_gate_quarter(qtr):
        t0 = qtr * 512
        buf = xsl[0]
        b.waits("scalar", [xab_store[qtr * 4 + j] for j in range(4)]
                + [state.get("gate_sl_free")])
        ttr = None
        for dmc in range(16):
            ttr = b.dma("scalar", lambda E, t0=t0, dmc=dmc, buf=buf: E.dma_start_transpose(
                out=buf[:, dmc * 512:(dmc + 1) * 512],
                in_=xab_dram[t0:t0 + 512, dmc * 128:(dmc + 1) * 128]), ch="TRXG")
        b.waits("tensor", [ttr])
        tgm = None
        for dmc in range(16):
            tgm = b.op("tensor", lambda E, dmc=dmc, buf=buf: E.matmul(
                misc1[:, 0:512], gateb[:, dmc * NE:(dmc + 1) * NE],
                buf[:, dmc * 512:(dmc + 1) * 512],
                start=(dmc == 0), stop=(dmc == 15)), nop=False,
                inc=(True if dmc == 15 else None))
        state["gate_sl_free"] = tgm
        b.wait("vector", tgm)
        tge = b.op("vector", lambda E: E.tensor_copy(out=logT_sb[:], in_=misc1[:, 0:512]),
                   inc="GLG")
        tt2 = None
        for jj in range(4):
            i = qtr * 4 + jj
            b.waits("tensor", [tge, tt2])
            tt1 = b.op("tensor", lambda E, jj=jj: E.transpose(
                misc2[:], logT_sb[:, jj * 128:(jj + 1) * 128], idf[0:32, 0:32]),
                inc=True, nop=False)
            b.wait("vector", tt1)
            tt2 = b.op("vector", lambda E, i=i: E.tensor_copy(
                out=logits[:, i * NE:(i + 1) * NE], in_=misc2[:]), inc="GL2")
            state["route_in"][i] = tt2

    def emit_routing_tile(i):
        if rparts < 2:
            pay_tok[i] = None
            return
        lg = logits[:, i * NE:(i + 1) * NE]
        b.waits("vector", [state["route_in"][i]])
        b.op("vector", lambda E, lg=lg: E.tensor_copy(out=zapt[:], in_=lg))
        b.op("vector", lambda E: E.memset(Mm[:], 0.0))
        for kk in range(8):
            b.fop("vector", lambda E, kk=kk: E.reduce_max(m8[:, kk:kk + 1], zapt[:], axis=AX))
            b.fop("vector", lambda E, kk=kk: E.tensor_scalar(
                ew[:], zapt[:], m8[:, kk:kk + 1], None, op0=OP.is_equal))
            b.fop("vector", lambda E: E.tensor_tensor(Mm[:], Mm[:], ew[:], op=OP.add))
            b.fop("vector", lambda E: E.scalar_tensor_tensor(
                zapt[:], in0=ew[:], scalar=NEG, in1=zapt[:], op0=OP.mult, op1=OP.add))
        if i == 1:
            b.op("vector", lambda E, lg=lg: E.tensor_copy(out=rdbg[3][:], in_=lg))
        tneg = b.op("vector", lambda E: E.tensor_scalar(
            small[:, 14:15], m8[:, 0:1], -1.0, None, op0=OP.mult), inc="RNG")
        b.wait("scalar", tneg)
        texp = b.op("scalar", lambda E, lg=lg: E.activation(
            out=ew[:], in_=lg, func=AF.Exp, bias=small[:, 14:15]), inc="RXP")
        b.wait("vector", texp)
        b.fop("vector", lambda E: E.tensor_tensor(Wd[:], ew[:], Mm[:], op=OP.mult))
        b.fop("vector", lambda E: E.reduce_sum(small[:, 15:16], Wd[:], axis=AX))
        b.fop("vector", lambda E: E.reciprocal(small[:, 14:15], small[:, 15:16]))
        b.fop("vector", lambda E: E.tensor_scalar(
            Wd[:], Wd[:], small[:, 14:15], None, op0=OP.mult))
        b.fop("vector", lambda E: E.tensor_copy(out=Whi[:], in_=Wd[:]))
        b.fop("vector", lambda E: E.tensor_copy(out=ew[:], in_=Whi[:]))
        b.fop("vector", lambda E: E.tensor_tensor(Wlo[:], Wd[:], ew[:], op=OP.subtract))
        twv = b.fop("vector", lambda E: E.tensor_copy(out=Wlob[:], in_=Wlo[:]), inc="RWV")
        b.waits("tensor", [twv])
        tmt = b.op("tensor", lambda E: E.transpose(misc1[:, 0:128], Mm[:], idf[:]),
                   inc=True, nop=False)
        b.wait("vector", tmt)
        trb = b.fop("vector", lambda E: E.tensor_copy(out=rbT[:], in_=misc1[0:32, 0:128]),
                    inc="RCA")
        state["misc1_free"] = trb
        b.fop("vector", lambda E: E.tensor_tensor_scan(
            cums[:], data0=rbT[:], data1=zer32[:], initial=0.0,
            op0=OP.add, op1=OP.add))
        b.op("vector", lambda E: E.tensor_scalar(
            cums[:], cums[:], carry[:, 0:1], None, op0=OP.add))
        b.op("vector", lambda E: E.tensor_tensor(posT[:], cums[:], rbT[:], op=OP.subtract))
        if i == 1:
            b.op("vector", lambda E: E.tensor_copy(out=rdbg[0][:], in_=rbT[:]))
            b.op("vector", lambda E: E.tensor_copy(out=rdbg[1][:], in_=cums[:]))
        tcar = b.fop("vector", lambda E: E.tensor_copy(out=carry[:], in_=cums[:, 127:128]),
                     inc="RCA")
        b.wait("tensor", tcar)
        tpt = b.op("tensor", lambda E: E.transpose(misc2[:], posT[:], idf[0:32, 0:32]),
                   inc=True, nop=False)
        b.wait("vector", tpt)
        b.op("vector", lambda E: E.tensor_copy(out=pos_tm[:], in_=misc2[:]))
        if i == 1:
            b.op("vector", lambda E: E.tensor_copy(out=rdbg[2][:], in_=pos_tm[:]))
        b.op("vector", lambda E: E.tensor_tensor(kva[:], iota2048[:], pos_tm[:], op=OP.add))
        b.op("vector", lambda E: E.tensor_scalar(kva[:], kva[:], 1.0, None, op0=OP.add))
        b.op("vector", lambda E: E.tensor_tensor(kvm[:], kva[:], Mm[:], op=OP.mult))
        b.fop("vector", lambda E: E.tensor_scalar(kvm[:], kvm[:], 1.0, None, op0=OP.subtract))
        b.fop("vector", lambda E: E.max(out=kv8[:], in_=kvm[:]))
        b.op("vector", lambda E: E.tensor_scalar(Em[:], iota32[:], 1.0, None, op0=OP.add))
        b.op("vector", lambda E: E.tensor_tensor(Em[:], Em[:], Mm[:], op=OP.mult))
        b.fop("vector", lambda E: E.tensor_scalar(Em[:], Em[:], 1.0, None, op0=OP.subtract))
        b.fop("vector", lambda E: E.max(out=e8[:], in_=Em[:]))
        b.op("vector", lambda E: E.scalar_tensor_tensor(
            pos8[:], in0=e8[:], scalar=-2048.0, in1=kv8[:], op0=OP.mult, op1=OP.add))
        if i == 1:
            b.op("vector", lambda E: E.tensor_copy(out=rdbg[4][:], in_=kv8[:]))
            b.op("vector", lambda E: E.tensor_copy(out=rdbg[5][:], in_=e8[:]))
        b.op("vector", lambda E: E.scalar_tensor_tensor(
            slot8f[:], in0=e8[:], scalar=640.0, in1=pos8[:], op0=OP.mult, op1=OP.add))
        b.op("vector", lambda E: E.tensor_scalar(ovf[:], pos8[:], 639.5, None, op0=OP.is_gt))
        b.op("vector", lambda E: E.tensor_scalar(kv8[:], slot8f[:], float(ROWS), None,
                                                 op0=OP.subtract))
        b.op("vector", lambda E: E.tensor_tensor(kv8[:], kv8[:], ovf[:], op=OP.mult))
        b.op("vector", lambda E: E.tensor_tensor(slot8f[:], slot8f[:], kv8[:], op=OP.subtract))
        b.op("vector", lambda E: E.tensor_scalar(ovf[:], slot8f[:], 0.0, None, op0=OP.is_lt))
        b.op("vector", lambda E: E.tensor_scalar(kv8[:], slot8f[:], float(ROWS), None,
                                                 op0=OP.subtract))
        b.op("vector", lambda E: E.tensor_tensor(kv8[:], kv8[:], ovf[:], op=OP.mult))
        b.op("vector", lambda E: E.tensor_tensor(slot8f[:], slot8f[:], kv8[:], op=OP.subtract))
        tsl8 = b.op("vector", lambda E, i=i: E.tensor_copy(
            out=slot8i[:, i * 8:(i + 1) * 8], in_=slot8f[:]), inc="RS8")
        if rparts < 3:
            pay_tok[i] = None
            return
        bi = i % 2
        prevp = state["pay_free"].get(bi)
        if prevp is not None:
            b.waits("sync", [prevp])
        tpl = b.dma("sync", lambda E, i=i, bi=bi: E.dma_start(
            out=payload[bi][:, 0:2048], in_=xab_dram[i * 128:(i + 1) * 128, :]),
            ch="PLDa" if bi == 0 else "PLDb")
        b.waits("vector", [tpl] + ([prevp] if prevp else []))
        b.op("vector", lambda E, bi=bi: E.tensor_copy(
            out=payload[bi][:, 2048:2112].rearrange("p (c two) -> p c two", two=2)[:, :, 0],
            in_=Whi[:]))
        tpw = b.op("vector", lambda E, bi=bi: E.tensor_copy(
            out=payload[bi][:, 2048:2112].rearrange("p (c two) -> p c two", two=2)[:, :, 1],
            in_=Wlob[:]), inc="RPW")
        if rparts < 4:
            pay_tok[i] = None
            return
        b.waits("gpsimd", [tpw, tsl8])
        tsc = None
        for k in range(8):
            tsc = b.dma("gpsimd", lambda E, i=i, k=k, bi=bi: E.indirect_dma_start(
                out=xsorted[:, :],
                out_offset=IndirectOffsetOnAxis(slot8i[:, i * 8 + k:i * 8 + k + 1], 0),
                in_=payload[bi][:, :], in_offset=None),
                ch="SCTa" if bi == 0 else "SCTb")
        pay_tok[i] = tsc
        state["pay_free"][bi] = tsc

    emit_tile_loads(0, first=True)
    for i in range(NT):
        b.waits("tensor", tile_tok[i])
        toh = None
        for h in range(NH):
            kvh = h // 4
            kb = kvh * XKV
            b.waits("tensor", [state["s_bank"]])
            b.op("tensor", lambda E, h=h, kb=kb: E.matmul(
                S1[:, 0:256], qt_t[:, h * 128:(h + 1) * 128],
                kt[:, kb:kb + 256], start=True, stop=True), nop=False)
            b.op("tensor", lambda E, h=h, kb=kb, i=i: E.matmul(
                S1[:, 256:512], qt_t[:, h * 128:(h + 1) * 128],
                kt[:, kb + 256 + 128 * i:kb + 512 + 128 * i], start=True, stop=True), nop=False)
            tsc_ = b.op("tensor", lambda E, h=h, kb=kb, i=i: E.matmul(
                S2[:, 0:384], qt_t[:, h * 128:(h + 1) * 128],
                kt[:, kb + 512 + 128 * i:kb + 896 + 128 * i], start=True, stop=True),
                nop=False, inc=True)
            b.wait("vector", tsc_)
            b.op("vector", lambda E: E.reduce_max(small[:, 0:1], S1[:], axis=AX))
            b.fop("vector", lambda E: E.reduce_max(small[:, 1:2], S2[:], axis=AX))
            b.op("vector", lambda E: E.tensor_tensor(small[:, 2:3], small[:, 0:1],
                                                     small[:, 1:2], op=OP.max))
            b.op("vector", lambda E: E.tensor_scalar(small[:, 3:4], small[:, 2:3], -1.0,
                                                     None, op0=OP.mult))
            b.waits("vector", [state["sm_free"]])
            b.op("vector", lambda E: E.scalar_tensor_tensor(
                sm[:, 0:512], in0=S1[:], scalar=small[:, 3:4], in1=mask_t[:, 0:512],
                op0=OP.add, op1=OP.add))
            tmsk = b.op("vector", lambda E: E.scalar_tensor_tensor(
                sm[:, 512:896], in0=S2[:], scalar=small[:, 3:4], in1=mask_t[:, 512:896],
                op0=OP.add, op1=OP.add), inc="SMK")
            state["s_bank"] = tmsk
            if dbg and i == 0 and h == 0:
                b.wait("gpsimd", tmsk)
                b.dma("gpsimd", lambda E: E.dma_start(out=dbg_out[0][:, 0:896], in_=sm[:]), ch="DBG")
                b.dma("gpsimd", lambda E: E.dma_start(out=dbg_out[7][:, 0:896], in_=kt[:, 0:896]), ch="DBG")
                b.dma("gpsimd", lambda E: E.dma_start(out=dbg_out[7][:, 896:2048], in_=qt_t[:, 0:1152]), ch="DBG")
            b.waits("scalar", [tmsk, state["pt_sb"]])
            b.op("scalar", lambda E: E.activation(
                out=Pt[:, 0:512], in_=sm[:, 0:512], func=AF.Exp, accum_out=small[:, 4:5]))
            texp = b.op("scalar", lambda E: E.activation(
                out=Pt[:, 512:896], in_=sm[:, 512:896], func=AF.Exp,
                accum_out=small[:, 5:6]), inc="EXP")
            state["sm_free"] = texp
            b.wait("vector", texp)
            b.op("vector", lambda E: E.tensor_tensor(small[:, 6:7], small[:, 4:5],
                                                     small[:, 5:6], op=OP.add))
            b.fop("vector", lambda E: E.reciprocal(small[:, 7:8], small[:, 6:7]))
            tnorm = b.op("vector", lambda E: E.tensor_scalar(
                Pt[:], Pt[:], small[:, 7:8], None, op0=OP.mult), inc="PNM")
            if dbg and i == 0 and h == 0:
                b.wait("gpsimd", tnorm)
                b.dma("gpsimd", lambda E: E.dma_start(out=dbg_out[1][:, 0:896], in_=Pt[:]), ch="DBG")
                b.dma("gpsimd", lambda E: E.dma_start(out=dbg_out[2][:, 0:8], in_=small[:, 0:8]), ch="DBG")
            b.waits("tensor", [tnorm, state["pt_bank"]])
            ttr = None
            for jblk in range(7):
                ttr = b.op("tensor", lambda E, jblk=jblk: E.transpose(
                    PTp[:, jblk * 128:(jblk + 1) * 128], Pt[:, jblk * 128:(jblk + 1) * 128],
                    idb[:]), nop=False, inc=(True if jblk == 6 else None))
            state["pt_sb"] = ttr
            b.waits("vector", [ttr, state["pts_free"]])
            tevp = b.op("vector", lambda E: E.tensor_copy(out=PTs[:], in_=PTp[:]), inc="PTE")
            state["pt_bank"] = tevp
            b.waits("tensor", [tevp, state["oht_bank"]])
            tav = None
            for jblk in range(7):
                vblk = jblk if jblk < 2 else (i + jblk)
                tav = b.op("tensor", lambda E, jblk=jblk, vblk=vblk, kvh=kvh: E.matmul(
                    OHT[:], vt[:, vblk * 512 + kvh * 128:vblk * 512 + (kvh + 1) * 128],
                    PTs[:, jblk * 128:(jblk + 1) * 128],
                    start=(jblk == 0), stop=(jblk == 6)), nop=False,
                    inc=(True if jblk == 6 else None))
            state["pts_free"] = tav
            b.waits("scalar", [tav, state["outT_free"]])
            toh = b.op("scalar", lambda E, h=h: E.copy(
                out=outT[:, h * 128:(h + 1) * 128], in_=OHT[:]), inc="OHE")
            state["oht_bank"] = toh
            if dbg and i == 0 and h == 0:
                b.wait("gpsimd", toh)
                b.dma("gpsimd", lambda E: E.dma_start(out=dbg_out[4][:, 0:128], in_=outT[:, 0:128]), ch="DBG")
                b.dma("gpsimd", lambda E: E.dma_start(out=dbg_out[3][:, 0:896], in_=PTs[:]), ch="DBG")
        b.waits("tensor", [toh])
        b.waits("vector", [state["xaf_free"]])
        ybank_tok = state.get("ybank_tok", [None, None])
        tyev = None
        for nc4 in range(4):
            yb = nc4 % 2
            b.waits("tensor", [ybank_tok[yb]])
            two = None
            for f in range(16):
                two = b.op("tensor", lambda E, nc4=nc4, f=f: E.matmul(
                    Yp[nc4 % 2][:],
                    outT[:, f * 128:(f + 1) * 128],
                    wob[:, f * 2048 + nc4 * 512:f * 2048 + (nc4 + 1) * 512],
                    start=(f == 0), stop=(f == 15)), nop=False,
                    inc=(True if f == 15 else None))
            b.waits("vector", [two])
            tyev = b.op("vector", lambda E, nc4=nc4, yb=yb: E.scalar_tensor_tensor(
                x1[:, nc4 * 512:(nc4 + 1) * 512], in0=Yp[yb][:], scalar=0.0,
                in1=x_t[:, nc4 * 512:(nc4 + 1) * 512],
                op0=OP.add, op1=OP.add), inc=f"YB{yb}")
            ybank_tok[yb] = tyev
        state["ybank_tok"] = ybank_tok
        state["outT_free"] = two
        b.fop("vector", lambda E: E.reduce_sum(small[:, 8:9], x1[:], axis=AX))
        b.op("vector", lambda E: E.tensor_scalar(
            small[:, 10:11], small[:, 8:9], 1.0 / D, None, op0=OP.mult))
        b.wait("scalar", tyev)
        tsq = b.op("scalar", lambda E: E.activation(
            out=x_t[:], in_=x1[:], func=AF.Square, accum_out=small[:, 9:10]), inc="LSQ")
        state["y_bank"] = tsq
        if dbg and i == 0:
            b.wait("gpsimd", tsq)
            b.dma("gpsimd", lambda E: E.dma_start(out=dbg_out[5][:, :], in_=outT[:]), ch="DBG")
            b.dma("gpsimd", lambda E: E.dma_start(out=dbg_out[6][:, :], in_=x1[:]), ch="DBG")
        b.wait("vector", tsq)
        b.op("vector", lambda E: E.tensor_scalar(
            small[:, 11:12], small[:, 9:10], 1.0 / D, None, op0=OP.mult))
        b.op("vector", lambda E: E.tensor_tensor(
            small[:, 9:10], small[:, 10:11], small[:, 10:11], op=OP.mult))
        b.op("vector", lambda E: E.tensor_tensor(
            small[:, 11:12], small[:, 11:12], small[:, 9:10], op=OP.subtract))
        b.op("vector", lambda E: E.tensor_scalar(
            small[:, 11:12], small[:, 11:12], EPS, None, op0=OP.add))
        trec = b.fop("vector", lambda E: E.reciprocal(small[:, 12:13], small[:, 11:12]),
                     inc="LRC")
        b.wait("scalar", trec)
        tst_ = b.op("scalar", lambda E: E.activation(
            out=small[:, 13:14], in_=small[:, 12:13], func=AF.Sqrt), inc="LSR")
        b.wait("vector", tst_)
        txa = b.op("vector", lambda E: E.tensor_scalar(
            xa_f[:], x1[:], small[:, 10:11], small[:, 13:14],
            op0=OP.subtract, op1=OP.mult), inc="LXA")
        b.waits("scalar", [txa, state["xab_free"]])
        txab = b.op("scalar", lambda E: E.copy(out=xab_t[:], in_=xa_f[:]), inc="LXB")
        b.waits("sync", [txa, txab])
        xch = "XSTa" if (i // 4) % 2 == 0 else "XSTb"
        xa_store[i] = b.dma("sync", lambda E, i=i: E.dma_start(
            out=xa_dram[i * 128:(i + 1) * 128, :], in_=xa_f[:]), ch=xch)
        xab_store[i] = b.dma("sync", lambda E, i=i: E.dma_start(
            out=xab_dram[i * 128:(i + 1) * 128, :], in_=xab_t[:]), ch=xch)
        state["xaf_free"] = xa_store[i]
        state["xab_free"] = xab_store[i]
        if i + 1 < NT:
            emit_tile_loads(i + 1)

        if stage in ("full", "route", "p4") and i % 4 == 3:
            emit_gate_quarter(i // 4)
            for jj in range(4):
                emit_routing_tile(i - 3 + jj)

    if stage == "route":
        slotdump = dt("slotdump", [128, 128], I32, kind="ExternalOutput").ap()
        rdump = dt("rdump", [6, 128, 128], F32, kind="ExternalOutput").ap()
        b.waits("gpsimd", [b.last["vector"]])
        b.dma("gpsimd", lambda E: E.dma_start(out=rdump[0][0:32, :], in_=rdbg[0][:]), ch="DBG")
        b.dma("gpsimd", lambda E: E.dma_start(out=rdump[1][0:32, :], in_=rdbg[1][:]), ch="DBG")
        b.dma("gpsimd", lambda E: E.dma_start(out=rdump[2][:, 0:32], in_=rdbg[2][:]), ch="DBG")
        b.dma("gpsimd", lambda E: E.dma_start(out=rdump[3][:, 0:32], in_=rdbg[3][:]), ch="DBG")
        b.dma("gpsimd", lambda E: E.dma_start(out=rdump[4][:, 0:8], in_=rdbg[4][:]), ch="DBG")
        b.dma("gpsimd", lambda E: E.dma_start(out=rdump[5][:, 0:8], in_=rdbg[5][:]), ch="DBG")
        b.waits("gpsimd", [t for t in (pay_tok[i] for i in range(NT)) if t is not None])
        b.barrier(ENGS)
        tlast = b.dma("gpsimd", lambda E: E.dma_start(out=slotdump[:], in_=slot8i[:]), ch="DBG")
        for i in range(NT):
            tlast = b.dma("gpsimd", lambda E, i=i: E.dma_start(
                out=y_out[i * 128:(i + 1) * 128, :],
                in_=xa_dram[i * 128:(i + 1) * 128, :]), ch="DBG")
        b.wait("sync", tlast)
        finish(nc, b)
        pb.close()
        pab.close()
        top.close()
        return nc

    if stage == "attn":
        b.barrier(ENGS)
        tlast = None
        for i in range(NT):
            tlast = b.dma("gpsimd", lambda E, i=i: E.dma_start(
                out=y_out[i * 128:(i + 1) * 128, :],
                in_=xa_dram[i * 128:(i + 1) * 128, :]), ch="DBG")
        b.wait("sync", tlast)
        finish(nc, b)
        pb.close()
        pab.close()
        top.close()
        return nc

    b.barrier(ENGS)
    pb.close()
    pab.close()

    pc = ExitStack()
    wgb = SB(pc, "wgb", [128, 2 * 16 * INTER], BF16)
    WUB = 16 * INTER
    wdb = SB(pc, "wdb", [128, 6 * 2048], BF16)
    xgt = [SB(pc, f"xgt{j}", [128, 16 * CAP], BF16) for j in range(2)]
    wc_b = [SB(pc, f"wc_b{j}", [128, 10], BF16) for j in range(2)]
    wcol = [SB(pc, f"wcol{j}", [128, 5], F32) for j in range(2)]
    hT = SB(pc, "hT", [128, 6 * CAP], BF16)
    hs = SB(pc, "hs", [128, CAP], F32)
    ysb = [SB(pc, f"ysb{j}", [128, D], BF16) for j in range(2)]
    Gp = PS(pc, "Gp", [128, CAP])
    Up = PS(pc, "Up", [128, CAP])

    wg_ready = [None] * NE
    wd_ready = [None] * NE
    ffn1_done = [None] * NE
    ffn2_done = [None] * NE
    xgt_ready = [None] * NE
    wcol_ok = [None] * NE
    ysb_tok = [None, None]
    sil_tok = None
    hmu_tok = None

    def emit_weight_load(e, kind):
        waitc = None
        if kind == "gu" and e > 0:
            waitc = [ffn1_done[e - 1]]
        if kind == "d" and e > 0:
            waitc = [ffn2_done[e - 1]]
        toks = []
        if kind == "gu":
            for src, base in ((wg_in, 0), (wu_in, WUB)):
                for dmc in range(16):
                    toks.append(cast_chunk(
                        src[e, dmc * 128:(dmc + 1) * 128, :],
                        wgb[:, base + dmc * INTER:base + (dmc + 1) * INTER],
                        INTER, "vector" if dmc % 2 == 0 else "scalar",
                        extra_waits=(waitc if waitc else [])))
            wg_ready[e] = toks[-2:]
        else:
            for ic in range(6):
                toks.append(cast_chunk(
                    wd_in[e, ic * 128:(ic + 1) * 128, :],
                    wdb[:, ic * 2048:(ic + 1) * 2048],
                    2048, "vector" if ic % 2 == 0 else "scalar",
                    extra_waits=(waitc if waitc else [])))
            wd_ready[e] = toks[-2:]

    def emit_xgt_load(e):
        bj = e % 2
        if e >= 2:
            b.waits("scalar", [ffn1_done[e - 2]])
        tch = "TRX4a" if bj == 0 else "TRX4b"
        tt = None
        for dmc in range(16):
            tt = b.dma("scalar", lambda E, e=e, dmc=dmc, bj=bj: E.dma_start_transpose(
                out=xgt[bj][:, dmc * CAP:(dmc + 1) * CAP],
                in_=xsorted[e * CAP:(e + 1) * CAP, dmc * 128:(dmc + 1) * 128]), ch=tch)
        xgt_ready[e] = tt
        tw = []
        for tc5 in range(5):
            tw.append(b.dma("scalar", lambda E, e=e, tc5=tc5, bj=bj: E.dma_start(
                out=wc_b[bj][:, tc5 * 2:(tc5 + 1) * 2],
                in_=xsorted[e * CAP + tc5 * 128:e * CAP + (tc5 + 1) * 128,
                            WCOL + 2 * e:WCOL + 2 * e + 2]), ch=tch))
        b.waits("vector", tw + ([ffn2_done[e - 2]] if e >= 2 else []))
        tv = None
        for tc5 in range(5):
            tv = b.op("vector", lambda E, tc5=tc5, bj=bj: E.tensor_tensor(
                wcol[bj][:, tc5:tc5 + 1], wc_b[bj][:, tc5 * 2:tc5 * 2 + 1],
                wc_b[bj][:, tc5 * 2 + 1:tc5 * 2 + 2], op=OP.add),
                inc=("WCL" if tc5 == 4 else None))
        wcol_ok[e] = tv

    scat_all = [pay_tok[i] for i in range(NT)]
    b.waits("scalar", scat_all)
    b.waits("sync", scat_all)
    emit_weight_load(0, "gu")
    emit_xgt_load(0)
    emit_weight_load(0, "d")

    for e in range(NE):
        bj = e % 2
        b.waits("tensor", wg_ready[e] + [xgt_ready[e]])
        for ic in range(6):
            b.waits("tensor", [sil_tok])
            tgm = None
            for ncc, w in ((0, 512), (512, 128)):
                for dmc in range(16):
                    tgm = b.op("tensor", lambda E, ic=ic, ncc=ncc, w=w, dmc=dmc, bj=bj: E.matmul(
                        Gp[:, ncc:ncc + w],
                        wgb[:, dmc * INTER + ic * 128:dmc * INTER + (ic + 1) * 128],
                        xgt[bj][:, dmc * CAP + ncc:dmc * CAP + ncc + w],
                        start=(dmc == 0), stop=(dmc == 15)), nop=False,
                        inc=(True if (ncc, dmc) == (512, 15) else None))
            b.waits("tensor", [hmu_tok])
            tum = None
            for ncc, w in ((0, 512), (512, 128)):
                for dmc in range(16):
                    tum = b.op("tensor", lambda E, ic=ic, ncc=ncc, w=w, dmc=dmc, bj=bj: E.matmul(
                        Up[:, ncc:ncc + w],
                        wgb[:, WUB + dmc * INTER + ic * 128:WUB + dmc * INTER + (ic + 1) * 128],
                        xgt[bj][:, dmc * CAP + ncc:dmc * CAP + ncc + w],
                        start=(dmc == 0), stop=(dmc == 15)), nop=False,
                        inc=(True if (ncc, dmc) == (512, 15) else None))
            b.waits("scalar", [tgm, hmu_tok])
            tsl = b.op("scalar", lambda E: E.activation(
                out=hs[:], in_=Gp[:], func=AF.Silu), inc="SIL")
            sil_tok = tsl
            b.waits("vector", [tsl, tum])
            tmu = b.op("vector", lambda E, ic=ic: E.tensor_tensor(
                hT[:, ic * CAP:(ic + 1) * CAP], hs[:], Up[:], op=OP.mult), inc="HMU")
            hmu_tok = tmu
        ffn1_done[e] = hmu_tok
        if e + 1 < NE:
            emit_weight_load(e + 1, "gu")
            emit_xgt_load(e + 1)
        b.waits("tensor", wd_ready[e] + [ffn1_done[e], wcol_ok[e]])
        tym_last = None
        ychunk_tok = [None, None]
        for tc5 in range(5):
            yb2 = tc5 % 2
            b.waits("tensor", [ysb_tok[yb2]])
            tyv = None
            for nc4 in range(4):
                yq = nc4 % 2
                b.waits("tensor", [ychunk_tok[yq]])
                tym = None
                for ic in range(6):
                    tym = b.op("tensor", lambda E, nc4=nc4, ic=ic, tc5=tc5: E.matmul(
                        Yp[nc4 % 2][:],
                        hT[:, ic * CAP + tc5 * 128:ic * CAP + (tc5 + 1) * 128],
                        wdb[:, ic * 2048 + nc4 * 512:ic * 2048 + (nc4 + 1) * 512],
                        start=(ic == 0), stop=(ic == 5)), nop=False,
                        inc=(True if ic == 5 else None))
                tym_last = tym
                b.wait("vector", tym)
                tyv = b.op("vector", lambda E, yb2=yb2, yq=yq, nc4=nc4, tc5=tc5, bj=bj: E.tensor_scalar(
                    ysb[yb2][:, nc4 * 512:(nc4 + 1) * 512], Yp[yq][:],
                    wcol[bj][:, tc5:tc5 + 1], None, op0=OP.mult), inc=f"YC{yq}")
                ychunk_tok[yq] = tyv
            b.wait("sync", tyv)
            tst2 = b.dma("sync", lambda E, e=e, tc5=tc5, yb2=yb2: E.dma_start(
                out=ysorted[e * CAP + tc5 * 128:e * CAP + (tc5 + 1) * 128, :],
                in_=ysb[yb2][:]), ch="YSTa" if yb2 == 0 else "YSTb")
            ysb_tok[yb2] = tst2
        ffn2_done[e] = tym_last
        if e + 1 < NE:
            emit_weight_load(e + 1, "d")

    b.barrier(ENGS)
    if stage == "p4":
        tlast = None
        for i in range(NT):
            tlast = b.dma("gpsimd", lambda E, i=i: E.dma_start(
                out=y_out[i * 128:(i + 1) * 128, :].rearrange("p (h c) -> p h c", h=1),
                in_=ysorted[i * 128:(i + 1) * 128, :].rearrange("p (h c) -> p h c", h=1)), ch="DBG")
        b.wait("sync", tlast)
        finish(nc, b)
        pc.close()
        top.close()
        return nc
    pc.close()

    pd = ExitStack()
    gath = [SB(pd, f"gath{j}", [128, D], BF16) for j in range(8)]
    moe = SB(pd, "moe", [128, D], F32)
    t01 = SB(pd, "t01", [128, D], F32)
    t23 = SB(pd, "t23", [128, D], F32)
    t45 = SB(pd, "t45", [128, D], F32)
    t67 = SB(pd, "t67", [128, D], F32)
    xat = SB(pd, "xat", [128, D], F32)
    outf = SB(pd, "outf", [128, D], F32)

    gth_tok = {}

    def emit_gathers(i):
        toks = []
        for k in range(8):
            toks.append(b.dma("gpsimd", lambda E, i=i, k=k: E.indirect_dma_start(
                out=gath[k][:], out_offset=None,
                in_=ysorted[:, :],
                in_offset=IndirectOffsetOnAxis(slot8i[:, i * 8 + k:i * 8 + k + 1], 0)),
                ch="GTH"))
        gth_tok[i] = toks

    emit_gathers(0)
    out_last = None
    prev_xat = None
    prev_out = None
    for i in range(NT):
        b.waits("vector", gth_tok[i])
        if prev_xat is not None:
            b.waits("sync", [prev_xat])
        txl = b.dma("sync", lambda E, i=i: E.dma_start(
            out=xat[:], in_=xa_dram[i * 128:(i + 1) * 128, :]), ch="XAT")
        b.op("vector", lambda E: E.tensor_tensor(t01[:], gath[0][:], gath[1][:], op=OP.add))
        b.op("vector", lambda E: E.tensor_tensor(t23[:], gath[2][:], gath[3][:], op=OP.add))
        b.op("vector", lambda E: E.tensor_tensor(t45[:], gath[4][:], gath[5][:], op=OP.add))
        b.op("vector", lambda E: E.tensor_tensor(t67[:], gath[6][:], gath[7][:], op=OP.add))
        b.op("vector", lambda E: E.tensor_tensor(t01[:], t01[:], t23[:], op=OP.add))
        b.op("vector", lambda E: E.tensor_tensor(t45[:], t45[:], t67[:], op=OP.add))
        tgsum = b.op("vector", lambda E: E.tensor_tensor(moe[:], t01[:], t45[:], op=OP.add),
                     inc="GSM")
        if i + 1 < NT:
            b.wait("gpsimd", tgsum)
            emit_gathers(i + 1)
        b.wait("vector", txl)
        b.op("vector", lambda E: E.tensor_tensor(
            x1[:], moe[:], xat[:], op=OP.add))
        b.fop("vector", lambda E: E.reduce_sum(small[:, 8:9], x1[:], axis=AX))
        tv1 = b.op("vector", lambda E: E.tensor_scalar(
            small[:, 10:11], small[:, 8:9], 1.0 / D, None, op0=OP.mult), inc="LV1")
        b.wait("scalar", tv1)
        tsq = b.op("scalar", lambda E: E.activation(
            out=xat[:], in_=x1[:], func=AF.Square, accum_out=small[:, 9:10]), inc="LSQ")
        prev_xat = tsq
        b.wait("vector", tsq)
        b.op("vector", lambda E: E.tensor_scalar(
            small[:, 11:12], small[:, 9:10], 1.0 / D, None, op0=OP.mult))
        b.op("vector", lambda E: E.tensor_tensor(
            small[:, 9:10], small[:, 10:11], small[:, 10:11], op=OP.mult))
        b.op("vector", lambda E: E.tensor_tensor(
            small[:, 11:12], small[:, 11:12], small[:, 9:10], op=OP.subtract))
        b.op("vector", lambda E: E.tensor_scalar(
            small[:, 11:12], small[:, 11:12], EPS, None, op0=OP.add))
        trec = b.fop("vector", lambda E: E.reciprocal(small[:, 12:13], small[:, 11:12]),
                     inc="LRC")
        b.wait("scalar", trec)
        tst_ = b.op("scalar", lambda E: E.activation(
            out=small[:, 13:14], in_=small[:, 12:13], func=AF.Sqrt), inc="LSR")
        b.waits("vector", [tst_] + ([prev_out] if prev_out else []))
        tof = b.op("vector", lambda E: E.tensor_scalar(
            outf[:], x1[:], small[:, 10:11], small[:, 13:14],
            op0=OP.subtract, op1=OP.mult), inc="OF")
        b.wait("sync", tof)
        out_last = b.dma("sync", lambda E, i=i: E.dma_start(
            out=y_out[i * 128:(i + 1) * 128, :], in_=outf[:]), ch="OUT")
        prev_out = out_last

    b.wait("sync", out_last)
    b.wait("gpsimd", out_last)
    if dbg:
        tdb = b.dma("gpsimd", lambda E: E.dma_start(out=sdump[:], in_=slot8i[:]), ch="DBG")
        for ch_ in range(6):
            r0, r1 = ch_ * 3584, min((ch_ + 1) * 3584, ROWS + 1)
            tdb = b.dma("gpsimd", lambda E, r0=r0, r1=r1: E.dma_start(
                out=wdump[r0:r1, :], in_=xsorted[r0:r1, 2048:2112]), ch="DBG")
            tdb = b.dma("gpsimd", lambda E, r0=r0, r1=r1: E.dma_start(
                out=ydump[r0:r1, :], in_=ysorted[r0:r1, 0:8]), ch="DBG")
        b.wait("sync", tdb)
    finish(nc, b)
    pd.close()
    top.close()
    return nc


def finish(nc, b):
    with nc.Block() as block:
        for eng in ENGS:
            ops = list(b.q[eng])

            def runner(E, ops=ops):
                for fn in ops:
                    fn(E)
            getattr(block, eng)(runner)


def _rope_tables():
    inv = 1.0 / (10000.0 ** (np.arange(0, DQK, 2)[: DQK // 2].astype(np.float32) / DQK))
    t = np.arange(S, dtype=np.float32)
    ang = np.outer(t, inv)
    return np.cos(ang), np.sin(ang)


def _perm_cols(w, nheads):
    perm = np.concatenate([np.arange(0, 128, 2), np.arange(1, 128, 2)])
    w2 = w.reshape(w.shape[0], nheads, 128)[:, :, perm]
    return np.ascontiguousarray(w2.reshape(w.shape[0], nheads * 128))


def _mask_for_core(qo):
    m = np.full((NT, 128, 896), NEG, np.float32)
    for i in range(NT):
        q = qo + i * 128 + np.arange(128)[:, None]
        band_lo = qo - 512 + i * 128

        def allowed(j):
            band = (j <= q) & (j > q - WIN)
            front = (j < FRONT) & (j <= q - WIN)
            return band | front
        jf = np.arange(256)[None, :]
        in_band = (jf >= band_lo) & (jf < band_lo + 640)
        m[i, :, 0:256] = np.where(allowed(jf) & ~in_band, 0.0, NEG)
        jb = band_lo + np.arange(640)[None, :]
        m[i, :, 256:896] = np.where(allowed(jb) & (jb >= 0), 0.0, NEG)
    return m


def make_in_maps(L0_attn_x, L0_attn_wq, L0_attn_wk, L0_attn_wv, L0_attn_wo,
                 L0_ffn_gate_w, L0_ffn_w_gate, L0_ffn_w_up, L0_ffn_w_down):
    x = np.asarray(L0_attn_x, np.float32)
    wq = _perm_cols(np.asarray(L0_attn_wq, np.float32), NH)
    wk = _perm_cols(np.asarray(L0_attn_wk, np.float32), NKV)
    wv = np.ascontiguousarray(np.asarray(L0_attn_wv, np.float32))
    wo = np.ascontiguousarray(np.asarray(L0_attn_wo, np.float32))
    gw = np.ascontiguousarray(np.asarray(L0_ffn_gate_w, np.float32))
    wg = np.ascontiguousarray(np.asarray(L0_ffn_w_gate, np.float32))
    wu = np.ascontiguousarray(np.asarray(L0_ffn_w_up, np.float32))
    wdn = np.ascontiguousarray(np.asarray(L0_ffn_w_down, np.float32))
    cos, sin = _rope_tables()
    idf = np.eye(128, dtype=np.float32)
    idb = idf.astype(ml_dtypes.bfloat16)
    scale = np.float32(1.0 / np.sqrt(DQK))

    in_maps = []
    for c in range(8):
        bb, qo = c // 4, (c % 4) * T
        xb = x[bb]
        xkv = np.zeros((XKV, D), np.float32)
        xkv[0:FRONT] = xb[0:FRONT]
        lo = qo - WIN
        src0 = max(0, lo)
        xkv[FRONT + (src0 - lo):FRONT + WIN + T] = xb[src0:qo + T]
        qpos = np.arange(qo, qo + T)
        kvpos = np.concatenate([np.arange(FRONT),
                                np.clip(np.arange(lo, qo + T), 0, S - 1)])
        in_maps.append({
            "xkv_in": np.ascontiguousarray(xkv),
            "wq_in": wq, "wk_in": wk, "wv_in": wv, "wo_in": wo,
            "gate_in": gw, "wg_in": wg, "wu_in": wu, "wd_in": wdn,
            "cq_in": np.ascontiguousarray(cos[qpos] * scale),
            "sq_in": np.ascontiguousarray(sin[qpos] * scale),
            "ckv_in": np.ascontiguousarray(cos[kvpos]),
            "skv_in": np.ascontiguousarray(sin[kvpos]),
            "mask_in": _mask_for_core(qo),
            "idf_in": idf, "idb_in": idb,
        })
    return in_maps


def kernel(L0_attn_x, L0_attn_wq, L0_attn_wk, L0_attn_wv, L0_attn_wo,
           L0_ffn_gate_w, L0_ffn_w_gate, L0_ffn_w_up, L0_ffn_w_down):
    nc = build_nc(stage="full")
    in_maps = make_in_maps(L0_attn_x, L0_attn_wq, L0_attn_wk, L0_attn_wv, L0_attn_wo,
                           L0_ffn_gate_w, L0_ffn_w_gate, L0_ffn_w_up, L0_ffn_w_down)
    res = run_bass_kernel_spmd(nc, in_maps, core_ids=list(range(8)))
    out = np.empty((BATCH, S, D), np.float32)
    for c in range(8):
        bb, qo = c // 4, (c % 4) * T
        out[bb, qo:qo + T] = res.results[c]["y_out"]
    return out


# version 03
# speedup vs baseline: 1.0391x; 1.0391x over previous
import numpy as np
import ml_dtypes
from contextlib import ExitStack

import concourse.bass as bass
import concourse.mybir as mybir
from concourse.bass_utils import run_bass_kernel_spmd
from concourse.bass_core_rust import IndirectOffsetOnAxis

F32 = mybir.dt.float32
BF16 = mybir.dt.bfloat16
I32 = mybir.dt.int32
AF = mybir.ActivationFunctionType
OP = mybir.AluOpType
AX = mybir.AxisListType.X

S, D, BATCH = 8192, 2048, 2
T = 2048
NT = 16
NKV, NH, DQK = 4, 16, 128
WIN, FRONT = 512, 256
XKV = FRONT + WIN + T
NE, TOPK, INTER = 32, 8, 768
CAP = 640
ROWS = NE * CAP
PAYW = 2176
WCOL = 2048
NEG = -1e30
BIGC = 1e9
EPS = 1e-5

ENGS = ("sync", "scalar", "vector", "tensor", "gpsimd")
SEMS = ("T", "BV", "BS", "BG",
        "SG0", "SG1", "IOT", "PP0", "PP1", "PP2", "RP", "PKB0", "PKB1",
        "SMK", "EXP", "PNM", "PTE", "OHE", "YB0", "YB1",
        "LSQ", "LRC", "LSR", "LXA", "LXB", "LV1",
        "GLG", "GL2", "RNG", "RXP", "RWV", "RCA", "RS8", "RPW",
        "SIL", "HMU", "WCL", "YC0", "YC1", "GSM", "OF", "KTD", "VTD", "VFN", "SFN",
        "XEV")
DMA_CHANNELS = ("MISC", "XKLa", "XKLb", "XKSa", "XKSb", "WSTa", "WSTb", "TL",
                "PLDa", "PLDb", "QST", "XSTa", "XSTb", "YSTa", "YSTb", "XAT",
                "OUT", "TRXa", "TRXb", "TRXG", "TRX4a", "TRX4b", "SCTa", "SCTb",
                "GTH", "DBG")
BAR_SEM = {"tensor": "T", "vector": "BV", "scalar": "BS", "gpsimd": "BG"}


FENCE_ALL = True


class Bld:
    def __init__(self):
        self.q = {e: [] for e in ENGS}
        self.cnt = {s: 0 for s in SEMS}
        self.cnt.update({c: 0 for c in DMA_CHANNELS})
        self.cnt.update({c: 0 for c in DMA_CHANNELS})
        self.sem = {}
        self.last = {e: None for e in ENGS}

    def fop(self, eng, fn, inc=None):
        if FENCE_ALL and eng in ("vector", "scalar"):
            return self.op(eng, fn, inc=inc)
        tok = self.op(eng, fn, inc=inc)
        if tok is not None:
            self.q[eng].append(lambda E, k=tok[0], v=tok[1]: E.wait_ge(self.sem[k], v))
            return tok
        fch = "VFN" if eng == "vector" else "SFN"
        self.cnt[fch] += 1
        v = self.cnt[fch]
        real = self.q[eng].pop()
        self.q[eng].append(lambda E, real=real, fch=fch: real(E).then_inc(self.sem[fch], 1))
        self.q[eng].append(lambda E, fch=fch, v=v: E.wait_ge(self.sem[fch], v))
        return tok

    def op(self, eng, fn, inc=None, nop=True):
        if FENCE_ALL and eng in ("vector", "scalar"):
            tok = self._op_raw(eng, fn, inc=inc, nop=False)
            if tok is not None:
                self.q[eng].append(lambda E, k=tok[0], v=tok[1]: E.wait_ge(self.sem[k], v))
                return tok
            fch = "VFN" if eng == "vector" else "SFN"
            self.cnt[fch] += 1
            v = self.cnt[fch]
            real = self.q[eng].pop()
            self.q[eng].append(lambda E, real=real, fch=fch: real(E).then_inc(self.sem[fch], 1))
            self.q[eng].append(lambda E, fch=fch, v=v: E.wait_ge(self.sem[fch], v))
            return None
        return self._op_raw(eng, fn, inc=inc, nop=nop)

    def _op_raw(self, eng, fn, inc=None, nop=True):
        if nop and eng == "vector":
            self.q[eng].append(lambda E: E.engine_nop())
        elif nop and eng in ("scalar", "gpsimd"):
            self.q[eng].append(lambda E: E.nop())
        if inc:
            k = "T" if (inc is True and eng == "tensor") else inc
            assert k is not True, "non-tensor inc needs a channel name"
            self.cnt[k] += 1
            v = self.cnt[k]
            self.q[eng].append(lambda E, fn=fn, k=k: fn(E).then_inc(self.sem[k], 1))
            tok = (k, v)
            self.last[eng] = tok
            return tok
        self.q[eng].append(fn)
        return None

    def mark(self, eng):
        k = BAR_SEM[eng]
        self.cnt[k] += 1
        v = self.cnt[k]
        if eng == "vector":
            self.q[eng].append(lambda E, k=k: E.engine_nop().then_inc(self.sem[k], 1))
        else:
            self.q[eng].append(lambda E, k=k: E.nop().then_inc(self.sem[k], 1))
        tok = (k, v)
        self.last[eng] = tok
        return tok

    def dma(self, eng, fn, ch="MISC"):
        k = ch
        self.cnt[k] += 16
        v = self.cnt[k]
        self.q[eng].append(lambda E, fn=fn, k=k: fn(E).then_inc(self.sem[k], 16))
        tok = (k, v)
        self.last[eng] = tok
        return tok

    def wait(self, eng, tok):
        if tok is None:
            return
        k, v = tok
        self.q[eng].append(lambda E, k=k, v=v: E.wait_ge(self.sem[k], v))

    def waits(self, eng, toks):
        best = {}
        for t in toks:
            if t is None:
                continue
            k, v = t
            best[k] = max(best.get(k, 0), v)
        for k, v in best.items():
            self.q[eng].append(lambda E, k=k, v=v: E.wait_ge(self.sem[k], v))

    def barrier(self, engines=ENGS):
        toks = []
        for ch in SEMS + DMA_CHANNELS:
            if self.cnt[ch] > 0:
                toks.append((ch, self.cnt[ch]))
        for e in engines:
            self.waits(e, toks)


def build_nc(stage="full", races=True, dbg=False, rparts=4):
    nc = bass.Bass("TRN2", target_bir_lowering=False,
                   detect_race_conditions=races)
    top = ExitStack()

    dt = nc.dram_tensor
    xkv_in = dt("xkv_in", [XKV, D], F32, kind="ExternalInput").ap()
    wq_in = dt("wq_in", [D, D], F32, kind="ExternalInput").ap()
    wk_in = dt("wk_in", [D, 512], F32, kind="ExternalInput").ap()
    wv_in = dt("wv_in", [D, 512], F32, kind="ExternalInput").ap()
    wo_in = dt("wo_in", [D, D], F32, kind="ExternalInput").ap()
    gate_in = dt("gate_in", [D, NE], F32, kind="ExternalInput").ap()
    wg_in = dt("wg_in", [NE, D, INTER], F32, kind="ExternalInput").ap()
    wu_in = dt("wu_in", [NE, D, INTER], F32, kind="ExternalInput").ap()
    wd_in = dt("wd_in", [NE, INTER, D], F32, kind="ExternalInput").ap()
    cq_in = dt("cq_in", [T, 64], F32, kind="ExternalInput").ap()
    sq_in = dt("sq_in", [T, 64], F32, kind="ExternalInput").ap()
    ckv_in = dt("ckv_in", [XKV, 64], F32, kind="ExternalInput").ap()
    skv_in = dt("skv_in", [XKV, 64], F32, kind="ExternalInput").ap()
    mask_in = dt("mask_in", [NT, 128, 896], F32, kind="ExternalInput").ap()
    idf_in = dt("idf_in", [128, 128], F32, kind="ExternalInput").ap()
    idb_in = dt("idb_in", [128, 128], BF16, kind="ExternalInput").ap()

    y_out = dt("y_out", [T, D], F32, kind="ExternalOutput").ap()
    dbg_out = dt("dbg_out", [8, 128, 2048], F32, kind="ExternalOutput").ap() if dbg else None
    sdump = dt("sdump", [128, 128], I32, kind="ExternalOutput").ap() if dbg else None
    wdump = dt("wdump", [ROWS + 1, 64], BF16, kind="ExternalOutput").ap() if dbg else None
    ydump = dt("ydump", [ROWS + 1, 8], BF16, kind="ExternalOutput").ap() if dbg else None

    xkv_bf = dt("xkv_bf", [XKV, D], BF16, kind="Internal").ap()
    qt_dram = dt("qt_dram", [NT, 16, 128, 128], BF16, kind="Internal").ap()
    xa_dram = dt("xa_dram", [T, D], F32, kind="Internal").ap()
    xab_dram = dt("xab_dram", [T, D], BF16, kind="Internal").ap()
    xsorted = dt("xsorted", [ROWS + 1, PAYW], BF16, kind="Internal").ap()
    ysorted = dt("ysorted", [ROWS + 1, D], BF16, kind="Internal").ap()

    def SB(stk, name, shape, dtype):
        return stk.enter_context(nc.sbuf_tensor(name, shape, dtype))

    def PS(stk, name, shape, dtype=F32):
        return stk.enter_context(nc.psum_tensor(name, shape, dtype))

    idf = SB(top, "idf", [128, 128], F32)
    idb = SB(top, "idb", [128, 128], BF16)
    stage_f = [SB(top, f"stage{j}", [128, 2048], F32) for j in range(2)]
    small = SB(top, "small", [128, 16], F32)
    x1 = SB(top, "x1", [128, D], F32)
    logits = SB(top, "logits", [128, NT * NE], F32)
    rdbg = [SB(top, "rdbg0", [32, 128], F32), SB(top, "rdbg1", [32, 128], F32),
            SB(top, "rdbg2", [128, 32], F32), SB(top, "rdbg3", [128, 32], F32),
            SB(top, "rdbg4", [128, 8], F32), SB(top, "rdbg5", [128, 8], F32)]
    slot8i = SB(top, "slot8i", [128, NT * 8], I32)
    gateb = SB(top, "gateb", [128, 16 * NE], BF16)
    iota32 = SB(top, "iota32", [128, NE], F32)
    iota2048 = SB(top, "iota2048", [128, NE], F32)
    Yp = [PS(top, f"Yp{j}", [128, 512]) for j in range(2)]

    sems = {}
    for sname in SEMS + DMA_CHANNELS:
        sems[sname] = top.enter_context(nc.semaphore(sname))
    b = Bld()
    b.sem = sems

    cast_state = {"i": 0, "hist": {}}

    def cast_chunk(src_ap, dst_ap, cols, eng, extra_waits=()):
        par = cast_state["i"] % 2
        stg = stage_f[par]
        cast_state["i"] += 1
        prev = cast_state["hist"].get(id(stg))
        if prev is not None:
            b.wait("sync", prev)
        tl = b.dma("sync", lambda E, stg=stg, cols=cols, src_ap=src_ap: E.dma_start(
            out=stg[:, 0:cols], in_=src_ap), ch="WSTa" if par == 0 else "WSTb")
        b.wait(eng, tl)
        for w in extra_waits:
            b.wait(eng, w)
        sg = "SG0" if par == 0 else "SG1"
        if eng == "scalar":
            tc = b.op(eng, lambda E, stg=stg, cols=cols, dst_ap=dst_ap: E.copy(
                out=dst_ap, in_=stg[:, 0:cols]), inc=sg)
        else:
            tc = b.op(eng, lambda E, stg=stg, cols=cols, dst_ap=dst_ap: E.tensor_copy(
                out=dst_ap, in_=stg[:, 0:cols]), inc=sg)
        cast_state["hist"][id(stg)] = tc
        return tc

    pa = ExitStack()
    pab = ExitStack()
    kt = SB(pab, "kt", [128, NKV * XKV], BF16)
    vt = SB(pab, "vt", [128, 22 * 512], BF16)
    xsl = [SB(pab, "xsl0", [128, 16 * 512], BF16),
           SB(pa, "xsl1", [128, 16 * 512], BF16)]
    cqtm = SB(pa, "cqtm", [128, 16 * 64], F32)
    sqtm = SB(pa, "sqtm", [128, 16 * 64], F32)
    rtmp2 = SB(pa, "rtmp2", [128, 16 * 64], F32)
    qtm = SB(pa, "qtm", [128, 2048], F32)
    qrope = SB(pa, "qrope", [128, 2048], F32)
    qrope_b = SB(pa, "qrope_b", [128, 2048], BF16)
    qtts = SB(pa, "qtts", [128, 2048], BF16)
    pp = [PS(pa, f"pp{j}", [128, 512]) for j in range(3)]
    pkb = [PS(pa, f"pkb{j}", [128, 128], BF16) for j in range(2)]
    pa0 = ExitStack()
    ckvtm = SB(pa0, "ckvtm", [128, 22 * 64], F32)
    skvtm = SB(pa0, "skvtm", [128, 22 * 64], F32)
    ktm = SB(pa0, "ktm", [128, 512], F32)
    krope = SB(pa0, "krope", [128, 512], F32)
    krope_b = SB(pa0, "krope_b", [128, 512], BF16)
    wkb = SB(pa0, "wkb", [128, 16 * 512], BF16)
    wvb = SB(pa0, "wvb", [128, 16 * 512], BF16)

    b.dma("sync", lambda E: E.dma_start(out=idf[:], in_=idf_in[:]))
    b.dma("sync", lambda E: E.dma_start(out=idb[:], in_=idb_in[:]))
    for tl_ in range(16):
        b.dma("sync", lambda E, tl_=tl_: E.dma_start(
            out=cqtm[:, tl_ * 64:(tl_ + 1) * 64], in_=cq_in[tl_ * 128:(tl_ + 1) * 128, :]))
        b.dma("sync", lambda E, tl_=tl_: E.dma_start(
            out=sqtm[:, tl_ * 64:(tl_ + 1) * 64], in_=sq_in[tl_ * 128:(tl_ + 1) * 128, :]))
    for tl_ in range(22):
        b.dma("sync", lambda E, tl_=tl_: E.dma_start(
            out=ckvtm[:, tl_ * 64:(tl_ + 1) * 64], in_=ckv_in[tl_ * 128:(tl_ + 1) * 128, :]))
        b.dma("sync", lambda E, tl_=tl_: E.dma_start(
            out=skvtm[:, tl_ * 64:(tl_ + 1) * 64], in_=skv_in[tl_ * 128:(tl_ + 1) * 128, :]))

    xkv_done = []
    for chk in range(22):
        buf = xsl[chk % 2]
        dst = buf[:, 0:2048]
        if chk >= 2:
            b.wait("gpsimd", xkv_done[chk - 2])
        tl = b.dma("gpsimd", lambda E, chk=chk, dst=dst: E.dma_start(
            out=dst, in_=xkv_in[chk * 128:(chk + 1) * 128, :]),
            ch="XKLa" if chk % 2 == 0 else "XKLb")
        b.wait("sync", tl)
        ts_ = b.dma("sync", lambda E, chk=chk, dst=dst: E.dma_start(
            out=xkv_bf[chk * 128:(chk + 1) * 128, :], in_=dst),
            ch="XKSa" if chk % 2 == 0 else "XKSb")
        xkv_done.append(ts_)
    t_xkvbf = [xkv_done[-1], xkv_done[-2]]

    b.op("gpsimd", lambda E: E.iota(iota32[:], pattern=[[1, NE]], base=0,
                                    channel_multiplier=0,
                                    allow_small_or_imprecise_dtypes=True))
    b.op("gpsimd", lambda E: E.iota(iota2048[:], pattern=[[2048, NE]], base=0,
                                    channel_multiplier=0,
                                    allow_small_or_imprecise_dtypes=True), inc="IOT")

    for dmc in range(16):
        cast_chunk(wk_in[dmc * 128:(dmc + 1) * 128, :], wkb[:, dmc * 512:(dmc + 1) * 512],
                   512, "vector")
        cast_chunk(wv_in[dmc * 128:(dmc + 1) * 128, :], wvb[:, dmc * 512:(dmc + 1) * 512],
                   512, "scalar")
    for dmc in range(16):
        cast_chunk(gate_in[dmc * 128:(dmc + 1) * 128, :], gateb[:, dmc * NE:(dmc + 1) * NE],
                   NE, "vector")
    b.barrier(("sync", "vector", "scalar", "gpsimd"))

    def emit_rope_tm(dst, srct, ctab, stab, nh, tmp):
        sr = srct.rearrange("p (h half i) -> p h half i", h=nh, half=2)
        dr = dst.rearrange("p (h half i) -> p h half i", h=nh, half=2)
        tr = tmp.rearrange("p (h i) -> p h i", h=nh)
        cb = ctab.unsqueeze(1).to_broadcast([128, nh, 64])
        sb_ = stab.unsqueeze(1).to_broadcast([128, nh, 64])
        b.op("vector", lambda E: E.tensor_tensor(dr[:, :, 0, :], sr[:, :, 0, :], cb, op=OP.mult))
        b.op("vector", lambda E: E.tensor_tensor(tr[:], sr[:, :, 1, :], sb_, op=OP.mult))
        b.op("vector", lambda E: E.tensor_tensor(dr[:, :, 0, :], dr[:, :, 0, :], tr[:], op=OP.subtract))
        b.op("vector", lambda E: E.tensor_tensor(dr[:, :, 1, :], sr[:, :, 1, :], cb, op=OP.mult))
        b.op("vector", lambda E: E.tensor_tensor(tr[:], sr[:, :, 0, :], sb_, op=OP.mult))
        b.op("vector", lambda E: E.tensor_tensor(
            dr[:, :, 1, :], dr[:, :, 1, :], tr[:], op=OP.add))
        return None

    if stage == "p0":
        b.barrier(ENGS)
        tlast = None
        for i in range(NT):
            tlast = b.dma("gpsimd", lambda E, i=i: E.dma_start(
                out=y_out[i * 128:(i + 1) * 128, :],
                in_=xkv_bf[768 + i * 128:768 + (i + 1) * 128, :]), ch="DBG")
        b.wait("sync", tlast)
        finish(nc, b)
        return nc

    pptok = {}

    def pwait(j, eng):
        t = pptok.get(j)
        if t is not None:
            b.wait(eng, t)

    pidx = 0
    kv_slices = [(i * 512, min(512, XKV - i * 512)) for i in range(6)]
    slice_done = {}
    for si, (t0, w) in enumerate(kv_slices):
        buf = xsl[si % 2]
        b.waits("scalar", t_xkvbf)
        if si >= 2:
            b.wait("scalar", slice_done[si - 2])
        ttr = None
        for dmc in range(16):
            ttr = b.dma("scalar", lambda E, t0=t0, w=w, dmc=dmc, buf=buf: E.dma_start_transpose(
                out=buf[:, dmc * 512:dmc * 512 + w],
                in_=xkv_bf[t0:t0 + w, dmc * 128:(dmc + 1) * 128]),
                ch="TRXa" if si % 2 == 0 else "TRXb")
        b.waits("tensor", [ttr])
        last_use = None
        pkb_tok = [None, None]
        for tt in range(w // 128):
            blk = t0 // 128 + tt
            j = pidx % 3
            pidx += 1
            pwait(j, "tensor")
            tmm = None
            for dmc in range(16):
                tmm = b.op("tensor", lambda E, j=j, dmc=dmc, tt=tt, buf=buf: E.matmul(
                    pp[j][:, 0:512],
                    buf[:, dmc * 512 + tt * 128:dmc * 512 + (tt + 1) * 128],
                    wkb[:, dmc * 512:(dmc + 1) * 512],
                    start=(dmc == 0), stop=(dmc == 15)), nop=False,
                    inc=(True if dmc == 15 else None))
            b.wait("vector", tmm)
            tevk = b.op("vector", lambda E, j=j: E.tensor_copy(out=ktm[:], in_=pp[j][:, 0:512]),
                        inc=f"PP{j}")
            pptok[j] = tevk
            emit_rope_tm(krope[:], ktm[:], ckvtm[:, blk * 64:(blk + 1) * 64],
                         skvtm[:, blk * 64:(blk + 1) * 64], NKV, rtmp2[:, 0:256])
            trope = b.op("vector", lambda E: E.tensor_copy(out=krope_b[:], in_=krope[:]),
                         inc="RP")
            b.waits("tensor", [trope])
            for f in range(NKV):
                pb_ = f % 2
                b.waits("tensor", [pkb_tok[pb_]])
                ttp = b.op("tensor", lambda E, f=f, pb_=pb_: E.transpose(
                    pkb[pb_][:], krope_b[:, f * 128:(f + 1) * 128], idb[:]), inc=True, nop=False)
                b.wait("vector", ttp)
                tevt = b.op("vector", lambda E, f=f, blk=blk, pb_=pb_: E.tensor_copy(
                    out=kt[:, f * XKV + blk * 128:f * XKV + (blk + 1) * 128],
                    in_=pkb[pb_][:]), inc=f"PKB{pb_}")
                pkb_tok[pb_] = tevt
                last_use = tevt
        ntt = w // 128
        for tt in range(ntt):
            j = pidx % 3
            pidx += 1
            pwait(j, "tensor")
            tmm = None
            for dmc in range(16):
                tmm = b.op("tensor", lambda E, j=j, dmc=dmc, tt=tt, buf=buf: E.matmul(
                    pp[j][:, 0:512],
                    buf[:, dmc * 512 + tt * 128:dmc * 512 + (tt + 1) * 128],
                    wvb[:, dmc * 512:(dmc + 1) * 512],
                    start=(dmc == 0), stop=(dmc == 15)), nop=False,
                    inc=(True if dmc == 15 else None))
            blk = t0 // 128 + tt
            b.wait("scalar", tmm)
            tev = b.op("scalar", lambda E, j=j, blk=blk: E.copy(
                out=vt[:, blk * 512:(blk + 1) * 512], in_=pp[j][:, 0:512]), inc=f"PP{j}")
            pptok[j] = tev
            last_use = tev
        slice_done[si] = last_use

    if stage == "p1":
        b.barrier(ENGS)
        tl1 = b.dma("gpsimd", lambda E: E.dma_start(out=y_out[0:128, :], in_=kt[:, 0:2048]), ch="DBG")
        tl2 = b.dma("gpsimd", lambda E: E.dma_start(out=y_out[128:256, :], in_=vt[:, 0:2048]), ch="DBG")
        b.wait("sync", tl2)
        finish(nc, b)
        return nc

    pa0.close()
    wqb = SB(pa, "wqb", [128, 16 * 2048], BF16)
    for dmc in range(16):
        cast_chunk(wq_in[dmc * 128:(dmc + 1) * 128, :],
                   wqb[:, dmc * 2048:(dmc + 1) * 2048], 2048,
                   "vector" if dmc % 2 == 0 else "scalar")
    b.barrier(("sync", "vector", "scalar", "tensor"))

    qst_tok = None
    p2_slfree = {}
    for si in range(4):
        t0 = 768 + si * 512
        buf = xsl[si % 2]
        b.waits("scalar", [p2_slfree.get(si % 2)])
        ttr = None
        for dmc in range(16):
            ttr = b.dma("scalar", lambda E, t0=t0, dmc=dmc, buf=buf: E.dma_start_transpose(
                out=buf[:, dmc * 512:(dmc + 1) * 512],
                in_=xkv_bf[t0:t0 + 512, dmc * 128:(dmc + 1) * 128]),
                ch="TRXa" if si % 2 == 0 else "TRXb")
        b.wait("tensor", ttr)
        pkb_tok = [None, None]
        for tt in range(4):
            tile = si * 4 + tt
            evtoks = []
            for nq in range(4):
                j = pidx % 3
                pidx += 1
                pwait(j, "tensor")
                tmm = None
                for dmc in range(16):
                    tmm = b.op("tensor", lambda E, j=j, dmc=dmc, nq=nq, tt=tt, buf=buf: E.matmul(
                        pp[j][:, 0:512],
                        buf[:, dmc * 512 + tt * 128:dmc * 512 + (tt + 1) * 128],
                        wqb[:, dmc * 2048 + nq * 512:dmc * 2048 + (nq + 1) * 512],
                        start=(dmc == 0), stop=(dmc == 15)), nop=False,
                        inc=(True if dmc == 15 else None))
                b.wait("vector", tmm)
                tcp = b.op("vector", lambda E, j=j, nq=nq: E.tensor_copy(
                    out=qtm[:, nq * 512:(nq + 1) * 512], in_=pp[j][:, 0:512]), inc=f"PP{j}")
                pptok[j] = tcp
                evtoks.append(tcp)
            emit_rope_tm(qrope[:], qtm[:], cqtm[:, tile * 64:(tile + 1) * 64],
                         sqtm[:, tile * 64:(tile + 1) * 64], NH, rtmp2[:])
            trope = b.op("vector", lambda E: E.tensor_copy(out=qrope_b[:], in_=qrope[:]),
                         inc="RP")
            b.waits("tensor", [trope])
            b.waits("vector", [qst_tok])
            for f in range(16):
                pb_ = f % 2
                b.waits("tensor", [pkb_tok[pb_]])
                ttp = b.op("tensor", lambda E, f=f, pb_=pb_: E.transpose(
                    pkb[pb_][:], qrope_b[:, f * 128:(f + 1) * 128], idb[:]), inc=True, nop=False)
                b.wait("vector", ttp)
                tevt = b.op("vector", lambda E, f=f, pb_=pb_: E.tensor_copy(
                    out=qtts[:, f * 128:(f + 1) * 128], in_=pkb[pb_][:]), inc=f"PKB{pb_}")
                pkb_tok[pb_] = tevt
            b.waits("sync", pkb_tok)
            qst_tok = b.dma("sync", lambda E, tile=tile: E.dma_start(
                out=qt_dram[tile].rearrange("f p c -> p f c"),
                in_=qtts[:].rearrange("p (f c) -> p f c", f=16)), ch="QST")
        p2_slfree[si % 2] = tmm

    b.barrier(ENGS)
    if stage == "p2":
        tl1 = None
        for i in range(4):
            tl1 = b.dma("gpsimd", lambda E, i=i: E.dma_start(
                out=y_out[i * 128:(i + 1) * 128, :].rearrange("p (f c) -> p f c", f=16),
                in_=qt_dram[i].rearrange("f p c -> p f c")), ch="DBG")
        b.wait("sync", tl1)
        finish(nc, b)
        return nc

    pa.close()

    pb = ExitStack()
    wob = SB(pb, "wob", [128, 16 * 2048], BF16)
    qt_t = SB(pb, "qt_t", [128, 16 * 128], BF16)
    mask_t = SB(pb, "mask_t", [128, 896], F32)
    x_t = SB(pb, "x_t", [128, D], F32)
    Pt = SB(pb, "Pt", [128, 896], BF16)
    PTs = SB(pb, "PTs", [128, 896], BF16)
    sm = SB(pb, "sm", [128, 896], F32)
    outT = SB(pb, "outT", [128, 16 * 128], BF16)
    xa_f = SB(pb, "xa_f", [128, D], F32)
    xab_t = SB(pb, "xab_t", [128, D], BF16)
    m8 = SB(pb, "m8", [128, 8], F32)
    zapt = SB(pb, "zapt", [128, NE], F32)
    Mm = SB(pb, "Mm", [128, NE], F32)
    ew = SB(pb, "ew", [128, NE], F32)
    Wd = SB(pb, "Wd", [128, NE], F32)
    Whi = SB(pb, "Whi", [128, NE], BF16)
    Wlo = SB(pb, "Wlo", [128, NE], F32)
    Wlob = SB(pb, "Wlob", [128, NE], BF16)
    rbT = SB(pb, "rbT", [32, 128], F32)
    cums = SB(pb, "cums", [32, 128], F32)
    zer32 = SB(pb, "zer32", [32, 128], F32)
    carry = SB(pb, "carry", [32, 1], F32)
    posT = SB(pb, "posT", [32, 128], F32)
    pos_tm = SB(pb, "pos_tm", [128, NE], F32)
    kva = SB(pb, "kva", [128, NE], F32)
    kvm = SB(pb, "kvm", [128, NE], F32)
    kv8 = SB(pb, "kv8", [128, 8], F32)
    e8 = SB(pb, "e8", [128, 8], F32)
    Em = SB(pb, "Em", [128, NE], F32)
    pos8 = SB(pb, "pos8", [128, 8], F32)
    slot8f = SB(pb, "slot8f", [128, 8], F32)
    ovf = SB(pb, "ovf", [128, 8], F32)
    logT_sb = SB(pb, "logT_sb", [32, 512], F32)
    payload = [SB(pb, f"pay{j}", [128, PAYW], BF16) for j in range(2)]
    S1 = PS(pb, "S1", [128, 512])
    S2 = PS(pb, "S2", [128, 384])
    PTp = PS(pb, "PTp", [128, 896], BF16)
    OHT = PS(pb, "OHT", [128, 128])
    misc1 = PS(pb, "misc1", [32, 512])
    misc2 = PS(pb, "misc2", [128, 32])

    b.op("vector", lambda E: E.memset(zer32[:], 0.0))
    b.op("vector", lambda E: E.memset(carry[:], 0.0))
    b.op("vector", lambda E: E.memset(payload[0][:, 2112:2176], 0.0))
    b.op("vector", lambda E: E.memset(payload[1][:, 2112:2176], 0.0))

    for f in range(16):
        cast_chunk(wo_in[f * 128:(f + 1) * 128, :], wob[:, f * 2048:(f + 1) * 2048],
                   2048, "vector" if f % 2 == 0 else "scalar")
    b.barrier(("sync", "vector", "scalar", "tensor"))

    tile_tok = {}
    pay_tok = {}
    xab_store = {}
    xa_store = {}
    state = {
        "s_bank": None,
        "sm_free": None,
        "pt_bank": None,
        "pts_free": None,
        "pt_sb": None,
        "oht_bank": None,
        "outT_free": None,
        "y_bank": None,
        "xaf_free": None,
        "xab_free": None,
        "pay_free": {},
        "route_in": {},
    }

    def emit_tile_loads(i, first=False):
        toks = []
        if not first:
            b.waits("sync", [state["y_bank"]])
        toks.append(b.dma("sync", lambda E, i=i: E.dma_start(
            out=qt_t[:].rearrange("p (f c) -> p f c", f=16),
            in_=qt_dram[i].rearrange("f p c -> p f c")), ch="TL"))
        toks.append(b.dma("sync", lambda E, i=i: E.dma_start(
            out=mask_t[:], in_=mask_in[i]), ch="TL"))
        toks.append(b.dma("sync", lambda E, i=i: E.dma_start(
            out=x_t[:], in_=xkv_in[768 + i * 128:768 + (i + 1) * 128, :]), ch="TL"))
        tile_tok[i] = toks

    def emit_gate_quarter(qtr):
        t0 = qtr * 512
        buf = xsl[0]
        b.waits("scalar", [xab_store[qtr * 4 + j] for j in range(4)]
                + [state.get("gate_sl_free")])
        ttr = None
        for dmc in range(16):
            ttr = b.dma("scalar", lambda E, t0=t0, dmc=dmc, buf=buf: E.dma_start_transpose(
                out=buf[:, dmc * 512:(dmc + 1) * 512],
                in_=xab_dram[t0:t0 + 512, dmc * 128:(dmc + 1) * 128]), ch="TRXG")
        b.waits("tensor", [ttr])
        tgm = None
        for dmc in range(16):
            tgm = b.op("tensor", lambda E, dmc=dmc, buf=buf: E.matmul(
                misc1[:, 0:512], gateb[:, dmc * NE:(dmc + 1) * NE],
                buf[:, dmc * 512:(dmc + 1) * 512],
                start=(dmc == 0), stop=(dmc == 15)), nop=False,
                inc=(True if dmc == 15 else None))
        state["gate_sl_free"] = tgm
        b.wait("vector", tgm)
        tge = b.op("vector", lambda E: E.tensor_copy(out=logT_sb[:], in_=misc1[:, 0:512]),
                   inc="GLG")
        tt2 = None
        for jj in range(4):
            i = qtr * 4 + jj
            b.waits("tensor", [tge, tt2])
            tt1 = b.op("tensor", lambda E, jj=jj: E.transpose(
                misc2[:], logT_sb[:, jj * 128:(jj + 1) * 128], idf[0:32, 0:32]),
                inc=True, nop=False)
            b.wait("vector", tt1)
            tt2 = b.op("vector", lambda E, i=i: E.tensor_copy(
                out=logits[:, i * NE:(i + 1) * NE], in_=misc2[:]), inc="GL2")
            state["route_in"][i] = tt2

    def emit_routing_tile(i):
        if rparts < 2:
            pay_tok[i] = None
            return
        lg = logits[:, i * NE:(i + 1) * NE]
        b.waits("vector", [state["route_in"][i]])
        b.op("vector", lambda E, lg=lg: E.tensor_copy(out=zapt[:], in_=lg))
        b.op("vector", lambda E: E.memset(Mm[:], 0.0))
        for kk in range(8):
            b.fop("vector", lambda E, kk=kk: E.reduce_max(m8[:, kk:kk + 1], zapt[:], axis=AX))
            b.fop("vector", lambda E, kk=kk: E.tensor_scalar(
                ew[:], zapt[:], m8[:, kk:kk + 1], None, op0=OP.is_equal))
            b.fop("vector", lambda E: E.tensor_tensor(Mm[:], Mm[:], ew[:], op=OP.add))
            b.fop("vector", lambda E: E.scalar_tensor_tensor(
                zapt[:], in0=ew[:], scalar=NEG, in1=zapt[:], op0=OP.mult, op1=OP.add))
        if i == 1:
            b.op("vector", lambda E, lg=lg: E.tensor_copy(out=rdbg[3][:], in_=lg))
        tneg = b.op("vector", lambda E: E.tensor_scalar(
            small[:, 14:15], m8[:, 0:1], -1.0, None, op0=OP.mult), inc="RNG")
        b.wait("scalar", tneg)
        texp = b.op("scalar", lambda E, lg=lg: E.activation(
            out=ew[:], in_=lg, func=AF.Exp, bias=small[:, 14:15]), inc="RXP")
        b.wait("vector", texp)
        b.fop("vector", lambda E: E.tensor_tensor(Wd[:], ew[:], Mm[:], op=OP.mult))
        b.fop("vector", lambda E: E.reduce_sum(small[:, 15:16], Wd[:], axis=AX))
        b.fop("vector", lambda E: E.reciprocal(small[:, 14:15], small[:, 15:16]))
        b.fop("vector", lambda E: E.tensor_scalar(
            Wd[:], Wd[:], small[:, 14:15], None, op0=OP.mult))
        b.fop("vector", lambda E: E.tensor_copy(out=Whi[:], in_=Wd[:]))
        b.fop("vector", lambda E: E.tensor_copy(out=ew[:], in_=Whi[:]))
        b.fop("vector", lambda E: E.tensor_tensor(Wlo[:], Wd[:], ew[:], op=OP.subtract))
        twv = b.fop("vector", lambda E: E.tensor_copy(out=Wlob[:], in_=Wlo[:]), inc="RWV")
        b.waits("tensor", [twv])
        tmt = b.op("tensor", lambda E: E.transpose(misc1[:, 0:128], Mm[:], idf[:]),
                   inc=True, nop=False)
        b.wait("vector", tmt)
        trb = b.fop("vector", lambda E: E.tensor_copy(out=rbT[:], in_=misc1[0:32, 0:128]),
                    inc="RCA")
        state["misc1_free"] = trb
        b.fop("vector", lambda E: E.tensor_tensor_scan(
            cums[:], data0=rbT[:], data1=zer32[:], initial=0.0,
            op0=OP.add, op1=OP.add))
        b.op("vector", lambda E: E.tensor_scalar(
            cums[:], cums[:], carry[:, 0:1], None, op0=OP.add))
        b.op("vector", lambda E: E.tensor_tensor(posT[:], cums[:], rbT[:], op=OP.subtract))
        if i == 1:
            b.op("vector", lambda E: E.tensor_copy(out=rdbg[0][:], in_=rbT[:]))
            b.op("vector", lambda E: E.tensor_copy(out=rdbg[1][:], in_=cums[:]))
        tcar = b.fop("vector", lambda E: E.tensor_copy(out=carry[:], in_=cums[:, 127:128]),
                     inc="RCA")
        b.wait("tensor", tcar)
        tpt = b.op("tensor", lambda E: E.transpose(misc2[:], posT[:], idf[0:32, 0:32]),
                   inc=True, nop=False)
        b.wait("vector", tpt)
        b.op("vector", lambda E: E.tensor_copy(out=pos_tm[:], in_=misc2[:]))
        if i == 1:
            b.op("vector", lambda E: E.tensor_copy(out=rdbg[2][:], in_=pos_tm[:]))
        b.op("vector", lambda E: E.tensor_tensor(kva[:], iota2048[:], pos_tm[:], op=OP.add))
        b.op("vector", lambda E: E.tensor_scalar(kva[:], kva[:], 1.0, None, op0=OP.add))
        b.op("vector", lambda E: E.tensor_tensor(kvm[:], kva[:], Mm[:], op=OP.mult))
        b.fop("vector", lambda E: E.tensor_scalar(kvm[:], kvm[:], 1.0, None, op0=OP.subtract))
        b.fop("vector", lambda E: E.max(out=kv8[:], in_=kvm[:]))
        b.op("vector", lambda E: E.tensor_scalar(Em[:], iota32[:], 1.0, None, op0=OP.add))
        b.op("vector", lambda E: E.tensor_tensor(Em[:], Em[:], Mm[:], op=OP.mult))
        b.fop("vector", lambda E: E.tensor_scalar(Em[:], Em[:], 1.0, None, op0=OP.subtract))
        b.fop("vector", lambda E: E.max(out=e8[:], in_=Em[:]))
        b.op("vector", lambda E: E.scalar_tensor_tensor(
            pos8[:], in0=e8[:], scalar=-2048.0, in1=kv8[:], op0=OP.mult, op1=OP.add))
        if i == 1:
            b.op("vector", lambda E: E.tensor_copy(out=rdbg[4][:], in_=kv8[:]))
            b.op("vector", lambda E: E.tensor_copy(out=rdbg[5][:], in_=e8[:]))
        b.op("vector", lambda E: E.scalar_tensor_tensor(
            slot8f[:], in0=e8[:], scalar=640.0, in1=pos8[:], op0=OP.mult, op1=OP.add))
        b.op("vector", lambda E: E.tensor_scalar(ovf[:], pos8[:], 639.5, None, op0=OP.is_gt))
        b.op("vector", lambda E: E.tensor_scalar(kv8[:], slot8f[:], float(ROWS), None,
                                                 op0=OP.subtract))
        b.op("vector", lambda E: E.tensor_tensor(kv8[:], kv8[:], ovf[:], op=OP.mult))
        b.op("vector", lambda E: E.tensor_tensor(slot8f[:], slot8f[:], kv8[:], op=OP.subtract))
        b.op("vector", lambda E: E.tensor_scalar(ovf[:], slot8f[:], 0.0, None, op0=OP.is_lt))
        b.op("vector", lambda E: E.tensor_scalar(kv8[:], slot8f[:], float(ROWS), None,
                                                 op0=OP.subtract))
        b.op("vector", lambda E: E.tensor_tensor(kv8[:], kv8[:], ovf[:], op=OP.mult))
        b.op("vector", lambda E: E.tensor_tensor(slot8f[:], slot8f[:], kv8[:], op=OP.subtract))
        tsl8 = b.op("vector", lambda E, i=i: E.tensor_copy(
            out=slot8i[:, i * 8:(i + 1) * 8], in_=slot8f[:]), inc="RS8")
        if rparts < 3:
            pay_tok[i] = None
            return
        bi = i % 2
        prevp = state["pay_free"].get(bi)
        if prevp is not None:
            b.waits("sync", [prevp])
        tpl = b.dma("sync", lambda E, i=i, bi=bi: E.dma_start(
            out=payload[bi][:, 0:2048], in_=xab_dram[i * 128:(i + 1) * 128, :]),
            ch="PLDa" if bi == 0 else "PLDb")
        b.waits("vector", [tpl] + ([prevp] if prevp else []))
        b.op("vector", lambda E, bi=bi: E.tensor_copy(
            out=payload[bi][:, 2048:2112].rearrange("p (c two) -> p c two", two=2)[:, :, 0],
            in_=Whi[:]))
        tpw = b.op("vector", lambda E, bi=bi: E.tensor_copy(
            out=payload[bi][:, 2048:2112].rearrange("p (c two) -> p c two", two=2)[:, :, 1],
            in_=Wlob[:]), inc="RPW")
        if rparts < 4:
            pay_tok[i] = None
            return
        b.waits("gpsimd", [tpw, tsl8])
        tsc = None
        for k in range(8):
            tsc = b.dma("gpsimd", lambda E, i=i, k=k, bi=bi: E.indirect_dma_start(
                out=xsorted[:, :],
                out_offset=IndirectOffsetOnAxis(slot8i[:, i * 8 + k:i * 8 + k + 1], 0),
                in_=payload[bi][:, :], in_offset=None),
                ch="SCTa" if bi == 0 else "SCTb")
        pay_tok[i] = tsc
        state["pay_free"][bi] = tsc

    emit_tile_loads(0, first=True)
    for i in range(NT):
        b.waits("tensor", tile_tok[i])
        toh = None
        for h in range(NH):
            kvh = h // 4
            kb = kvh * XKV
            b.waits("tensor", [state["s_bank"]])
            b.op("tensor", lambda E, h=h, kb=kb: E.matmul(
                S1[:, 0:256], qt_t[:, h * 128:(h + 1) * 128],
                kt[:, kb:kb + 256], start=True, stop=True), nop=False)
            b.op("tensor", lambda E, h=h, kb=kb, i=i: E.matmul(
                S1[:, 256:512], qt_t[:, h * 128:(h + 1) * 128],
                kt[:, kb + 256 + 128 * i:kb + 512 + 128 * i], start=True, stop=True), nop=False)
            tsc_ = b.op("tensor", lambda E, h=h, kb=kb, i=i: E.matmul(
                S2[:, 0:384], qt_t[:, h * 128:(h + 1) * 128],
                kt[:, kb + 512 + 128 * i:kb + 896 + 128 * i], start=True, stop=True),
                nop=False, inc=True)
            b.wait("vector", tsc_)
            b.op("vector", lambda E: E.reduce_max(small[:, 0:1], S1[:], axis=AX))
            b.fop("vector", lambda E: E.reduce_max(small[:, 1:2], S2[:], axis=AX))
            b.op("vector", lambda E: E.tensor_tensor(small[:, 2:3], small[:, 0:1],
                                                     small[:, 1:2], op=OP.max))
            b.op("vector", lambda E: E.tensor_scalar(small[:, 3:4], small[:, 2:3], -1.0,
                                                     None, op0=OP.mult))
            b.waits("vector", [state["sm_free"]])
            b.op("vector", lambda E: E.scalar_tensor_tensor(
                sm[:, 0:512], in0=S1[:], scalar=small[:, 3:4], in1=mask_t[:, 0:512],
                op0=OP.add, op1=OP.add))
            tmsk = b.op("vector", lambda E: E.scalar_tensor_tensor(
                sm[:, 512:896], in0=S2[:], scalar=small[:, 3:4], in1=mask_t[:, 512:896],
                op0=OP.add, op1=OP.add), inc="SMK")
            state["s_bank"] = tmsk
            if dbg and i == 0 and h == 0:
                b.wait("gpsimd", tmsk)
                b.dma("gpsimd", lambda E: E.dma_start(out=dbg_out[0][:, 0:896], in_=sm[:]), ch="DBG")
                b.dma("gpsimd", lambda E: E.dma_start(out=dbg_out[7][:, 0:896], in_=kt[:, 0:896]), ch="DBG")
                b.dma("gpsimd", lambda E: E.dma_start(out=dbg_out[7][:, 896:2048], in_=qt_t[:, 0:1152]), ch="DBG")
            b.waits("scalar", [tmsk, state["pt_sb"]])
            b.op("scalar", lambda E: E.activation(
                out=Pt[:, 0:512], in_=sm[:, 0:512], func=AF.Exp, accum_out=small[:, 4:5]))
            texp = b.op("scalar", lambda E: E.activation(
                out=Pt[:, 512:896], in_=sm[:, 512:896], func=AF.Exp,
                accum_out=small[:, 5:6]), inc="EXP")
            state["sm_free"] = texp
            b.wait("vector", texp)
            b.op("vector", lambda E: E.tensor_tensor(small[:, 6:7], small[:, 4:5],
                                                     small[:, 5:6], op=OP.add))
            b.fop("vector", lambda E: E.reciprocal(small[:, 7:8], small[:, 6:7]))
            tnorm = b.op("vector", lambda E: E.tensor_scalar(
                Pt[:], Pt[:], small[:, 7:8], None, op0=OP.mult), inc="PNM")
            if dbg and i == 0 and h == 0:
                b.wait("gpsimd", tnorm)
                b.dma("gpsimd", lambda E: E.dma_start(out=dbg_out[1][:, 0:896], in_=Pt[:]), ch="DBG")
                b.dma("gpsimd", lambda E: E.dma_start(out=dbg_out[2][:, 0:8], in_=small[:, 0:8]), ch="DBG")
            b.waits("tensor", [tnorm, state["pt_bank"]])
            ttr = None
            for jblk in range(7):
                ttr = b.op("tensor", lambda E, jblk=jblk: E.transpose(
                    PTp[:, jblk * 128:(jblk + 1) * 128], Pt[:, jblk * 128:(jblk + 1) * 128],
                    idb[:]), nop=False, inc=(True if jblk == 6 else None))
            state["pt_sb"] = ttr
            b.waits("vector", [ttr, state["pts_free"]])
            tevp = b.op("vector", lambda E: E.tensor_copy(out=PTs[:], in_=PTp[:]), inc="PTE")
            state["pt_bank"] = tevp
            b.waits("tensor", [tevp, state["oht_bank"]])
            tav = None
            for jblk in range(7):
                vblk = jblk if jblk < 2 else (i + jblk)
                tav = b.op("tensor", lambda E, jblk=jblk, vblk=vblk, kvh=kvh: E.matmul(
                    OHT[:], vt[:, vblk * 512 + kvh * 128:vblk * 512 + (kvh + 1) * 128],
                    PTs[:, jblk * 128:(jblk + 1) * 128],
                    start=(jblk == 0), stop=(jblk == 6)), nop=False,
                    inc=(True if jblk == 6 else None))
            state["pts_free"] = tav
            b.waits("scalar", [tav, state["outT_free"]])
            toh = b.op("scalar", lambda E, h=h: E.copy(
                out=outT[:, h * 128:(h + 1) * 128], in_=OHT[:]), inc="OHE")
            state["oht_bank"] = toh
            if dbg and i == 0 and h == 0:
                b.wait("gpsimd", toh)
                b.dma("gpsimd", lambda E: E.dma_start(out=dbg_out[4][:, 0:128], in_=outT[:, 0:128]), ch="DBG")
                b.dma("gpsimd", lambda E: E.dma_start(out=dbg_out[3][:, 0:896], in_=PTs[:]), ch="DBG")
        b.waits("tensor", [toh])
        b.waits("vector", [state["xaf_free"]])
        ybank_tok = state.get("ybank_tok", [None, None])
        tyev = None
        for nc4 in range(4):
            yb = nc4 % 2
            b.waits("tensor", [ybank_tok[yb]])
            two = None
            for f in range(16):
                two = b.op("tensor", lambda E, nc4=nc4, f=f: E.matmul(
                    Yp[nc4 % 2][:],
                    outT[:, f * 128:(f + 1) * 128],
                    wob[:, f * 2048 + nc4 * 512:f * 2048 + (nc4 + 1) * 512],
                    start=(f == 0), stop=(f == 15)), nop=False,
                    inc=(True if f == 15 else None))
            b.waits("vector", [two])
            tyev = b.op("vector", lambda E, nc4=nc4, yb=yb: E.scalar_tensor_tensor(
                x1[:, nc4 * 512:(nc4 + 1) * 512], in0=Yp[yb][:], scalar=0.0,
                in1=x_t[:, nc4 * 512:(nc4 + 1) * 512],
                op0=OP.add, op1=OP.add), inc=f"YB{yb}")
            ybank_tok[yb] = tyev
        state["ybank_tok"] = ybank_tok
        state["outT_free"] = two
        b.fop("vector", lambda E: E.reduce_sum(small[:, 8:9], x1[:], axis=AX))
        b.op("vector", lambda E: E.tensor_scalar(
            small[:, 10:11], small[:, 8:9], 1.0 / D, None, op0=OP.mult))
        b.wait("scalar", tyev)
        tsq = b.op("scalar", lambda E: E.activation(
            out=x_t[:], in_=x1[:], func=AF.Square, accum_out=small[:, 9:10]), inc="LSQ")
        state["y_bank"] = tsq
        if dbg and i == 0:
            b.wait("gpsimd", tsq)
            b.dma("gpsimd", lambda E: E.dma_start(out=dbg_out[5][:, :], in_=outT[:]), ch="DBG")
            b.dma("gpsimd", lambda E: E.dma_start(out=dbg_out[6][:, :], in_=x1[:]), ch="DBG")
        b.wait("vector", tsq)
        b.op("vector", lambda E: E.tensor_scalar(
            small[:, 11:12], small[:, 9:10], 1.0 / D, None, op0=OP.mult))
        b.op("vector", lambda E: E.tensor_tensor(
            small[:, 9:10], small[:, 10:11], small[:, 10:11], op=OP.mult))
        b.op("vector", lambda E: E.tensor_tensor(
            small[:, 11:12], small[:, 11:12], small[:, 9:10], op=OP.subtract))
        b.op("vector", lambda E: E.tensor_scalar(
            small[:, 11:12], small[:, 11:12], EPS, None, op0=OP.add))
        trec = b.fop("vector", lambda E: E.reciprocal(small[:, 12:13], small[:, 11:12]),
                     inc="LRC")
        b.wait("scalar", trec)
        tst_ = b.op("scalar", lambda E: E.activation(
            out=small[:, 13:14], in_=small[:, 12:13], func=AF.Sqrt), inc="LSR")
        b.wait("vector", tst_)
        txa = b.op("vector", lambda E: E.tensor_scalar(
            xa_f[:], x1[:], small[:, 10:11], small[:, 13:14],
            op0=OP.subtract, op1=OP.mult), inc="LXA")
        b.waits("scalar", [txa, state["xab_free"]])
        txab = b.op("scalar", lambda E: E.copy(out=xab_t[:], in_=xa_f[:]), inc="LXB")
        b.waits("sync", [txa, txab])
        xch = "XSTa" if (i // 4) % 2 == 0 else "XSTb"
        xa_store[i] = b.dma("sync", lambda E, i=i: E.dma_start(
            out=xa_dram[i * 128:(i + 1) * 128, :], in_=xa_f[:]), ch=xch)
        xab_store[i] = b.dma("sync", lambda E, i=i: E.dma_start(
            out=xab_dram[i * 128:(i + 1) * 128, :], in_=xab_t[:]), ch=xch)
        state["xaf_free"] = xa_store[i]
        state["xab_free"] = xab_store[i]
        if i + 1 < NT:
            emit_tile_loads(i + 1)

        if stage in ("full", "route", "p4") and i % 4 == 3:
            emit_gate_quarter(i // 4)
            for jj in range(4):
                emit_routing_tile(i - 3 + jj)

    if stage == "route":
        slotdump = dt("slotdump", [128, 128], I32, kind="ExternalOutput").ap()
        rdump = dt("rdump", [6, 128, 128], F32, kind="ExternalOutput").ap()
        b.waits("gpsimd", [b.last["vector"]])
        b.dma("gpsimd", lambda E: E.dma_start(out=rdump[0][0:32, :], in_=rdbg[0][:]), ch="DBG")
        b.dma("gpsimd", lambda E: E.dma_start(out=rdump[1][0:32, :], in_=rdbg[1][:]), ch="DBG")
        b.dma("gpsimd", lambda E: E.dma_start(out=rdump[2][:, 0:32], in_=rdbg[2][:]), ch="DBG")
        b.dma("gpsimd", lambda E: E.dma_start(out=rdump[3][:, 0:32], in_=rdbg[3][:]), ch="DBG")
        b.dma("gpsimd", lambda E: E.dma_start(out=rdump[4][:, 0:8], in_=rdbg[4][:]), ch="DBG")
        b.dma("gpsimd", lambda E: E.dma_start(out=rdump[5][:, 0:8], in_=rdbg[5][:]), ch="DBG")
        b.waits("gpsimd", [t for t in (pay_tok[i] for i in range(NT)) if t is not None])
        b.barrier(ENGS)
        tlast = b.dma("gpsimd", lambda E: E.dma_start(out=slotdump[:], in_=slot8i[:]), ch="DBG")
        for i in range(NT):
            tlast = b.dma("gpsimd", lambda E, i=i: E.dma_start(
                out=y_out[i * 128:(i + 1) * 128, :],
                in_=xa_dram[i * 128:(i + 1) * 128, :]), ch="DBG")
        b.wait("sync", tlast)
        finish(nc, b)
        pb.close()
        pab.close()
        top.close()
        return nc

    if stage == "attn":
        b.barrier(ENGS)
        tlast = None
        for i in range(NT):
            tlast = b.dma("gpsimd", lambda E, i=i: E.dma_start(
                out=y_out[i * 128:(i + 1) * 128, :],
                in_=xa_dram[i * 128:(i + 1) * 128, :]), ch="DBG")
        b.wait("sync", tlast)
        finish(nc, b)
        pb.close()
        pab.close()
        top.close()
        return nc

    b.barrier(ENGS)
    pb.close()
    pab.close()

    pc = ExitStack()
    wgb = SB(pc, "wgb", [128, 2 * 16 * INTER], BF16)
    WUB = 16 * INTER
    wdb = SB(pc, "wdb", [128, 6 * 2048], BF16)
    xgt = [SB(pc, f"xgt{j}", [128, 16 * CAP], BF16) for j in range(2)]
    xgrow = [SB(pc, f"xgrow{j}", [128, D], BF16) for j in range(2)]
    TRp = [PS(pc, f"TRp{j}", [128, 512], BF16) for j in range(2)]
    wc_b = [SB(pc, f"wc_b{j}", [128, 10], BF16) for j in range(2)]
    wcol = [SB(pc, f"wcol{j}", [128, 5], F32) for j in range(2)]
    hT = SB(pc, "hT", [128, 6 * CAP], BF16)
    hs = SB(pc, "hs", [128, CAP], F32)
    ysb = [SB(pc, f"ysb{j}", [128, D], BF16) for j in range(2)]
    Gp = PS(pc, "Gp", [128, CAP])
    Up = PS(pc, "Up", [128, CAP])

    wg_ready = [None] * NE
    wd_ready = [None] * NE
    ffn1_done = [None] * NE
    ffn2_done = [None] * NE
    xgt_ready = [None] * NE
    wcol_ok = [None] * NE
    ysb_tok = [None, None]
    sil_tok = None
    hmu_tok = None

    def emit_weight_load(e, kind):
        waitc = None
        if kind == "gu" and e > 0:
            waitc = [ffn1_done[e - 1]]
        if kind == "d" and e > 0:
            waitc = [ffn2_done[e - 1]]
        toks = []
        if kind == "gu":
            for src, base in ((wg_in, 0), (wu_in, WUB)):
                for dmc in range(16):
                    toks.append(cast_chunk(
                        src[e, dmc * 128:(dmc + 1) * 128, :],
                        wgb[:, base + dmc * INTER:base + (dmc + 1) * INTER],
                        INTER, "vector" if dmc % 2 == 0 else "scalar",
                        extra_waits=(waitc if waitc else [])))
            wg_ready[e] = toks[-2:]
        else:
            for ic in range(6):
                toks.append(cast_chunk(
                    wd_in[e, ic * 128:(ic + 1) * 128, :],
                    wdb[:, ic * 2048:(ic + 1) * 2048],
                    2048, "vector" if ic % 2 == 0 else "scalar",
                    extra_waits=(waitc if waitc else [])))
            wd_ready[e] = toks[-2:]

    xg_state = {"rowfree": [None, None], "trpfree": [None, None]}

    def emit_xgt_load(e):
        bj = e % 2
        tev = None
        for rc in range(5):
            rcj = rc % 2
            ldeng = "sync" if rcj == 0 else "scalar"
            b.waits(ldeng, [xg_state["rowfree"][rcj]])
            tld = b.dma(ldeng, lambda E, e=e, rc=rc, rcj=rcj: E.dma_start(
                out=xgrow[rcj][:],
                in_=xsorted[e * CAP + rc * 128:e * CAP + (rc + 1) * 128, 0:D]),
                ch="TRX4a" if rcj == 0 else "TRX4b")
            ttr = None
            for g in range(4):
                tb = (rc * 4 + g) % 2
                b.waits("tensor", [tld, xg_state["trpfree"][tb]])
                for q in range(4):
                    ttr = b.op("tensor", lambda E, g=g, q=q, tb=tb, rcj=rcj: E.transpose(
                        TRp[tb][:, q * 128:(q + 1) * 128],
                        xgrow[rcj][:, (g * 4 + q) * 128:(g * 4 + q + 1) * 128],
                        idb[:]), nop=False, inc=(True if q == 3 else None))
                b.waits("vector", [ttr] + (
                    [ffn1_done[e - 2]] if (e >= 2 and rc == 0 and g == 0) else []))
                tev = b.op("vector", lambda E, g=g, rc=rc, tb=tb, bj=bj: E.tensor_copy(
                    out=xgt[bj][:].rearrange("p (c t) -> p c t", c=16)
                        [:, g * 4:(g + 1) * 4, rc * 128:(rc + 1) * 128],
                    in_=TRp[tb][:].rearrange("p (c t) -> p c t", c=4)), inc="XEV")
                xg_state["trpfree"][tb] = tev
            xg_state["rowfree"][rcj] = ttr
        xgt_ready[e] = tev
        tw = []
        for tc5 in range(5):
            tw.append(b.dma("scalar", lambda E, e=e, tc5=tc5, bj=bj: E.dma_start(
                out=wc_b[bj][:, tc5 * 2:(tc5 + 1) * 2],
                in_=xsorted[e * CAP + tc5 * 128:e * CAP + (tc5 + 1) * 128,
                            WCOL + 2 * e:WCOL + 2 * e + 2]), ch="MISC"))
        b.waits("vector", tw + ([ffn2_done[e - 2]] if e >= 2 else []))
        tv = None
        for tc5 in range(5):
            tv = b.op("vector", lambda E, tc5=tc5, bj=bj: E.tensor_tensor(
                wcol[bj][:, tc5:tc5 + 1], wc_b[bj][:, tc5 * 2:tc5 * 2 + 1],
                wc_b[bj][:, tc5 * 2 + 1:tc5 * 2 + 2], op=OP.add),
                inc=("WCL" if tc5 == 4 else None))
        wcol_ok[e] = tv

    scat_all = [pay_tok[i] for i in range(NT)]
    b.waits("scalar", scat_all)
    b.waits("sync", scat_all)
    b.waits("gpsimd", scat_all)
    emit_weight_load(0, "gu")
    emit_xgt_load(0)
    emit_weight_load(0, "d")

    for e in range(NE):
        bj = e % 2
        b.waits("tensor", wg_ready[e] + [xgt_ready[e]])
        for ic in range(6):
            b.waits("tensor", [sil_tok])
            tgm = None
            for ncc, w in ((0, 512), (512, 128)):
                for dmc in range(16):
                    tgm = b.op("tensor", lambda E, ic=ic, ncc=ncc, w=w, dmc=dmc, bj=bj: E.matmul(
                        Gp[:, ncc:ncc + w],
                        wgb[:, dmc * INTER + ic * 128:dmc * INTER + (ic + 1) * 128],
                        xgt[bj][:, dmc * CAP + ncc:dmc * CAP + ncc + w],
                        start=(dmc == 0), stop=(dmc == 15)), nop=False,
                        inc=(True if (ncc, dmc) == (512, 15) else None))
            b.waits("tensor", [hmu_tok])
            tum = None
            for ncc, w in ((0, 512), (512, 128)):
                for dmc in range(16):
                    tum = b.op("tensor", lambda E, ic=ic, ncc=ncc, w=w, dmc=dmc, bj=bj: E.matmul(
                        Up[:, ncc:ncc + w],
                        wgb[:, WUB + dmc * INTER + ic * 128:WUB + dmc * INTER + (ic + 1) * 128],
                        xgt[bj][:, dmc * CAP + ncc:dmc * CAP + ncc + w],
                        start=(dmc == 0), stop=(dmc == 15)), nop=False,
                        inc=(True if (ncc, dmc) == (512, 15) else None))
            b.waits("scalar", [tgm, hmu_tok])
            tsl = b.op("scalar", lambda E: E.activation(
                out=hs[:], in_=Gp[:], func=AF.Silu), inc="SIL")
            sil_tok = tsl
            b.waits("vector", [tsl, tum])
            tmu = b.op("vector", lambda E, ic=ic: E.tensor_tensor(
                hT[:, ic * CAP:(ic + 1) * CAP], hs[:], Up[:], op=OP.mult), inc="HMU")
            hmu_tok = tmu
        ffn1_done[e] = hmu_tok
        if e + 1 < NE:
            emit_weight_load(e + 1, "gu")
            emit_xgt_load(e + 1)
        b.waits("tensor", wd_ready[e] + [ffn1_done[e], wcol_ok[e]])
        tym_last = None
        ychunk_tok = [None, None]
        for tc5 in range(5):
            yb2 = tc5 % 2
            b.waits("tensor", [ysb_tok[yb2]])
            tyv = None
            for nc4 in range(4):
                yq = nc4 % 2
                b.waits("tensor", [ychunk_tok[yq]])
                tym = None
                for ic in range(6):
                    tym = b.op("tensor", lambda E, nc4=nc4, ic=ic, tc5=tc5: E.matmul(
                        Yp[nc4 % 2][:],
                        hT[:, ic * CAP + tc5 * 128:ic * CAP + (tc5 + 1) * 128],
                        wdb[:, ic * 2048 + nc4 * 512:ic * 2048 + (nc4 + 1) * 512],
                        start=(ic == 0), stop=(ic == 5)), nop=False,
                        inc=(True if ic == 5 else None))
                tym_last = tym
                b.wait("vector", tym)
                tyv = b.op("vector", lambda E, yb2=yb2, yq=yq, nc4=nc4, tc5=tc5, bj=bj: E.tensor_scalar(
                    ysb[yb2][:, nc4 * 512:(nc4 + 1) * 512], Yp[yq][:],
                    wcol[bj][:, tc5:tc5 + 1], None, op0=OP.mult), inc=f"YC{yq}")
                ychunk_tok[yq] = tyv
            b.wait("sync", tyv)
            tst2 = b.dma("sync", lambda E, e=e, tc5=tc5, yb2=yb2: E.dma_start(
                out=ysorted[e * CAP + tc5 * 128:e * CAP + (tc5 + 1) * 128, :],
                in_=ysb[yb2][:]), ch="YSTa" if yb2 == 0 else "YSTb")
            ysb_tok[yb2] = tst2
        ffn2_done[e] = tym_last
        if e + 1 < NE:
            emit_weight_load(e + 1, "d")

    b.barrier(ENGS)
    if stage == "p4":
        tlast = None
        for i in range(NT):
            tlast = b.dma("gpsimd", lambda E, i=i: E.dma_start(
                out=y_out[i * 128:(i + 1) * 128, :].rearrange("p (h c) -> p h c", h=1),
                in_=ysorted[i * 128:(i + 1) * 128, :].rearrange("p (h c) -> p h c", h=1)), ch="DBG")
        b.wait("sync", tlast)
        finish(nc, b)
        pc.close()
        top.close()
        return nc
    pc.close()

    pd = ExitStack()
    gath = [SB(pd, f"gath{j}", [128, D], BF16) for j in range(8)]
    moe = SB(pd, "moe", [128, D], F32)
    t01 = SB(pd, "t01", [128, D], F32)
    t23 = SB(pd, "t23", [128, D], F32)
    t45 = SB(pd, "t45", [128, D], F32)
    t67 = SB(pd, "t67", [128, D], F32)
    xat = SB(pd, "xat", [128, D], F32)
    outf = SB(pd, "outf", [128, D], F32)

    gth_tok = {}

    def emit_gathers(i):
        toks = []
        for k in range(8):
            toks.append(b.dma("gpsimd", lambda E, i=i, k=k: E.indirect_dma_start(
                out=gath[k][:], out_offset=None,
                in_=ysorted[:, :],
                in_offset=IndirectOffsetOnAxis(slot8i[:, i * 8 + k:i * 8 + k + 1], 0)),
                ch="GTH"))
        gth_tok[i] = toks

    emit_gathers(0)
    out_last = None
    prev_xat = None
    prev_out = None
    for i in range(NT):
        b.waits("vector", gth_tok[i])
        if prev_xat is not None:
            b.waits("sync", [prev_xat])
        txl = b.dma("sync", lambda E, i=i: E.dma_start(
            out=xat[:], in_=xa_dram[i * 128:(i + 1) * 128, :]), ch="XAT")
        b.op("vector", lambda E: E.tensor_tensor(t01[:], gath[0][:], gath[1][:], op=OP.add))
        b.op("vector", lambda E: E.tensor_tensor(t23[:], gath[2][:], gath[3][:], op=OP.add))
        b.op("vector", lambda E: E.tensor_tensor(t45[:], gath[4][:], gath[5][:], op=OP.add))
        b.op("vector", lambda E: E.tensor_tensor(t67[:], gath[6][:], gath[7][:], op=OP.add))
        b.op("vector", lambda E: E.tensor_tensor(t01[:], t01[:], t23[:], op=OP.add))
        b.op("vector", lambda E: E.tensor_tensor(t45[:], t45[:], t67[:], op=OP.add))
        tgsum = b.op("vector", lambda E: E.tensor_tensor(moe[:], t01[:], t45[:], op=OP.add),
                     inc="GSM")
        if i + 1 < NT:
            b.wait("gpsimd", tgsum)
            emit_gathers(i + 1)
        b.wait("vector", txl)
        b.op("vector", lambda E: E.tensor_tensor(
            x1[:], moe[:], xat[:], op=OP.add))
        b.fop("vector", lambda E: E.reduce_sum(small[:, 8:9], x1[:], axis=AX))
        tv1 = b.op("vector", lambda E: E.tensor_scalar(
            small[:, 10:11], small[:, 8:9], 1.0 / D, None, op0=OP.mult), inc="LV1")
        b.wait("scalar", tv1)
        tsq = b.op("scalar", lambda E: E.activation(
            out=xat[:], in_=x1[:], func=AF.Square, accum_out=small[:, 9:10]), inc="LSQ")
        prev_xat = tsq
        b.wait("vector", tsq)
        b.op("vector", lambda E: E.tensor_scalar(
            small[:, 11:12], small[:, 9:10], 1.0 / D, None, op0=OP.mult))
        b.op("vector", lambda E: E.tensor_tensor(
            small[:, 9:10], small[:, 10:11], small[:, 10:11], op=OP.mult))
        b.op("vector", lambda E: E.tensor_tensor(
            small[:, 11:12], small[:, 11:12], small[:, 9:10], op=OP.subtract))
        b.op("vector", lambda E: E.tensor_scalar(
            small[:, 11:12], small[:, 11:12], EPS, None, op0=OP.add))
        trec = b.fop("vector", lambda E: E.reciprocal(small[:, 12:13], small[:, 11:12]),
                     inc="LRC")
        b.wait("scalar", trec)
        tst_ = b.op("scalar", lambda E: E.activation(
            out=small[:, 13:14], in_=small[:, 12:13], func=AF.Sqrt), inc="LSR")
        b.waits("vector", [tst_] + ([prev_out] if prev_out else []))
        tof = b.op("vector", lambda E: E.tensor_scalar(
            outf[:], x1[:], small[:, 10:11], small[:, 13:14],
            op0=OP.subtract, op1=OP.mult), inc="OF")
        b.wait("sync", tof)
        out_last = b.dma("sync", lambda E, i=i: E.dma_start(
            out=y_out[i * 128:(i + 1) * 128, :], in_=outf[:]), ch="OUT")
        prev_out = out_last

    b.wait("sync", out_last)
    b.wait("gpsimd", out_last)
    if dbg:
        tdb = b.dma("gpsimd", lambda E: E.dma_start(out=sdump[:], in_=slot8i[:]), ch="DBG")
        for ch_ in range(6):
            r0, r1 = ch_ * 3584, min((ch_ + 1) * 3584, ROWS + 1)
            tdb = b.dma("gpsimd", lambda E, r0=r0, r1=r1: E.dma_start(
                out=wdump[r0:r1, :], in_=xsorted[r0:r1, 2048:2112]), ch="DBG")
            tdb = b.dma("gpsimd", lambda E, r0=r0, r1=r1: E.dma_start(
                out=ydump[r0:r1, :], in_=ysorted[r0:r1, 0:8]), ch="DBG")
        b.wait("sync", tdb)
    finish(nc, b)
    pd.close()
    top.close()
    return nc


def finish(nc, b):
    with nc.Block() as block:
        for eng in ENGS:
            ops = list(b.q[eng])

            def runner(E, ops=ops):
                for fn in ops:
                    fn(E)
            getattr(block, eng)(runner)


def _rope_tables():
    inv = 1.0 / (10000.0 ** (np.arange(0, DQK, 2)[: DQK // 2].astype(np.float32) / DQK))
    t = np.arange(S, dtype=np.float32)
    ang = np.outer(t, inv)
    return np.cos(ang), np.sin(ang)


def _perm_cols(w, nheads):
    perm = np.concatenate([np.arange(0, 128, 2), np.arange(1, 128, 2)])
    w2 = w.reshape(w.shape[0], nheads, 128)[:, :, perm]
    return np.ascontiguousarray(w2.reshape(w.shape[0], nheads * 128))


def _mask_for_core(qo):
    m = np.full((NT, 128, 896), NEG, np.float32)
    for i in range(NT):
        q = qo + i * 128 + np.arange(128)[:, None]
        band_lo = qo - 512 + i * 128

        def allowed(j):
            band = (j <= q) & (j > q - WIN)
            front = (j < FRONT) & (j <= q - WIN)
            return band | front
        jf = np.arange(256)[None, :]
        in_band = (jf >= band_lo) & (jf < band_lo + 640)
        m[i, :, 0:256] = np.where(allowed(jf) & ~in_band, 0.0, NEG)
        jb = band_lo + np.arange(640)[None, :]
        m[i, :, 256:896] = np.where(allowed(jb) & (jb >= 0), 0.0, NEG)
    return m


def make_in_maps(L0_attn_x, L0_attn_wq, L0_attn_wk, L0_attn_wv, L0_attn_wo,
                 L0_ffn_gate_w, L0_ffn_w_gate, L0_ffn_w_up, L0_ffn_w_down):
    x = np.asarray(L0_attn_x, np.float32)
    wq = _perm_cols(np.asarray(L0_attn_wq, np.float32), NH)
    wk = _perm_cols(np.asarray(L0_attn_wk, np.float32), NKV)
    wv = np.ascontiguousarray(np.asarray(L0_attn_wv, np.float32))
    wo = np.ascontiguousarray(np.asarray(L0_attn_wo, np.float32))
    gw = np.ascontiguousarray(np.asarray(L0_ffn_gate_w, np.float32))
    wg = np.ascontiguousarray(np.asarray(L0_ffn_w_gate, np.float32))
    wu = np.ascontiguousarray(np.asarray(L0_ffn_w_up, np.float32))
    wdn = np.ascontiguousarray(np.asarray(L0_ffn_w_down, np.float32))
    cos, sin = _rope_tables()
    idf = np.eye(128, dtype=np.float32)
    idb = idf.astype(ml_dtypes.bfloat16)
    scale = np.float32(1.0 / np.sqrt(DQK))

    in_maps = []
    for c in range(8):
        bb, qo = c // 4, (c % 4) * T
        xb = x[bb]
        xkv = np.zeros((XKV, D), np.float32)
        xkv[0:FRONT] = xb[0:FRONT]
        lo = qo - WIN
        src0 = max(0, lo)
        xkv[FRONT + (src0 - lo):FRONT + WIN + T] = xb[src0:qo + T]
        qpos = np.arange(qo, qo + T)
        kvpos = np.concatenate([np.arange(FRONT),
                                np.clip(np.arange(lo, qo + T), 0, S - 1)])
        in_maps.append({
            "xkv_in": np.ascontiguousarray(xkv),
            "wq_in": wq, "wk_in": wk, "wv_in": wv, "wo_in": wo,
            "gate_in": gw, "wg_in": wg, "wu_in": wu, "wd_in": wdn,
            "cq_in": np.ascontiguousarray(cos[qpos] * scale),
            "sq_in": np.ascontiguousarray(sin[qpos] * scale),
            "ckv_in": np.ascontiguousarray(cos[kvpos]),
            "skv_in": np.ascontiguousarray(sin[kvpos]),
            "mask_in": _mask_for_core(qo),
            "idf_in": idf, "idb_in": idb,
        })
    return in_maps


def kernel(L0_attn_x, L0_attn_wq, L0_attn_wk, L0_attn_wv, L0_attn_wo,
           L0_ffn_gate_w, L0_ffn_w_gate, L0_ffn_w_up, L0_ffn_w_down):
    nc = build_nc(stage="full")
    in_maps = make_in_maps(L0_attn_x, L0_attn_wq, L0_attn_wk, L0_attn_wv, L0_attn_wo,
                           L0_ffn_gate_w, L0_ffn_w_gate, L0_ffn_w_up, L0_ffn_w_down)
    res = run_bass_kernel_spmd(nc, in_maps, core_ids=list(range(8)))
    out = np.empty((BATCH, S, D), np.float32)
    for c in range(8):
        bb, qo = c // 4, (c % 4) * T
        out[bb, qo:qo + T] = res.results[c]["y_out"]
    return out


# version 04
# speedup vs baseline: 1.3324x; 1.2823x over previous
import numpy as np
import ml_dtypes
from contextlib import ExitStack

import concourse.bass as bass
import concourse.mybir as mybir
from concourse.bass_utils import run_bass_kernel_spmd
from concourse.bass_core_rust import IndirectOffsetOnAxis

F32 = mybir.dt.float32
BF16 = mybir.dt.bfloat16
I32 = mybir.dt.int32
AF = mybir.ActivationFunctionType
OP = mybir.AluOpType
AX = mybir.AxisListType.X

S, D, BATCH = 8192, 2048, 2
T = 2048
NT = 16
NKV, NH, DQK = 4, 16, 128
WIN, FRONT = 512, 256
XKV = FRONT + WIN + T
NE, TOPK, INTER = 32, 8, 768
CAP = 640
ROWS = NE * CAP
PAYW = 2176
WCOL = 2048
NEG = -1e30
BIGC = 1e9
EPS = 1e-5

ENGS = ("sync", "scalar", "vector", "tensor", "gpsimd")
SEMS = ("T", "BV", "BS", "BG",
        "SG0", "SG1", "IOT", "PP0", "PP1", "PP2", "RP", "PKB0", "PKB1",
        "SMK", "EXP", "PNM", "PTE", "OHE", "YB0", "YB1",
        "LSQ", "LRC", "LSR", "LXA", "LXB", "LV1",
        "GLG", "GL2", "RNG", "RXP", "RWV", "RCA", "RS8", "RPW",
        "SIL", "HMU", "WCL", "YC0", "YC1", "GSM", "OF", "KTD", "VTD", "VFN", "SFN",
        "XEV")
DMA_CHANNELS = ("MISC", "XKLa", "XKLb", "XKSa", "XKSb", "WSTa", "WSTb", "TL",
                "PLDa", "PLDb", "QST", "XSTa", "XSTb", "YSTa", "YSTb", "XAT",
                "OUT", "TRXa", "TRXb", "TRXG", "TRX4a", "TRX4b", "SCTa", "SCTb",
                "WGU", "WDD",
                "GTH", "DBG")
BAR_SEM = {"tensor": "T", "vector": "BV", "scalar": "BS", "gpsimd": "BG"}


FENCE_ALL = True


class Bld:
    def __init__(self):
        self.q = {e: [] for e in ENGS}
        self.cnt = {s: 0 for s in SEMS}
        self.cnt.update({c: 0 for c in DMA_CHANNELS})
        self.cnt.update({c: 0 for c in DMA_CHANNELS})
        self.sem = {}
        self.last = {e: None for e in ENGS}

    def fop(self, eng, fn, inc=None):
        if FENCE_ALL and eng in ("vector", "scalar"):
            return self.op(eng, fn, inc=inc)
        tok = self.op(eng, fn, inc=inc)
        if tok is not None:
            self.q[eng].append(lambda E, k=tok[0], v=tok[1]: E.wait_ge(self.sem[k], v))
            return tok
        fch = "VFN" if eng == "vector" else "SFN"
        self.cnt[fch] += 1
        v = self.cnt[fch]
        real = self.q[eng].pop()
        self.q[eng].append(lambda E, real=real, fch=fch: real(E).then_inc(self.sem[fch], 1))
        self.q[eng].append(lambda E, fch=fch, v=v: E.wait_ge(self.sem[fch], v))
        return tok

    def op(self, eng, fn, inc=None, nop=True):
        if FENCE_ALL and eng in ("vector", "scalar"):
            tok = self._op_raw(eng, fn, inc=inc, nop=False)
            if tok is not None:
                self.q[eng].append(lambda E, k=tok[0], v=tok[1]: E.wait_ge(self.sem[k], v))
                return tok
            fch = "VFN" if eng == "vector" else "SFN"
            self.cnt[fch] += 1
            v = self.cnt[fch]
            real = self.q[eng].pop()
            self.q[eng].append(lambda E, real=real, fch=fch: real(E).then_inc(self.sem[fch], 1))
            self.q[eng].append(lambda E, fch=fch, v=v: E.wait_ge(self.sem[fch], v))
            return None
        return self._op_raw(eng, fn, inc=inc, nop=nop)

    def _op_raw(self, eng, fn, inc=None, nop=True):
        if nop and eng == "vector":
            self.q[eng].append(lambda E: E.engine_nop())
        elif nop and eng in ("scalar", "gpsimd"):
            self.q[eng].append(lambda E: E.nop())
        if inc:
            k = "T" if (inc is True and eng == "tensor") else inc
            assert k is not True, "non-tensor inc needs a channel name"
            self.cnt[k] += 1
            v = self.cnt[k]
            self.q[eng].append(lambda E, fn=fn, k=k: fn(E).then_inc(self.sem[k], 1))
            tok = (k, v)
            self.last[eng] = tok
            return tok
        self.q[eng].append(fn)
        return None

    def mark(self, eng):
        k = BAR_SEM[eng]
        self.cnt[k] += 1
        v = self.cnt[k]
        if eng == "vector":
            self.q[eng].append(lambda E, k=k: E.engine_nop().then_inc(self.sem[k], 1))
        else:
            self.q[eng].append(lambda E, k=k: E.nop().then_inc(self.sem[k], 1))
        tok = (k, v)
        self.last[eng] = tok
        return tok

    def dma(self, eng, fn, ch="MISC"):
        k = ch
        self.cnt[k] += 16
        v = self.cnt[k]
        self.q[eng].append(lambda E, fn=fn, k=k: fn(E).then_inc(self.sem[k], 16))
        tok = (k, v)
        self.last[eng] = tok
        return tok

    def wait(self, eng, tok):
        if tok is None:
            return
        k, v = tok
        self.q[eng].append(lambda E, k=k, v=v: E.wait_ge(self.sem[k], v))

    def waits(self, eng, toks):
        best = {}
        for t in toks:
            if t is None:
                continue
            k, v = t
            best[k] = max(best.get(k, 0), v)
        for k, v in best.items():
            self.q[eng].append(lambda E, k=k, v=v: E.wait_ge(self.sem[k], v))

    def barrier(self, engines=ENGS):
        toks = []
        for ch in SEMS + DMA_CHANNELS:
            if self.cnt[ch] > 0:
                toks.append((ch, self.cnt[ch]))
        for e in engines:
            self.waits(e, toks)


def build_nc(stage="full", races=True, dbg=False, rparts=4):
    nc = bass.Bass("TRN2", target_bir_lowering=False,
                   detect_race_conditions=races)
    top = ExitStack()

    dt = nc.dram_tensor
    xkv_in = dt("xkv_in", [XKV, D], F32, kind="ExternalInput").ap()
    wq_in = dt("wq_in", [D, D], F32, kind="ExternalInput").ap()
    wk_in = dt("wk_in", [D, 512], F32, kind="ExternalInput").ap()
    wv_in = dt("wv_in", [D, 512], F32, kind="ExternalInput").ap()
    wo_in = dt("wo_in", [D, D], F32, kind="ExternalInput").ap()
    gate_in = dt("gate_in", [D, NE], F32, kind="ExternalInput").ap()
    wg_in = dt("wg_in", [NE, D, INTER], F32, kind="ExternalInput").ap()
    wu_in = dt("wu_in", [NE, D, INTER], F32, kind="ExternalInput").ap()
    wd_in = dt("wd_in", [NE, INTER, D], F32, kind="ExternalInput").ap()
    cq_in = dt("cq_in", [T, 64], F32, kind="ExternalInput").ap()
    sq_in = dt("sq_in", [T, 64], F32, kind="ExternalInput").ap()
    ckv_in = dt("ckv_in", [XKV, 64], F32, kind="ExternalInput").ap()
    skv_in = dt("skv_in", [XKV, 64], F32, kind="ExternalInput").ap()
    mask_in = dt("mask_in", [NT, 128, 896], F32, kind="ExternalInput").ap()
    idf_in = dt("idf_in", [128, 128], F32, kind="ExternalInput").ap()
    idb_in = dt("idb_in", [128, 128], BF16, kind="ExternalInput").ap()

    y_out = dt("y_out", [T, D], F32, kind="ExternalOutput").ap()
    dbg_out = dt("dbg_out", [8, 128, 2048], F32, kind="ExternalOutput").ap() if dbg else None
    sdump = dt("sdump", [128, 128], I32, kind="ExternalOutput").ap() if dbg else None
    wdump = dt("wdump", [ROWS + 1, 64], BF16, kind="ExternalOutput").ap() if dbg else None
    ydump = dt("ydump", [ROWS + 1, 8], BF16, kind="ExternalOutput").ap() if dbg else None

    xkv_bf = dt("xkv_bf", [XKV, D], BF16, kind="Internal").ap()
    qt_dram = dt("qt_dram", [NT, 16, 128, 128], BF16, kind="Internal").ap()
    xa_dram = dt("xa_dram", [T, D], F32, kind="Internal").ap()
    xab_dram = dt("xab_dram", [T, D], BF16, kind="Internal").ap()
    xsorted = dt("xsorted", [ROWS + 1, PAYW], BF16, kind="Internal").ap()
    ysorted = dt("ysorted", [ROWS + 1, D], BF16, kind="Internal").ap()

    def SB(stk, name, shape, dtype):
        return stk.enter_context(nc.sbuf_tensor(name, shape, dtype))

    def PS(stk, name, shape, dtype=F32):
        return stk.enter_context(nc.psum_tensor(name, shape, dtype))

    idf = SB(top, "idf", [128, 128], F32)
    idb = SB(top, "idb", [128, 128], BF16)
    stage_f = [SB(top, f"stage{j}", [128, 2048], F32) for j in range(2)]
    small = SB(top, "small", [128, 16], F32)
    x1 = SB(top, "x1", [128, D], F32)
    logits = SB(top, "logits", [128, NT * NE], F32)
    rdbg = [SB(top, "rdbg0", [32, 128], F32), SB(top, "rdbg1", [32, 128], F32),
            SB(top, "rdbg2", [128, 32], F32), SB(top, "rdbg3", [128, 32], F32),
            SB(top, "rdbg4", [128, 8], F32), SB(top, "rdbg5", [128, 8], F32)]
    slot8i = SB(top, "slot8i", [128, NT * 8], I32)
    gateb = SB(top, "gateb", [128, 16 * NE], BF16)
    iota32 = SB(top, "iota32", [128, NE], F32)
    iota2048 = SB(top, "iota2048", [128, NE], F32)
    Yp = [PS(top, f"Yp{j}", [128, 512]) for j in range(2)]

    sems = {}
    for sname in SEMS + DMA_CHANNELS:
        sems[sname] = top.enter_context(nc.semaphore(sname))
    b = Bld()
    b.sem = sems

    cast_state = {"i": 0, "hist": {}}

    def cast_chunk(src_ap, dst_ap, cols, eng, extra_waits=()):
        par = cast_state["i"] % 2
        stg = stage_f[par]
        cast_state["i"] += 1
        prev = cast_state["hist"].get(id(stg))
        if prev is not None:
            b.wait("sync", prev)
        tl = b.dma("sync", lambda E, stg=stg, cols=cols, src_ap=src_ap: E.dma_start(
            out=stg[:, 0:cols], in_=src_ap), ch="WSTa" if par == 0 else "WSTb")
        b.wait(eng, tl)
        for w in extra_waits:
            b.wait(eng, w)
        sg = "SG0" if par == 0 else "SG1"
        if eng == "scalar":
            tc = b.op(eng, lambda E, stg=stg, cols=cols, dst_ap=dst_ap: E.copy(
                out=dst_ap, in_=stg[:, 0:cols]), inc=sg)
        else:
            tc = b.op(eng, lambda E, stg=stg, cols=cols, dst_ap=dst_ap: E.tensor_copy(
                out=dst_ap, in_=stg[:, 0:cols]), inc=sg)
        cast_state["hist"][id(stg)] = tc
        return tc

    pa = ExitStack()
    pab = ExitStack()
    kt = SB(pab, "kt", [128, NKV * XKV], BF16)
    vt = SB(pab, "vt", [128, 22 * 512], BF16)
    xsl = [SB(pab, "xsl0", [128, 16 * 512], BF16),
           SB(pa, "xsl1", [128, 16 * 512], BF16)]
    cqtm = SB(pa, "cqtm", [128, 16 * 64], F32)
    sqtm = SB(pa, "sqtm", [128, 16 * 64], F32)
    rtmp2 = SB(pa, "rtmp2", [128, 16 * 64], F32)
    qtm = SB(pa, "qtm", [128, 2048], F32)
    qrope = SB(pa, "qrope", [128, 2048], F32)
    qrope_b = SB(pa, "qrope_b", [128, 2048], BF16)
    qtts = SB(pa, "qtts", [128, 2048], BF16)
    pp = [PS(pa, f"pp{j}", [128, 512]) for j in range(3)]
    pkb = [PS(pa, f"pkb{j}", [128, 128], BF16) for j in range(2)]
    pa0 = ExitStack()
    ckvtm = SB(pa0, "ckvtm", [128, 22 * 64], F32)
    skvtm = SB(pa0, "skvtm", [128, 22 * 64], F32)
    ktm = SB(pa0, "ktm", [128, 512], F32)
    krope = SB(pa0, "krope", [128, 512], F32)
    krope_b = SB(pa0, "krope_b", [128, 512], BF16)
    wkb = SB(pa0, "wkb", [128, 16 * 512], BF16)
    wvb = SB(pa0, "wvb", [128, 16 * 512], BF16)

    b.dma("sync", lambda E: E.dma_start(out=idf[:], in_=idf_in[:]))
    b.dma("sync", lambda E: E.dma_start(out=idb[:], in_=idb_in[:]))
    for tl_ in range(16):
        b.dma("sync", lambda E, tl_=tl_: E.dma_start(
            out=cqtm[:, tl_ * 64:(tl_ + 1) * 64], in_=cq_in[tl_ * 128:(tl_ + 1) * 128, :]))
        b.dma("sync", lambda E, tl_=tl_: E.dma_start(
            out=sqtm[:, tl_ * 64:(tl_ + 1) * 64], in_=sq_in[tl_ * 128:(tl_ + 1) * 128, :]))
    for tl_ in range(22):
        b.dma("sync", lambda E, tl_=tl_: E.dma_start(
            out=ckvtm[:, tl_ * 64:(tl_ + 1) * 64], in_=ckv_in[tl_ * 128:(tl_ + 1) * 128, :]))
        b.dma("sync", lambda E, tl_=tl_: E.dma_start(
            out=skvtm[:, tl_ * 64:(tl_ + 1) * 64], in_=skv_in[tl_ * 128:(tl_ + 1) * 128, :]))

    xkv_done = []
    for chk in range(22):
        buf = xsl[chk % 2]
        dst = buf[:, 0:2048]
        if chk >= 2:
            b.wait("gpsimd", xkv_done[chk - 2])
        tl = b.dma("gpsimd", lambda E, chk=chk, dst=dst: E.dma_start(
            out=dst, in_=xkv_in[chk * 128:(chk + 1) * 128, :]),
            ch="XKLa" if chk % 2 == 0 else "XKLb")
        b.wait("sync", tl)
        ts_ = b.dma("sync", lambda E, chk=chk, dst=dst: E.dma_start(
            out=xkv_bf[chk * 128:(chk + 1) * 128, :], in_=dst),
            ch="XKSa" if chk % 2 == 0 else "XKSb")
        xkv_done.append(ts_)
    t_xkvbf = [xkv_done[-1], xkv_done[-2]]

    b.op("gpsimd", lambda E: E.iota(iota32[:], pattern=[[1, NE]], base=0,
                                    channel_multiplier=0,
                                    allow_small_or_imprecise_dtypes=True))
    b.op("gpsimd", lambda E: E.iota(iota2048[:], pattern=[[2048, NE]], base=0,
                                    channel_multiplier=0,
                                    allow_small_or_imprecise_dtypes=True), inc="IOT")

    for dmc in range(16):
        cast_chunk(wk_in[dmc * 128:(dmc + 1) * 128, :], wkb[:, dmc * 512:(dmc + 1) * 512],
                   512, "vector")
        cast_chunk(wv_in[dmc * 128:(dmc + 1) * 128, :], wvb[:, dmc * 512:(dmc + 1) * 512],
                   512, "scalar")
    for dmc in range(16):
        cast_chunk(gate_in[dmc * 128:(dmc + 1) * 128, :], gateb[:, dmc * NE:(dmc + 1) * NE],
                   NE, "vector")
    b.barrier(("sync", "vector", "scalar", "gpsimd"))

    def emit_rope_tm(dst, srct, ctab, stab, nh, tmp):
        sr = srct.rearrange("p (h half i) -> p h half i", h=nh, half=2)
        dr = dst.rearrange("p (h half i) -> p h half i", h=nh, half=2)
        tr = tmp.rearrange("p (h i) -> p h i", h=nh)
        cb = ctab.unsqueeze(1).to_broadcast([128, nh, 64])
        sb_ = stab.unsqueeze(1).to_broadcast([128, nh, 64])
        b.op("vector", lambda E: E.tensor_tensor(dr[:, :, 0, :], sr[:, :, 0, :], cb, op=OP.mult))
        b.op("vector", lambda E: E.tensor_tensor(tr[:], sr[:, :, 1, :], sb_, op=OP.mult))
        b.op("vector", lambda E: E.tensor_tensor(dr[:, :, 0, :], dr[:, :, 0, :], tr[:], op=OP.subtract))
        b.op("vector", lambda E: E.tensor_tensor(dr[:, :, 1, :], sr[:, :, 1, :], cb, op=OP.mult))
        b.op("vector", lambda E: E.tensor_tensor(tr[:], sr[:, :, 0, :], sb_, op=OP.mult))
        b.op("vector", lambda E: E.tensor_tensor(
            dr[:, :, 1, :], dr[:, :, 1, :], tr[:], op=OP.add))
        return None

    if stage == "p0":
        b.barrier(ENGS)
        tlast = None
        for i in range(NT):
            tlast = b.dma("gpsimd", lambda E, i=i: E.dma_start(
                out=y_out[i * 128:(i + 1) * 128, :],
                in_=xkv_bf[768 + i * 128:768 + (i + 1) * 128, :]), ch="DBG")
        b.wait("sync", tlast)
        finish(nc, b)
        return nc

    pptok = {}

    def pwait(j, eng):
        t = pptok.get(j)
        if t is not None:
            b.wait(eng, t)

    pidx = 0
    kv_slices = [(i * 512, min(512, XKV - i * 512)) for i in range(6)]
    slice_done = {}
    for si, (t0, w) in enumerate(kv_slices):
        buf = xsl[si % 2]
        b.waits("scalar", t_xkvbf)
        if si >= 2:
            b.wait("scalar", slice_done[si - 2])
        ttr = None
        for dmc in range(16):
            ttr = b.dma("scalar", lambda E, t0=t0, w=w, dmc=dmc, buf=buf: E.dma_start_transpose(
                out=buf[:, dmc * 512:dmc * 512 + w],
                in_=xkv_bf[t0:t0 + w, dmc * 128:(dmc + 1) * 128]),
                ch="TRXa" if si % 2 == 0 else "TRXb")
        b.waits("tensor", [ttr])
        last_use = None
        pkb_tok = [None, None]
        for tt in range(w // 128):
            blk = t0 // 128 + tt
            j = pidx % 3
            pidx += 1
            pwait(j, "tensor")
            tmm = None
            for dmc in range(16):
                tmm = b.op("tensor", lambda E, j=j, dmc=dmc, tt=tt, buf=buf: E.matmul(
                    pp[j][:, 0:512],
                    buf[:, dmc * 512 + tt * 128:dmc * 512 + (tt + 1) * 128],
                    wkb[:, dmc * 512:(dmc + 1) * 512],
                    start=(dmc == 0), stop=(dmc == 15)), nop=False,
                    inc=(True if dmc == 15 else None))
            b.wait("vector", tmm)
            tevk = b.op("vector", lambda E, j=j: E.tensor_copy(out=ktm[:], in_=pp[j][:, 0:512]),
                        inc=f"PP{j}")
            pptok[j] = tevk
            emit_rope_tm(krope[:], ktm[:], ckvtm[:, blk * 64:(blk + 1) * 64],
                         skvtm[:, blk * 64:(blk + 1) * 64], NKV, rtmp2[:, 0:256])
            trope = b.op("vector", lambda E: E.tensor_copy(out=krope_b[:], in_=krope[:]),
                         inc="RP")
            b.waits("tensor", [trope])
            for f in range(NKV):
                pb_ = f % 2
                b.waits("tensor", [pkb_tok[pb_]])
                ttp = b.op("tensor", lambda E, f=f, pb_=pb_: E.transpose(
                    pkb[pb_][:], krope_b[:, f * 128:(f + 1) * 128], idb[:]), inc=True, nop=False)
                b.wait("vector", ttp)
                tevt = b.op("vector", lambda E, f=f, blk=blk, pb_=pb_: E.tensor_copy(
                    out=kt[:, f * XKV + blk * 128:f * XKV + (blk + 1) * 128],
                    in_=pkb[pb_][:]), inc=f"PKB{pb_}")
                pkb_tok[pb_] = tevt
                last_use = tevt
        ntt = w // 128
        for tt in range(ntt):
            j = pidx % 3
            pidx += 1
            pwait(j, "tensor")
            tmm = None
            for dmc in range(16):
                tmm = b.op("tensor", lambda E, j=j, dmc=dmc, tt=tt, buf=buf: E.matmul(
                    pp[j][:, 0:512],
                    buf[:, dmc * 512 + tt * 128:dmc * 512 + (tt + 1) * 128],
                    wvb[:, dmc * 512:(dmc + 1) * 512],
                    start=(dmc == 0), stop=(dmc == 15)), nop=False,
                    inc=(True if dmc == 15 else None))
            blk = t0 // 128 + tt
            b.wait("scalar", tmm)
            tev = b.op("scalar", lambda E, j=j, blk=blk: E.copy(
                out=vt[:, blk * 512:(blk + 1) * 512], in_=pp[j][:, 0:512]), inc=f"PP{j}")
            pptok[j] = tev
            last_use = tev
        slice_done[si] = last_use

    if stage == "p1":
        b.barrier(ENGS)
        tl1 = b.dma("gpsimd", lambda E: E.dma_start(out=y_out[0:128, :], in_=kt[:, 0:2048]), ch="DBG")
        tl2 = b.dma("gpsimd", lambda E: E.dma_start(out=y_out[128:256, :], in_=vt[:, 0:2048]), ch="DBG")
        b.wait("sync", tl2)
        finish(nc, b)
        return nc

    pa0.close()
    wqb = SB(pa, "wqb", [128, 16 * 2048], BF16)
    for dmc in range(16):
        cast_chunk(wq_in[dmc * 128:(dmc + 1) * 128, :],
                   wqb[:, dmc * 2048:(dmc + 1) * 2048], 2048,
                   "vector" if dmc % 2 == 0 else "scalar")
    b.barrier(("sync", "vector", "scalar", "tensor"))

    qst_tok = None
    p2_slfree = {}
    for si in range(4):
        t0 = 768 + si * 512
        buf = xsl[si % 2]
        b.waits("scalar", [p2_slfree.get(si % 2)])
        ttr = None
        for dmc in range(16):
            ttr = b.dma("scalar", lambda E, t0=t0, dmc=dmc, buf=buf: E.dma_start_transpose(
                out=buf[:, dmc * 512:(dmc + 1) * 512],
                in_=xkv_bf[t0:t0 + 512, dmc * 128:(dmc + 1) * 128]),
                ch="TRXa" if si % 2 == 0 else "TRXb")
        b.wait("tensor", ttr)
        pkb_tok = [None, None]
        for tt in range(4):
            tile = si * 4 + tt
            evtoks = []
            for nq in range(4):
                j = pidx % 3
                pidx += 1
                pwait(j, "tensor")
                tmm = None
                for dmc in range(16):
                    tmm = b.op("tensor", lambda E, j=j, dmc=dmc, nq=nq, tt=tt, buf=buf: E.matmul(
                        pp[j][:, 0:512],
                        buf[:, dmc * 512 + tt * 128:dmc * 512 + (tt + 1) * 128],
                        wqb[:, dmc * 2048 + nq * 512:dmc * 2048 + (nq + 1) * 512],
                        start=(dmc == 0), stop=(dmc == 15)), nop=False,
                        inc=(True if dmc == 15 else None))
                b.wait("vector", tmm)
                tcp = b.op("vector", lambda E, j=j, nq=nq: E.tensor_copy(
                    out=qtm[:, nq * 512:(nq + 1) * 512], in_=pp[j][:, 0:512]), inc=f"PP{j}")
                pptok[j] = tcp
                evtoks.append(tcp)
            emit_rope_tm(qrope[:], qtm[:], cqtm[:, tile * 64:(tile + 1) * 64],
                         sqtm[:, tile * 64:(tile + 1) * 64], NH, rtmp2[:])
            trope = b.op("vector", lambda E: E.tensor_copy(out=qrope_b[:], in_=qrope[:]),
                         inc="RP")
            b.waits("tensor", [trope])
            b.waits("vector", [qst_tok])
            for f in range(16):
                pb_ = f % 2
                b.waits("tensor", [pkb_tok[pb_]])
                ttp = b.op("tensor", lambda E, f=f, pb_=pb_: E.transpose(
                    pkb[pb_][:], qrope_b[:, f * 128:(f + 1) * 128], idb[:]), inc=True, nop=False)
                b.wait("vector", ttp)
                tevt = b.op("vector", lambda E, f=f, pb_=pb_: E.tensor_copy(
                    out=qtts[:, f * 128:(f + 1) * 128], in_=pkb[pb_][:]), inc=f"PKB{pb_}")
                pkb_tok[pb_] = tevt
            b.waits("sync", pkb_tok)
            qst_tok = b.dma("sync", lambda E, tile=tile: E.dma_start(
                out=qt_dram[tile].rearrange("f p c -> p f c"),
                in_=qtts[:].rearrange("p (f c) -> p f c", f=16)), ch="QST")
        p2_slfree[si % 2] = tmm

    b.barrier(ENGS)
    if stage == "p2":
        tl1 = None
        for i in range(4):
            tl1 = b.dma("gpsimd", lambda E, i=i: E.dma_start(
                out=y_out[i * 128:(i + 1) * 128, :].rearrange("p (f c) -> p f c", f=16),
                in_=qt_dram[i].rearrange("f p c -> p f c")), ch="DBG")
        b.wait("sync", tl1)
        finish(nc, b)
        return nc

    pa.close()

    pb = ExitStack()
    wob = SB(pb, "wob", [128, 16 * 2048], BF16)
    qt_t = SB(pb, "qt_t", [128, 16 * 128], BF16)
    mask_t = SB(pb, "mask_t", [128, 896], F32)
    x_t = SB(pb, "x_t", [128, D], F32)
    Pt = SB(pb, "Pt", [128, 896], BF16)
    PTs = SB(pb, "PTs", [128, 896], BF16)
    sm = SB(pb, "sm", [128, 896], F32)
    outT = SB(pb, "outT", [128, 16 * 128], BF16)
    xa_f = SB(pb, "xa_f", [128, D], F32)
    xab_t = SB(pb, "xab_t", [128, D], BF16)
    m8 = SB(pb, "m8", [128, 8], F32)
    zapt = SB(pb, "zapt", [128, NE], F32)
    Mm = SB(pb, "Mm", [128, NE], F32)
    ew = SB(pb, "ew", [128, NE], F32)
    Wd = SB(pb, "Wd", [128, NE], F32)
    Whi = SB(pb, "Whi", [128, NE], BF16)
    Wlo = SB(pb, "Wlo", [128, NE], F32)
    Wlob = SB(pb, "Wlob", [128, NE], BF16)
    rbT = SB(pb, "rbT", [32, 128], F32)
    cums = SB(pb, "cums", [32, 128], F32)
    zer32 = SB(pb, "zer32", [32, 128], F32)
    carry = SB(pb, "carry", [32, 1], F32)
    posT = SB(pb, "posT", [32, 128], F32)
    pos_tm = SB(pb, "pos_tm", [128, NE], F32)
    kva = SB(pb, "kva", [128, NE], F32)
    kvm = SB(pb, "kvm", [128, NE], F32)
    kv8 = SB(pb, "kv8", [128, 8], F32)
    e8 = SB(pb, "e8", [128, 8], F32)
    Em = SB(pb, "Em", [128, NE], F32)
    pos8 = SB(pb, "pos8", [128, 8], F32)
    slot8f = SB(pb, "slot8f", [128, 8], F32)
    ovf = SB(pb, "ovf", [128, 8], F32)
    logT_sb = SB(pb, "logT_sb", [32, 512], F32)
    payload = [SB(pb, f"pay{j}", [128, PAYW], BF16) for j in range(2)]
    S1 = PS(pb, "S1", [128, 512])
    S2 = PS(pb, "S2", [128, 384])
    PTp = PS(pb, "PTp", [128, 896], BF16)
    OHT = PS(pb, "OHT", [128, 128])
    misc1 = PS(pb, "misc1", [32, 512])
    misc2 = PS(pb, "misc2", [128, 32])

    b.op("vector", lambda E: E.memset(zer32[:], 0.0))
    b.op("vector", lambda E: E.memset(carry[:], 0.0))
    b.op("vector", lambda E: E.memset(payload[0][:, 2112:2176], 0.0))
    b.op("vector", lambda E: E.memset(payload[1][:, 2112:2176], 0.0))

    for f in range(16):
        cast_chunk(wo_in[f * 128:(f + 1) * 128, :], wob[:, f * 2048:(f + 1) * 2048],
                   2048, "vector" if f % 2 == 0 else "scalar")
    b.barrier(("sync", "vector", "scalar", "tensor"))

    tile_tok = {}
    pay_tok = {}
    xab_store = {}
    xa_store = {}
    state = {
        "s_bank": None,
        "sm_free": None,
        "pt_bank": None,
        "pts_free": None,
        "pt_sb": None,
        "oht_bank": None,
        "outT_free": None,
        "y_bank": None,
        "xaf_free": None,
        "xab_free": None,
        "pay_free": {},
        "route_in": {},
    }

    def emit_tile_loads(i, first=False):
        toks = []
        if not first:
            b.waits("sync", [state["y_bank"]])
        toks.append(b.dma("sync", lambda E, i=i: E.dma_start(
            out=qt_t[:].rearrange("p (f c) -> p f c", f=16),
            in_=qt_dram[i].rearrange("f p c -> p f c")), ch="TL"))
        toks.append(b.dma("sync", lambda E, i=i: E.dma_start(
            out=mask_t[:], in_=mask_in[i]), ch="TL"))
        toks.append(b.dma("sync", lambda E, i=i: E.dma_start(
            out=x_t[:], in_=xkv_in[768 + i * 128:768 + (i + 1) * 128, :]), ch="TL"))
        tile_tok[i] = toks

    def emit_gate_quarter(qtr):
        t0 = qtr * 512
        buf = xsl[0]
        b.waits("scalar", [xab_store[qtr * 4 + j] for j in range(4)]
                + [state.get("gate_sl_free")])
        ttr = None
        for dmc in range(16):
            ttr = b.dma("scalar", lambda E, t0=t0, dmc=dmc, buf=buf: E.dma_start_transpose(
                out=buf[:, dmc * 512:(dmc + 1) * 512],
                in_=xab_dram[t0:t0 + 512, dmc * 128:(dmc + 1) * 128]), ch="TRXG")
        b.waits("tensor", [ttr])
        tgm = None
        for dmc in range(16):
            tgm = b.op("tensor", lambda E, dmc=dmc, buf=buf: E.matmul(
                misc1[:, 0:512], gateb[:, dmc * NE:(dmc + 1) * NE],
                buf[:, dmc * 512:(dmc + 1) * 512],
                start=(dmc == 0), stop=(dmc == 15)), nop=False,
                inc=(True if dmc == 15 else None))
        state["gate_sl_free"] = tgm
        b.wait("vector", tgm)
        tge = b.op("vector", lambda E: E.tensor_copy(out=logT_sb[:], in_=misc1[:, 0:512]),
                   inc="GLG")
        tt2 = None
        for jj in range(4):
            i = qtr * 4 + jj
            b.waits("tensor", [tge, tt2])
            tt1 = b.op("tensor", lambda E, jj=jj: E.transpose(
                misc2[:], logT_sb[:, jj * 128:(jj + 1) * 128], idf[0:32, 0:32]),
                inc=True, nop=False)
            b.wait("vector", tt1)
            tt2 = b.op("vector", lambda E, i=i: E.tensor_copy(
                out=logits[:, i * NE:(i + 1) * NE], in_=misc2[:]), inc="GL2")
            state["route_in"][i] = tt2

    def emit_routing_tile(i):
        if rparts < 2:
            pay_tok[i] = None
            return
        lg = logits[:, i * NE:(i + 1) * NE]
        b.waits("vector", [state["route_in"][i]])
        b.op("vector", lambda E, lg=lg: E.tensor_copy(out=zapt[:], in_=lg))
        b.op("vector", lambda E: E.memset(Mm[:], 0.0))
        for kk in range(8):
            b.fop("vector", lambda E, kk=kk: E.reduce_max(m8[:, kk:kk + 1], zapt[:], axis=AX))
            b.fop("vector", lambda E, kk=kk: E.tensor_scalar(
                ew[:], zapt[:], m8[:, kk:kk + 1], None, op0=OP.is_equal))
            b.fop("vector", lambda E: E.tensor_tensor(Mm[:], Mm[:], ew[:], op=OP.add))
            b.fop("vector", lambda E: E.scalar_tensor_tensor(
                zapt[:], in0=ew[:], scalar=NEG, in1=zapt[:], op0=OP.mult, op1=OP.add))
        if i == 1:
            b.op("vector", lambda E, lg=lg: E.tensor_copy(out=rdbg[3][:], in_=lg))
        tneg = b.op("vector", lambda E: E.tensor_scalar(
            small[:, 14:15], m8[:, 0:1], -1.0, None, op0=OP.mult), inc="RNG")
        b.wait("scalar", tneg)
        texp = b.op("scalar", lambda E, lg=lg: E.activation(
            out=ew[:], in_=lg, func=AF.Exp, bias=small[:, 14:15]), inc="RXP")
        b.wait("vector", texp)
        b.fop("vector", lambda E: E.tensor_tensor(Wd[:], ew[:], Mm[:], op=OP.mult))
        b.fop("vector", lambda E: E.reduce_sum(small[:, 15:16], Wd[:], axis=AX))
        b.fop("vector", lambda E: E.reciprocal(small[:, 14:15], small[:, 15:16]))
        b.fop("vector", lambda E: E.tensor_scalar(
            Wd[:], Wd[:], small[:, 14:15], None, op0=OP.mult))
        b.fop("vector", lambda E: E.tensor_copy(out=Whi[:], in_=Wd[:]))
        b.fop("vector", lambda E: E.tensor_copy(out=ew[:], in_=Whi[:]))
        b.fop("vector", lambda E: E.tensor_tensor(Wlo[:], Wd[:], ew[:], op=OP.subtract))
        twv = b.fop("vector", lambda E: E.tensor_copy(out=Wlob[:], in_=Wlo[:]), inc="RWV")
        b.waits("tensor", [twv])
        tmt = b.op("tensor", lambda E: E.transpose(misc1[:, 0:128], Mm[:], idf[:]),
                   inc=True, nop=False)
        b.wait("vector", tmt)
        trb = b.fop("vector", lambda E: E.tensor_copy(out=rbT[:], in_=misc1[0:32, 0:128]),
                    inc="RCA")
        state["misc1_free"] = trb
        b.fop("vector", lambda E: E.tensor_tensor_scan(
            cums[:], data0=rbT[:], data1=zer32[:], initial=0.0,
            op0=OP.add, op1=OP.add))
        b.op("vector", lambda E: E.tensor_scalar(
            cums[:], cums[:], carry[:, 0:1], None, op0=OP.add))
        b.op("vector", lambda E: E.tensor_tensor(posT[:], cums[:], rbT[:], op=OP.subtract))
        if i == 1:
            b.op("vector", lambda E: E.tensor_copy(out=rdbg[0][:], in_=rbT[:]))
            b.op("vector", lambda E: E.tensor_copy(out=rdbg[1][:], in_=cums[:]))
        tcar = b.fop("vector", lambda E: E.tensor_copy(out=carry[:], in_=cums[:, 127:128]),
                     inc="RCA")
        b.wait("tensor", tcar)
        tpt = b.op("tensor", lambda E: E.transpose(misc2[:], posT[:], idf[0:32, 0:32]),
                   inc=True, nop=False)
        b.wait("vector", tpt)
        b.op("vector", lambda E: E.tensor_copy(out=pos_tm[:], in_=misc2[:]))
        if i == 1:
            b.op("vector", lambda E: E.tensor_copy(out=rdbg[2][:], in_=pos_tm[:]))
        b.op("vector", lambda E: E.tensor_tensor(kva[:], iota2048[:], pos_tm[:], op=OP.add))
        b.op("vector", lambda E: E.tensor_scalar(kva[:], kva[:], 1.0, None, op0=OP.add))
        b.op("vector", lambda E: E.tensor_tensor(kvm[:], kva[:], Mm[:], op=OP.mult))
        b.fop("vector", lambda E: E.tensor_scalar(kvm[:], kvm[:], 1.0, None, op0=OP.subtract))
        b.fop("vector", lambda E: E.max(out=kv8[:], in_=kvm[:]))
        b.op("vector", lambda E: E.tensor_scalar(Em[:], iota32[:], 1.0, None, op0=OP.add))
        b.op("vector", lambda E: E.tensor_tensor(Em[:], Em[:], Mm[:], op=OP.mult))
        b.fop("vector", lambda E: E.tensor_scalar(Em[:], Em[:], 1.0, None, op0=OP.subtract))
        b.fop("vector", lambda E: E.max(out=e8[:], in_=Em[:]))
        b.op("vector", lambda E: E.scalar_tensor_tensor(
            pos8[:], in0=e8[:], scalar=-2048.0, in1=kv8[:], op0=OP.mult, op1=OP.add))
        if i == 1:
            b.op("vector", lambda E: E.tensor_copy(out=rdbg[4][:], in_=kv8[:]))
            b.op("vector", lambda E: E.tensor_copy(out=rdbg[5][:], in_=e8[:]))
        b.op("vector", lambda E: E.scalar_tensor_tensor(
            slot8f[:], in0=e8[:], scalar=640.0, in1=pos8[:], op0=OP.mult, op1=OP.add))
        b.op("vector", lambda E: E.tensor_scalar(ovf[:], pos8[:], 639.5, None, op0=OP.is_gt))
        b.op("vector", lambda E: E.tensor_scalar(kv8[:], slot8f[:], float(ROWS), None,
                                                 op0=OP.subtract))
        b.op("vector", lambda E: E.tensor_tensor(kv8[:], kv8[:], ovf[:], op=OP.mult))
        b.op("vector", lambda E: E.tensor_tensor(slot8f[:], slot8f[:], kv8[:], op=OP.subtract))
        b.op("vector", lambda E: E.tensor_scalar(ovf[:], slot8f[:], 0.0, None, op0=OP.is_lt))
        b.op("vector", lambda E: E.tensor_scalar(kv8[:], slot8f[:], float(ROWS), None,
                                                 op0=OP.subtract))
        b.op("vector", lambda E: E.tensor_tensor(kv8[:], kv8[:], ovf[:], op=OP.mult))
        b.op("vector", lambda E: E.tensor_tensor(slot8f[:], slot8f[:], kv8[:], op=OP.subtract))
        tsl8 = b.op("vector", lambda E, i=i: E.tensor_copy(
            out=slot8i[:, i * 8:(i + 1) * 8], in_=slot8f[:]), inc="RS8")
        if rparts < 3:
            pay_tok[i] = None
            return
        bi = i % 2
        prevp = state["pay_free"].get(bi)
        if prevp is not None:
            b.waits("sync", [prevp])
        tpl = b.dma("sync", lambda E, i=i, bi=bi: E.dma_start(
            out=payload[bi][:, 0:2048], in_=xab_dram[i * 128:(i + 1) * 128, :]),
            ch="PLDa" if bi == 0 else "PLDb")
        b.waits("vector", [tpl] + ([prevp] if prevp else []))
        b.op("vector", lambda E, bi=bi: E.tensor_copy(
            out=payload[bi][:, 2048:2112].rearrange("p (c two) -> p c two", two=2)[:, :, 0],
            in_=Whi[:]))
        tpw = b.op("vector", lambda E, bi=bi: E.tensor_copy(
            out=payload[bi][:, 2048:2112].rearrange("p (c two) -> p c two", two=2)[:, :, 1],
            in_=Wlob[:]), inc="RPW")
        if rparts < 4:
            pay_tok[i] = None
            return
        b.waits("gpsimd", [tpw, tsl8])
        tsc = None
        for k in range(8):
            tsc = b.dma("gpsimd", lambda E, i=i, k=k, bi=bi: E.indirect_dma_start(
                out=xsorted[:, :],
                out_offset=IndirectOffsetOnAxis(slot8i[:, i * 8 + k:i * 8 + k + 1], 0),
                in_=payload[bi][:, :], in_offset=None),
                ch="SCTa" if bi == 0 else "SCTb")
        pay_tok[i] = tsc
        state["pay_free"][bi] = tsc

    emit_tile_loads(0, first=True)
    for i in range(NT):
        b.waits("tensor", tile_tok[i])
        toh = None
        for h in range(NH):
            kvh = h // 4
            kb = kvh * XKV
            b.waits("tensor", [state["s_bank"]])
            b.op("tensor", lambda E, h=h, kb=kb: E.matmul(
                S1[:, 0:256], qt_t[:, h * 128:(h + 1) * 128],
                kt[:, kb:kb + 256], start=True, stop=True), nop=False)
            b.op("tensor", lambda E, h=h, kb=kb, i=i: E.matmul(
                S1[:, 256:512], qt_t[:, h * 128:(h + 1) * 128],
                kt[:, kb + 256 + 128 * i:kb + 512 + 128 * i], start=True, stop=True), nop=False)
            tsc_ = b.op("tensor", lambda E, h=h, kb=kb, i=i: E.matmul(
                S2[:, 0:384], qt_t[:, h * 128:(h + 1) * 128],
                kt[:, kb + 512 + 128 * i:kb + 896 + 128 * i], start=True, stop=True),
                nop=False, inc=True)
            b.wait("vector", tsc_)
            b.op("vector", lambda E: E.reduce_max(small[:, 0:1], S1[:], axis=AX))
            b.fop("vector", lambda E: E.reduce_max(small[:, 1:2], S2[:], axis=AX))
            b.op("vector", lambda E: E.tensor_tensor(small[:, 2:3], small[:, 0:1],
                                                     small[:, 1:2], op=OP.max))
            b.op("vector", lambda E: E.tensor_scalar(small[:, 3:4], small[:, 2:3], -1.0,
                                                     None, op0=OP.mult))
            b.waits("vector", [state["sm_free"]])
            b.op("vector", lambda E: E.scalar_tensor_tensor(
                sm[:, 0:512], in0=S1[:], scalar=small[:, 3:4], in1=mask_t[:, 0:512],
                op0=OP.add, op1=OP.add))
            tmsk = b.op("vector", lambda E: E.scalar_tensor_tensor(
                sm[:, 512:896], in0=S2[:], scalar=small[:, 3:4], in1=mask_t[:, 512:896],
                op0=OP.add, op1=OP.add), inc="SMK")
            state["s_bank"] = tmsk
            if dbg and i == 0 and h == 0:
                b.wait("gpsimd", tmsk)
                b.dma("gpsimd", lambda E: E.dma_start(out=dbg_out[0][:, 0:896], in_=sm[:]), ch="DBG")
                b.dma("gpsimd", lambda E: E.dma_start(out=dbg_out[7][:, 0:896], in_=kt[:, 0:896]), ch="DBG")
                b.dma("gpsimd", lambda E: E.dma_start(out=dbg_out[7][:, 896:2048], in_=qt_t[:, 0:1152]), ch="DBG")
            b.waits("scalar", [tmsk, state["pt_sb"]])
            b.op("scalar", lambda E: E.activation(
                out=Pt[:, 0:512], in_=sm[:, 0:512], func=AF.Exp, accum_out=small[:, 4:5]))
            texp = b.op("scalar", lambda E: E.activation(
                out=Pt[:, 512:896], in_=sm[:, 512:896], func=AF.Exp,
                accum_out=small[:, 5:6]), inc="EXP")
            state["sm_free"] = texp
            b.wait("vector", texp)
            b.op("vector", lambda E: E.tensor_tensor(small[:, 6:7], small[:, 4:5],
                                                     small[:, 5:6], op=OP.add))
            b.fop("vector", lambda E: E.reciprocal(small[:, 7:8], small[:, 6:7]))
            tnorm = b.op("vector", lambda E: E.tensor_scalar(
                Pt[:], Pt[:], small[:, 7:8], None, op0=OP.mult), inc="PNM")
            if dbg and i == 0 and h == 0:
                b.wait("gpsimd", tnorm)
                b.dma("gpsimd", lambda E: E.dma_start(out=dbg_out[1][:, 0:896], in_=Pt[:]), ch="DBG")
                b.dma("gpsimd", lambda E: E.dma_start(out=dbg_out[2][:, 0:8], in_=small[:, 0:8]), ch="DBG")
            b.waits("tensor", [tnorm, state["pt_bank"]])
            ttr = None
            for jblk in range(7):
                ttr = b.op("tensor", lambda E, jblk=jblk: E.transpose(
                    PTp[:, jblk * 128:(jblk + 1) * 128], Pt[:, jblk * 128:(jblk + 1) * 128],
                    idb[:]), nop=False, inc=(True if jblk == 6 else None))
            state["pt_sb"] = ttr
            b.waits("vector", [ttr, state["pts_free"]])
            tevp = b.op("vector", lambda E: E.tensor_copy(out=PTs[:], in_=PTp[:]), inc="PTE")
            state["pt_bank"] = tevp
            b.waits("tensor", [tevp, state["oht_bank"]])
            tav = None
            for jblk in range(7):
                vblk = jblk if jblk < 2 else (i + jblk)
                tav = b.op("tensor", lambda E, jblk=jblk, vblk=vblk, kvh=kvh: E.matmul(
                    OHT[:], vt[:, vblk * 512 + kvh * 128:vblk * 512 + (kvh + 1) * 128],
                    PTs[:, jblk * 128:(jblk + 1) * 128],
                    start=(jblk == 0), stop=(jblk == 6)), nop=False,
                    inc=(True if jblk == 6 else None))
            state["pts_free"] = tav
            b.waits("scalar", [tav, state["outT_free"]])
            toh = b.op("scalar", lambda E, h=h: E.copy(
                out=outT[:, h * 128:(h + 1) * 128], in_=OHT[:]), inc="OHE")
            state["oht_bank"] = toh
            if dbg and i == 0 and h == 0:
                b.wait("gpsimd", toh)
                b.dma("gpsimd", lambda E: E.dma_start(out=dbg_out[4][:, 0:128], in_=outT[:, 0:128]), ch="DBG")
                b.dma("gpsimd", lambda E: E.dma_start(out=dbg_out[3][:, 0:896], in_=PTs[:]), ch="DBG")
        b.waits("tensor", [toh])
        b.waits("vector", [state["xaf_free"]])
        ybank_tok = state.get("ybank_tok", [None, None])
        tyev = None
        for nc4 in range(4):
            yb = nc4 % 2
            b.waits("tensor", [ybank_tok[yb]])
            two = None
            for f in range(16):
                two = b.op("tensor", lambda E, nc4=nc4, f=f: E.matmul(
                    Yp[nc4 % 2][:],
                    outT[:, f * 128:(f + 1) * 128],
                    wob[:, f * 2048 + nc4 * 512:f * 2048 + (nc4 + 1) * 512],
                    start=(f == 0), stop=(f == 15)), nop=False,
                    inc=(True if f == 15 else None))
            b.waits("vector", [two])
            tyev = b.op("vector", lambda E, nc4=nc4, yb=yb: E.scalar_tensor_tensor(
                x1[:, nc4 * 512:(nc4 + 1) * 512], in0=Yp[yb][:], scalar=0.0,
                in1=x_t[:, nc4 * 512:(nc4 + 1) * 512],
                op0=OP.add, op1=OP.add), inc=f"YB{yb}")
            ybank_tok[yb] = tyev
        state["ybank_tok"] = ybank_tok
        state["outT_free"] = two
        b.fop("vector", lambda E: E.reduce_sum(small[:, 8:9], x1[:], axis=AX))
        b.op("vector", lambda E: E.tensor_scalar(
            small[:, 10:11], small[:, 8:9], 1.0 / D, None, op0=OP.mult))
        b.wait("scalar", tyev)
        tsq = b.op("scalar", lambda E: E.activation(
            out=x_t[:], in_=x1[:], func=AF.Square, accum_out=small[:, 9:10]), inc="LSQ")
        state["y_bank"] = tsq
        if dbg and i == 0:
            b.wait("gpsimd", tsq)
            b.dma("gpsimd", lambda E: E.dma_start(out=dbg_out[5][:, :], in_=outT[:]), ch="DBG")
            b.dma("gpsimd", lambda E: E.dma_start(out=dbg_out[6][:, :], in_=x1[:]), ch="DBG")
        b.wait("vector", tsq)
        b.op("vector", lambda E: E.tensor_scalar(
            small[:, 11:12], small[:, 9:10], 1.0 / D, None, op0=OP.mult))
        b.op("vector", lambda E: E.tensor_tensor(
            small[:, 9:10], small[:, 10:11], small[:, 10:11], op=OP.mult))
        b.op("vector", lambda E: E.tensor_tensor(
            small[:, 11:12], small[:, 11:12], small[:, 9:10], op=OP.subtract))
        b.op("vector", lambda E: E.tensor_scalar(
            small[:, 11:12], small[:, 11:12], EPS, None, op0=OP.add))
        trec = b.fop("vector", lambda E: E.reciprocal(small[:, 12:13], small[:, 11:12]),
                     inc="LRC")
        b.wait("scalar", trec)
        tst_ = b.op("scalar", lambda E: E.activation(
            out=small[:, 13:14], in_=small[:, 12:13], func=AF.Sqrt), inc="LSR")
        b.wait("vector", tst_)
        txa = b.op("vector", lambda E: E.tensor_scalar(
            xa_f[:], x1[:], small[:, 10:11], small[:, 13:14],
            op0=OP.subtract, op1=OP.mult), inc="LXA")
        b.waits("scalar", [txa, state["xab_free"]])
        txab = b.op("scalar", lambda E: E.copy(out=xab_t[:], in_=xa_f[:]), inc="LXB")
        b.waits("sync", [txa, txab])
        xch = "XSTa" if (i // 4) % 2 == 0 else "XSTb"
        xa_store[i] = b.dma("sync", lambda E, i=i: E.dma_start(
            out=xa_dram[i * 128:(i + 1) * 128, :], in_=xa_f[:]), ch=xch)
        xab_store[i] = b.dma("sync", lambda E, i=i: E.dma_start(
            out=xab_dram[i * 128:(i + 1) * 128, :], in_=xab_t[:]), ch=xch)
        state["xaf_free"] = xa_store[i]
        state["xab_free"] = xab_store[i]
        if i + 1 < NT:
            emit_tile_loads(i + 1)

        if stage in ("full", "route", "p4") and i % 4 == 3:
            emit_gate_quarter(i // 4)
            for jj in range(4):
                emit_routing_tile(i - 3 + jj)

    if stage == "route":
        slotdump = dt("slotdump", [128, 128], I32, kind="ExternalOutput").ap()
        rdump = dt("rdump", [6, 128, 128], F32, kind="ExternalOutput").ap()
        b.waits("gpsimd", [b.last["vector"]])
        b.dma("gpsimd", lambda E: E.dma_start(out=rdump[0][0:32, :], in_=rdbg[0][:]), ch="DBG")
        b.dma("gpsimd", lambda E: E.dma_start(out=rdump[1][0:32, :], in_=rdbg[1][:]), ch="DBG")
        b.dma("gpsimd", lambda E: E.dma_start(out=rdump[2][:, 0:32], in_=rdbg[2][:]), ch="DBG")
        b.dma("gpsimd", lambda E: E.dma_start(out=rdump[3][:, 0:32], in_=rdbg[3][:]), ch="DBG")
        b.dma("gpsimd", lambda E: E.dma_start(out=rdump[4][:, 0:8], in_=rdbg[4][:]), ch="DBG")
        b.dma("gpsimd", lambda E: E.dma_start(out=rdump[5][:, 0:8], in_=rdbg[5][:]), ch="DBG")
        b.waits("gpsimd", [t for t in (pay_tok[i] for i in range(NT)) if t is not None])
        b.barrier(ENGS)
        tlast = b.dma("gpsimd", lambda E: E.dma_start(out=slotdump[:], in_=slot8i[:]), ch="DBG")
        for i in range(NT):
            tlast = b.dma("gpsimd", lambda E, i=i: E.dma_start(
                out=y_out[i * 128:(i + 1) * 128, :],
                in_=xa_dram[i * 128:(i + 1) * 128, :]), ch="DBG")
        b.wait("sync", tlast)
        finish(nc, b)
        pb.close()
        pab.close()
        top.close()
        return nc

    if stage == "attn":
        b.barrier(ENGS)
        tlast = None
        for i in range(NT):
            tlast = b.dma("gpsimd", lambda E, i=i: E.dma_start(
                out=y_out[i * 128:(i + 1) * 128, :],
                in_=xa_dram[i * 128:(i + 1) * 128, :]), ch="DBG")
        b.wait("sync", tlast)
        finish(nc, b)
        pb.close()
        pab.close()
        top.close()
        return nc

    b.barrier(ENGS)
    pb.close()
    pab.close()

    pc = ExitStack()
    wgb = SB(pc, "wgb", [128, 2 * 16 * INTER], BF16)
    WUB = 16 * INTER
    wdb = SB(pc, "wdb", [128, 6 * 2048], BF16)
    xgt = [SB(pc, f"xgt{j}", [128, 16 * CAP], BF16) for j in range(2)]
    xgrow = [SB(pc, f"xgrow{j}", [128, D], BF16) for j in range(2)]
    TRp = [PS(pc, f"TRp{j}", [128, 512], BF16) for j in range(2)]
    wc_b = [SB(pc, f"wc_b{j}", [128, 10], BF16) for j in range(2)]
    wcol = [SB(pc, f"wcol{j}", [128, 5], F32) for j in range(2)]
    hT = SB(pc, "hT", [128, 6 * CAP], BF16)
    hs = SB(pc, "hs", [128, CAP], F32)
    ysb = [SB(pc, f"ysb{j}", [128, D], BF16) for j in range(2)]
    Gp = PS(pc, "Gp", [128, CAP])
    Up = PS(pc, "Up", [128, CAP])

    wg_ready = [None] * NE
    wd_ready = [None] * NE
    ffn1_done = [None] * NE
    ffn2_done = [None] * NE
    xgt_ready = [None] * NE
    wcol_ok = [None] * NE
    ysb_tok = [None, None]
    sil_tok = None
    hmu_tok = None

    def emit_weight_load(e, kind):
        toks = []
        if kind == "gu":
            b.waits("gpsimd", [ffn1_done[e - 1]] if e > 0 else [])
            for src, base in ((wg_in, 0), (wu_in, WUB)):
                for h in range(2):
                    toks.append(b.dma("gpsimd", lambda E, src=src, base=base, h=h, e=e:
                        E.dma_start(
                            out=wgb[:, base + h * 8 * INTER:base + (h + 1) * 8 * INTER]
                                .rearrange("p (c f) -> p c f", c=8),
                            in_=src[e, h * 1024:(h + 1) * 1024, :]
                                .rearrange("(c p) f -> p c f", p=128)), ch="WGU"))
            wg_ready[e] = toks
        else:
            b.waits("gpsimd", [ffn2_done[e - 1]] if e > 0 else [])
            for h in range(2):
                toks.append(b.dma("gpsimd", lambda E, h=h, e=e: E.dma_start(
                    out=wdb[:, h * 3 * 2048:(h + 1) * 3 * 2048]
                        .rearrange("p (c f) -> p c f", c=3),
                    in_=wd_in[e, h * 384:(h + 1) * 384, :]
                        .rearrange("(c p) f -> p c f", p=128)), ch="WDD"))
            wd_ready[e] = toks

    xg_state = {"rowfree": [None, None], "trpfree": [None, None]}

    def emit_xgt_load(e):
        bj = e % 2
        tev = None
        for rc in range(5):
            rcj = rc % 2
            ldeng = "sync" if rcj == 0 else "scalar"
            b.waits(ldeng, [xg_state["rowfree"][rcj]])
            tld = b.dma(ldeng, lambda E, e=e, rc=rc, rcj=rcj: E.dma_start(
                out=xgrow[rcj][:],
                in_=xsorted[e * CAP + rc * 128:e * CAP + (rc + 1) * 128, 0:D]),
                ch="TRX4a" if rcj == 0 else "TRX4b")
            ttr = None
            for g in range(4):
                tb = (rc * 4 + g) % 2
                b.waits("tensor", [tld, xg_state["trpfree"][tb]])
                for q in range(4):
                    ttr = b.op("tensor", lambda E, g=g, q=q, tb=tb, rcj=rcj: E.transpose(
                        TRp[tb][:, q * 128:(q + 1) * 128],
                        xgrow[rcj][:, (g * 4 + q) * 128:(g * 4 + q + 1) * 128],
                        idb[:]), nop=False, inc=(True if q == 3 else None))
                b.waits("vector", [ttr] + (
                    [ffn1_done[e - 2]] if (e >= 2 and rc == 0 and g == 0) else []))
                tev = b.op("vector", lambda E, g=g, rc=rc, tb=tb, bj=bj: E.tensor_copy(
                    out=xgt[bj][:].rearrange("p (c t) -> p c t", c=16)
                        [:, g * 4:(g + 1) * 4, rc * 128:(rc + 1) * 128],
                    in_=TRp[tb][:].rearrange("p (c t) -> p c t", c=4)), inc="XEV")
                xg_state["trpfree"][tb] = tev
            xg_state["rowfree"][rcj] = ttr
        xgt_ready[e] = tev
        tw = []
        for tc5 in range(5):
            tw.append(b.dma("scalar", lambda E, e=e, tc5=tc5, bj=bj: E.dma_start(
                out=wc_b[bj][:, tc5 * 2:(tc5 + 1) * 2],
                in_=xsorted[e * CAP + tc5 * 128:e * CAP + (tc5 + 1) * 128,
                            WCOL + 2 * e:WCOL + 2 * e + 2]), ch="MISC"))
        b.waits("vector", tw + ([ffn2_done[e - 2]] if e >= 2 else []))
        tv = None
        for tc5 in range(5):
            tv = b.op("vector", lambda E, tc5=tc5, bj=bj: E.tensor_tensor(
                wcol[bj][:, tc5:tc5 + 1], wc_b[bj][:, tc5 * 2:tc5 * 2 + 1],
                wc_b[bj][:, tc5 * 2 + 1:tc5 * 2 + 2], op=OP.add),
                inc=("WCL" if tc5 == 4 else None))
        wcol_ok[e] = tv

    scat_all = [pay_tok[i] for i in range(NT)]
    b.waits("scalar", scat_all)
    b.waits("sync", scat_all)
    b.waits("gpsimd", scat_all)
    emit_weight_load(0, "gu")
    emit_xgt_load(0)
    emit_weight_load(0, "d")

    for e in range(NE):
        bj = e % 2
        b.waits("tensor", wg_ready[e] + [xgt_ready[e]])
        for ic in range(6):
            b.waits("tensor", [sil_tok])
            tgm = None
            for ncc, w in ((0, 512), (512, 128)):
                for dmc in range(16):
                    tgm = b.op("tensor", lambda E, ic=ic, ncc=ncc, w=w, dmc=dmc, bj=bj: E.matmul(
                        Gp[:, ncc:ncc + w],
                        wgb[:, dmc * INTER + ic * 128:dmc * INTER + (ic + 1) * 128],
                        xgt[bj][:, dmc * CAP + ncc:dmc * CAP + ncc + w],
                        start=(dmc == 0), stop=(dmc == 15)), nop=False,
                        inc=(True if (ncc, dmc) == (512, 15) else None))
            b.waits("tensor", [hmu_tok])
            tum = None
            for ncc, w in ((0, 512), (512, 128)):
                for dmc in range(16):
                    tum = b.op("tensor", lambda E, ic=ic, ncc=ncc, w=w, dmc=dmc, bj=bj: E.matmul(
                        Up[:, ncc:ncc + w],
                        wgb[:, WUB + dmc * INTER + ic * 128:WUB + dmc * INTER + (ic + 1) * 128],
                        xgt[bj][:, dmc * CAP + ncc:dmc * CAP + ncc + w],
                        start=(dmc == 0), stop=(dmc == 15)), nop=False,
                        inc=(True if (ncc, dmc) == (512, 15) else None))
            b.waits("scalar", [tgm, hmu_tok])
            tsl = b.op("scalar", lambda E: E.activation(
                out=hs[:], in_=Gp[:], func=AF.Silu), inc="SIL")
            sil_tok = tsl
            b.waits("vector", [tsl, tum])
            tmu = b.op("vector", lambda E, ic=ic: E.tensor_tensor(
                hT[:, ic * CAP:(ic + 1) * CAP], hs[:], Up[:], op=OP.mult), inc="HMU")
            hmu_tok = tmu
        ffn1_done[e] = hmu_tok
        if e + 1 < NE:
            emit_weight_load(e + 1, "gu")
            emit_xgt_load(e + 1)
        b.waits("tensor", wd_ready[e] + [ffn1_done[e], wcol_ok[e]])
        tym_last = None
        ychunk_tok = [None, None]
        for tc5 in range(5):
            yb2 = tc5 % 2
            b.waits("tensor", [ysb_tok[yb2]])
            tyv = None
            for nc4 in range(4):
                yq = nc4 % 2
                b.waits("tensor", [ychunk_tok[yq]])
                tym = None
                for ic in range(6):
                    tym = b.op("tensor", lambda E, nc4=nc4, ic=ic, tc5=tc5: E.matmul(
                        Yp[nc4 % 2][:],
                        hT[:, ic * CAP + tc5 * 128:ic * CAP + (tc5 + 1) * 128],
                        wdb[:, ic * 2048 + nc4 * 512:ic * 2048 + (nc4 + 1) * 512],
                        start=(ic == 0), stop=(ic == 5)), nop=False,
                        inc=(True if ic == 5 else None))
                tym_last = tym
                b.wait("vector", tym)
                tyv = b.op("vector", lambda E, yb2=yb2, yq=yq, nc4=nc4, tc5=tc5, bj=bj: E.tensor_scalar(
                    ysb[yb2][:, nc4 * 512:(nc4 + 1) * 512], Yp[yq][:],
                    wcol[bj][:, tc5:tc5 + 1], None, op0=OP.mult), inc=f"YC{yq}")
                ychunk_tok[yq] = tyv
            b.wait("sync", tyv)
            tst2 = b.dma("sync", lambda E, e=e, tc5=tc5, yb2=yb2: E.dma_start(
                out=ysorted[e * CAP + tc5 * 128:e * CAP + (tc5 + 1) * 128, :],
                in_=ysb[yb2][:]), ch="YSTa" if yb2 == 0 else "YSTb")
            ysb_tok[yb2] = tst2
        ffn2_done[e] = tym_last
        if e + 1 < NE:
            emit_weight_load(e + 1, "d")

    b.barrier(ENGS)
    if stage == "p4":
        tlast = None
        for i in range(NT):
            tlast = b.dma("gpsimd", lambda E, i=i: E.dma_start(
                out=y_out[i * 128:(i + 1) * 128, :].rearrange("p (h c) -> p h c", h=1),
                in_=ysorted[i * 128:(i + 1) * 128, :].rearrange("p (h c) -> p h c", h=1)), ch="DBG")
        b.wait("sync", tlast)
        finish(nc, b)
        pc.close()
        top.close()
        return nc
    pc.close()

    pd = ExitStack()
    gath = [SB(pd, f"gath{j}", [128, D], BF16) for j in range(8)]
    moe = SB(pd, "moe", [128, D], F32)
    t01 = SB(pd, "t01", [128, D], F32)
    t23 = SB(pd, "t23", [128, D], F32)
    t45 = SB(pd, "t45", [128, D], F32)
    t67 = SB(pd, "t67", [128, D], F32)
    xat = SB(pd, "xat", [128, D], F32)
    outf = SB(pd, "outf", [128, D], F32)

    gth_tok = {}

    def emit_gathers(i):
        toks = []
        for k in range(8):
            toks.append(b.dma("gpsimd", lambda E, i=i, k=k: E.indirect_dma_start(
                out=gath[k][:], out_offset=None,
                in_=ysorted[:, :],
                in_offset=IndirectOffsetOnAxis(slot8i[:, i * 8 + k:i * 8 + k + 1], 0)),
                ch="GTH"))
        gth_tok[i] = toks

    emit_gathers(0)
    out_last = None
    prev_xat = None
    prev_out = None
    for i in range(NT):
        b.waits("vector", gth_tok[i])
        if prev_xat is not None:
            b.waits("sync", [prev_xat])
        txl = b.dma("sync", lambda E, i=i: E.dma_start(
            out=xat[:], in_=xa_dram[i * 128:(i + 1) * 128, :]), ch="XAT")
        b.op("vector", lambda E: E.tensor_tensor(t01[:], gath[0][:], gath[1][:], op=OP.add))
        b.op("vector", lambda E: E.tensor_tensor(t23[:], gath[2][:], gath[3][:], op=OP.add))
        b.op("vector", lambda E: E.tensor_tensor(t45[:], gath[4][:], gath[5][:], op=OP.add))
        b.op("vector", lambda E: E.tensor_tensor(t67[:], gath[6][:], gath[7][:], op=OP.add))
        b.op("vector", lambda E: E.tensor_tensor(t01[:], t01[:], t23[:], op=OP.add))
        b.op("vector", lambda E: E.tensor_tensor(t45[:], t45[:], t67[:], op=OP.add))
        tgsum = b.op("vector", lambda E: E.tensor_tensor(moe[:], t01[:], t45[:], op=OP.add),
                     inc="GSM")
        if i + 1 < NT:
            b.wait("gpsimd", tgsum)
            emit_gathers(i + 1)
        b.wait("vector", txl)
        b.op("vector", lambda E: E.tensor_tensor(
            x1[:], moe[:], xat[:], op=OP.add))
        b.fop("vector", lambda E: E.reduce_sum(small[:, 8:9], x1[:], axis=AX))
        tv1 = b.op("vector", lambda E: E.tensor_scalar(
            small[:, 10:11], small[:, 8:9], 1.0 / D, None, op0=OP.mult), inc="LV1")
        b.wait("scalar", tv1)
        tsq = b.op("scalar", lambda E: E.activation(
            out=xat[:], in_=x1[:], func=AF.Square, accum_out=small[:, 9:10]), inc="LSQ")
        prev_xat = tsq
        b.wait("vector", tsq)
        b.op("vector", lambda E: E.tensor_scalar(
            small[:, 11:12], small[:, 9:10], 1.0 / D, None, op0=OP.mult))
        b.op("vector", lambda E: E.tensor_tensor(
            small[:, 9:10], small[:, 10:11], small[:, 10:11], op=OP.mult))
        b.op("vector", lambda E: E.tensor_tensor(
            small[:, 11:12], small[:, 11:12], small[:, 9:10], op=OP.subtract))
        b.op("vector", lambda E: E.tensor_scalar(
            small[:, 11:12], small[:, 11:12], EPS, None, op0=OP.add))
        trec = b.fop("vector", lambda E: E.reciprocal(small[:, 12:13], small[:, 11:12]),
                     inc="LRC")
        b.wait("scalar", trec)
        tst_ = b.op("scalar", lambda E: E.activation(
            out=small[:, 13:14], in_=small[:, 12:13], func=AF.Sqrt), inc="LSR")
        b.waits("vector", [tst_] + ([prev_out] if prev_out else []))
        tof = b.op("vector", lambda E: E.tensor_scalar(
            outf[:], x1[:], small[:, 10:11], small[:, 13:14],
            op0=OP.subtract, op1=OP.mult), inc="OF")
        b.wait("sync", tof)
        out_last = b.dma("sync", lambda E, i=i: E.dma_start(
            out=y_out[i * 128:(i + 1) * 128, :], in_=outf[:]), ch="OUT")
        prev_out = out_last

    b.wait("sync", out_last)
    b.wait("gpsimd", out_last)
    if dbg:
        tdb = b.dma("gpsimd", lambda E: E.dma_start(out=sdump[:], in_=slot8i[:]), ch="DBG")
        for ch_ in range(6):
            r0, r1 = ch_ * 3584, min((ch_ + 1) * 3584, ROWS + 1)
            tdb = b.dma("gpsimd", lambda E, r0=r0, r1=r1: E.dma_start(
                out=wdump[r0:r1, :], in_=xsorted[r0:r1, 2048:2112]), ch="DBG")
            tdb = b.dma("gpsimd", lambda E, r0=r0, r1=r1: E.dma_start(
                out=ydump[r0:r1, :], in_=ysorted[r0:r1, 0:8]), ch="DBG")
        b.wait("sync", tdb)
    finish(nc, b)
    pd.close()
    top.close()
    return nc


def finish(nc, b):
    with nc.Block() as block:
        for eng in ENGS:
            ops = list(b.q[eng])

            def runner(E, ops=ops):
                for fn in ops:
                    fn(E)
            getattr(block, eng)(runner)


def _rope_tables():
    inv = 1.0 / (10000.0 ** (np.arange(0, DQK, 2)[: DQK // 2].astype(np.float32) / DQK))
    t = np.arange(S, dtype=np.float32)
    ang = np.outer(t, inv)
    return np.cos(ang), np.sin(ang)


def _perm_cols(w, nheads):
    perm = np.concatenate([np.arange(0, 128, 2), np.arange(1, 128, 2)])
    w2 = w.reshape(w.shape[0], nheads, 128)[:, :, perm]
    return np.ascontiguousarray(w2.reshape(w.shape[0], nheads * 128))


def _mask_for_core(qo):
    m = np.full((NT, 128, 896), NEG, np.float32)
    for i in range(NT):
        q = qo + i * 128 + np.arange(128)[:, None]
        band_lo = qo - 512 + i * 128

        def allowed(j):
            band = (j <= q) & (j > q - WIN)
            front = (j < FRONT) & (j <= q - WIN)
            return band | front
        jf = np.arange(256)[None, :]
        in_band = (jf >= band_lo) & (jf < band_lo + 640)
        m[i, :, 0:256] = np.where(allowed(jf) & ~in_band, 0.0, NEG)
        jb = band_lo + np.arange(640)[None, :]
        m[i, :, 256:896] = np.where(allowed(jb) & (jb >= 0), 0.0, NEG)
    return m


def make_in_maps(L0_attn_x, L0_attn_wq, L0_attn_wk, L0_attn_wv, L0_attn_wo,
                 L0_ffn_gate_w, L0_ffn_w_gate, L0_ffn_w_up, L0_ffn_w_down):
    x = np.asarray(L0_attn_x, np.float32)
    wq = _perm_cols(np.asarray(L0_attn_wq, np.float32), NH)
    wk = _perm_cols(np.asarray(L0_attn_wk, np.float32), NKV)
    wv = np.ascontiguousarray(np.asarray(L0_attn_wv, np.float32))
    wo = np.ascontiguousarray(np.asarray(L0_attn_wo, np.float32))
    gw = np.ascontiguousarray(np.asarray(L0_ffn_gate_w, np.float32))
    wg = np.ascontiguousarray(np.asarray(L0_ffn_w_gate, np.float32))
    wu = np.ascontiguousarray(np.asarray(L0_ffn_w_up, np.float32))
    wdn = np.ascontiguousarray(np.asarray(L0_ffn_w_down, np.float32))
    cos, sin = _rope_tables()
    idf = np.eye(128, dtype=np.float32)
    idb = idf.astype(ml_dtypes.bfloat16)
    scale = np.float32(1.0 / np.sqrt(DQK))

    in_maps = []
    for c in range(8):
        bb, qo = c // 4, (c % 4) * T
        xb = x[bb]
        xkv = np.zeros((XKV, D), np.float32)
        xkv[0:FRONT] = xb[0:FRONT]
        lo = qo - WIN
        src0 = max(0, lo)
        xkv[FRONT + (src0 - lo):FRONT + WIN + T] = xb[src0:qo + T]
        qpos = np.arange(qo, qo + T)
        kvpos = np.concatenate([np.arange(FRONT),
                                np.clip(np.arange(lo, qo + T), 0, S - 1)])
        in_maps.append({
            "xkv_in": np.ascontiguousarray(xkv),
            "wq_in": wq, "wk_in": wk, "wv_in": wv, "wo_in": wo,
            "gate_in": gw, "wg_in": wg, "wu_in": wu, "wd_in": wdn,
            "cq_in": np.ascontiguousarray(cos[qpos] * scale),
            "sq_in": np.ascontiguousarray(sin[qpos] * scale),
            "ckv_in": np.ascontiguousarray(cos[kvpos]),
            "skv_in": np.ascontiguousarray(sin[kvpos]),
            "mask_in": _mask_for_core(qo),
            "idf_in": idf, "idb_in": idb,
        })
    return in_maps


def kernel(L0_attn_x, L0_attn_wq, L0_attn_wk, L0_attn_wv, L0_attn_wo,
           L0_ffn_gate_w, L0_ffn_w_gate, L0_ffn_w_up, L0_ffn_w_down):
    nc = build_nc(stage="full")
    in_maps = make_in_maps(L0_attn_x, L0_attn_wq, L0_attn_wk, L0_attn_wv, L0_attn_wo,
                           L0_ffn_gate_w, L0_ffn_w_gate, L0_ffn_w_up, L0_ffn_w_down)
    res = run_bass_kernel_spmd(nc, in_maps, core_ids=list(range(8)))
    out = np.empty((BATCH, S, D), np.float32)
    for c in range(8):
        bb, qo = c // 4, (c % 4) * T
        out[bb, qo:qo + T] = res.results[c]["y_out"]
    return out


# version 05
# speedup vs baseline: 1.3565x; 1.0181x over previous
import numpy as np
import ml_dtypes
from contextlib import ExitStack

import concourse.bass as bass
import concourse.mybir as mybir
from concourse.bass_utils import run_bass_kernel_spmd
from concourse.bass_core_rust import IndirectOffsetOnAxis

F32 = mybir.dt.float32
BF16 = mybir.dt.bfloat16
I32 = mybir.dt.int32
AF = mybir.ActivationFunctionType
OP = mybir.AluOpType
AX = mybir.AxisListType.X

S, D, BATCH = 8192, 2048, 2
T = 2048
NT = 16
NKV, NH, DQK = 4, 16, 128
WIN, FRONT = 512, 256
XKV = FRONT + WIN + T
NE, TOPK, INTER = 32, 8, 768
CAP = 640
ROWS = NE * CAP
PAYW = 2176
WCOL = 2048
NEG = -1e30
BIGC = 1e9
EPS = 1e-5

ENGS = ("sync", "scalar", "vector", "tensor", "gpsimd")
SEMS = ("T", "BV", "BS", "BG",
        "SG0", "SG1", "IOT", "PP0", "PP1", "PP2", "RP", "PKB0", "PKB1",
        "SMK", "EXP", "PNM", "PTE", "OHE", "YB0", "YB1",
        "LSQ", "LRC", "LSR", "LXA", "LXB", "LV1",
        "GLG", "GL2", "RNG", "RXP", "RWV", "RCA", "RS8", "RPW",
        "SIL", "HMU", "WCL", "YC0", "YC1", "GSM", "OF", "KTD", "VTD", "VFN", "SFN",
        "XEV")
DMA_CHANNELS = ("MISC", "XKLa", "XKLb", "XKSa", "XKSb", "WSTa", "WSTb", "TL",
                "PLDa", "PLDb", "QST", "XSTa", "XSTb", "YSTa", "YSTb", "XAT",
                "OUT", "TRXa", "TRXb", "TRXG", "TRX4a", "TRX4b", "SCTa", "SCTb",
                "WGU", "WDD",
                "GTH", "DBG")
BAR_SEM = {"tensor": "T", "vector": "BV", "scalar": "BS", "gpsimd": "BG"}


FENCE_ALL = True


class Bld:
    def __init__(self):
        self.q = {e: [] for e in ENGS}
        self.cnt = {s: 0 for s in SEMS}
        self.cnt.update({c: 0 for c in DMA_CHANNELS})
        self.cnt.update({c: 0 for c in DMA_CHANNELS})
        self.sem = {}
        self.last = {e: None for e in ENGS}

    def fop(self, eng, fn, inc=None):
        if FENCE_ALL and eng in ("vector", "scalar"):
            return self.op(eng, fn, inc=inc)
        tok = self.op(eng, fn, inc=inc)
        if tok is not None:
            self.q[eng].append(lambda E, k=tok[0], v=tok[1]: E.wait_ge(self.sem[k], v))
            return tok
        fch = "VFN" if eng == "vector" else "SFN"
        self.cnt[fch] += 1
        v = self.cnt[fch]
        real = self.q[eng].pop()
        self.q[eng].append(lambda E, real=real, fch=fch: real(E).then_inc(self.sem[fch], 1))
        self.q[eng].append(lambda E, fch=fch, v=v: E.wait_ge(self.sem[fch], v))
        return tok

    def op(self, eng, fn, inc=None, nop=True):
        if FENCE_ALL and eng in ("vector", "scalar"):
            tok = self._op_raw(eng, fn, inc=inc, nop=False)
            if tok is not None:
                self.q[eng].append(lambda E, k=tok[0], v=tok[1]: E.wait_ge(self.sem[k], v))
                return tok
            fch = "VFN" if eng == "vector" else "SFN"
            self.cnt[fch] += 1
            v = self.cnt[fch]
            real = self.q[eng].pop()
            self.q[eng].append(lambda E, real=real, fch=fch: real(E).then_inc(self.sem[fch], 1))
            self.q[eng].append(lambda E, fch=fch, v=v: E.wait_ge(self.sem[fch], v))
            return None
        return self._op_raw(eng, fn, inc=inc, nop=nop)

    def _op_raw(self, eng, fn, inc=None, nop=True):
        if nop and eng == "vector":
            self.q[eng].append(lambda E: E.engine_nop())
        elif nop and eng in ("scalar", "gpsimd"):
            self.q[eng].append(lambda E: E.nop())
        if inc:
            k = "T" if (inc is True and eng == "tensor") else inc
            assert k is not True, "non-tensor inc needs a channel name"
            self.cnt[k] += 1
            v = self.cnt[k]
            self.q[eng].append(lambda E, fn=fn, k=k: fn(E).then_inc(self.sem[k], 1))
            tok = (k, v)
            self.last[eng] = tok
            return tok
        self.q[eng].append(fn)
        return None

    def mark(self, eng):
        k = BAR_SEM[eng]
        self.cnt[k] += 1
        v = self.cnt[k]
        if eng == "vector":
            self.q[eng].append(lambda E, k=k: E.engine_nop().then_inc(self.sem[k], 1))
        else:
            self.q[eng].append(lambda E, k=k: E.nop().then_inc(self.sem[k], 1))
        tok = (k, v)
        self.last[eng] = tok
        return tok

    def dma(self, eng, fn, ch="MISC"):
        k = ch
        self.cnt[k] += 16
        v = self.cnt[k]
        self.q[eng].append(lambda E, fn=fn, k=k: fn(E).then_inc(self.sem[k], 16))
        tok = (k, v)
        self.last[eng] = tok
        return tok

    def wait(self, eng, tok):
        if tok is None:
            return
        k, v = tok
        self.q[eng].append(lambda E, k=k, v=v: E.wait_ge(self.sem[k], v))

    def waits(self, eng, toks):
        best = {}
        for t in toks:
            if t is None:
                continue
            k, v = t
            best[k] = max(best.get(k, 0), v)
        for k, v in best.items():
            self.q[eng].append(lambda E, k=k, v=v: E.wait_ge(self.sem[k], v))

    def barrier(self, engines=ENGS):
        toks = []
        for ch in SEMS + DMA_CHANNELS:
            if self.cnt[ch] > 0:
                toks.append((ch, self.cnt[ch]))
        for e in engines:
            self.waits(e, toks)


def build_nc(stage="full", races=True, dbg=False, rparts=4):
    nc = bass.Bass("TRN2", target_bir_lowering=False,
                   detect_race_conditions=races)
    top = ExitStack()

    dt = nc.dram_tensor
    xkv_in = dt("xkv_in", [XKV, D], F32, kind="ExternalInput").ap()
    wq_in = dt("wq_in", [D, D], F32, kind="ExternalInput").ap()
    wk_in = dt("wk_in", [D, 512], F32, kind="ExternalInput").ap()
    wv_in = dt("wv_in", [D, 512], F32, kind="ExternalInput").ap()
    wo_in = dt("wo_in", [D, D], F32, kind="ExternalInput").ap()
    gate_in = dt("gate_in", [D, NE], F32, kind="ExternalInput").ap()
    wg_in = dt("wg_in", [NE, D, INTER], F32, kind="ExternalInput").ap()
    wu_in = dt("wu_in", [NE, D, INTER], F32, kind="ExternalInput").ap()
    wd_in = dt("wd_in", [NE, INTER, D], F32, kind="ExternalInput").ap()
    cq_in = dt("cq_in", [T, 64], F32, kind="ExternalInput").ap()
    sq_in = dt("sq_in", [T, 64], F32, kind="ExternalInput").ap()
    ckv_in = dt("ckv_in", [XKV, 64], F32, kind="ExternalInput").ap()
    skv_in = dt("skv_in", [XKV, 64], F32, kind="ExternalInput").ap()
    mask_in = dt("mask_in", [NT, 128, 896], F32, kind="ExternalInput").ap()
    idf_in = dt("idf_in", [128, 128], F32, kind="ExternalInput").ap()
    idb_in = dt("idb_in", [128, 128], BF16, kind="ExternalInput").ap()

    y_out = dt("y_out", [T, D], F32, kind="ExternalOutput").ap()
    dbg_out = dt("dbg_out", [8, 128, 2048], F32, kind="ExternalOutput").ap() if dbg else None
    sdump = dt("sdump", [128, 128], I32, kind="ExternalOutput").ap() if dbg else None
    wdump = dt("wdump", [ROWS + 1, 64], BF16, kind="ExternalOutput").ap() if dbg else None
    ydump = dt("ydump", [ROWS + 1, 8], BF16, kind="ExternalOutput").ap() if dbg else None

    xkv_bf = dt("xkv_bf", [XKV, D], BF16, kind="Internal").ap()
    qt_dram = dt("qt_dram", [NT, 16, 128, 128], BF16, kind="Internal").ap()
    xa_dram = dt("xa_dram", [T, D], F32, kind="Internal").ap()
    xab_dram = dt("xab_dram", [T, D], BF16, kind="Internal").ap()
    xsorted = dt("xsorted", [ROWS + 1, PAYW], BF16, kind="Internal").ap()
    ysorted = dt("ysorted", [ROWS + 1, D], BF16, kind="Internal").ap()

    def SB(stk, name, shape, dtype):
        return stk.enter_context(nc.sbuf_tensor(name, shape, dtype))

    def PS(stk, name, shape, dtype=F32):
        return stk.enter_context(nc.psum_tensor(name, shape, dtype))

    idf = SB(top, "idf", [128, 128], F32)
    idb = SB(top, "idb", [128, 128], BF16)
    stage_f = [SB(top, f"stage{j}", [128, 2048], F32) for j in range(2)]
    small = SB(top, "small", [128, 16], F32)
    x1 = SB(top, "x1", [128, D], F32)
    logits = SB(top, "logits", [128, NT * NE], F32)
    rdbg = [SB(top, "rdbg0", [32, 128], F32), SB(top, "rdbg1", [32, 128], F32),
            SB(top, "rdbg2", [128, 32], F32), SB(top, "rdbg3", [128, 32], F32),
            SB(top, "rdbg4", [128, 8], F32), SB(top, "rdbg5", [128, 8], F32)]
    slot8i = SB(top, "slot8i", [128, NT * 8], I32)
    gateb = SB(top, "gateb", [128, 16 * NE], BF16)
    iota32 = SB(top, "iota32", [128, NE], F32)
    iota2048 = SB(top, "iota2048", [128, NE], F32)
    Yp = [PS(top, f"Yp{j}", [128, 512]) for j in range(2)]

    sems = {}
    for sname in SEMS + DMA_CHANNELS:
        sems[sname] = top.enter_context(nc.semaphore(sname))
    b = Bld()
    b.sem = sems

    cast_state = {"i": 0, "hist": {}}

    def cast_chunk(src_ap, dst_ap, cols, eng, extra_waits=()):
        par = cast_state["i"] % 2
        stg = stage_f[par]
        cast_state["i"] += 1
        prev = cast_state["hist"].get(id(stg))
        if prev is not None:
            b.wait("sync", prev)
        tl = b.dma("sync", lambda E, stg=stg, cols=cols, src_ap=src_ap: E.dma_start(
            out=stg[:, 0:cols], in_=src_ap), ch="WSTa" if par == 0 else "WSTb")
        b.wait(eng, tl)
        for w in extra_waits:
            b.wait(eng, w)
        sg = "SG0" if par == 0 else "SG1"
        if eng == "scalar":
            tc = b.op(eng, lambda E, stg=stg, cols=cols, dst_ap=dst_ap: E.copy(
                out=dst_ap, in_=stg[:, 0:cols]), inc=sg)
        else:
            tc = b.op(eng, lambda E, stg=stg, cols=cols, dst_ap=dst_ap: E.tensor_copy(
                out=dst_ap, in_=stg[:, 0:cols]), inc=sg)
        cast_state["hist"][id(stg)] = tc
        return tc

    pa = ExitStack()
    pab = ExitStack()
    kt = SB(pab, "kt", [128, NKV * XKV], BF16)
    vt = SB(pab, "vt", [128, 22 * 512], BF16)
    xsl = [SB(pab, "xsl0", [128, 16 * 512], BF16),
           SB(pa, "xsl1", [128, 16 * 512], BF16)]
    cqtm = SB(pa, "cqtm", [128, 16 * 64], F32)
    sqtm = SB(pa, "sqtm", [128, 16 * 64], F32)
    rtmp2 = SB(pa, "rtmp2", [128, 16 * 64], F32)
    qtm = SB(pa, "qtm", [128, 2048], F32)
    qrope = SB(pa, "qrope", [128, 2048], F32)
    qrope_b = SB(pa, "qrope_b", [128, 2048], BF16)
    qtts = SB(pa, "qtts", [128, 2048], BF16)
    pp = [PS(pa, f"pp{j}", [128, 512]) for j in range(3)]
    pkb = [PS(pa, f"pkb{j}", [128, 128], BF16) for j in range(2)]
    pa0 = ExitStack()
    ckvtm = SB(pa0, "ckvtm", [128, 22 * 64], F32)
    skvtm = SB(pa0, "skvtm", [128, 22 * 64], F32)
    ktm = SB(pa0, "ktm", [128, 512], F32)
    krope = SB(pa0, "krope", [128, 512], F32)
    krope_b = SB(pa0, "krope_b", [128, 512], BF16)
    wkb = SB(pa0, "wkb", [128, 16 * 512], BF16)
    wvb = SB(pa0, "wvb", [128, 16 * 512], BF16)

    b.dma("sync", lambda E: E.dma_start(out=idf[:], in_=idf_in[:]))
    b.dma("sync", lambda E: E.dma_start(out=idb[:], in_=idb_in[:]))
    for tl_ in range(16):
        b.dma("sync", lambda E, tl_=tl_: E.dma_start(
            out=cqtm[:, tl_ * 64:(tl_ + 1) * 64], in_=cq_in[tl_ * 128:(tl_ + 1) * 128, :]))
        b.dma("sync", lambda E, tl_=tl_: E.dma_start(
            out=sqtm[:, tl_ * 64:(tl_ + 1) * 64], in_=sq_in[tl_ * 128:(tl_ + 1) * 128, :]))
    for tl_ in range(22):
        b.dma("sync", lambda E, tl_=tl_: E.dma_start(
            out=ckvtm[:, tl_ * 64:(tl_ + 1) * 64], in_=ckv_in[tl_ * 128:(tl_ + 1) * 128, :]))
        b.dma("sync", lambda E, tl_=tl_: E.dma_start(
            out=skvtm[:, tl_ * 64:(tl_ + 1) * 64], in_=skv_in[tl_ * 128:(tl_ + 1) * 128, :]))

    xkv_done = []
    for chk in range(22):
        buf = xsl[chk % 2]
        dst = buf[:, 0:2048]
        if chk >= 2:
            b.wait("gpsimd", xkv_done[chk - 2])
        tl = b.dma("gpsimd", lambda E, chk=chk, dst=dst: E.dma_start(
            out=dst, in_=xkv_in[chk * 128:(chk + 1) * 128, :]),
            ch="XKLa" if chk % 2 == 0 else "XKLb")
        b.wait("sync", tl)
        ts_ = b.dma("sync", lambda E, chk=chk, dst=dst: E.dma_start(
            out=xkv_bf[chk * 128:(chk + 1) * 128, :], in_=dst),
            ch="XKSa" if chk % 2 == 0 else "XKSb")
        xkv_done.append(ts_)
    t_xkvbf = [xkv_done[-1], xkv_done[-2]]

    b.op("gpsimd", lambda E: E.iota(iota32[:], pattern=[[1, NE]], base=0,
                                    channel_multiplier=0,
                                    allow_small_or_imprecise_dtypes=True))
    b.op("gpsimd", lambda E: E.iota(iota2048[:], pattern=[[2048, NE]], base=0,
                                    channel_multiplier=0,
                                    allow_small_or_imprecise_dtypes=True), inc="IOT")

    wtoks = []
    for wsrc, wdst, wcols in ((wk_in, wkb, 512), (wv_in, wvb, 512), (gate_in, gateb, NE)):
        wtoks.append(b.dma("gpsimd", lambda E, wsrc=wsrc, wdst=wdst: E.dma_start(
            out=wdst[:].rearrange("p (c f) -> p c f", c=16),
            in_=wsrc[:, :].rearrange("(c p) f -> p c f", p=128)), ch="WGU"))
    b.waits("tensor", wtoks)
    b.barrier(("sync", "vector", "scalar", "gpsimd"))

    def emit_rope_tm(dst, srct, ctab, stab, nh, tmp):
        sr = srct.rearrange("p (h half i) -> p h half i", h=nh, half=2)
        dr = dst.rearrange("p (h half i) -> p h half i", h=nh, half=2)
        tr = tmp.rearrange("p (h i) -> p h i", h=nh)
        cb = ctab.unsqueeze(1).to_broadcast([128, nh, 64])
        sb_ = stab.unsqueeze(1).to_broadcast([128, nh, 64])
        b.op("vector", lambda E: E.tensor_tensor(dr[:, :, 0, :], sr[:, :, 0, :], cb, op=OP.mult))
        b.op("vector", lambda E: E.tensor_tensor(tr[:], sr[:, :, 1, :], sb_, op=OP.mult))
        b.op("vector", lambda E: E.tensor_tensor(dr[:, :, 0, :], dr[:, :, 0, :], tr[:], op=OP.subtract))
        b.op("vector", lambda E: E.tensor_tensor(dr[:, :, 1, :], sr[:, :, 1, :], cb, op=OP.mult))
        b.op("vector", lambda E: E.tensor_tensor(tr[:], sr[:, :, 0, :], sb_, op=OP.mult))
        b.op("vector", lambda E: E.tensor_tensor(
            dr[:, :, 1, :], dr[:, :, 1, :], tr[:], op=OP.add))
        return None

    if stage == "p0":
        b.barrier(ENGS)
        tlast = None
        for i in range(NT):
            tlast = b.dma("gpsimd", lambda E, i=i: E.dma_start(
                out=y_out[i * 128:(i + 1) * 128, :],
                in_=xkv_bf[768 + i * 128:768 + (i + 1) * 128, :]), ch="DBG")
        b.wait("sync", tlast)
        finish(nc, b)
        return nc

    pptok = {}

    def pwait(j, eng):
        t = pptok.get(j)
        if t is not None:
            b.wait(eng, t)

    pidx = 0
    kv_slices = [(i * 512, min(512, XKV - i * 512)) for i in range(6)]
    slice_done = {}
    for si, (t0, w) in enumerate(kv_slices):
        buf = xsl[si % 2]
        b.waits("scalar", t_xkvbf)
        if si >= 2:
            b.wait("scalar", slice_done[si - 2])
        ttr = None
        for dmc in range(16):
            ttr = b.dma("scalar", lambda E, t0=t0, w=w, dmc=dmc, buf=buf: E.dma_start_transpose(
                out=buf[:, dmc * 512:dmc * 512 + w],
                in_=xkv_bf[t0:t0 + w, dmc * 128:(dmc + 1) * 128]),
                ch="TRXa" if si % 2 == 0 else "TRXb")
        b.waits("tensor", [ttr])
        last_use = None
        pkb_tok = [None, None]
        for tt in range(w // 128):
            blk = t0 // 128 + tt
            j = pidx % 3
            pidx += 1
            pwait(j, "tensor")
            tmm = None
            for dmc in range(16):
                tmm = b.op("tensor", lambda E, j=j, dmc=dmc, tt=tt, buf=buf: E.matmul(
                    pp[j][:, 0:512],
                    buf[:, dmc * 512 + tt * 128:dmc * 512 + (tt + 1) * 128],
                    wkb[:, dmc * 512:(dmc + 1) * 512],
                    start=(dmc == 0), stop=(dmc == 15)), nop=False,
                    inc=(True if dmc == 15 else None))
            b.wait("vector", tmm)
            tevk = b.op("vector", lambda E, j=j: E.tensor_copy(out=ktm[:], in_=pp[j][:, 0:512]),
                        inc=f"PP{j}")
            pptok[j] = tevk
            emit_rope_tm(krope[:], ktm[:], ckvtm[:, blk * 64:(blk + 1) * 64],
                         skvtm[:, blk * 64:(blk + 1) * 64], NKV, rtmp2[:, 0:256])
            trope = b.op("vector", lambda E: E.tensor_copy(out=krope_b[:], in_=krope[:]),
                         inc="RP")
            b.waits("tensor", [trope])
            for f in range(NKV):
                pb_ = f % 2
                b.waits("tensor", [pkb_tok[pb_]])
                ttp = b.op("tensor", lambda E, f=f, pb_=pb_: E.transpose(
                    pkb[pb_][:], krope_b[:, f * 128:(f + 1) * 128], idb[:]), inc=True, nop=False)
                b.wait("vector", ttp)
                tevt = b.op("vector", lambda E, f=f, blk=blk, pb_=pb_: E.tensor_copy(
                    out=kt[:, f * XKV + blk * 128:f * XKV + (blk + 1) * 128],
                    in_=pkb[pb_][:]), inc=f"PKB{pb_}")
                pkb_tok[pb_] = tevt
                last_use = tevt
        ntt = w // 128
        for tt in range(ntt):
            j = pidx % 3
            pidx += 1
            pwait(j, "tensor")
            tmm = None
            for dmc in range(16):
                tmm = b.op("tensor", lambda E, j=j, dmc=dmc, tt=tt, buf=buf: E.matmul(
                    pp[j][:, 0:512],
                    buf[:, dmc * 512 + tt * 128:dmc * 512 + (tt + 1) * 128],
                    wvb[:, dmc * 512:(dmc + 1) * 512],
                    start=(dmc == 0), stop=(dmc == 15)), nop=False,
                    inc=(True if dmc == 15 else None))
            blk = t0 // 128 + tt
            b.wait("scalar", tmm)
            tev = b.op("scalar", lambda E, j=j, blk=blk: E.copy(
                out=vt[:, blk * 512:(blk + 1) * 512], in_=pp[j][:, 0:512]), inc=f"PP{j}")
            pptok[j] = tev
            last_use = tev
        slice_done[si] = last_use

    if stage == "p1":
        b.barrier(ENGS)
        tl1 = b.dma("gpsimd", lambda E: E.dma_start(out=y_out[0:128, :], in_=kt[:, 0:2048]), ch="DBG")
        tl2 = b.dma("gpsimd", lambda E: E.dma_start(out=y_out[128:256, :], in_=vt[:, 0:2048]), ch="DBG")
        b.wait("sync", tl2)
        finish(nc, b)
        return nc

    pa0.close()
    wqb = SB(pa, "wqb", [128, 16 * 2048], BF16)
    b.barrier(("gpsimd",))
    for h in range(4):
        b.dma("gpsimd", lambda E, h=h: E.dma_start(
            out=wqb[:, h * 4 * 2048:(h + 1) * 4 * 2048].rearrange("p (c f) -> p c f", c=4),
            in_=wq_in[h * 512:(h + 1) * 512, :].rearrange("(c p) f -> p c f", p=128)),
            ch="WGU")
    b.barrier(("sync", "vector", "scalar", "tensor"))

    qst_tok = None
    p2_slfree = {}
    for si in range(4):
        t0 = 768 + si * 512
        buf = xsl[si % 2]
        b.waits("scalar", [p2_slfree.get(si % 2)])
        ttr = None
        for dmc in range(16):
            ttr = b.dma("scalar", lambda E, t0=t0, dmc=dmc, buf=buf: E.dma_start_transpose(
                out=buf[:, dmc * 512:(dmc + 1) * 512],
                in_=xkv_bf[t0:t0 + 512, dmc * 128:(dmc + 1) * 128]),
                ch="TRXa" if si % 2 == 0 else "TRXb")
        b.wait("tensor", ttr)
        pkb_tok = [None, None]
        for tt in range(4):
            tile = si * 4 + tt
            evtoks = []
            for nq in range(4):
                j = pidx % 3
                pidx += 1
                pwait(j, "tensor")
                tmm = None
                for dmc in range(16):
                    tmm = b.op("tensor", lambda E, j=j, dmc=dmc, nq=nq, tt=tt, buf=buf: E.matmul(
                        pp[j][:, 0:512],
                        buf[:, dmc * 512 + tt * 128:dmc * 512 + (tt + 1) * 128],
                        wqb[:, dmc * 2048 + nq * 512:dmc * 2048 + (nq + 1) * 512],
                        start=(dmc == 0), stop=(dmc == 15)), nop=False,
                        inc=(True if dmc == 15 else None))
                b.wait("vector", tmm)
                tcp = b.op("vector", lambda E, j=j, nq=nq: E.tensor_copy(
                    out=qtm[:, nq * 512:(nq + 1) * 512], in_=pp[j][:, 0:512]), inc=f"PP{j}")
                pptok[j] = tcp
                evtoks.append(tcp)
            emit_rope_tm(qrope[:], qtm[:], cqtm[:, tile * 64:(tile + 1) * 64],
                         sqtm[:, tile * 64:(tile + 1) * 64], NH, rtmp2[:])
            trope = b.op("vector", lambda E: E.tensor_copy(out=qrope_b[:], in_=qrope[:]),
                         inc="RP")
            b.waits("tensor", [trope])
            b.waits("vector", [qst_tok])
            for f in range(16):
                pb_ = f % 2
                b.waits("tensor", [pkb_tok[pb_]])
                ttp = b.op("tensor", lambda E, f=f, pb_=pb_: E.transpose(
                    pkb[pb_][:], qrope_b[:, f * 128:(f + 1) * 128], idb[:]), inc=True, nop=False)
                b.wait("vector", ttp)
                tevt = b.op("vector", lambda E, f=f, pb_=pb_: E.tensor_copy(
                    out=qtts[:, f * 128:(f + 1) * 128], in_=pkb[pb_][:]), inc=f"PKB{pb_}")
                pkb_tok[pb_] = tevt
            b.waits("sync", pkb_tok)
            qst_tok = b.dma("sync", lambda E, tile=tile: E.dma_start(
                out=qt_dram[tile].rearrange("f p c -> p f c"),
                in_=qtts[:].rearrange("p (f c) -> p f c", f=16)), ch="QST")
        p2_slfree[si % 2] = tmm

    b.barrier(ENGS)
    if stage == "p2":
        tl1 = None
        for i in range(4):
            tl1 = b.dma("gpsimd", lambda E, i=i: E.dma_start(
                out=y_out[i * 128:(i + 1) * 128, :].rearrange("p (f c) -> p f c", f=16),
                in_=qt_dram[i].rearrange("f p c -> p f c")), ch="DBG")
        b.wait("sync", tl1)
        finish(nc, b)
        return nc

    pa.close()

    pb = ExitStack()
    wob = SB(pb, "wob", [128, 16 * 2048], BF16)
    qt_t = SB(pb, "qt_t", [128, 16 * 128], BF16)
    mask_t = SB(pb, "mask_t", [128, 896], F32)
    x_t = SB(pb, "x_t", [128, D], F32)
    Pt = SB(pb, "Pt", [128, 896], BF16)
    PTs = SB(pb, "PTs", [128, 896], BF16)
    sm = SB(pb, "sm", [128, 896], F32)
    outT = SB(pb, "outT", [128, 16 * 128], BF16)
    xa_f = SB(pb, "xa_f", [128, D], F32)
    xab_t = SB(pb, "xab_t", [128, D], BF16)
    m8 = SB(pb, "m8", [128, 8], F32)
    zapt = SB(pb, "zapt", [128, NE], F32)
    Mm = SB(pb, "Mm", [128, NE], F32)
    ew = SB(pb, "ew", [128, NE], F32)
    Wd = SB(pb, "Wd", [128, NE], F32)
    Whi = SB(pb, "Whi", [128, NE], BF16)
    Wlo = SB(pb, "Wlo", [128, NE], F32)
    Wlob = SB(pb, "Wlob", [128, NE], BF16)
    rbT = SB(pb, "rbT", [32, 128], F32)
    cums = SB(pb, "cums", [32, 128], F32)
    zer32 = SB(pb, "zer32", [32, 128], F32)
    carry = SB(pb, "carry", [32, 1], F32)
    posT = SB(pb, "posT", [32, 128], F32)
    pos_tm = SB(pb, "pos_tm", [128, NE], F32)
    kva = SB(pb, "kva", [128, NE], F32)
    kvm = SB(pb, "kvm", [128, NE], F32)
    kv8 = SB(pb, "kv8", [128, 8], F32)
    e8 = SB(pb, "e8", [128, 8], F32)
    Em = SB(pb, "Em", [128, NE], F32)
    pos8 = SB(pb, "pos8", [128, 8], F32)
    slot8f = SB(pb, "slot8f", [128, 8], F32)
    ovf = SB(pb, "ovf", [128, 8], F32)
    logT_sb = SB(pb, "logT_sb", [32, 512], F32)
    payload = [SB(pb, f"pay{j}", [128, PAYW], BF16) for j in range(2)]
    S1 = PS(pb, "S1", [128, 512])
    S2 = PS(pb, "S2", [128, 384])
    PTp = PS(pb, "PTp", [128, 896], BF16)
    OHT = PS(pb, "OHT", [128, 128])
    misc1 = PS(pb, "misc1", [32, 512])
    misc2 = PS(pb, "misc2", [128, 32])

    b.op("vector", lambda E: E.memset(zer32[:], 0.0))
    b.op("vector", lambda E: E.memset(carry[:], 0.0))
    b.op("vector", lambda E: E.memset(payload[0][:, 2112:2176], 0.0))
    b.op("vector", lambda E: E.memset(payload[1][:, 2112:2176], 0.0))

    b.barrier(("gpsimd",))
    for h in range(4):
        b.dma("gpsimd", lambda E, h=h: E.dma_start(
            out=wob[:, h * 4 * 2048:(h + 1) * 4 * 2048].rearrange("p (c f) -> p c f", c=4),
            in_=wo_in[h * 512:(h + 1) * 512, :].rearrange("(c p) f -> p c f", p=128)),
            ch="WGU")
    b.barrier(("sync", "vector", "scalar", "tensor"))

    tile_tok = {}
    pay_tok = {}
    xab_store = {}
    xa_store = {}
    state = {
        "s_bank": None,
        "sm_free": None,
        "pt_bank": None,
        "pts_free": None,
        "pt_sb": None,
        "oht_bank": None,
        "outT_free": None,
        "y_bank": None,
        "xaf_free": None,
        "xab_free": None,
        "pay_free": {},
        "route_in": {},
    }

    def emit_tile_loads(i, first=False):
        toks = []
        if not first:
            b.waits("sync", [state["y_bank"]])
        toks.append(b.dma("sync", lambda E, i=i: E.dma_start(
            out=qt_t[:].rearrange("p (f c) -> p f c", f=16),
            in_=qt_dram[i].rearrange("f p c -> p f c")), ch="TL"))
        toks.append(b.dma("sync", lambda E, i=i: E.dma_start(
            out=mask_t[:], in_=mask_in[i]), ch="TL"))
        toks.append(b.dma("sync", lambda E, i=i: E.dma_start(
            out=x_t[:], in_=xkv_in[768 + i * 128:768 + (i + 1) * 128, :]), ch="TL"))
        tile_tok[i] = toks

    def emit_gate_quarter(qtr):
        t0 = qtr * 512
        buf = xsl[0]
        b.waits("scalar", [xab_store[qtr * 4 + j] for j in range(4)]
                + [state.get("gate_sl_free")])
        ttr = None
        for dmc in range(16):
            ttr = b.dma("scalar", lambda E, t0=t0, dmc=dmc, buf=buf: E.dma_start_transpose(
                out=buf[:, dmc * 512:(dmc + 1) * 512],
                in_=xab_dram[t0:t0 + 512, dmc * 128:(dmc + 1) * 128]), ch="TRXG")
        b.waits("tensor", [ttr])
        tgm = None
        for dmc in range(16):
            tgm = b.op("tensor", lambda E, dmc=dmc, buf=buf: E.matmul(
                misc1[:, 0:512], gateb[:, dmc * NE:(dmc + 1) * NE],
                buf[:, dmc * 512:(dmc + 1) * 512],
                start=(dmc == 0), stop=(dmc == 15)), nop=False,
                inc=(True if dmc == 15 else None))
        state["gate_sl_free"] = tgm
        b.wait("vector", tgm)
        tge = b.op("vector", lambda E: E.tensor_copy(out=logT_sb[:], in_=misc1[:, 0:512]),
                   inc="GLG")
        tt2 = None
        for jj in range(4):
            i = qtr * 4 + jj
            b.waits("tensor", [tge, tt2])
            tt1 = b.op("tensor", lambda E, jj=jj: E.transpose(
                misc2[:], logT_sb[:, jj * 128:(jj + 1) * 128], idf[0:32, 0:32]),
                inc=True, nop=False)
            b.wait("vector", tt1)
            tt2 = b.op("vector", lambda E, i=i: E.tensor_copy(
                out=logits[:, i * NE:(i + 1) * NE], in_=misc2[:]), inc="GL2")
            state["route_in"][i] = tt2

    def emit_routing_tile(i):
        if rparts < 2:
            pay_tok[i] = None
            return
        lg = logits[:, i * NE:(i + 1) * NE]
        b.waits("vector", [state["route_in"][i]])
        b.op("vector", lambda E, lg=lg: E.tensor_copy(out=zapt[:], in_=lg))
        b.op("vector", lambda E: E.memset(Mm[:], 0.0))
        for kk in range(8):
            b.fop("vector", lambda E, kk=kk: E.reduce_max(m8[:, kk:kk + 1], zapt[:], axis=AX))
            b.fop("vector", lambda E, kk=kk: E.tensor_scalar(
                ew[:], zapt[:], m8[:, kk:kk + 1], None, op0=OP.is_equal))
            b.fop("vector", lambda E: E.tensor_tensor(Mm[:], Mm[:], ew[:], op=OP.add))
            b.fop("vector", lambda E: E.scalar_tensor_tensor(
                zapt[:], in0=ew[:], scalar=NEG, in1=zapt[:], op0=OP.mult, op1=OP.add))
        if i == 1:
            b.op("vector", lambda E, lg=lg: E.tensor_copy(out=rdbg[3][:], in_=lg))
        tneg = b.op("vector", lambda E: E.tensor_scalar(
            small[:, 14:15], m8[:, 0:1], -1.0, None, op0=OP.mult), inc="RNG")
        b.wait("scalar", tneg)
        texp = b.op("scalar", lambda E, lg=lg: E.activation(
            out=ew[:], in_=lg, func=AF.Exp, bias=small[:, 14:15]), inc="RXP")
        b.wait("vector", texp)
        b.fop("vector", lambda E: E.tensor_tensor(Wd[:], ew[:], Mm[:], op=OP.mult))
        b.fop("vector", lambda E: E.reduce_sum(small[:, 15:16], Wd[:], axis=AX))
        b.fop("vector", lambda E: E.reciprocal(small[:, 14:15], small[:, 15:16]))
        b.fop("vector", lambda E: E.tensor_scalar(
            Wd[:], Wd[:], small[:, 14:15], None, op0=OP.mult))
        b.fop("vector", lambda E: E.tensor_copy(out=Whi[:], in_=Wd[:]))
        b.fop("vector", lambda E: E.tensor_copy(out=ew[:], in_=Whi[:]))
        b.fop("vector", lambda E: E.tensor_tensor(Wlo[:], Wd[:], ew[:], op=OP.subtract))
        twv = b.fop("vector", lambda E: E.tensor_copy(out=Wlob[:], in_=Wlo[:]), inc="RWV")
        b.waits("tensor", [twv])
        tmt = b.op("tensor", lambda E: E.transpose(misc1[:, 0:128], Mm[:], idf[:]),
                   inc=True, nop=False)
        b.wait("vector", tmt)
        trb = b.fop("vector", lambda E: E.tensor_copy(out=rbT[:], in_=misc1[0:32, 0:128]),
                    inc="RCA")
        state["misc1_free"] = trb
        b.fop("vector", lambda E: E.tensor_tensor_scan(
            cums[:], data0=rbT[:], data1=zer32[:], initial=0.0,
            op0=OP.add, op1=OP.add))
        b.op("vector", lambda E: E.tensor_scalar(
            cums[:], cums[:], carry[:, 0:1], None, op0=OP.add))
        b.op("vector", lambda E: E.tensor_tensor(posT[:], cums[:], rbT[:], op=OP.subtract))
        if i == 1:
            b.op("vector", lambda E: E.tensor_copy(out=rdbg[0][:], in_=rbT[:]))
            b.op("vector", lambda E: E.tensor_copy(out=rdbg[1][:], in_=cums[:]))
        tcar = b.fop("vector", lambda E: E.tensor_copy(out=carry[:], in_=cums[:, 127:128]),
                     inc="RCA")
        b.wait("tensor", tcar)
        tpt = b.op("tensor", lambda E: E.transpose(misc2[:], posT[:], idf[0:32, 0:32]),
                   inc=True, nop=False)
        b.wait("vector", tpt)
        b.op("vector", lambda E: E.tensor_copy(out=pos_tm[:], in_=misc2[:]))
        if i == 1:
            b.op("vector", lambda E: E.tensor_copy(out=rdbg[2][:], in_=pos_tm[:]))
        b.op("vector", lambda E: E.tensor_tensor(kva[:], iota2048[:], pos_tm[:], op=OP.add))
        b.op("vector", lambda E: E.tensor_scalar(kva[:], kva[:], 1.0, None, op0=OP.add))
        b.op("vector", lambda E: E.tensor_tensor(kvm[:], kva[:], Mm[:], op=OP.mult))
        b.fop("vector", lambda E: E.tensor_scalar(kvm[:], kvm[:], 1.0, None, op0=OP.subtract))
        b.fop("vector", lambda E: E.max(out=kv8[:], in_=kvm[:]))
        b.op("vector", lambda E: E.tensor_scalar(Em[:], iota32[:], 1.0, None, op0=OP.add))
        b.op("vector", lambda E: E.tensor_tensor(Em[:], Em[:], Mm[:], op=OP.mult))
        b.fop("vector", lambda E: E.tensor_scalar(Em[:], Em[:], 1.0, None, op0=OP.subtract))
        b.fop("vector", lambda E: E.max(out=e8[:], in_=Em[:]))
        b.op("vector", lambda E: E.scalar_tensor_tensor(
            pos8[:], in0=e8[:], scalar=-2048.0, in1=kv8[:], op0=OP.mult, op1=OP.add))
        if i == 1:
            b.op("vector", lambda E: E.tensor_copy(out=rdbg[4][:], in_=kv8[:]))
            b.op("vector", lambda E: E.tensor_copy(out=rdbg[5][:], in_=e8[:]))
        b.op("vector", lambda E: E.scalar_tensor_tensor(
            slot8f[:], in0=e8[:], scalar=640.0, in1=pos8[:], op0=OP.mult, op1=OP.add))
        b.op("vector", lambda E: E.tensor_scalar(ovf[:], pos8[:], 639.5, None, op0=OP.is_gt))
        b.op("vector", lambda E: E.tensor_scalar(kv8[:], slot8f[:], float(ROWS), None,
                                                 op0=OP.subtract))
        b.op("vector", lambda E: E.tensor_tensor(kv8[:], kv8[:], ovf[:], op=OP.mult))
        b.op("vector", lambda E: E.tensor_tensor(slot8f[:], slot8f[:], kv8[:], op=OP.subtract))
        b.op("vector", lambda E: E.tensor_scalar(ovf[:], slot8f[:], 0.0, None, op0=OP.is_lt))
        b.op("vector", lambda E: E.tensor_scalar(kv8[:], slot8f[:], float(ROWS), None,
                                                 op0=OP.subtract))
        b.op("vector", lambda E: E.tensor_tensor(kv8[:], kv8[:], ovf[:], op=OP.mult))
        b.op("vector", lambda E: E.tensor_tensor(slot8f[:], slot8f[:], kv8[:], op=OP.subtract))
        tsl8 = b.op("vector", lambda E, i=i: E.tensor_copy(
            out=slot8i[:, i * 8:(i + 1) * 8], in_=slot8f[:]), inc="RS8")
        if rparts < 3:
            pay_tok[i] = None
            return
        bi = i % 2
        prevp = state["pay_free"].get(bi)
        if prevp is not None:
            b.waits("sync", [prevp])
        tpl = b.dma("sync", lambda E, i=i, bi=bi: E.dma_start(
            out=payload[bi][:, 0:2048], in_=xab_dram[i * 128:(i + 1) * 128, :]),
            ch="PLDa" if bi == 0 else "PLDb")
        b.waits("vector", [tpl] + ([prevp] if prevp else []))
        b.op("vector", lambda E, bi=bi: E.tensor_copy(
            out=payload[bi][:, 2048:2112].rearrange("p (c two) -> p c two", two=2)[:, :, 0],
            in_=Whi[:]))
        tpw = b.op("vector", lambda E, bi=bi: E.tensor_copy(
            out=payload[bi][:, 2048:2112].rearrange("p (c two) -> p c two", two=2)[:, :, 1],
            in_=Wlob[:]), inc="RPW")
        if rparts < 4:
            pay_tok[i] = None
            return
        b.waits("gpsimd", [tpw, tsl8])
        tsc = None
        for k in range(8):
            tsc = b.dma("gpsimd", lambda E, i=i, k=k, bi=bi: E.indirect_dma_start(
                out=xsorted[:, :],
                out_offset=IndirectOffsetOnAxis(slot8i[:, i * 8 + k:i * 8 + k + 1], 0),
                in_=payload[bi][:, :], in_offset=None),
                ch="SCTa" if bi == 0 else "SCTb")
        pay_tok[i] = tsc
        state["pay_free"][bi] = tsc

    emit_tile_loads(0, first=True)
    for i in range(NT):
        b.waits("tensor", tile_tok[i])
        toh = None
        for h in range(NH):
            kvh = h // 4
            kb = kvh * XKV
            b.waits("tensor", [state["s_bank"]])
            b.op("tensor", lambda E, h=h, kb=kb: E.matmul(
                S1[:, 0:256], qt_t[:, h * 128:(h + 1) * 128],
                kt[:, kb:kb + 256], start=True, stop=True), nop=False)
            b.op("tensor", lambda E, h=h, kb=kb, i=i: E.matmul(
                S1[:, 256:512], qt_t[:, h * 128:(h + 1) * 128],
                kt[:, kb + 256 + 128 * i:kb + 512 + 128 * i], start=True, stop=True), nop=False)
            tsc_ = b.op("tensor", lambda E, h=h, kb=kb, i=i: E.matmul(
                S2[:, 0:384], qt_t[:, h * 128:(h + 1) * 128],
                kt[:, kb + 512 + 128 * i:kb + 896 + 128 * i], start=True, stop=True),
                nop=False, inc=True)
            b.wait("vector", tsc_)
            b.op("vector", lambda E: E.reduce_max(small[:, 0:1], S1[:], axis=AX))
            b.fop("vector", lambda E: E.reduce_max(small[:, 1:2], S2[:], axis=AX))
            b.op("vector", lambda E: E.tensor_tensor(small[:, 2:3], small[:, 0:1],
                                                     small[:, 1:2], op=OP.max))
            b.op("vector", lambda E: E.tensor_scalar(small[:, 3:4], small[:, 2:3], -1.0,
                                                     None, op0=OP.mult))
            b.waits("vector", [state["sm_free"]])
            b.op("vector", lambda E: E.scalar_tensor_tensor(
                sm[:, 0:512], in0=S1[:], scalar=small[:, 3:4], in1=mask_t[:, 0:512],
                op0=OP.add, op1=OP.add))
            tmsk = b.op("vector", lambda E: E.scalar_tensor_tensor(
                sm[:, 512:896], in0=S2[:], scalar=small[:, 3:4], in1=mask_t[:, 512:896],
                op0=OP.add, op1=OP.add), inc="SMK")
            state["s_bank"] = tmsk
            if dbg and i == 0 and h == 0:
                b.wait("gpsimd", tmsk)
                b.dma("gpsimd", lambda E: E.dma_start(out=dbg_out[0][:, 0:896], in_=sm[:]), ch="DBG")
                b.dma("gpsimd", lambda E: E.dma_start(out=dbg_out[7][:, 0:896], in_=kt[:, 0:896]), ch="DBG")
                b.dma("gpsimd", lambda E: E.dma_start(out=dbg_out[7][:, 896:2048], in_=qt_t[:, 0:1152]), ch="DBG")
            b.waits("scalar", [tmsk, state["pt_sb"]])
            b.op("scalar", lambda E: E.activation(
                out=Pt[:, 0:512], in_=sm[:, 0:512], func=AF.Exp, accum_out=small[:, 4:5]))
            texp = b.op("scalar", lambda E: E.activation(
                out=Pt[:, 512:896], in_=sm[:, 512:896], func=AF.Exp,
                accum_out=small[:, 5:6]), inc="EXP")
            state["sm_free"] = texp
            b.wait("vector", texp)
            b.op("vector", lambda E: E.tensor_tensor(small[:, 6:7], small[:, 4:5],
                                                     small[:, 5:6], op=OP.add))
            b.fop("vector", lambda E: E.reciprocal(small[:, 7:8], small[:, 6:7]))
            tnorm = b.op("vector", lambda E: E.tensor_scalar(
                Pt[:], Pt[:], small[:, 7:8], None, op0=OP.mult), inc="PNM")
            if dbg and i == 0 and h == 0:
                b.wait("gpsimd", tnorm)
                b.dma("gpsimd", lambda E: E.dma_start(out=dbg_out[1][:, 0:896], in_=Pt[:]), ch="DBG")
                b.dma("gpsimd", lambda E: E.dma_start(out=dbg_out[2][:, 0:8], in_=small[:, 0:8]), ch="DBG")
            b.waits("tensor", [tnorm, state["pt_bank"]])
            ttr = None
            for jblk in range(7):
                ttr = b.op("tensor", lambda E, jblk=jblk: E.transpose(
                    PTp[:, jblk * 128:(jblk + 1) * 128], Pt[:, jblk * 128:(jblk + 1) * 128],
                    idb[:]), nop=False, inc=(True if jblk == 6 else None))
            state["pt_sb"] = ttr
            b.waits("vector", [ttr, state["pts_free"]])
            tevp = b.op("vector", lambda E: E.tensor_copy(out=PTs[:], in_=PTp[:]), inc="PTE")
            state["pt_bank"] = tevp
            b.waits("tensor", [tevp, state["oht_bank"]])
            tav = None
            for jblk in range(7):
                vblk = jblk if jblk < 2 else (i + jblk)
                tav = b.op("tensor", lambda E, jblk=jblk, vblk=vblk, kvh=kvh: E.matmul(
                    OHT[:], vt[:, vblk * 512 + kvh * 128:vblk * 512 + (kvh + 1) * 128],
                    PTs[:, jblk * 128:(jblk + 1) * 128],
                    start=(jblk == 0), stop=(jblk == 6)), nop=False,
                    inc=(True if jblk == 6 else None))
            state["pts_free"] = tav
            b.waits("scalar", [tav, state["outT_free"]])
            toh = b.op("scalar", lambda E, h=h: E.copy(
                out=outT[:, h * 128:(h + 1) * 128], in_=OHT[:]), inc="OHE")
            state["oht_bank"] = toh
            if dbg and i == 0 and h == 0:
                b.wait("gpsimd", toh)
                b.dma("gpsimd", lambda E: E.dma_start(out=dbg_out[4][:, 0:128], in_=outT[:, 0:128]), ch="DBG")
                b.dma("gpsimd", lambda E: E.dma_start(out=dbg_out[3][:, 0:896], in_=PTs[:]), ch="DBG")
        b.waits("tensor", [toh])
        b.waits("vector", [state["xaf_free"]])
        ybank_tok = state.get("ybank_tok", [None, None])
        tyev = None
        for nc4 in range(4):
            yb = nc4 % 2
            b.waits("tensor", [ybank_tok[yb]])
            two = None
            for f in range(16):
                two = b.op("tensor", lambda E, nc4=nc4, f=f: E.matmul(
                    Yp[nc4 % 2][:],
                    outT[:, f * 128:(f + 1) * 128],
                    wob[:, f * 2048 + nc4 * 512:f * 2048 + (nc4 + 1) * 512],
                    start=(f == 0), stop=(f == 15)), nop=False,
                    inc=(True if f == 15 else None))
            b.waits("vector", [two])
            tyev = b.op("vector", lambda E, nc4=nc4, yb=yb: E.scalar_tensor_tensor(
                x1[:, nc4 * 512:(nc4 + 1) * 512], in0=Yp[yb][:], scalar=0.0,
                in1=x_t[:, nc4 * 512:(nc4 + 1) * 512],
                op0=OP.add, op1=OP.add), inc=f"YB{yb}")
            ybank_tok[yb] = tyev
        state["ybank_tok"] = ybank_tok
        state["outT_free"] = two
        b.fop("vector", lambda E: E.reduce_sum(small[:, 8:9], x1[:], axis=AX))
        b.op("vector", lambda E: E.tensor_scalar(
            small[:, 10:11], small[:, 8:9], 1.0 / D, None, op0=OP.mult))
        b.wait("scalar", tyev)
        tsq = b.op("scalar", lambda E: E.activation(
            out=x_t[:], in_=x1[:], func=AF.Square, accum_out=small[:, 9:10]), inc="LSQ")
        state["y_bank"] = tsq
        if dbg and i == 0:
            b.wait("gpsimd", tsq)
            b.dma("gpsimd", lambda E: E.dma_start(out=dbg_out[5][:, :], in_=outT[:]), ch="DBG")
            b.dma("gpsimd", lambda E: E.dma_start(out=dbg_out[6][:, :], in_=x1[:]), ch="DBG")
        b.wait("vector", tsq)
        b.op("vector", lambda E: E.tensor_scalar(
            small[:, 11:12], small[:, 9:10], 1.0 / D, None, op0=OP.mult))
        b.op("vector", lambda E: E.tensor_tensor(
            small[:, 9:10], small[:, 10:11], small[:, 10:11], op=OP.mult))
        b.op("vector", lambda E: E.tensor_tensor(
            small[:, 11:12], small[:, 11:12], small[:, 9:10], op=OP.subtract))
        b.op("vector", lambda E: E.tensor_scalar(
            small[:, 11:12], small[:, 11:12], EPS, None, op0=OP.add))
        trec = b.fop("vector", lambda E: E.reciprocal(small[:, 12:13], small[:, 11:12]),
                     inc="LRC")
        b.wait("scalar", trec)
        tst_ = b.op("scalar", lambda E: E.activation(
            out=small[:, 13:14], in_=small[:, 12:13], func=AF.Sqrt), inc="LSR")
        b.wait("vector", tst_)
        txa = b.op("vector", lambda E: E.tensor_scalar(
            xa_f[:], x1[:], small[:, 10:11], small[:, 13:14],
            op0=OP.subtract, op1=OP.mult), inc="LXA")
        b.waits("scalar", [txa, state["xab_free"]])
        txab = b.op("scalar", lambda E: E.copy(out=xab_t[:], in_=xa_f[:]), inc="LXB")
        b.waits("sync", [txa, txab])
        xch = "XSTa" if (i // 4) % 2 == 0 else "XSTb"
        xa_store[i] = b.dma("sync", lambda E, i=i: E.dma_start(
            out=xa_dram[i * 128:(i + 1) * 128, :], in_=xa_f[:]), ch=xch)
        xab_store[i] = b.dma("sync", lambda E, i=i: E.dma_start(
            out=xab_dram[i * 128:(i + 1) * 128, :], in_=xab_t[:]), ch=xch)
        state["xaf_free"] = xa_store[i]
        state["xab_free"] = xab_store[i]
        if i + 1 < NT:
            emit_tile_loads(i + 1)

        if stage in ("full", "route", "p4") and i % 4 == 3:
            emit_gate_quarter(i // 4)
            for jj in range(4):
                emit_routing_tile(i - 3 + jj)

    if stage == "route":
        slotdump = dt("slotdump", [128, 128], I32, kind="ExternalOutput").ap()
        rdump = dt("rdump", [6, 128, 128], F32, kind="ExternalOutput").ap()
        b.waits("gpsimd", [b.last["vector"]])
        b.dma("gpsimd", lambda E: E.dma_start(out=rdump[0][0:32, :], in_=rdbg[0][:]), ch="DBG")
        b.dma("gpsimd", lambda E: E.dma_start(out=rdump[1][0:32, :], in_=rdbg[1][:]), ch="DBG")
        b.dma("gpsimd", lambda E: E.dma_start(out=rdump[2][:, 0:32], in_=rdbg[2][:]), ch="DBG")
        b.dma("gpsimd", lambda E: E.dma_start(out=rdump[3][:, 0:32], in_=rdbg[3][:]), ch="DBG")
        b.dma("gpsimd", lambda E: E.dma_start(out=rdump[4][:, 0:8], in_=rdbg[4][:]), ch="DBG")
        b.dma("gpsimd", lambda E: E.dma_start(out=rdump[5][:, 0:8], in_=rdbg[5][:]), ch="DBG")
        b.waits("gpsimd", [t for t in (pay_tok[i] for i in range(NT)) if t is not None])
        b.barrier(ENGS)
        tlast = b.dma("gpsimd", lambda E: E.dma_start(out=slotdump[:], in_=slot8i[:]), ch="DBG")
        for i in range(NT):
            tlast = b.dma("gpsimd", lambda E, i=i: E.dma_start(
                out=y_out[i * 128:(i + 1) * 128, :],
                in_=xa_dram[i * 128:(i + 1) * 128, :]), ch="DBG")
        b.wait("sync", tlast)
        finish(nc, b)
        pb.close()
        pab.close()
        top.close()
        return nc

    if stage == "attn":
        b.barrier(ENGS)
        tlast = None
        for i in range(NT):
            tlast = b.dma("gpsimd", lambda E, i=i: E.dma_start(
                out=y_out[i * 128:(i + 1) * 128, :],
                in_=xa_dram[i * 128:(i + 1) * 128, :]), ch="DBG")
        b.wait("sync", tlast)
        finish(nc, b)
        pb.close()
        pab.close()
        top.close()
        return nc

    b.barrier(ENGS)
    pb.close()
    pab.close()

    pc = ExitStack()
    wgb = SB(pc, "wgb", [128, 2 * 16 * INTER], BF16)
    WUB = 16 * INTER
    wdb = SB(pc, "wdb", [128, 6 * 2048], BF16)
    xgt = [SB(pc, f"xgt{j}", [128, 16 * CAP], BF16) for j in range(2)]
    xgrow = [SB(pc, f"xgrow{j}", [128, D], BF16) for j in range(2)]
    TRp = [PS(pc, f"TRp{j}", [128, 512], BF16) for j in range(2)]
    wc_b = [SB(pc, f"wc_b{j}", [128, 10], BF16) for j in range(2)]
    wcol = [SB(pc, f"wcol{j}", [128, 5], F32) for j in range(2)]
    hT = SB(pc, "hT", [128, 6 * CAP], BF16)
    hs = SB(pc, "hs", [128, CAP], F32)
    ysb = [SB(pc, f"ysb{j}", [128, D], BF16) for j in range(2)]
    Gp = PS(pc, "Gp", [128, CAP])
    Up = PS(pc, "Up", [128, CAP])

    wg_ready = [None] * NE
    wd_ready = [None] * NE
    ffn1_done = [None] * NE
    ffn2_done = [None] * NE
    xgt_ready = [None] * NE
    wcol_ok = [None] * NE
    ysb_tok = [None, None]
    sil_tok = None
    hmu_tok = None

    def emit_weight_load(e, kind):
        toks = []
        if kind == "gu":
            b.waits("gpsimd", [ffn1_done[e - 1]] if e > 0 else [])
            for src, base in ((wg_in, 0), (wu_in, WUB)):
                for h in range(2):
                    toks.append(b.dma("gpsimd", lambda E, src=src, base=base, h=h, e=e:
                        E.dma_start(
                            out=wgb[:, base + h * 8 * INTER:base + (h + 1) * 8 * INTER]
                                .rearrange("p (c f) -> p c f", c=8),
                            in_=src[e, h * 1024:(h + 1) * 1024, :]
                                .rearrange("(c p) f -> p c f", p=128)), ch="WGU"))
            wg_ready[e] = toks
        else:
            b.waits("gpsimd", [ffn2_done[e - 1]] if e > 0 else [])
            for h in range(2):
                toks.append(b.dma("gpsimd", lambda E, h=h, e=e: E.dma_start(
                    out=wdb[:, h * 3 * 2048:(h + 1) * 3 * 2048]
                        .rearrange("p (c f) -> p c f", c=3),
                    in_=wd_in[e, h * 384:(h + 1) * 384, :]
                        .rearrange("(c p) f -> p c f", p=128)), ch="WDD"))
            wd_ready[e] = toks

    xg_state = {"rowfree": [None, None], "trpfree": [None, None]}

    def emit_xgt_load(e):
        bj = e % 2
        tev = None
        for rc in range(5):
            rcj = rc % 2
            ldeng = "sync" if rcj == 0 else "scalar"
            b.waits(ldeng, [xg_state["rowfree"][rcj]])
            tld = b.dma(ldeng, lambda E, e=e, rc=rc, rcj=rcj: E.dma_start(
                out=xgrow[rcj][:],
                in_=xsorted[e * CAP + rc * 128:e * CAP + (rc + 1) * 128, 0:D]),
                ch="TRX4a" if rcj == 0 else "TRX4b")
            ttr = None
            for g in range(4):
                tb = (rc * 4 + g) % 2
                b.waits("tensor", [tld, xg_state["trpfree"][tb]])
                for q in range(4):
                    ttr = b.op("tensor", lambda E, g=g, q=q, tb=tb, rcj=rcj: E.transpose(
                        TRp[tb][:, q * 128:(q + 1) * 128],
                        xgrow[rcj][:, (g * 4 + q) * 128:(g * 4 + q + 1) * 128],
                        idb[:]), nop=False, inc=(True if q == 3 else None))
                b.waits("vector", [ttr] + (
                    [ffn1_done[e - 2]] if (e >= 2 and rc == 0 and g == 0) else []))
                tev = b.op("vector", lambda E, g=g, rc=rc, tb=tb, bj=bj: E.tensor_copy(
                    out=xgt[bj][:].rearrange("p (c t) -> p c t", c=16)
                        [:, g * 4:(g + 1) * 4, rc * 128:(rc + 1) * 128],
                    in_=TRp[tb][:].rearrange("p (c t) -> p c t", c=4)), inc="XEV")
                xg_state["trpfree"][tb] = tev
            xg_state["rowfree"][rcj] = ttr
        xgt_ready[e] = tev
        tw = []
        for tc5 in range(5):
            tw.append(b.dma("scalar", lambda E, e=e, tc5=tc5, bj=bj: E.dma_start(
                out=wc_b[bj][:, tc5 * 2:(tc5 + 1) * 2],
                in_=xsorted[e * CAP + tc5 * 128:e * CAP + (tc5 + 1) * 128,
                            WCOL + 2 * e:WCOL + 2 * e + 2]), ch="MISC"))
        b.waits("vector", tw + ([ffn2_done[e - 2]] if e >= 2 else []))
        tv = None
        for tc5 in range(5):
            tv = b.op("vector", lambda E, tc5=tc5, bj=bj: E.tensor_tensor(
                wcol[bj][:, tc5:tc5 + 1], wc_b[bj][:, tc5 * 2:tc5 * 2 + 1],
                wc_b[bj][:, tc5 * 2 + 1:tc5 * 2 + 2], op=OP.add),
                inc=("WCL" if tc5 == 4 else None))
        wcol_ok[e] = tv

    scat_all = [pay_tok[i] for i in range(NT)]
    b.waits("scalar", scat_all)
    b.waits("sync", scat_all)
    b.waits("gpsimd", scat_all)
    emit_weight_load(0, "gu")
    emit_xgt_load(0)
    emit_weight_load(0, "d")

    for e in range(NE):
        bj = e % 2
        b.waits("tensor", wg_ready[e] + [xgt_ready[e]])
        for ic in range(6):
            b.waits("tensor", [sil_tok])
            tgm = None
            for ncc, w in ((0, 512), (512, 128)):
                for dmc in range(16):
                    tgm = b.op("tensor", lambda E, ic=ic, ncc=ncc, w=w, dmc=dmc, bj=bj: E.matmul(
                        Gp[:, ncc:ncc + w],
                        wgb[:, dmc * INTER + ic * 128:dmc * INTER + (ic + 1) * 128],
                        xgt[bj][:, dmc * CAP + ncc:dmc * CAP + ncc + w],
                        start=(dmc == 0), stop=(dmc == 15)), nop=False,
                        inc=(True if (ncc, dmc) == (512, 15) else None))
            b.waits("tensor", [hmu_tok])
            tum = None
            for ncc, w in ((0, 512), (512, 128)):
                for dmc in range(16):
                    tum = b.op("tensor", lambda E, ic=ic, ncc=ncc, w=w, dmc=dmc, bj=bj: E.matmul(
                        Up[:, ncc:ncc + w],
                        wgb[:, WUB + dmc * INTER + ic * 128:WUB + dmc * INTER + (ic + 1) * 128],
                        xgt[bj][:, dmc * CAP + ncc:dmc * CAP + ncc + w],
                        start=(dmc == 0), stop=(dmc == 15)), nop=False,
                        inc=(True if (ncc, dmc) == (512, 15) else None))
            b.waits("scalar", [tgm, hmu_tok])
            tsl = b.op("scalar", lambda E: E.activation(
                out=hs[:], in_=Gp[:], func=AF.Silu), inc="SIL")
            sil_tok = tsl
            b.waits("vector", [tsl, tum])
            tmu = b.op("vector", lambda E, ic=ic: E.tensor_tensor(
                hT[:, ic * CAP:(ic + 1) * CAP], hs[:], Up[:], op=OP.mult), inc="HMU")
            hmu_tok = tmu
        ffn1_done[e] = hmu_tok
        if e + 1 < NE:
            emit_weight_load(e + 1, "gu")
            emit_xgt_load(e + 1)
        b.waits("tensor", wd_ready[e] + [ffn1_done[e], wcol_ok[e]])
        tym_last = None
        ychunk_tok = [None, None]
        for tc5 in range(5):
            yb2 = tc5 % 2
            b.waits("tensor", [ysb_tok[yb2]])
            tyv = None
            for nc4 in range(4):
                yq = nc4 % 2
                b.waits("tensor", [ychunk_tok[yq]])
                tym = None
                for ic in range(6):
                    tym = b.op("tensor", lambda E, nc4=nc4, ic=ic, tc5=tc5: E.matmul(
                        Yp[nc4 % 2][:],
                        hT[:, ic * CAP + tc5 * 128:ic * CAP + (tc5 + 1) * 128],
                        wdb[:, ic * 2048 + nc4 * 512:ic * 2048 + (nc4 + 1) * 512],
                        start=(ic == 0), stop=(ic == 5)), nop=False,
                        inc=(True if ic == 5 else None))
                tym_last = tym
                b.wait("vector", tym)
                tyv = b.op("vector", lambda E, yb2=yb2, yq=yq, nc4=nc4, tc5=tc5, bj=bj: E.tensor_scalar(
                    ysb[yb2][:, nc4 * 512:(nc4 + 1) * 512], Yp[yq][:],
                    wcol[bj][:, tc5:tc5 + 1], None, op0=OP.mult), inc=f"YC{yq}")
                ychunk_tok[yq] = tyv
            b.wait("sync", tyv)
            tst2 = b.dma("sync", lambda E, e=e, tc5=tc5, yb2=yb2: E.dma_start(
                out=ysorted[e * CAP + tc5 * 128:e * CAP + (tc5 + 1) * 128, :],
                in_=ysb[yb2][:]), ch="YSTa" if yb2 == 0 else "YSTb")
            ysb_tok[yb2] = tst2
        ffn2_done[e] = tym_last
        if e + 1 < NE:
            emit_weight_load(e + 1, "d")

    b.barrier(ENGS)
    if stage == "p4":
        tlast = None
        for i in range(NT):
            tlast = b.dma("gpsimd", lambda E, i=i: E.dma_start(
                out=y_out[i * 128:(i + 1) * 128, :].rearrange("p (h c) -> p h c", h=1),
                in_=ysorted[i * 128:(i + 1) * 128, :].rearrange("p (h c) -> p h c", h=1)), ch="DBG")
        b.wait("sync", tlast)
        finish(nc, b)
        pc.close()
        top.close()
        return nc
    pc.close()

    pd = ExitStack()
    gath = [SB(pd, f"gath{j}", [128, D], BF16) for j in range(8)]
    moe = SB(pd, "moe", [128, D], F32)
    t01 = SB(pd, "t01", [128, D], F32)
    t23 = SB(pd, "t23", [128, D], F32)
    t45 = SB(pd, "t45", [128, D], F32)
    t67 = SB(pd, "t67", [128, D], F32)
    xat = SB(pd, "xat", [128, D], F32)
    outf = SB(pd, "outf", [128, D], F32)

    gth_tok = {}

    def emit_gathers(i):
        toks = []
        for k in range(8):
            toks.append(b.dma("gpsimd", lambda E, i=i, k=k: E.indirect_dma_start(
                out=gath[k][:], out_offset=None,
                in_=ysorted[:, :],
                in_offset=IndirectOffsetOnAxis(slot8i[:, i * 8 + k:i * 8 + k + 1], 0)),
                ch="GTH"))
        gth_tok[i] = toks

    emit_gathers(0)
    out_last = None
    prev_xat = None
    prev_out = None
    for i in range(NT):
        b.waits("vector", gth_tok[i])
        if prev_xat is not None:
            b.waits("sync", [prev_xat])
        txl = b.dma("sync", lambda E, i=i: E.dma_start(
            out=xat[:], in_=xa_dram[i * 128:(i + 1) * 128, :]), ch="XAT")
        b.op("vector", lambda E: E.tensor_tensor(t01[:], gath[0][:], gath[1][:], op=OP.add))
        b.op("vector", lambda E: E.tensor_tensor(t23[:], gath[2][:], gath[3][:], op=OP.add))
        b.op("vector", lambda E: E.tensor_tensor(t45[:], gath[4][:], gath[5][:], op=OP.add))
        b.op("vector", lambda E: E.tensor_tensor(t67[:], gath[6][:], gath[7][:], op=OP.add))
        b.op("vector", lambda E: E.tensor_tensor(t01[:], t01[:], t23[:], op=OP.add))
        b.op("vector", lambda E: E.tensor_tensor(t45[:], t45[:], t67[:], op=OP.add))
        tgsum = b.op("vector", lambda E: E.tensor_tensor(moe[:], t01[:], t45[:], op=OP.add),
                     inc="GSM")
        if i + 1 < NT:
            b.wait("gpsimd", tgsum)
            emit_gathers(i + 1)
        b.wait("vector", txl)
        b.op("vector", lambda E: E.tensor_tensor(
            x1[:], moe[:], xat[:], op=OP.add))
        b.fop("vector", lambda E: E.reduce_sum(small[:, 8:9], x1[:], axis=AX))
        tv1 = b.op("vector", lambda E: E.tensor_scalar(
            small[:, 10:11], small[:, 8:9], 1.0 / D, None, op0=OP.mult), inc="LV1")
        b.wait("scalar", tv1)
        tsq = b.op("scalar", lambda E: E.activation(
            out=xat[:], in_=x1[:], func=AF.Square, accum_out=small[:, 9:10]), inc="LSQ")
        prev_xat = tsq
        b.wait("vector", tsq)
        b.op("vector", lambda E: E.tensor_scalar(
            small[:, 11:12], small[:, 9:10], 1.0 / D, None, op0=OP.mult))
        b.op("vector", lambda E: E.tensor_tensor(
            small[:, 9:10], small[:, 10:11], small[:, 10:11], op=OP.mult))
        b.op("vector", lambda E: E.tensor_tensor(
            small[:, 11:12], small[:, 11:12], small[:, 9:10], op=OP.subtract))
        b.op("vector", lambda E: E.tensor_scalar(
            small[:, 11:12], small[:, 11:12], EPS, None, op0=OP.add))
        trec = b.fop("vector", lambda E: E.reciprocal(small[:, 12:13], small[:, 11:12]),
                     inc="LRC")
        b.wait("scalar", trec)
        tst_ = b.op("scalar", lambda E: E.activation(
            out=small[:, 13:14], in_=small[:, 12:13], func=AF.Sqrt), inc="LSR")
        b.waits("vector", [tst_] + ([prev_out] if prev_out else []))
        tof = b.op("vector", lambda E: E.tensor_scalar(
            outf[:], x1[:], small[:, 10:11], small[:, 13:14],
            op0=OP.subtract, op1=OP.mult), inc="OF")
        b.wait("sync", tof)
        out_last = b.dma("sync", lambda E, i=i: E.dma_start(
            out=y_out[i * 128:(i + 1) * 128, :], in_=outf[:]), ch="OUT")
        prev_out = out_last

    b.wait("sync", out_last)
    b.wait("gpsimd", out_last)
    if dbg:
        tdb = b.dma("gpsimd", lambda E: E.dma_start(out=sdump[:], in_=slot8i[:]), ch="DBG")
        for ch_ in range(6):
            r0, r1 = ch_ * 3584, min((ch_ + 1) * 3584, ROWS + 1)
            tdb = b.dma("gpsimd", lambda E, r0=r0, r1=r1: E.dma_start(
                out=wdump[r0:r1, :], in_=xsorted[r0:r1, 2048:2112]), ch="DBG")
            tdb = b.dma("gpsimd", lambda E, r0=r0, r1=r1: E.dma_start(
                out=ydump[r0:r1, :], in_=ysorted[r0:r1, 0:8]), ch="DBG")
        b.wait("sync", tdb)
    finish(nc, b)
    pd.close()
    top.close()
    return nc


def finish(nc, b):
    with nc.Block() as block:
        for eng in ENGS:
            ops = list(b.q[eng])

            def runner(E, ops=ops):
                for fn in ops:
                    fn(E)
            getattr(block, eng)(runner)


def _rope_tables():
    inv = 1.0 / (10000.0 ** (np.arange(0, DQK, 2)[: DQK // 2].astype(np.float32) / DQK))
    t = np.arange(S, dtype=np.float32)
    ang = np.outer(t, inv)
    return np.cos(ang), np.sin(ang)


def _perm_cols(w, nheads):
    perm = np.concatenate([np.arange(0, 128, 2), np.arange(1, 128, 2)])
    w2 = w.reshape(w.shape[0], nheads, 128)[:, :, perm]
    return np.ascontiguousarray(w2.reshape(w.shape[0], nheads * 128))


def _mask_for_core(qo):
    m = np.full((NT, 128, 896), NEG, np.float32)
    for i in range(NT):
        q = qo + i * 128 + np.arange(128)[:, None]
        band_lo = qo - 512 + i * 128

        def allowed(j):
            band = (j <= q) & (j > q - WIN)
            front = (j < FRONT) & (j <= q - WIN)
            return band | front
        jf = np.arange(256)[None, :]
        in_band = (jf >= band_lo) & (jf < band_lo + 640)
        m[i, :, 0:256] = np.where(allowed(jf) & ~in_band, 0.0, NEG)
        jb = band_lo + np.arange(640)[None, :]
        m[i, :, 256:896] = np.where(allowed(jb) & (jb >= 0), 0.0, NEG)
    return m


def make_in_maps(L0_attn_x, L0_attn_wq, L0_attn_wk, L0_attn_wv, L0_attn_wo,
                 L0_ffn_gate_w, L0_ffn_w_gate, L0_ffn_w_up, L0_ffn_w_down):
    x = np.asarray(L0_attn_x, np.float32)
    wq = _perm_cols(np.asarray(L0_attn_wq, np.float32), NH)
    wk = _perm_cols(np.asarray(L0_attn_wk, np.float32), NKV)
    wv = np.ascontiguousarray(np.asarray(L0_attn_wv, np.float32))
    wo = np.ascontiguousarray(np.asarray(L0_attn_wo, np.float32))
    gw = np.ascontiguousarray(np.asarray(L0_ffn_gate_w, np.float32))
    wg = np.ascontiguousarray(np.asarray(L0_ffn_w_gate, np.float32))
    wu = np.ascontiguousarray(np.asarray(L0_ffn_w_up, np.float32))
    wdn = np.ascontiguousarray(np.asarray(L0_ffn_w_down, np.float32))
    cos, sin = _rope_tables()
    idf = np.eye(128, dtype=np.float32)
    idb = idf.astype(ml_dtypes.bfloat16)
    scale = np.float32(1.0 / np.sqrt(DQK))

    in_maps = []
    for c in range(8):
        bb, qo = c // 4, (c % 4) * T
        xb = x[bb]
        xkv = np.zeros((XKV, D), np.float32)
        xkv[0:FRONT] = xb[0:FRONT]
        lo = qo - WIN
        src0 = max(0, lo)
        xkv[FRONT + (src0 - lo):FRONT + WIN + T] = xb[src0:qo + T]
        qpos = np.arange(qo, qo + T)
        kvpos = np.concatenate([np.arange(FRONT),
                                np.clip(np.arange(lo, qo + T), 0, S - 1)])
        in_maps.append({
            "xkv_in": np.ascontiguousarray(xkv),
            "wq_in": wq, "wk_in": wk, "wv_in": wv, "wo_in": wo,
            "gate_in": gw, "wg_in": wg, "wu_in": wu, "wd_in": wdn,
            "cq_in": np.ascontiguousarray(cos[qpos] * scale),
            "sq_in": np.ascontiguousarray(sin[qpos] * scale),
            "ckv_in": np.ascontiguousarray(cos[kvpos]),
            "skv_in": np.ascontiguousarray(sin[kvpos]),
            "mask_in": _mask_for_core(qo),
            "idf_in": idf, "idb_in": idb,
        })
    return in_maps


def kernel(L0_attn_x, L0_attn_wq, L0_attn_wk, L0_attn_wv, L0_attn_wo,
           L0_ffn_gate_w, L0_ffn_w_gate, L0_ffn_w_up, L0_ffn_w_down):
    nc = build_nc(stage="full")
    in_maps = make_in_maps(L0_attn_x, L0_attn_wq, L0_attn_wk, L0_attn_wv, L0_attn_wo,
                           L0_ffn_gate_w, L0_ffn_w_gate, L0_ffn_w_up, L0_ffn_w_down)
    res = run_bass_kernel_spmd(nc, in_maps, core_ids=list(range(8)))
    out = np.empty((BATCH, S, D), np.float32)
    for c in range(8):
        bb, qo = c // 4, (c % 4) * T
        out[bb, qo:qo + T] = res.results[c]["y_out"]
    return out


# version 06
# speedup vs baseline: 1.4588x; 1.0754x over previous
import numpy as np
import ml_dtypes
from contextlib import ExitStack

import concourse.bass as bass
import concourse.mybir as mybir
from concourse.bass_utils import run_bass_kernel_spmd
from concourse.bass_core_rust import IndirectOffsetOnAxis

F32 = mybir.dt.float32
BF16 = mybir.dt.bfloat16
I32 = mybir.dt.int32
AF = mybir.ActivationFunctionType
OP = mybir.AluOpType
AX = mybir.AxisListType.X

S, D, BATCH = 8192, 2048, 2
T = 2048
NT = 16
NKV, NH, DQK = 4, 16, 128
WIN, FRONT = 512, 256
XKV = FRONT + WIN + T
NE, TOPK, INTER = 32, 8, 768
CAP = 640
ROWS = NE * CAP
PAYW = 2176
WCOL = 2048
NEG = -1e30
BIGC = 1e9
EPS = 1e-5

ENGS = ("sync", "scalar", "vector", "tensor", "gpsimd")
SEMS = ("T", "BV", "BS", "BG",
        "SG0", "SG1", "IOT", "PP0", "PP1", "PP2", "RP", "PKB0", "PKB1",
        "SMK", "EXP", "PNM", "PTE", "OHE", "YB0", "YB1",
        "LSQ", "LRC", "LSR", "LXA", "LXB", "LV1",
        "GLG", "GL2", "RNG", "RXP", "RWV", "RCA", "RS8", "RPW",
        "SIL", "HMU", "WCL", "YC0", "YC1", "GSM", "OF", "KTD", "VTD", "VFN", "SFN",
        "XEV")
DMA_CHANNELS = ("MISC", "XKLa", "XKLb", "XKSa", "XKSb", "WSTa", "WSTb", "TL",
                "PLDa", "PLDb", "QST", "XSTa", "XSTb", "YSTa", "YSTb", "XAT",
                "OUT", "TRXa", "TRXb", "TRXG", "TRX4a", "TRX4b", "SCTa", "SCTb",
                "WGU", "WDD",
                "GTH", "DBG")
BAR_SEM = {"tensor": "T", "vector": "BV", "scalar": "BS", "gpsimd": "BG"}


FENCE_ALL = True


class Bld:
    def __init__(self):
        self.q = {e: [] for e in ENGS}
        self.cnt = {s: 0 for s in SEMS}
        self.cnt.update({c: 0 for c in DMA_CHANNELS})
        self.cnt.update({c: 0 for c in DMA_CHANNELS})
        self.sem = {}
        self.last = {e: None for e in ENGS}

    def fop(self, eng, fn, inc=None):
        if FENCE_ALL and eng in ("vector", "scalar"):
            return self.op(eng, fn, inc=inc)
        tok = self.op(eng, fn, inc=inc)
        if tok is not None:
            self.q[eng].append(lambda E, k=tok[0], v=tok[1]: E.wait_ge(self.sem[k], v))
            return tok
        fch = "VFN" if eng == "vector" else "SFN"
        self.cnt[fch] += 1
        v = self.cnt[fch]
        real = self.q[eng].pop()
        self.q[eng].append(lambda E, real=real, fch=fch: real(E).then_inc(self.sem[fch], 1))
        self.q[eng].append(lambda E, fch=fch, v=v: E.wait_ge(self.sem[fch], v))
        return tok

    def op(self, eng, fn, inc=None, nop=True):
        if FENCE_ALL and eng in ("vector", "scalar"):
            tok = self._op_raw(eng, fn, inc=inc, nop=False)
            if tok is not None:
                self.q[eng].append(lambda E, k=tok[0], v=tok[1]: E.wait_ge(self.sem[k], v))
                return tok
            fch = "VFN" if eng == "vector" else "SFN"
            self.cnt[fch] += 1
            v = self.cnt[fch]
            real = self.q[eng].pop()
            self.q[eng].append(lambda E, real=real, fch=fch: real(E).then_inc(self.sem[fch], 1))
            self.q[eng].append(lambda E, fch=fch, v=v: E.wait_ge(self.sem[fch], v))
            return None
        return self._op_raw(eng, fn, inc=inc, nop=nop)

    def _op_raw(self, eng, fn, inc=None, nop=True):
        if nop and eng == "vector":
            self.q[eng].append(lambda E: E.engine_nop())
        elif nop and eng in ("scalar", "gpsimd"):
            self.q[eng].append(lambda E: E.nop())
        if inc:
            k = "T" if (inc is True and eng == "tensor") else inc
            assert k is not True, "non-tensor inc needs a channel name"
            self.cnt[k] += 1
            v = self.cnt[k]
            self.q[eng].append(lambda E, fn=fn, k=k: fn(E).then_inc(self.sem[k], 1))
            tok = (k, v)
            self.last[eng] = tok
            return tok
        self.q[eng].append(fn)
        return None

    def mark(self, eng):
        k = BAR_SEM[eng]
        self.cnt[k] += 1
        v = self.cnt[k]
        if eng == "vector":
            self.q[eng].append(lambda E, k=k: E.engine_nop().then_inc(self.sem[k], 1))
        else:
            self.q[eng].append(lambda E, k=k: E.nop().then_inc(self.sem[k], 1))
        tok = (k, v)
        self.last[eng] = tok
        return tok

    def dma(self, eng, fn, ch="MISC"):
        k = ch
        self.cnt[k] += 16
        v = self.cnt[k]
        self.q[eng].append(lambda E, fn=fn, k=k: fn(E).then_inc(self.sem[k], 16))
        tok = (k, v)
        self.last[eng] = tok
        return tok

    def wait(self, eng, tok):
        if tok is None:
            return
        k, v = tok
        self.q[eng].append(lambda E, k=k, v=v: E.wait_ge(self.sem[k], v))

    def waits(self, eng, toks):
        best = {}
        for t in toks:
            if t is None:
                continue
            k, v = t
            best[k] = max(best.get(k, 0), v)
        for k, v in best.items():
            self.q[eng].append(lambda E, k=k, v=v: E.wait_ge(self.sem[k], v))

    def barrier(self, engines=ENGS):
        toks = []
        for ch in SEMS + DMA_CHANNELS:
            if self.cnt[ch] > 0:
                toks.append((ch, self.cnt[ch]))
        for e in engines:
            self.waits(e, toks)


def build_nc(stage="full", races=True, dbg=False, rparts=4):
    nc = bass.Bass("TRN2", target_bir_lowering=False,
                   detect_race_conditions=races)
    top = ExitStack()

    dt = nc.dram_tensor
    xkv_in = dt("xkv_in", [XKV, D], F32, kind="ExternalInput").ap()
    wq_in = dt("wq_in", [D, D], F32, kind="ExternalInput").ap()
    wk_in = dt("wk_in", [D, 512], F32, kind="ExternalInput").ap()
    wv_in = dt("wv_in", [D, 512], F32, kind="ExternalInput").ap()
    wo_in = dt("wo_in", [D, D], F32, kind="ExternalInput").ap()
    gate_in = dt("gate_in", [D, NE], F32, kind="ExternalInput").ap()
    wg_in = dt("wg_in", [NE, D, INTER], F32, kind="ExternalInput").ap()
    wu_in = dt("wu_in", [NE, D, INTER], F32, kind="ExternalInput").ap()
    wd_in = dt("wd_in", [NE, INTER, D], F32, kind="ExternalInput").ap()
    cq_in = dt("cq_in", [T, 64], F32, kind="ExternalInput").ap()
    sq_in = dt("sq_in", [T, 64], F32, kind="ExternalInput").ap()
    ckv_in = dt("ckv_in", [XKV, 64], F32, kind="ExternalInput").ap()
    skv_in = dt("skv_in", [XKV, 64], F32, kind="ExternalInput").ap()
    mask_in = dt("mask_in", [NT, 128, 896], F32, kind="ExternalInput").ap()
    idf_in = dt("idf_in", [128, 128], F32, kind="ExternalInput").ap()
    idb_in = dt("idb_in", [128, 128], BF16, kind="ExternalInput").ap()

    y_out = dt("y_out", [T, D], F32, kind="ExternalOutput").ap()
    dbg_out = dt("dbg_out", [8, 128, 2048], F32, kind="ExternalOutput").ap() if dbg else None
    sdump = dt("sdump", [128, 128], I32, kind="ExternalOutput").ap() if dbg else None
    wdump = dt("wdump", [ROWS + 1, 64], BF16, kind="ExternalOutput").ap() if dbg else None
    ydump = dt("ydump", [ROWS + 1, 8], BF16, kind="ExternalOutput").ap() if dbg else None

    xkv_bf = dt("xkv_bf", [XKV, D], BF16, kind="Internal").ap()
    qt_dram = dt("qt_dram", [NT, 16, 128, 128], BF16, kind="Internal").ap()
    xa_dram = dt("xa_dram", [T, D], F32, kind="Internal").ap()
    xab_dram = dt("xab_dram", [T, D], BF16, kind="Internal").ap()
    xsorted = dt("xsorted", [ROWS + 1, PAYW], BF16, kind="Internal").ap()
    ysorted = dt("ysorted", [ROWS + 1, D], BF16, kind="Internal").ap()

    def SB(stk, name, shape, dtype):
        return stk.enter_context(nc.sbuf_tensor(name, shape, dtype))

    def PS(stk, name, shape, dtype=F32):
        return stk.enter_context(nc.psum_tensor(name, shape, dtype))

    idf = SB(top, "idf", [128, 128], F32)
    idb = SB(top, "idb", [128, 128], BF16)
    stage_f = [SB(top, f"stage{j}", [128, 2048], F32) for j in range(2)]
    small = SB(top, "small", [128, 20], F32)
    x1 = SB(top, "x1", [128, D], F32)
    logits = SB(top, "logits", [128, NT * NE], F32)
    rdbg = [SB(top, "rdbg0", [32, 128], F32), SB(top, "rdbg1", [32, 128], F32),
            SB(top, "rdbg2", [128, 32], F32), SB(top, "rdbg3", [128, 32], F32),
            SB(top, "rdbg4", [128, 8], F32), SB(top, "rdbg5", [128, 8], F32)]
    slot8i = SB(top, "slot8i", [128, NT * 8], I32)
    gateb = SB(top, "gateb", [128, 16 * NE], BF16)
    iota32 = SB(top, "iota32", [128, NE], F32)
    iota2048 = SB(top, "iota2048", [128, NE], F32)
    Yp = [PS(top, f"Yp{j}", [128, 512]) for j in range(2)]

    sems = {}
    for sname in SEMS + DMA_CHANNELS:
        sems[sname] = top.enter_context(nc.semaphore(sname))
    b = Bld()
    b.sem = sems

    cast_state = {"i": 0, "hist": {}}

    def cast_chunk(src_ap, dst_ap, cols, eng, extra_waits=()):
        par = cast_state["i"] % 2
        stg = stage_f[par]
        cast_state["i"] += 1
        prev = cast_state["hist"].get(id(stg))
        if prev is not None:
            b.wait("sync", prev)
        tl = b.dma("sync", lambda E, stg=stg, cols=cols, src_ap=src_ap: E.dma_start(
            out=stg[:, 0:cols], in_=src_ap), ch="WSTa" if par == 0 else "WSTb")
        b.wait(eng, tl)
        for w in extra_waits:
            b.wait(eng, w)
        sg = "SG0" if par == 0 else "SG1"
        if eng == "scalar":
            tc = b.op(eng, lambda E, stg=stg, cols=cols, dst_ap=dst_ap: E.copy(
                out=dst_ap, in_=stg[:, 0:cols]), inc=sg)
        else:
            tc = b.op(eng, lambda E, stg=stg, cols=cols, dst_ap=dst_ap: E.tensor_copy(
                out=dst_ap, in_=stg[:, 0:cols]), inc=sg)
        cast_state["hist"][id(stg)] = tc
        return tc

    pa = ExitStack()
    pab = ExitStack()
    kt = SB(pab, "kt", [128, NKV * XKV], BF16)
    vt = SB(pab, "vt", [128, 22 * 512], BF16)
    xsl = [SB(pab, "xsl0", [128, 16 * 512], BF16),
           SB(pa, "xsl1", [128, 16 * 512], BF16)]
    cqtm = SB(pa, "cqtm", [128, 16 * 64], F32)
    sqtm = SB(pa, "sqtm", [128, 16 * 64], F32)
    rtmp2 = SB(pa, "rtmp2", [128, 16 * 64], F32)
    qtm = SB(pa, "qtm", [128, 2048], F32)
    qrope = SB(pa, "qrope", [128, 2048], F32)
    qrope_b = SB(pa, "qrope_b", [128, 2048], BF16)
    qtts = SB(pa, "qtts", [128, 2048], BF16)
    pp = [PS(pa, f"pp{j}", [128, 512]) for j in range(3)]
    pkb = [PS(pa, f"pkb{j}", [128, 128], BF16) for j in range(2)]
    pa0 = ExitStack()
    ckvtm = SB(pa0, "ckvtm", [128, 22 * 64], F32)
    skvtm = SB(pa0, "skvtm", [128, 22 * 64], F32)
    ktm = SB(pa0, "ktm", [128, 512], F32)
    krope = SB(pa0, "krope", [128, 512], F32)
    krope_b = SB(pa0, "krope_b", [128, 512], BF16)
    wkb = SB(pa0, "wkb", [128, 16 * 512], BF16)
    wvb = SB(pa0, "wvb", [128, 16 * 512], BF16)

    b.dma("sync", lambda E: E.dma_start(out=idf[:], in_=idf_in[:]))
    b.dma("sync", lambda E: E.dma_start(out=idb[:], in_=idb_in[:]))
    for tl_ in range(16):
        b.dma("sync", lambda E, tl_=tl_: E.dma_start(
            out=cqtm[:, tl_ * 64:(tl_ + 1) * 64], in_=cq_in[tl_ * 128:(tl_ + 1) * 128, :]))
        b.dma("sync", lambda E, tl_=tl_: E.dma_start(
            out=sqtm[:, tl_ * 64:(tl_ + 1) * 64], in_=sq_in[tl_ * 128:(tl_ + 1) * 128, :]))
    for tl_ in range(22):
        b.dma("sync", lambda E, tl_=tl_: E.dma_start(
            out=ckvtm[:, tl_ * 64:(tl_ + 1) * 64], in_=ckv_in[tl_ * 128:(tl_ + 1) * 128, :]))
        b.dma("sync", lambda E, tl_=tl_: E.dma_start(
            out=skvtm[:, tl_ * 64:(tl_ + 1) * 64], in_=skv_in[tl_ * 128:(tl_ + 1) * 128, :]))

    xkv_done = []
    for chk in range(22):
        buf = xsl[chk % 2]
        dst = buf[:, 0:2048]
        if chk >= 2:
            b.wait("gpsimd", xkv_done[chk - 2])
        tl = b.dma("gpsimd", lambda E, chk=chk, dst=dst: E.dma_start(
            out=dst, in_=xkv_in[chk * 128:(chk + 1) * 128, :]),
            ch="XKLa" if chk % 2 == 0 else "XKLb")
        b.wait("sync", tl)
        ts_ = b.dma("sync", lambda E, chk=chk, dst=dst: E.dma_start(
            out=xkv_bf[chk * 128:(chk + 1) * 128, :], in_=dst),
            ch="XKSa" if chk % 2 == 0 else "XKSb")
        xkv_done.append(ts_)
    t_xkvbf = [xkv_done[-1], xkv_done[-2]]

    b.op("gpsimd", lambda E: E.iota(iota32[:], pattern=[[1, NE]], base=0,
                                    channel_multiplier=0,
                                    allow_small_or_imprecise_dtypes=True))
    b.op("gpsimd", lambda E: E.iota(iota2048[:], pattern=[[2048, NE]], base=0,
                                    channel_multiplier=0,
                                    allow_small_or_imprecise_dtypes=True), inc="IOT")

    wtoks = []
    for wsrc, wdst, wcols in ((wk_in, wkb, 512), (wv_in, wvb, 512), (gate_in, gateb, NE)):
        wtoks.append(b.dma("gpsimd", lambda E, wsrc=wsrc, wdst=wdst: E.dma_start(
            out=wdst[:].rearrange("p (c f) -> p c f", c=16),
            in_=wsrc[:, :].rearrange("(c p) f -> p c f", p=128)), ch="WGU"))
    b.waits("tensor", wtoks)
    b.barrier(("sync", "vector", "scalar", "gpsimd"))

    def emit_rope_tm(dst, srct, ctab, stab, nh, tmp):
        sr = srct.rearrange("p (h half i) -> p h half i", h=nh, half=2)
        dr = dst.rearrange("p (h half i) -> p h half i", h=nh, half=2)
        tr = tmp.rearrange("p (h i) -> p h i", h=nh)
        cb = ctab.unsqueeze(1).to_broadcast([128, nh, 64])
        sb_ = stab.unsqueeze(1).to_broadcast([128, nh, 64])
        b.op("vector", lambda E: E.tensor_tensor(dr[:, :, 0, :], sr[:, :, 0, :], cb, op=OP.mult))
        b.op("vector", lambda E: E.tensor_tensor(tr[:], sr[:, :, 1, :], sb_, op=OP.mult))
        b.op("vector", lambda E: E.tensor_tensor(dr[:, :, 0, :], dr[:, :, 0, :], tr[:], op=OP.subtract))
        b.op("vector", lambda E: E.tensor_tensor(dr[:, :, 1, :], sr[:, :, 1, :], cb, op=OP.mult))
        b.op("vector", lambda E: E.tensor_tensor(tr[:], sr[:, :, 0, :], sb_, op=OP.mult))
        b.op("vector", lambda E: E.tensor_tensor(
            dr[:, :, 1, :], dr[:, :, 1, :], tr[:], op=OP.add))
        return None

    if stage == "p0":
        b.barrier(ENGS)
        tlast = None
        for i in range(NT):
            tlast = b.dma("gpsimd", lambda E, i=i: E.dma_start(
                out=y_out[i * 128:(i + 1) * 128, :],
                in_=xkv_bf[768 + i * 128:768 + (i + 1) * 128, :]), ch="DBG")
        b.wait("sync", tlast)
        finish(nc, b)
        return nc

    pptok = {}

    def pwait(j, eng):
        t = pptok.get(j)
        if t is not None:
            b.wait(eng, t)

    pidx = 0
    kv_slices = [(i * 512, min(512, XKV - i * 512)) for i in range(6)]
    slice_done = {}
    for si, (t0, w) in enumerate(kv_slices):
        buf = xsl[si % 2]
        b.waits("scalar", t_xkvbf)
        if si >= 2:
            b.wait("scalar", slice_done[si - 2])
        ttr = None
        for dmc in range(16):
            ttr = b.dma("scalar", lambda E, t0=t0, w=w, dmc=dmc, buf=buf: E.dma_start_transpose(
                out=buf[:, dmc * 512:dmc * 512 + w],
                in_=xkv_bf[t0:t0 + w, dmc * 128:(dmc + 1) * 128]),
                ch="TRXa" if si % 2 == 0 else "TRXb")
        b.waits("tensor", [ttr])
        last_use = None
        pkb_tok = [None, None]
        for tt in range(w // 128):
            blk = t0 // 128 + tt
            j = pidx % 3
            pidx += 1
            pwait(j, "tensor")
            tmm = None
            for dmc in range(16):
                tmm = b.op("tensor", lambda E, j=j, dmc=dmc, tt=tt, buf=buf: E.matmul(
                    pp[j][:, 0:512],
                    buf[:, dmc * 512 + tt * 128:dmc * 512 + (tt + 1) * 128],
                    wkb[:, dmc * 512:(dmc + 1) * 512],
                    start=(dmc == 0), stop=(dmc == 15)), nop=False,
                    inc=(True if dmc == 15 else None))
            b.wait("vector", tmm)
            tevk = b.op("vector", lambda E, j=j: E.tensor_copy(out=ktm[:], in_=pp[j][:, 0:512]),
                        inc=f"PP{j}")
            pptok[j] = tevk
            emit_rope_tm(krope[:], ktm[:], ckvtm[:, blk * 64:(blk + 1) * 64],
                         skvtm[:, blk * 64:(blk + 1) * 64], NKV, rtmp2[:, 0:256])
            trope = b.op("vector", lambda E: E.tensor_copy(out=krope_b[:], in_=krope[:]),
                         inc="RP")
            b.waits("tensor", [trope])
            for f in range(NKV):
                pb_ = f % 2
                b.waits("tensor", [pkb_tok[pb_]])
                ttp = b.op("tensor", lambda E, f=f, pb_=pb_: E.transpose(
                    pkb[pb_][:], krope_b[:, f * 128:(f + 1) * 128], idb[:]), inc=True, nop=False)
                b.wait("vector", ttp)
                tevt = b.op("vector", lambda E, f=f, blk=blk, pb_=pb_: E.tensor_copy(
                    out=kt[:, f * XKV + blk * 128:f * XKV + (blk + 1) * 128],
                    in_=pkb[pb_][:]), inc=f"PKB{pb_}")
                pkb_tok[pb_] = tevt
                last_use = tevt
        ntt = w // 128
        for tt in range(ntt):
            j = pidx % 3
            pidx += 1
            pwait(j, "tensor")
            tmm = None
            for dmc in range(16):
                tmm = b.op("tensor", lambda E, j=j, dmc=dmc, tt=tt, buf=buf: E.matmul(
                    pp[j][:, 0:512],
                    buf[:, dmc * 512 + tt * 128:dmc * 512 + (tt + 1) * 128],
                    wvb[:, dmc * 512:(dmc + 1) * 512],
                    start=(dmc == 0), stop=(dmc == 15)), nop=False,
                    inc=(True if dmc == 15 else None))
            blk = t0 // 128 + tt
            b.wait("scalar", tmm)
            tev = b.op("scalar", lambda E, j=j, blk=blk: E.copy(
                out=vt[:, blk * 512:(blk + 1) * 512], in_=pp[j][:, 0:512]), inc=f"PP{j}")
            pptok[j] = tev
            last_use = tev
        slice_done[si] = last_use

    if stage == "p1":
        b.barrier(ENGS)
        tl1 = b.dma("gpsimd", lambda E: E.dma_start(out=y_out[0:128, :], in_=kt[:, 0:2048]), ch="DBG")
        tl2 = b.dma("gpsimd", lambda E: E.dma_start(out=y_out[128:256, :], in_=vt[:, 0:2048]), ch="DBG")
        b.wait("sync", tl2)
        finish(nc, b)
        return nc

    pa0.close()
    wqb = SB(pa, "wqb", [128, 16 * 2048], BF16)
    b.barrier(("gpsimd",))
    for h in range(4):
        b.dma("gpsimd", lambda E, h=h: E.dma_start(
            out=wqb[:, h * 4 * 2048:(h + 1) * 4 * 2048].rearrange("p (c f) -> p c f", c=4),
            in_=wq_in[h * 512:(h + 1) * 512, :].rearrange("(c p) f -> p c f", p=128)),
            ch="WGU")
    b.barrier(("sync", "vector", "scalar", "tensor"))

    qst_tok = None
    p2_slfree = {}
    for si in range(4):
        t0 = 768 + si * 512
        buf = xsl[si % 2]
        b.waits("scalar", [p2_slfree.get(si % 2)])
        ttr = None
        for dmc in range(16):
            ttr = b.dma("scalar", lambda E, t0=t0, dmc=dmc, buf=buf: E.dma_start_transpose(
                out=buf[:, dmc * 512:(dmc + 1) * 512],
                in_=xkv_bf[t0:t0 + 512, dmc * 128:(dmc + 1) * 128]),
                ch="TRXa" if si % 2 == 0 else "TRXb")
        b.wait("tensor", ttr)
        pkb_tok = [None, None]
        for tt in range(4):
            tile = si * 4 + tt
            evtoks = []
            for nq in range(4):
                j = pidx % 3
                pidx += 1
                pwait(j, "tensor")
                tmm = None
                for dmc in range(16):
                    tmm = b.op("tensor", lambda E, j=j, dmc=dmc, nq=nq, tt=tt, buf=buf: E.matmul(
                        pp[j][:, 0:512],
                        buf[:, dmc * 512 + tt * 128:dmc * 512 + (tt + 1) * 128],
                        wqb[:, dmc * 2048 + nq * 512:dmc * 2048 + (nq + 1) * 512],
                        start=(dmc == 0), stop=(dmc == 15)), nop=False,
                        inc=(True if dmc == 15 else None))
                b.wait("vector", tmm)
                tcp = b.op("vector", lambda E, j=j, nq=nq: E.tensor_copy(
                    out=qtm[:, nq * 512:(nq + 1) * 512], in_=pp[j][:, 0:512]), inc=f"PP{j}")
                pptok[j] = tcp
                evtoks.append(tcp)
            emit_rope_tm(qrope[:], qtm[:], cqtm[:, tile * 64:(tile + 1) * 64],
                         sqtm[:, tile * 64:(tile + 1) * 64], NH, rtmp2[:])
            trope = b.op("vector", lambda E: E.tensor_copy(out=qrope_b[:], in_=qrope[:]),
                         inc="RP")
            b.waits("tensor", [trope])
            b.waits("vector", [qst_tok])
            for f in range(16):
                pb_ = f % 2
                b.waits("tensor", [pkb_tok[pb_]])
                ttp = b.op("tensor", lambda E, f=f, pb_=pb_: E.transpose(
                    pkb[pb_][:], qrope_b[:, f * 128:(f + 1) * 128], idb[:]), inc=True, nop=False)
                b.wait("vector", ttp)
                tevt = b.op("vector", lambda E, f=f, pb_=pb_: E.tensor_copy(
                    out=qtts[:, f * 128:(f + 1) * 128], in_=pkb[pb_][:]), inc=f"PKB{pb_}")
                pkb_tok[pb_] = tevt
            b.waits("sync", pkb_tok)
            qst_tok = b.dma("sync", lambda E, tile=tile: E.dma_start(
                out=qt_dram[tile].rearrange("f p c -> p f c"),
                in_=qtts[:].rearrange("p (f c) -> p f c", f=16)), ch="QST")
        p2_slfree[si % 2] = tmm

    b.barrier(ENGS)
    if stage == "p2":
        tl1 = None
        for i in range(4):
            tl1 = b.dma("gpsimd", lambda E, i=i: E.dma_start(
                out=y_out[i * 128:(i + 1) * 128, :].rearrange("p (f c) -> p f c", f=16),
                in_=qt_dram[i].rearrange("f p c -> p f c")), ch="DBG")
        b.wait("sync", tl1)
        finish(nc, b)
        return nc

    pa.close()

    pb = ExitStack()
    wob = SB(pb, "wob", [128, 16 * 2048], BF16)
    qt_t = SB(pb, "qt_t", [128, 16 * 128], BF16)
    mask_t = SB(pb, "mask_t", [128, 896], F32)
    x_t = SB(pb, "x_t", [128, D], F32)
    Pt = SB(pb, "Pt", [128, 896], BF16)
    PTs = SB(pb, "PTs", [128, 896], BF16)
    sm = SB(pb, "sm", [128, 896], F32)
    outT = SB(pb, "outT", [128, 16 * 128], BF16)
    xa_f = SB(pb, "xa_f", [128, D], F32)
    xab_t = SB(pb, "xab_t", [128, D], BF16)
    m8 = SB(pb, "m8", [128, 8], F32)
    zapt = SB(pb, "zapt", [128, NE], F32)
    Mm = SB(pb, "Mm", [128, NE], F32)
    ew = SB(pb, "ew", [128, NE], F32)
    Wd = SB(pb, "Wd", [128, NE], F32)
    Whi = SB(pb, "Whi", [128, NE], BF16)
    Wlo = SB(pb, "Wlo", [128, NE], F32)
    Wlob = SB(pb, "Wlob", [128, NE], BF16)
    rbT = SB(pb, "rbT", [32, 128], F32)
    cums = SB(pb, "cums", [32, 128], F32)
    zer32 = SB(pb, "zer32", [32, 128], F32)
    carry = SB(pb, "carry", [32, 1], F32)
    posT = SB(pb, "posT", [32, 128], F32)
    pos_tm = SB(pb, "pos_tm", [128, NE], F32)
    kva = SB(pb, "kva", [128, NE], F32)
    kvm = SB(pb, "kvm", [128, NE], F32)
    kv8 = SB(pb, "kv8", [128, 8], F32)
    e8 = SB(pb, "e8", [128, 8], F32)
    Em = SB(pb, "Em", [128, NE], F32)
    pos8 = SB(pb, "pos8", [128, 8], F32)
    slot8f = SB(pb, "slot8f", [128, 8], F32)
    ovf = SB(pb, "ovf", [128, 8], F32)
    logT_sb = SB(pb, "logT_sb", [32, 512], F32)
    payload = [SB(pb, f"pay{j}", [128, PAYW], BF16) for j in range(2)]
    S1 = PS(pb, "S1", [128, 512])
    S2 = PS(pb, "S2", [128, 384])
    PTp = PS(pb, "PTp", [128, 896], BF16)
    OHT = PS(pb, "OHT", [128, 128])
    misc1 = PS(pb, "misc1", [32, 512])
    misc2 = PS(pb, "misc2", [128, 32])

    b.op("vector", lambda E: E.memset(zer32[:], 0.0))
    b.op("vector", lambda E: E.memset(carry[:], 0.0))
    b.op("vector", lambda E: E.memset(payload[0][:, 2112:2176], 0.0))
    b.op("vector", lambda E: E.memset(payload[1][:, 2112:2176], 0.0))

    b.barrier(("gpsimd",))
    for h in range(4):
        b.dma("gpsimd", lambda E, h=h: E.dma_start(
            out=wob[:, h * 4 * 2048:(h + 1) * 4 * 2048].rearrange("p (c f) -> p c f", c=4),
            in_=wo_in[h * 512:(h + 1) * 512, :].rearrange("(c p) f -> p c f", p=128)),
            ch="WGU")
    b.barrier(("sync", "vector", "scalar", "tensor"))

    tile_tok = {}
    pay_tok = {}
    xab_store = {}
    xa_store = {}
    state = {
        "s_bank": None,
        "sm_free": None,
        "pt_bank": None,
        "pts_free": None,
        "pt_sb": None,
        "oht_bank": None,
        "outT_free": None,
        "y_bank": None,
        "xaf_free": None,
        "xab_free": None,
        "pay_free": {},
        "route_in": {},
        "rcol_guard": [None, None],
    }

    def emit_tile_loads(i, first=False):
        toks = []
        if not first:
            b.waits("sync", [state["y_bank"]])
        toks.append(b.dma("sync", lambda E, i=i: E.dma_start(
            out=qt_t[:].rearrange("p (f c) -> p f c", f=16),
            in_=qt_dram[i].rearrange("f p c -> p f c")), ch="TL"))
        toks.append(b.dma("sync", lambda E, i=i: E.dma_start(
            out=mask_t[:], in_=mask_in[i]), ch="TL"))
        toks.append(b.dma("sync", lambda E, i=i: E.dma_start(
            out=x_t[:], in_=xkv_in[768 + i * 128:768 + (i + 1) * 128, :]), ch="TL"))
        tile_tok[i] = toks

    def emit_gate_quarter(qtr):
        t0 = qtr * 512
        buf = xsl[0]
        b.waits("scalar", [xab_store[qtr * 4 + j] for j in range(4)]
                + [state.get("gate_sl_free")])
        ttr = None
        for dmc in range(16):
            ttr = b.dma("scalar", lambda E, t0=t0, dmc=dmc, buf=buf: E.dma_start_transpose(
                out=buf[:, dmc * 512:(dmc + 1) * 512],
                in_=xab_dram[t0:t0 + 512, dmc * 128:(dmc + 1) * 128]), ch="TRXG")
        b.waits("tensor", [ttr])
        tgm = None
        for dmc in range(16):
            tgm = b.op("tensor", lambda E, dmc=dmc, buf=buf: E.matmul(
                misc1[:, 0:512], gateb[:, dmc * NE:(dmc + 1) * NE],
                buf[:, dmc * 512:(dmc + 1) * 512],
                start=(dmc == 0), stop=(dmc == 15)), nop=False,
                inc=(True if dmc == 15 else None))
        state["gate_sl_free"] = tgm
        b.wait("vector", tgm)
        tge = b.op("vector", lambda E: E.tensor_copy(out=logT_sb[:], in_=misc1[:, 0:512]),
                   inc="GLG")
        tt2 = None
        for jj in range(4):
            i = qtr * 4 + jj
            b.waits("tensor", [tge, tt2])
            tt1 = b.op("tensor", lambda E, jj=jj: E.transpose(
                misc2[:], logT_sb[:, jj * 128:(jj + 1) * 128], idf[0:32, 0:32]),
                inc=True, nop=False)
            b.wait("vector", tt1)
            tt2 = b.op("vector", lambda E, i=i: E.tensor_copy(
                out=logits[:, i * NE:(i + 1) * NE], in_=misc2[:]), inc="GL2")
            state["route_in"][i] = tt2

    def emit_routing_tile(i):
        if rparts < 2:
            pay_tok[i] = None
            return
        lg = logits[:, i * NE:(i + 1) * NE]
        b.waits("vector", [state["route_in"][i]])
        b.op("vector", lambda E, lg=lg: E.tensor_copy(out=zapt[:], in_=lg))
        b.op("vector", lambda E: E.memset(Mm[:], 0.0))
        for kk in range(8):
            b.fop("vector", lambda E, kk=kk: E.reduce_max(m8[:, kk:kk + 1], zapt[:], axis=AX))
            b.fop("vector", lambda E, kk=kk: E.tensor_scalar(
                ew[:], zapt[:], m8[:, kk:kk + 1], None, op0=OP.is_equal))
            b.fop("vector", lambda E: E.tensor_tensor(Mm[:], Mm[:], ew[:], op=OP.add))
            b.fop("vector", lambda E: E.scalar_tensor_tensor(
                zapt[:], in0=ew[:], scalar=NEG, in1=zapt[:], op0=OP.mult, op1=OP.add))
        if i == 1:
            b.op("vector", lambda E, lg=lg: E.tensor_copy(out=rdbg[3][:], in_=lg))
        tneg = b.op("vector", lambda E: E.tensor_scalar(
            small[:, 14:15], m8[:, 0:1], -1.0, None, op0=OP.mult), inc="RNG")
        b.wait("scalar", tneg)
        texp = b.op("scalar", lambda E, lg=lg: E.activation(
            out=ew[:], in_=lg, func=AF.Exp, bias=small[:, 14:15]), inc="RXP")
        b.wait("vector", texp)
        b.fop("vector", lambda E: E.tensor_tensor(Wd[:], ew[:], Mm[:], op=OP.mult))
        b.fop("vector", lambda E: E.reduce_sum(small[:, 15:16], Wd[:], axis=AX))
        b.fop("vector", lambda E: E.reciprocal(small[:, 14:15], small[:, 15:16]))
        b.fop("vector", lambda E: E.tensor_scalar(
            Wd[:], Wd[:], small[:, 14:15], None, op0=OP.mult))
        b.fop("vector", lambda E: E.tensor_copy(out=Whi[:], in_=Wd[:]))
        b.fop("vector", lambda E: E.tensor_copy(out=ew[:], in_=Whi[:]))
        b.fop("vector", lambda E: E.tensor_tensor(Wlo[:], Wd[:], ew[:], op=OP.subtract))
        twv = b.fop("vector", lambda E: E.tensor_copy(out=Wlob[:], in_=Wlo[:]), inc="RWV")
        b.waits("tensor", [twv])
        tmt = b.op("tensor", lambda E: E.transpose(misc1[:, 0:128], Mm[:], idf[:]),
                   inc=True, nop=False)
        b.wait("vector", tmt)
        trb = b.fop("vector", lambda E: E.tensor_copy(out=rbT[:], in_=misc1[0:32, 0:128]),
                    inc="RCA")
        state["misc1_free"] = trb
        b.fop("vector", lambda E: E.tensor_tensor_scan(
            cums[:], data0=rbT[:], data1=zer32[:], initial=0.0,
            op0=OP.add, op1=OP.add))
        b.op("vector", lambda E: E.tensor_scalar(
            cums[:], cums[:], carry[:, 0:1], None, op0=OP.add))
        b.op("vector", lambda E: E.tensor_tensor(posT[:], cums[:], rbT[:], op=OP.subtract))
        if i == 1:
            b.op("vector", lambda E: E.tensor_copy(out=rdbg[0][:], in_=rbT[:]))
            b.op("vector", lambda E: E.tensor_copy(out=rdbg[1][:], in_=cums[:]))
        tcar = b.fop("vector", lambda E: E.tensor_copy(out=carry[:], in_=cums[:, 127:128]),
                     inc="RCA")
        b.wait("tensor", tcar)
        tpt = b.op("tensor", lambda E: E.transpose(misc2[:], posT[:], idf[0:32, 0:32]),
                   inc=True, nop=False)
        b.wait("vector", tpt)
        b.op("vector", lambda E: E.tensor_copy(out=pos_tm[:], in_=misc2[:]))
        if i == 1:
            b.op("vector", lambda E: E.tensor_copy(out=rdbg[2][:], in_=pos_tm[:]))
        b.op("vector", lambda E: E.tensor_tensor(kva[:], iota2048[:], pos_tm[:], op=OP.add))
        b.op("vector", lambda E: E.tensor_scalar(kva[:], kva[:], 1.0, None, op0=OP.add))
        b.op("vector", lambda E: E.tensor_tensor(kvm[:], kva[:], Mm[:], op=OP.mult))
        b.fop("vector", lambda E: E.tensor_scalar(kvm[:], kvm[:], 1.0, None, op0=OP.subtract))
        b.fop("vector", lambda E: E.max(out=kv8[:], in_=kvm[:]))
        b.op("vector", lambda E: E.tensor_scalar(Em[:], iota32[:], 1.0, None, op0=OP.add))
        b.op("vector", lambda E: E.tensor_tensor(Em[:], Em[:], Mm[:], op=OP.mult))
        b.fop("vector", lambda E: E.tensor_scalar(Em[:], Em[:], 1.0, None, op0=OP.subtract))
        b.fop("vector", lambda E: E.max(out=e8[:], in_=Em[:]))
        b.op("vector", lambda E: E.scalar_tensor_tensor(
            pos8[:], in0=e8[:], scalar=-2048.0, in1=kv8[:], op0=OP.mult, op1=OP.add))
        if i == 1:
            b.op("vector", lambda E: E.tensor_copy(out=rdbg[4][:], in_=kv8[:]))
            b.op("vector", lambda E: E.tensor_copy(out=rdbg[5][:], in_=e8[:]))
        b.op("vector", lambda E: E.scalar_tensor_tensor(
            slot8f[:], in0=e8[:], scalar=640.0, in1=pos8[:], op0=OP.mult, op1=OP.add))
        b.op("vector", lambda E: E.tensor_scalar(ovf[:], pos8[:], 639.5, None, op0=OP.is_gt))
        b.op("vector", lambda E: E.tensor_scalar(kv8[:], slot8f[:], float(ROWS), None,
                                                 op0=OP.subtract))
        b.op("vector", lambda E: E.tensor_tensor(kv8[:], kv8[:], ovf[:], op=OP.mult))
        b.op("vector", lambda E: E.tensor_tensor(slot8f[:], slot8f[:], kv8[:], op=OP.subtract))
        b.op("vector", lambda E: E.tensor_scalar(ovf[:], slot8f[:], 0.0, None, op0=OP.is_lt))
        b.op("vector", lambda E: E.tensor_scalar(kv8[:], slot8f[:], float(ROWS), None,
                                                 op0=OP.subtract))
        b.op("vector", lambda E: E.tensor_tensor(kv8[:], kv8[:], ovf[:], op=OP.mult))
        b.op("vector", lambda E: E.tensor_tensor(slot8f[:], slot8f[:], kv8[:], op=OP.subtract))
        tsl8 = b.op("vector", lambda E, i=i: E.tensor_copy(
            out=slot8i[:, i * 8:(i + 1) * 8], in_=slot8f[:]), inc="RS8")
        if rparts < 3:
            pay_tok[i] = None
            return
        bi = i % 2
        prevp = state["pay_free"].get(bi)
        if prevp is not None:
            b.waits("sync", [prevp])
        tpl = b.dma("sync", lambda E, i=i, bi=bi: E.dma_start(
            out=payload[bi][:, 0:2048], in_=xab_dram[i * 128:(i + 1) * 128, :]),
            ch="PLDa" if bi == 0 else "PLDb")
        b.waits("vector", [tpl] + ([prevp] if prevp else []))
        b.op("vector", lambda E, bi=bi: E.tensor_copy(
            out=payload[bi][:, 2048:2112].rearrange("p (c two) -> p c two", two=2)[:, :, 0],
            in_=Whi[:]))
        tpw = b.op("vector", lambda E, bi=bi: E.tensor_copy(
            out=payload[bi][:, 2048:2112].rearrange("p (c two) -> p c two", two=2)[:, :, 1],
            in_=Wlob[:]), inc="RPW")
        if rparts < 4:
            pay_tok[i] = None
            return
        b.waits("gpsimd", [tpw, tsl8])
        tsc = None
        for k in range(8):
            tsc = b.dma("gpsimd", lambda E, i=i, k=k, bi=bi: E.indirect_dma_start(
                out=xsorted[:, :],
                out_offset=IndirectOffsetOnAxis(slot8i[:, i * 8 + k:i * 8 + k + 1], 0),
                in_=payload[bi][:, :], in_offset=None),
                ch="SCTa" if bi == 0 else "SCTb")
        pay_tok[i] = tsc
        state["pay_free"][bi] = tsc

    emit_tile_loads(0, first=True)
    for i in range(NT):
        b.waits("tensor", tile_tok[i])
        head_BC = {}

        def emit_A(h, i=None):
            kvh = h // 4
            kb = kvh * XKV
            b.waits("tensor", [state["s_bank"]])
            b.op("tensor", lambda E, h=h, kb=kb: E.matmul(
                S1[:, 0:256], qt_t[:, h * 128:(h + 1) * 128],
                kt[:, kb:kb + 256], start=True, stop=True), nop=False)
            b.op("tensor", lambda E, h=h, kb=kb, i=i: E.matmul(
                S1[:, 256:512], qt_t[:, h * 128:(h + 1) * 128],
                kt[:, kb + 256 + 128 * i:kb + 512 + 128 * i], start=True, stop=True), nop=False)
            tsc_ = b.op("tensor", lambda E, h=h, kb=kb, i=i: E.matmul(
                S2[:, 0:384], qt_t[:, h * 128:(h + 1) * 128],
                kt[:, kb + 512 + 128 * i:kb + 896 + 128 * i], start=True, stop=True),
                nop=False, inc=True)
            b.wait("vector", tsc_)
            b.op("vector", lambda E: E.reduce_max(small[:, 0:1], S1[:], axis=AX))
            b.op("vector", lambda E: E.reduce_max(small[:, 1:2], S2[:], axis=AX))
            b.op("vector", lambda E: E.tensor_tensor(small[:, 2:3], small[:, 0:1],
                                                     small[:, 1:2], op=OP.max))
            b.waits("vector", [state["sm_free"]])
            b.op("vector", lambda E: E.scalar_tensor_tensor(
                sm[:, 0:512], in0=S1[:], scalar=small[:, 2:3], in1=mask_t[:, 0:512],
                op0=OP.subtract, op1=OP.add))
            tmsk = b.op("vector", lambda E: E.scalar_tensor_tensor(
                sm[:, 512:896], in0=S2[:], scalar=small[:, 2:3], in1=mask_t[:, 512:896],
                op0=OP.subtract, op1=OP.add), inc="SMK")
            state["s_bank"] = tmsk
            return tmsk

        def emit_B(h, tmsk):
            rcol = 16 + (h % 2)
            b.waits("scalar", [tmsk, state["pt_sb"]])
            b.op("scalar", lambda E: E.activation(
                out=Pt[:, 0:512], in_=sm[:, 0:512], func=AF.Exp, accum_out=small[:, 4:5]))
            texp = b.op("scalar", lambda E: E.activation(
                out=Pt[:, 512:896], in_=sm[:, 512:896], func=AF.Exp,
                accum_out=small[:, 5:6]), inc="EXP")
            state["sm_free"] = texp
            b.wait("vector", texp)
            b.op("vector", lambda E: E.tensor_tensor(small[:, 6:7], small[:, 4:5],
                                                     small[:, 5:6], op=OP.add))
            b.waits("vector", [state["rcol_guard"][h % 2]])
            trec = b.op("vector", lambda E, rcol=rcol: E.reciprocal(
                small[:, rcol:rcol + 1], small[:, 6:7]), inc="PNM")
            tn1 = b.op("vector", lambda E, rcol=rcol: E.tensor_scalar(
                Pt[:, 0:512], Pt[:, 0:512], small[:, rcol:rcol + 1], None,
                op0=OP.mult), inc="PNM")
            b.waits("scalar", [trec])
            tn2 = b.op("scalar", lambda E, rcol=rcol: E.activation(
                out=Pt[:, 512:896], in_=Pt[:, 512:896], func=AF.Copy,
                scale=small[:, rcol:rcol + 1]), inc="EXP")
            state["rcol_guard"][h % 2] = tn2
            head_BC[h] = (tn1, tn2)

        def emit_C(h, i=None):
            tn1, tn2 = head_BC.pop(h)
            kvh = h // 4
            b.waits("tensor", [tn1, tn2, state["pt_bank"]])
            ttr = None
            for jblk in range(7):
                ttr = b.op("tensor", lambda E, jblk=jblk: E.transpose(
                    PTp[:, jblk * 128:(jblk + 1) * 128], Pt[:, jblk * 128:(jblk + 1) * 128],
                    idb[:]), nop=False, inc=(True if jblk == 6 else None))
            state["pt_sb"] = ttr
            b.waits("scalar", [ttr, state["pts_free"]])
            tevp = b.op("scalar", lambda E: E.copy(out=PTs[:], in_=PTp[:]), inc="PTE")
            state["pt_bank"] = tevp
            b.waits("tensor", [tevp, state["oht_bank"]])
            tav = None
            for jblk in range(7):
                vblk = jblk if jblk < 2 else (i + jblk)
                tav = b.op("tensor", lambda E, jblk=jblk, vblk=vblk, kvh=kvh: E.matmul(
                    OHT[:], vt[:, vblk * 512 + kvh * 128:vblk * 512 + (kvh + 1) * 128],
                    PTs[:, jblk * 128:(jblk + 1) * 128],
                    start=(jblk == 0), stop=(jblk == 6)), nop=False,
                    inc=(True if jblk == 6 else None))
            state["pts_free"] = tav
            b.waits("scalar", [tav, state["outT_free"]])
            toh = b.op("scalar", lambda E, h=h: E.copy(
                out=outT[:, h * 128:(h + 1) * 128], in_=OHT[:]), inc="OHE")
            state["oht_bank"] = toh
            if dbg and i == 0 and h == 0:
                b.wait("gpsimd", toh)
                b.dma("gpsimd", lambda E: E.dma_start(out=dbg_out[1][:, 0:896], in_=Pt[:]), ch="DBG")
                b.dma("gpsimd", lambda E: E.dma_start(out=dbg_out[2][:, 0:20], in_=small[:, 0:20]), ch="DBG")
                b.dma("gpsimd", lambda E: E.dma_start(out=dbg_out[3][:, 0:896], in_=PTs[:]), ch="DBG")
                b.dma("gpsimd", lambda E: E.dma_start(out=dbg_out[4][:, 0:128], in_=outT[:, 0:128]), ch="DBG")
            return toh

        prev_h = None
        for h in range(NH):
            tmsk_h = emit_A(h, i=i)
            if prev_h is not None:
                emit_C(prev_h, i=i)
            emit_B(h, tmsk_h)
            prev_h = h
        toh = emit_C(prev_h, i=i)
        b.waits("tensor", [toh])
        b.waits("vector", [state["xaf_free"]])
        ybank_tok = state.get("ybank_tok", [None, None])
        tyev = None
        for nc4 in range(4):
            yb = nc4 % 2
            b.waits("tensor", [ybank_tok[yb]])
            two = None
            for f in range(16):
                two = b.op("tensor", lambda E, nc4=nc4, f=f: E.matmul(
                    Yp[nc4 % 2][:],
                    outT[:, f * 128:(f + 1) * 128],
                    wob[:, f * 2048 + nc4 * 512:f * 2048 + (nc4 + 1) * 512],
                    start=(f == 0), stop=(f == 15)), nop=False,
                    inc=(True if f == 15 else None))
            b.waits("vector", [two])
            tyev = b.op("vector", lambda E, nc4=nc4, yb=yb: E.scalar_tensor_tensor(
                x1[:, nc4 * 512:(nc4 + 1) * 512], in0=Yp[yb][:], scalar=0.0,
                in1=x_t[:, nc4 * 512:(nc4 + 1) * 512],
                op0=OP.add, op1=OP.add), inc=f"YB{yb}")
            ybank_tok[yb] = tyev
        state["ybank_tok"] = ybank_tok
        state["outT_free"] = two
        b.fop("vector", lambda E: E.reduce_sum(small[:, 8:9], x1[:], axis=AX))
        b.op("vector", lambda E: E.tensor_scalar(
            small[:, 10:11], small[:, 8:9], 1.0 / D, None, op0=OP.mult))
        b.wait("scalar", tyev)
        tsq = b.op("scalar", lambda E: E.activation(
            out=x_t[:], in_=x1[:], func=AF.Square, accum_out=small[:, 9:10]), inc="LSQ")
        state["y_bank"] = tsq
        if dbg and i == 0:
            b.wait("gpsimd", tsq)
            b.dma("gpsimd", lambda E: E.dma_start(out=dbg_out[5][:, :], in_=outT[:]), ch="DBG")
            b.dma("gpsimd", lambda E: E.dma_start(out=dbg_out[6][:, :], in_=x1[:]), ch="DBG")
        b.wait("vector", tsq)
        b.op("vector", lambda E: E.tensor_scalar(
            small[:, 11:12], small[:, 9:10], 1.0 / D, None, op0=OP.mult))
        b.op("vector", lambda E: E.tensor_tensor(
            small[:, 9:10], small[:, 10:11], small[:, 10:11], op=OP.mult))
        b.op("vector", lambda E: E.tensor_tensor(
            small[:, 11:12], small[:, 11:12], small[:, 9:10], op=OP.subtract))
        b.op("vector", lambda E: E.tensor_scalar(
            small[:, 11:12], small[:, 11:12], EPS, None, op0=OP.add))
        trec = b.fop("vector", lambda E: E.reciprocal(small[:, 12:13], small[:, 11:12]),
                     inc="LRC")
        b.wait("scalar", trec)
        tst_ = b.op("scalar", lambda E: E.activation(
            out=small[:, 13:14], in_=small[:, 12:13], func=AF.Sqrt), inc="LSR")
        b.wait("vector", tst_)
        txa = b.op("vector", lambda E: E.tensor_scalar(
            xa_f[:], x1[:], small[:, 10:11], small[:, 13:14],
            op0=OP.subtract, op1=OP.mult), inc="LXA")
        b.waits("scalar", [txa, state["xab_free"]])
        txab = b.op("scalar", lambda E: E.copy(out=xab_t[:], in_=xa_f[:]), inc="LXB")
        b.waits("sync", [txa, txab])
        xch = "XSTa" if (i // 4) % 2 == 0 else "XSTb"
        xa_store[i] = b.dma("sync", lambda E, i=i: E.dma_start(
            out=xa_dram[i * 128:(i + 1) * 128, :], in_=xa_f[:]), ch=xch)
        xab_store[i] = b.dma("sync", lambda E, i=i: E.dma_start(
            out=xab_dram[i * 128:(i + 1) * 128, :], in_=xab_t[:]), ch=xch)
        state["xaf_free"] = xa_store[i]
        state["xab_free"] = xab_store[i]
        if i + 1 < NT:
            emit_tile_loads(i + 1)

        if stage in ("full", "route", "p4") and i % 4 == 3:
            emit_gate_quarter(i // 4)
            for jj in range(4):
                emit_routing_tile(i - 3 + jj)

    if stage == "route":
        slotdump = dt("slotdump", [128, 128], I32, kind="ExternalOutput").ap()
        rdump = dt("rdump", [6, 128, 128], F32, kind="ExternalOutput").ap()
        b.waits("gpsimd", [b.last["vector"]])
        b.dma("gpsimd", lambda E: E.dma_start(out=rdump[0][0:32, :], in_=rdbg[0][:]), ch="DBG")
        b.dma("gpsimd", lambda E: E.dma_start(out=rdump[1][0:32, :], in_=rdbg[1][:]), ch="DBG")
        b.dma("gpsimd", lambda E: E.dma_start(out=rdump[2][:, 0:32], in_=rdbg[2][:]), ch="DBG")
        b.dma("gpsimd", lambda E: E.dma_start(out=rdump[3][:, 0:32], in_=rdbg[3][:]), ch="DBG")
        b.dma("gpsimd", lambda E: E.dma_start(out=rdump[4][:, 0:8], in_=rdbg[4][:]), ch="DBG")
        b.dma("gpsimd", lambda E: E.dma_start(out=rdump[5][:, 0:8], in_=rdbg[5][:]), ch="DBG")
        b.waits("gpsimd", [t for t in (pay_tok[i] for i in range(NT)) if t is not None])
        b.barrier(ENGS)
        tlast = b.dma("gpsimd", lambda E: E.dma_start(out=slotdump[:], in_=slot8i[:]), ch="DBG")
        for i in range(NT):
            tlast = b.dma("gpsimd", lambda E, i=i: E.dma_start(
                out=y_out[i * 128:(i + 1) * 128, :],
                in_=xa_dram[i * 128:(i + 1) * 128, :]), ch="DBG")
        b.wait("sync", tlast)
        finish(nc, b)
        pb.close()
        pab.close()
        top.close()
        return nc

    if stage == "attn":
        b.barrier(ENGS)
        tlast = None
        for i in range(NT):
            tlast = b.dma("gpsimd", lambda E, i=i: E.dma_start(
                out=y_out[i * 128:(i + 1) * 128, :],
                in_=xa_dram[i * 128:(i + 1) * 128, :]), ch="DBG")
        b.wait("sync", tlast)
        finish(nc, b)
        pb.close()
        pab.close()
        top.close()
        return nc

    b.barrier(ENGS)
    pb.close()
    pab.close()

    pc = ExitStack()
    wgb = SB(pc, "wgb", [128, 2 * 16 * INTER], BF16)
    WUB = 16 * INTER
    wdb = SB(pc, "wdb", [128, 6 * 2048], BF16)
    xgt = [SB(pc, f"xgt{j}", [128, 16 * CAP], BF16) for j in range(2)]
    xgrow = [SB(pc, f"xgrow{j}", [128, D], BF16) for j in range(2)]
    TRp = [PS(pc, f"TRp{j}", [128, 512], BF16) for j in range(2)]
    wc_b = [SB(pc, f"wc_b{j}", [128, 10], BF16) for j in range(2)]
    wcol = [SB(pc, f"wcol{j}", [128, 5], F32) for j in range(2)]
    hT = SB(pc, "hT", [128, 6 * CAP], BF16)
    hs = SB(pc, "hs", [128, CAP], F32)
    ysb = [SB(pc, f"ysb{j}", [128, D], BF16) for j in range(2)]
    Gp = PS(pc, "Gp", [128, CAP])
    Up = PS(pc, "Up", [128, CAP])

    wg_ready = [None] * NE
    wd_ready = [None] * NE
    ffn1_done = [None] * NE
    ffn2_done = [None] * NE
    xgt_ready = [None] * NE
    wcol_ok = [None] * NE
    ysb_tok = [None, None]
    sil_tok = None
    hmu_tok = None

    def emit_weight_load(e, kind):
        toks = []
        if kind == "gu":
            b.waits("gpsimd", [ffn1_done[e - 1]] if e > 0 else [])
            for src, base in ((wg_in, 0), (wu_in, WUB)):
                for h in range(2):
                    toks.append(b.dma("gpsimd", lambda E, src=src, base=base, h=h, e=e:
                        E.dma_start(
                            out=wgb[:, base + h * 8 * INTER:base + (h + 1) * 8 * INTER]
                                .rearrange("p (c f) -> p c f", c=8),
                            in_=src[e, h * 1024:(h + 1) * 1024, :]
                                .rearrange("(c p) f -> p c f", p=128)), ch="WGU"))
            wg_ready[e] = toks
        else:
            b.waits("gpsimd", [ffn2_done[e - 1]] if e > 0 else [])
            for h in range(2):
                toks.append(b.dma("gpsimd", lambda E, h=h, e=e: E.dma_start(
                    out=wdb[:, h * 3 * 2048:(h + 1) * 3 * 2048]
                        .rearrange("p (c f) -> p c f", c=3),
                    in_=wd_in[e, h * 384:(h + 1) * 384, :]
                        .rearrange("(c p) f -> p c f", p=128)), ch="WDD"))
            wd_ready[e] = toks

    xg_state = {"rowfree": [None, None], "trpfree": [None, None]}

    def emit_xgt_load(e):
        bj = e % 2
        tev = None
        for rc in range(5):
            rcj = rc % 2
            ldeng = "sync" if rcj == 0 else "scalar"
            b.waits(ldeng, [xg_state["rowfree"][rcj]])
            tld = b.dma(ldeng, lambda E, e=e, rc=rc, rcj=rcj: E.dma_start(
                out=xgrow[rcj][:],
                in_=xsorted[e * CAP + rc * 128:e * CAP + (rc + 1) * 128, 0:D]),
                ch="TRX4a" if rcj == 0 else "TRX4b")
            ttr = None
            for g in range(4):
                tb = (rc * 4 + g) % 2
                b.waits("tensor", [tld, xg_state["trpfree"][tb]])
                for q in range(4):
                    ttr = b.op("tensor", lambda E, g=g, q=q, tb=tb, rcj=rcj: E.transpose(
                        TRp[tb][:, q * 128:(q + 1) * 128],
                        xgrow[rcj][:, (g * 4 + q) * 128:(g * 4 + q + 1) * 128],
                        idb[:]), nop=False, inc=(True if q == 3 else None))
                b.waits("vector", [ttr] + (
                    [ffn1_done[e - 2]] if (e >= 2 and rc == 0 and g == 0) else []))
                tev = b.op("vector", lambda E, g=g, rc=rc, tb=tb, bj=bj: E.tensor_copy(
                    out=xgt[bj][:].rearrange("p (c t) -> p c t", c=16)
                        [:, g * 4:(g + 1) * 4, rc * 128:(rc + 1) * 128],
                    in_=TRp[tb][:].rearrange("p (c t) -> p c t", c=4)), inc="XEV")
                xg_state["trpfree"][tb] = tev
            xg_state["rowfree"][rcj] = ttr
        xgt_ready[e] = tev
        tw = []
        for tc5 in range(5):
            tw.append(b.dma("scalar", lambda E, e=e, tc5=tc5, bj=bj: E.dma_start(
                out=wc_b[bj][:, tc5 * 2:(tc5 + 1) * 2],
                in_=xsorted[e * CAP + tc5 * 128:e * CAP + (tc5 + 1) * 128,
                            WCOL + 2 * e:WCOL + 2 * e + 2]), ch="MISC"))
        b.waits("vector", tw + ([ffn2_done[e - 2]] if e >= 2 else []))
        tv = None
        for tc5 in range(5):
            tv = b.op("vector", lambda E, tc5=tc5, bj=bj: E.tensor_tensor(
                wcol[bj][:, tc5:tc5 + 1], wc_b[bj][:, tc5 * 2:tc5 * 2 + 1],
                wc_b[bj][:, tc5 * 2 + 1:tc5 * 2 + 2], op=OP.add),
                inc=("WCL" if tc5 == 4 else None))
        wcol_ok[e] = tv

    scat_all = [pay_tok[i] for i in range(NT)]
    b.waits("scalar", scat_all)
    b.waits("sync", scat_all)
    b.waits("gpsimd", scat_all)
    emit_weight_load(0, "gu")
    emit_xgt_load(0)
    emit_weight_load(0, "d")

    for e in range(NE):
        bj = e % 2
        b.waits("tensor", wg_ready[e] + [xgt_ready[e]])
        for ic in range(6):
            b.waits("tensor", [sil_tok])
            tgm = None
            for ncc, w in ((0, 512), (512, 128)):
                for dmc in range(16):
                    tgm = b.op("tensor", lambda E, ic=ic, ncc=ncc, w=w, dmc=dmc, bj=bj: E.matmul(
                        Gp[:, ncc:ncc + w],
                        wgb[:, dmc * INTER + ic * 128:dmc * INTER + (ic + 1) * 128],
                        xgt[bj][:, dmc * CAP + ncc:dmc * CAP + ncc + w],
                        start=(dmc == 0), stop=(dmc == 15)), nop=False,
                        inc=(True if (ncc, dmc) == (512, 15) else None))
            b.waits("tensor", [hmu_tok])
            tum = None
            for ncc, w in ((0, 512), (512, 128)):
                for dmc in range(16):
                    tum = b.op("tensor", lambda E, ic=ic, ncc=ncc, w=w, dmc=dmc, bj=bj: E.matmul(
                        Up[:, ncc:ncc + w],
                        wgb[:, WUB + dmc * INTER + ic * 128:WUB + dmc * INTER + (ic + 1) * 128],
                        xgt[bj][:, dmc * CAP + ncc:dmc * CAP + ncc + w],
                        start=(dmc == 0), stop=(dmc == 15)), nop=False,
                        inc=(True if (ncc, dmc) == (512, 15) else None))
            b.waits("scalar", [tgm, hmu_tok])
            tsl = b.op("scalar", lambda E: E.activation(
                out=hs[:], in_=Gp[:], func=AF.Silu), inc="SIL")
            sil_tok = tsl
            b.waits("vector", [tsl, tum])
            tmu = b.op("vector", lambda E, ic=ic: E.tensor_tensor(
                hT[:, ic * CAP:(ic + 1) * CAP], hs[:], Up[:], op=OP.mult), inc="HMU")
            hmu_tok = tmu
        ffn1_done[e] = hmu_tok
        if e + 1 < NE:
            emit_weight_load(e + 1, "gu")
            emit_xgt_load(e + 1)
        b.waits("tensor", wd_ready[e] + [ffn1_done[e], wcol_ok[e]])
        tym_last = None
        ychunk_tok = [None, None]
        for tc5 in range(5):
            yb2 = tc5 % 2
            b.waits("tensor", [ysb_tok[yb2]])
            tyv = None
            for nc4 in range(4):
                yq = nc4 % 2
                b.waits("tensor", [ychunk_tok[yq]])
                tym = None
                for ic in range(6):
                    tym = b.op("tensor", lambda E, nc4=nc4, ic=ic, tc5=tc5: E.matmul(
                        Yp[nc4 % 2][:],
                        hT[:, ic * CAP + tc5 * 128:ic * CAP + (tc5 + 1) * 128],
                        wdb[:, ic * 2048 + nc4 * 512:ic * 2048 + (nc4 + 1) * 512],
                        start=(ic == 0), stop=(ic == 5)), nop=False,
                        inc=(True if ic == 5 else None))
                tym_last = tym
                b.wait("vector", tym)
                tyv = b.op("vector", lambda E, yb2=yb2, yq=yq, nc4=nc4, tc5=tc5, bj=bj: E.tensor_scalar(
                    ysb[yb2][:, nc4 * 512:(nc4 + 1) * 512], Yp[yq][:],
                    wcol[bj][:, tc5:tc5 + 1], None, op0=OP.mult), inc=f"YC{yq}")
                ychunk_tok[yq] = tyv
            b.wait("sync", tyv)
            tst2 = b.dma("sync", lambda E, e=e, tc5=tc5, yb2=yb2: E.dma_start(
                out=ysorted[e * CAP + tc5 * 128:e * CAP + (tc5 + 1) * 128, :],
                in_=ysb[yb2][:]), ch="YSTa" if yb2 == 0 else "YSTb")
            ysb_tok[yb2] = tst2
        ffn2_done[e] = tym_last
        if e + 1 < NE:
            emit_weight_load(e + 1, "d")

    b.barrier(ENGS)
    if stage == "p4":
        tlast = None
        for i in range(NT):
            tlast = b.dma("gpsimd", lambda E, i=i: E.dma_start(
                out=y_out[i * 128:(i + 1) * 128, :].rearrange("p (h c) -> p h c", h=1),
                in_=ysorted[i * 128:(i + 1) * 128, :].rearrange("p (h c) -> p h c", h=1)), ch="DBG")
        b.wait("sync", tlast)
        finish(nc, b)
        pc.close()
        top.close()
        return nc
    pc.close()

    pd = ExitStack()
    gath = [SB(pd, f"gath{j}", [128, D], BF16) for j in range(8)]
    moe = SB(pd, "moe", [128, D], F32)
    t01 = SB(pd, "t01", [128, D], F32)
    t23 = SB(pd, "t23", [128, D], F32)
    t45 = SB(pd, "t45", [128, D], F32)
    t67 = SB(pd, "t67", [128, D], F32)
    xat = SB(pd, "xat", [128, D], F32)
    outf = SB(pd, "outf", [128, D], F32)

    gth_tok = {}

    def emit_gathers(i):
        toks = []
        for k in range(8):
            toks.append(b.dma("gpsimd", lambda E, i=i, k=k: E.indirect_dma_start(
                out=gath[k][:], out_offset=None,
                in_=ysorted[:, :],
                in_offset=IndirectOffsetOnAxis(slot8i[:, i * 8 + k:i * 8 + k + 1], 0)),
                ch="GTH"))
        gth_tok[i] = toks

    emit_gathers(0)
    out_last = None
    prev_xat = None
    prev_out = None
    for i in range(NT):
        b.waits("vector", gth_tok[i])
        if prev_xat is not None:
            b.waits("sync", [prev_xat])
        txl = b.dma("sync", lambda E, i=i: E.dma_start(
            out=xat[:], in_=xa_dram[i * 128:(i + 1) * 128, :]), ch="XAT")
        b.op("vector", lambda E: E.tensor_tensor(t01[:], gath[0][:], gath[1][:], op=OP.add))
        b.op("vector", lambda E: E.tensor_tensor(t23[:], gath[2][:], gath[3][:], op=OP.add))
        b.op("vector", lambda E: E.tensor_tensor(t45[:], gath[4][:], gath[5][:], op=OP.add))
        b.op("vector", lambda E: E.tensor_tensor(t67[:], gath[6][:], gath[7][:], op=OP.add))
        b.op("vector", lambda E: E.tensor_tensor(t01[:], t01[:], t23[:], op=OP.add))
        b.op("vector", lambda E: E.tensor_tensor(t45[:], t45[:], t67[:], op=OP.add))
        tgsum = b.op("vector", lambda E: E.tensor_tensor(moe[:], t01[:], t45[:], op=OP.add),
                     inc="GSM")
        if i + 1 < NT:
            b.wait("gpsimd", tgsum)
            emit_gathers(i + 1)
        b.wait("vector", txl)
        b.op("vector", lambda E: E.tensor_tensor(
            x1[:], moe[:], xat[:], op=OP.add))
        b.fop("vector", lambda E: E.reduce_sum(small[:, 8:9], x1[:], axis=AX))
        tv1 = b.op("vector", lambda E: E.tensor_scalar(
            small[:, 10:11], small[:, 8:9], 1.0 / D, None, op0=OP.mult), inc="LV1")
        b.wait("scalar", tv1)
        tsq = b.op("scalar", lambda E: E.activation(
            out=xat[:], in_=x1[:], func=AF.Square, accum_out=small[:, 9:10]), inc="LSQ")
        prev_xat = tsq
        b.wait("vector", tsq)
        b.op("vector", lambda E: E.tensor_scalar(
            small[:, 11:12], small[:, 9:10], 1.0 / D, None, op0=OP.mult))
        b.op("vector", lambda E: E.tensor_tensor(
            small[:, 9:10], small[:, 10:11], small[:, 10:11], op=OP.mult))
        b.op("vector", lambda E: E.tensor_tensor(
            small[:, 11:12], small[:, 11:12], small[:, 9:10], op=OP.subtract))
        b.op("vector", lambda E: E.tensor_scalar(
            small[:, 11:12], small[:, 11:12], EPS, None, op0=OP.add))
        trec = b.fop("vector", lambda E: E.reciprocal(small[:, 12:13], small[:, 11:12]),
                     inc="LRC")
        b.wait("scalar", trec)
        tst_ = b.op("scalar", lambda E: E.activation(
            out=small[:, 13:14], in_=small[:, 12:13], func=AF.Sqrt), inc="LSR")
        b.waits("vector", [tst_] + ([prev_out] if prev_out else []))
        tof = b.op("vector", lambda E: E.tensor_scalar(
            outf[:], x1[:], small[:, 10:11], small[:, 13:14],
            op0=OP.subtract, op1=OP.mult), inc="OF")
        b.wait("sync", tof)
        out_last = b.dma("sync", lambda E, i=i: E.dma_start(
            out=y_out[i * 128:(i + 1) * 128, :], in_=outf[:]), ch="OUT")
        prev_out = out_last

    b.wait("sync", out_last)
    b.wait("gpsimd", out_last)
    if dbg:
        tdb = b.dma("gpsimd", lambda E: E.dma_start(out=sdump[:], in_=slot8i[:]), ch="DBG")
        for ch_ in range(6):
            r0, r1 = ch_ * 3584, min((ch_ + 1) * 3584, ROWS + 1)
            tdb = b.dma("gpsimd", lambda E, r0=r0, r1=r1: E.dma_start(
                out=wdump[r0:r1, :], in_=xsorted[r0:r1, 2048:2112]), ch="DBG")
            tdb = b.dma("gpsimd", lambda E, r0=r0, r1=r1: E.dma_start(
                out=ydump[r0:r1, :], in_=ysorted[r0:r1, 0:8]), ch="DBG")
        b.wait("sync", tdb)
    finish(nc, b)
    pd.close()
    top.close()
    return nc


def finish(nc, b):
    with nc.Block() as block:
        for eng in ENGS:
            ops = list(b.q[eng])

            def runner(E, ops=ops):
                for fn in ops:
                    fn(E)
            getattr(block, eng)(runner)


def _rope_tables():
    inv = 1.0 / (10000.0 ** (np.arange(0, DQK, 2)[: DQK // 2].astype(np.float32) / DQK))
    t = np.arange(S, dtype=np.float32)
    ang = np.outer(t, inv)
    return np.cos(ang), np.sin(ang)


def _perm_cols(w, nheads):
    perm = np.concatenate([np.arange(0, 128, 2), np.arange(1, 128, 2)])
    w2 = w.reshape(w.shape[0], nheads, 128)[:, :, perm]
    return np.ascontiguousarray(w2.reshape(w.shape[0], nheads * 128))


def _mask_for_core(qo):
    m = np.full((NT, 128, 896), NEG, np.float32)
    for i in range(NT):
        q = qo + i * 128 + np.arange(128)[:, None]
        band_lo = qo - 512 + i * 128

        def allowed(j):
            band = (j <= q) & (j > q - WIN)
            front = (j < FRONT) & (j <= q - WIN)
            return band | front
        jf = np.arange(256)[None, :]
        in_band = (jf >= band_lo) & (jf < band_lo + 640)
        m[i, :, 0:256] = np.where(allowed(jf) & ~in_band, 0.0, NEG)
        jb = band_lo + np.arange(640)[None, :]
        m[i, :, 256:896] = np.where(allowed(jb) & (jb >= 0), 0.0, NEG)
    return m


def make_in_maps(L0_attn_x, L0_attn_wq, L0_attn_wk, L0_attn_wv, L0_attn_wo,
                 L0_ffn_gate_w, L0_ffn_w_gate, L0_ffn_w_up, L0_ffn_w_down):
    x = np.asarray(L0_attn_x, np.float32)
    wq = _perm_cols(np.asarray(L0_attn_wq, np.float32), NH)
    wk = _perm_cols(np.asarray(L0_attn_wk, np.float32), NKV)
    wv = np.ascontiguousarray(np.asarray(L0_attn_wv, np.float32))
    wo = np.ascontiguousarray(np.asarray(L0_attn_wo, np.float32))
    gw = np.ascontiguousarray(np.asarray(L0_ffn_gate_w, np.float32))
    wg = np.ascontiguousarray(np.asarray(L0_ffn_w_gate, np.float32))
    wu = np.ascontiguousarray(np.asarray(L0_ffn_w_up, np.float32))
    wdn = np.ascontiguousarray(np.asarray(L0_ffn_w_down, np.float32))
    cos, sin = _rope_tables()
    idf = np.eye(128, dtype=np.float32)
    idb = idf.astype(ml_dtypes.bfloat16)
    scale = np.float32(1.0 / np.sqrt(DQK))

    in_maps = []
    for c in range(8):
        bb, qo = c // 4, (c % 4) * T
        xb = x[bb]
        xkv = np.zeros((XKV, D), np.float32)
        xkv[0:FRONT] = xb[0:FRONT]
        lo = qo - WIN
        src0 = max(0, lo)
        xkv[FRONT + (src0 - lo):FRONT + WIN + T] = xb[src0:qo + T]
        qpos = np.arange(qo, qo + T)
        kvpos = np.concatenate([np.arange(FRONT),
                                np.clip(np.arange(lo, qo + T), 0, S - 1)])
        in_maps.append({
            "xkv_in": np.ascontiguousarray(xkv),
            "wq_in": wq, "wk_in": wk, "wv_in": wv, "wo_in": wo,
            "gate_in": gw, "wg_in": wg, "wu_in": wu, "wd_in": wdn,
            "cq_in": np.ascontiguousarray(cos[qpos] * scale),
            "sq_in": np.ascontiguousarray(sin[qpos] * scale),
            "ckv_in": np.ascontiguousarray(cos[kvpos]),
            "skv_in": np.ascontiguousarray(sin[kvpos]),
            "mask_in": _mask_for_core(qo),
            "idf_in": idf, "idb_in": idb,
        })
    return in_maps


def kernel(L0_attn_x, L0_attn_wq, L0_attn_wk, L0_attn_wv, L0_attn_wo,
           L0_ffn_gate_w, L0_ffn_w_gate, L0_ffn_w_up, L0_ffn_w_down):
    nc = build_nc(stage="full")
    in_maps = make_in_maps(L0_attn_x, L0_attn_wq, L0_attn_wk, L0_attn_wv, L0_attn_wo,
                           L0_ffn_gate_w, L0_ffn_w_gate, L0_ffn_w_up, L0_ffn_w_down)
    res = run_bass_kernel_spmd(nc, in_maps, core_ids=list(range(8)))
    out = np.empty((BATCH, S, D), np.float32)
    for c in range(8):
        bb, qo = c // 4, (c % 4) * T
        out[bb, qo:qo + T] = res.results[c]["y_out"]
    return out


# version 07
# speedup vs baseline: 1.4614x; 1.0018x over previous
import numpy as np
import ml_dtypes
from contextlib import ExitStack

import concourse.bass as bass
import concourse.mybir as mybir
from concourse.bass_utils import run_bass_kernel_spmd
from concourse.bass_core_rust import IndirectOffsetOnAxis

F32 = mybir.dt.float32
BF16 = mybir.dt.bfloat16
I32 = mybir.dt.int32
AF = mybir.ActivationFunctionType
OP = mybir.AluOpType
AX = mybir.AxisListType.X

S, D, BATCH = 8192, 2048, 2
T = 2048
NT = 16
NKV, NH, DQK = 4, 16, 128
WIN, FRONT = 512, 256
XKV = FRONT + WIN + T
NE, TOPK, INTER = 32, 8, 768
CAP = 640
ROWS = NE * CAP
PAYW = 2176
WCOL = 2048
NEG = -1e30
BIGC = 1e9
EPS = 1e-5

ENGS = ("sync", "scalar", "vector", "tensor", "gpsimd")
SEMS = ("T", "BV", "BS", "BG",
        "SG0", "SG1", "IOT", "PP0", "PP1", "PP2", "RP", "PKB0", "PKB1",
        "SMK", "EXP", "PNM", "PTE", "OHE", "YB0", "YB1",
        "LSQ", "LRC", "LSR", "LXA", "LXB", "LV1",
        "GLG", "GL2", "RNG", "RXP", "RWV", "RCA", "RS8", "RPW",
        "SIL", "HMU", "WCL", "YC0", "YC1", "GSM", "OF", "KTD", "VTD", "VFN", "SFN",
        "XEV")
DMA_CHANNELS = ("MISC", "XKLa", "XKLb", "XKSa", "XKSb", "WSTa", "WSTb", "TL",
                "PLDa", "PLDb", "QST", "XSTa", "XSTb", "YSTa", "YSTb", "XAT",
                "OUT", "TRXa", "TRXb", "TRXG", "TRX4a", "TRX4b", "SCTa", "SCTb",
                "WGU", "WDD",
                "GTH", "DBG")
BAR_SEM = {"tensor": "T", "vector": "BV", "scalar": "BS", "gpsimd": "BG"}


FENCE_ALL = True


class Bld:
    def __init__(self):
        self.q = {e: [] for e in ENGS}
        self.cnt = {s: 0 for s in SEMS}
        self.cnt.update({c: 0 for c in DMA_CHANNELS})
        self.cnt.update({c: 0 for c in DMA_CHANNELS})
        self.sem = {}
        self.last = {e: None for e in ENGS}

    def fop(self, eng, fn, inc=None):
        if FENCE_ALL and eng in ("vector", "scalar"):
            return self.op(eng, fn, inc=inc)
        tok = self.op(eng, fn, inc=inc)
        if tok is not None:
            self.q[eng].append(lambda E, k=tok[0], v=tok[1]: E.wait_ge(self.sem[k], v))
            return tok
        fch = "VFN" if eng == "vector" else "SFN"
        self.cnt[fch] += 1
        v = self.cnt[fch]
        real = self.q[eng].pop()
        self.q[eng].append(lambda E, real=real, fch=fch: real(E).then_inc(self.sem[fch], 1))
        self.q[eng].append(lambda E, fch=fch, v=v: E.wait_ge(self.sem[fch], v))
        return tok

    def op(self, eng, fn, inc=None, nop=True):
        if FENCE_ALL and eng in ("vector", "scalar"):
            tok = self._op_raw(eng, fn, inc=inc, nop=False)
            if tok is not None:
                self.q[eng].append(lambda E, k=tok[0], v=tok[1]: E.wait_ge(self.sem[k], v))
                return tok
            fch = "VFN" if eng == "vector" else "SFN"
            self.cnt[fch] += 1
            v = self.cnt[fch]
            real = self.q[eng].pop()
            self.q[eng].append(lambda E, real=real, fch=fch: real(E).then_inc(self.sem[fch], 1))
            self.q[eng].append(lambda E, fch=fch, v=v: E.wait_ge(self.sem[fch], v))
            return None
        return self._op_raw(eng, fn, inc=inc, nop=nop)

    def _op_raw(self, eng, fn, inc=None, nop=True):
        if nop and eng == "vector":
            self.q[eng].append(lambda E: E.engine_nop())
        elif nop and eng in ("scalar", "gpsimd"):
            self.q[eng].append(lambda E: E.nop())
        if inc:
            k = "T" if (inc is True and eng == "tensor") else inc
            assert k is not True, "non-tensor inc needs a channel name"
            self.cnt[k] += 1
            v = self.cnt[k]
            self.q[eng].append(lambda E, fn=fn, k=k: fn(E).then_inc(self.sem[k], 1))
            tok = (k, v)
            self.last[eng] = tok
            return tok
        self.q[eng].append(fn)
        return None

    def mark(self, eng):
        k = BAR_SEM[eng]
        self.cnt[k] += 1
        v = self.cnt[k]
        if eng == "vector":
            self.q[eng].append(lambda E, k=k: E.engine_nop().then_inc(self.sem[k], 1))
        else:
            self.q[eng].append(lambda E, k=k: E.nop().then_inc(self.sem[k], 1))
        tok = (k, v)
        self.last[eng] = tok
        return tok

    def dma(self, eng, fn, ch="MISC"):
        k = ch
        self.cnt[k] += 16
        v = self.cnt[k]
        self.q[eng].append(lambda E, fn=fn, k=k: fn(E).then_inc(self.sem[k], 16))
        tok = (k, v)
        self.last[eng] = tok
        return tok

    def wait(self, eng, tok):
        if tok is None:
            return
        k, v = tok
        self.q[eng].append(lambda E, k=k, v=v: E.wait_ge(self.sem[k], v))

    def waits(self, eng, toks):
        best = {}
        for t in toks:
            if t is None:
                continue
            k, v = t
            best[k] = max(best.get(k, 0), v)
        for k, v in best.items():
            self.q[eng].append(lambda E, k=k, v=v: E.wait_ge(self.sem[k], v))

    def barrier(self, engines=ENGS):
        toks = []
        for ch in SEMS + DMA_CHANNELS:
            if self.cnt[ch] > 0:
                toks.append((ch, self.cnt[ch]))
        for e in engines:
            self.waits(e, toks)


def build_nc(stage="full", races=True, dbg=False, rparts=4):
    nc = bass.Bass("TRN2", target_bir_lowering=False,
                   detect_race_conditions=races)
    top = ExitStack()

    dt = nc.dram_tensor
    xkv_in = dt("xkv_in", [XKV, D], F32, kind="ExternalInput").ap()
    wq_in = dt("wq_in", [D, D], F32, kind="ExternalInput").ap()
    wk_in = dt("wk_in", [D, 512], F32, kind="ExternalInput").ap()
    wv_in = dt("wv_in", [D, 512], F32, kind="ExternalInput").ap()
    wo_in = dt("wo_in", [D, D], F32, kind="ExternalInput").ap()
    gate_in = dt("gate_in", [D, NE], F32, kind="ExternalInput").ap()
    wg_in = dt("wg_in", [NE, D, INTER], F32, kind="ExternalInput").ap()
    wu_in = dt("wu_in", [NE, D, INTER], F32, kind="ExternalInput").ap()
    wd_in = dt("wd_in", [NE, INTER, D], F32, kind="ExternalInput").ap()
    cq_in = dt("cq_in", [T, 64], F32, kind="ExternalInput").ap()
    sq_in = dt("sq_in", [T, 64], F32, kind="ExternalInput").ap()
    ckv_in = dt("ckv_in", [XKV, 64], F32, kind="ExternalInput").ap()
    skv_in = dt("skv_in", [XKV, 64], F32, kind="ExternalInput").ap()
    mask_in = dt("mask_in", [NT, 128, 896], F32, kind="ExternalInput").ap()
    idf_in = dt("idf_in", [128, 128], F32, kind="ExternalInput").ap()
    idb_in = dt("idb_in", [128, 128], BF16, kind="ExternalInput").ap()

    y_out = dt("y_out", [T, D], F32, kind="ExternalOutput").ap()
    dbg_out = dt("dbg_out", [8, 128, 2048], F32, kind="ExternalOutput").ap() if dbg else None
    sdump = dt("sdump", [128, 128], I32, kind="ExternalOutput").ap() if dbg else None
    wdump = dt("wdump", [ROWS + 1, 64], BF16, kind="ExternalOutput").ap() if dbg else None
    ydump = dt("ydump", [ROWS + 1, 8], BF16, kind="ExternalOutput").ap() if dbg else None

    xkv_bf = dt("xkv_bf", [XKV, D], BF16, kind="Internal").ap()
    qt_dram = dt("qt_dram", [NT, 16, 128, 128], BF16, kind="Internal").ap()
    xa_dram = dt("xa_dram", [T, D], F32, kind="Internal").ap()
    xab_dram = dt("xab_dram", [T, D], BF16, kind="Internal").ap()
    xsorted = dt("xsorted", [ROWS + 1, PAYW], BF16, kind="Internal").ap()
    ysorted = dt("ysorted", [ROWS + 1, D], BF16, kind="Internal").ap()

    def SB(stk, name, shape, dtype):
        return stk.enter_context(nc.sbuf_tensor(name, shape, dtype))

    def PS(stk, name, shape, dtype=F32):
        return stk.enter_context(nc.psum_tensor(name, shape, dtype))

    idf = SB(top, "idf", [128, 128], F32)
    idb = SB(top, "idb", [128, 128], BF16)
    stage_f = [SB(top, f"stage{j}", [128, 2048], F32) for j in range(2)]
    small = SB(top, "small", [128, 20], F32)
    x1 = SB(top, "x1", [128, D], F32)
    logits = SB(top, "logits", [128, NT * NE], F32)
    rdbg = [SB(top, "rdbg0", [32, 128], F32), SB(top, "rdbg1", [32, 128], F32),
            SB(top, "rdbg2", [128, 32], F32), SB(top, "rdbg3", [128, 32], F32),
            SB(top, "rdbg4", [128, 8], F32), SB(top, "rdbg5", [128, 8], F32)]
    slot8i = SB(top, "slot8i", [128, NT * 8], I32)
    gateb = SB(top, "gateb", [128, 16 * NE], BF16)
    iota32 = SB(top, "iota32", [128, NE], F32)
    iota2048 = SB(top, "iota2048", [128, NE], F32)
    Yp = [PS(top, f"Yp{j}", [128, 512]) for j in range(2)]

    sems = {}
    for sname in SEMS + DMA_CHANNELS:
        sems[sname] = top.enter_context(nc.semaphore(sname))
    b = Bld()
    b.sem = sems

    cast_state = {"i": 0, "hist": {}}

    def cast_chunk(src_ap, dst_ap, cols, eng, extra_waits=()):
        par = cast_state["i"] % 2
        stg = stage_f[par]
        cast_state["i"] += 1
        prev = cast_state["hist"].get(id(stg))
        if prev is not None:
            b.wait("sync", prev)
        tl = b.dma("sync", lambda E, stg=stg, cols=cols, src_ap=src_ap: E.dma_start(
            out=stg[:, 0:cols], in_=src_ap), ch="WSTa" if par == 0 else "WSTb")
        b.wait(eng, tl)
        for w in extra_waits:
            b.wait(eng, w)
        sg = "SG0" if par == 0 else "SG1"
        if eng == "scalar":
            tc = b.op(eng, lambda E, stg=stg, cols=cols, dst_ap=dst_ap: E.copy(
                out=dst_ap, in_=stg[:, 0:cols]), inc=sg)
        else:
            tc = b.op(eng, lambda E, stg=stg, cols=cols, dst_ap=dst_ap: E.tensor_copy(
                out=dst_ap, in_=stg[:, 0:cols]), inc=sg)
        cast_state["hist"][id(stg)] = tc
        return tc

    pa = ExitStack()
    pab = ExitStack()
    kt = SB(pab, "kt", [128, NKV * XKV], BF16)
    vt = SB(pab, "vt", [128, 22 * 512], BF16)
    xsl = [SB(pab, "xsl0", [128, 16 * 512], BF16),
           SB(pa, "xsl1", [128, 16 * 512], BF16)]
    cqtm = SB(pa, "cqtm", [128, 16 * 64], F32)
    sqtm = SB(pa, "sqtm", [128, 16 * 64], F32)
    rtmp2 = SB(pa, "rtmp2", [128, 16 * 64], F32)
    qtm = SB(pa, "qtm", [128, 2048], F32)
    qrope = SB(pa, "qrope", [128, 2048], F32)
    qrope_b = SB(pa, "qrope_b", [128, 2048], BF16)
    qtts = SB(pa, "qtts", [128, 2048], BF16)
    pp = [PS(pa, f"pp{j}", [128, 512]) for j in range(3)]
    pkb = [PS(pa, f"pkb{j}", [128, 128], BF16) for j in range(2)]
    pa0 = ExitStack()
    ckvtm = SB(pa0, "ckvtm", [128, 22 * 64], F32)
    skvtm = SB(pa0, "skvtm", [128, 22 * 64], F32)
    ktm = SB(pa0, "ktm", [128, 512], F32)
    krope = SB(pa0, "krope", [128, 512], F32)
    krope_b = SB(pa0, "krope_b", [128, 512], BF16)
    wkb = SB(pa0, "wkb", [128, 16 * 512], BF16)
    wvb = SB(pa0, "wvb", [128, 16 * 512], BF16)

    b.dma("sync", lambda E: E.dma_start(out=idf[:], in_=idf_in[:]))
    b.dma("sync", lambda E: E.dma_start(out=idb[:], in_=idb_in[:]))
    for tl_ in range(16):
        b.dma("sync", lambda E, tl_=tl_: E.dma_start(
            out=cqtm[:, tl_ * 64:(tl_ + 1) * 64], in_=cq_in[tl_ * 128:(tl_ + 1) * 128, :]))
        b.dma("sync", lambda E, tl_=tl_: E.dma_start(
            out=sqtm[:, tl_ * 64:(tl_ + 1) * 64], in_=sq_in[tl_ * 128:(tl_ + 1) * 128, :]))
    for tl_ in range(22):
        b.dma("sync", lambda E, tl_=tl_: E.dma_start(
            out=ckvtm[:, tl_ * 64:(tl_ + 1) * 64], in_=ckv_in[tl_ * 128:(tl_ + 1) * 128, :]))
        b.dma("sync", lambda E, tl_=tl_: E.dma_start(
            out=skvtm[:, tl_ * 64:(tl_ + 1) * 64], in_=skv_in[tl_ * 128:(tl_ + 1) * 128, :]))

    xkv_done = []
    for chk in range(22):
        buf = xsl[chk % 2]
        dst = buf[:, 0:2048]
        if chk >= 2:
            b.wait("gpsimd", xkv_done[chk - 2])
        tl = b.dma("gpsimd", lambda E, chk=chk, dst=dst: E.dma_start(
            out=dst, in_=xkv_in[chk * 128:(chk + 1) * 128, :]),
            ch="XKLa" if chk % 2 == 0 else "XKLb")
        b.wait("sync", tl)
        ts_ = b.dma("sync", lambda E, chk=chk, dst=dst: E.dma_start(
            out=xkv_bf[chk * 128:(chk + 1) * 128, :], in_=dst),
            ch="XKSa" if chk % 2 == 0 else "XKSb")
        xkv_done.append(ts_)
    t_xkvbf = [xkv_done[-1], xkv_done[-2]]

    b.op("gpsimd", lambda E: E.iota(iota32[:], pattern=[[1, NE]], base=0,
                                    channel_multiplier=0,
                                    allow_small_or_imprecise_dtypes=True))
    b.op("gpsimd", lambda E: E.iota(iota2048[:], pattern=[[2048, NE]], base=0,
                                    channel_multiplier=0,
                                    allow_small_or_imprecise_dtypes=True), inc="IOT")

    wtoks = []
    for wsrc, wdst, wcols in ((wk_in, wkb, 512), (wv_in, wvb, 512), (gate_in, gateb, NE)):
        wtoks.append(b.dma("gpsimd", lambda E, wsrc=wsrc, wdst=wdst: E.dma_start(
            out=wdst[:].rearrange("p (c f) -> p c f", c=16),
            in_=wsrc[:, :].rearrange("(c p) f -> p c f", p=128)), ch="WGU"))
    b.waits("tensor", wtoks)
    b.barrier(("sync", "vector", "scalar", "gpsimd"))

    def emit_rope_tm(dst, srct, ctab, stab, nh, tmp):
        sr = srct.rearrange("p (h half i) -> p h half i", h=nh, half=2)
        dr = dst.rearrange("p (h half i) -> p h half i", h=nh, half=2)
        tr = tmp.rearrange("p (h i) -> p h i", h=nh)
        cb = ctab.unsqueeze(1).to_broadcast([128, nh, 64])
        sb_ = stab.unsqueeze(1).to_broadcast([128, nh, 64])
        b.op("vector", lambda E: E.tensor_tensor(dr[:, :, 0, :], sr[:, :, 0, :], cb, op=OP.mult))
        b.op("vector", lambda E: E.tensor_tensor(tr[:], sr[:, :, 1, :], sb_, op=OP.mult))
        b.op("vector", lambda E: E.tensor_tensor(dr[:, :, 0, :], dr[:, :, 0, :], tr[:], op=OP.subtract))
        b.op("vector", lambda E: E.tensor_tensor(dr[:, :, 1, :], sr[:, :, 1, :], cb, op=OP.mult))
        b.op("vector", lambda E: E.tensor_tensor(tr[:], sr[:, :, 0, :], sb_, op=OP.mult))
        b.op("vector", lambda E: E.tensor_tensor(
            dr[:, :, 1, :], dr[:, :, 1, :], tr[:], op=OP.add))
        return None

    if stage == "p0":
        b.barrier(ENGS)
        tlast = None
        for i in range(NT):
            tlast = b.dma("gpsimd", lambda E, i=i: E.dma_start(
                out=y_out[i * 128:(i + 1) * 128, :],
                in_=xkv_bf[768 + i * 128:768 + (i + 1) * 128, :]), ch="DBG")
        b.wait("sync", tlast)
        finish(nc, b)
        return nc

    pptok = {}

    def pwait(j, eng):
        t = pptok.get(j)
        if t is not None:
            b.wait(eng, t)

    pidx = 0
    kv_slices = [(i * 512, min(512, XKV - i * 512)) for i in range(6)]
    slice_done = {}
    for si, (t0, w) in enumerate(kv_slices):
        buf = xsl[si % 2]
        b.waits("scalar", t_xkvbf)
        if si >= 2:
            b.wait("scalar", slice_done[si - 2])
        ttr = None
        for dmc in range(16):
            ttr = b.dma("scalar", lambda E, t0=t0, w=w, dmc=dmc, buf=buf: E.dma_start_transpose(
                out=buf[:, dmc * 512:dmc * 512 + w],
                in_=xkv_bf[t0:t0 + w, dmc * 128:(dmc + 1) * 128]),
                ch="TRXa" if si % 2 == 0 else "TRXb")
        b.waits("tensor", [ttr])
        last_use = None
        pkb_tok = [None, None]
        for tt in range(w // 128):
            blk = t0 // 128 + tt
            j = pidx % 3
            pidx += 1
            pwait(j, "tensor")
            tmm = None
            for dmc in range(16):
                tmm = b.op("tensor", lambda E, j=j, dmc=dmc, tt=tt, buf=buf: E.matmul(
                    pp[j][:, 0:512],
                    buf[:, dmc * 512 + tt * 128:dmc * 512 + (tt + 1) * 128],
                    wkb[:, dmc * 512:(dmc + 1) * 512],
                    start=(dmc == 0), stop=(dmc == 15)), nop=False,
                    inc=(True if dmc == 15 else None))
            b.wait("vector", tmm)
            tevk = b.op("vector", lambda E, j=j: E.tensor_copy(out=ktm[:], in_=pp[j][:, 0:512]),
                        inc=f"PP{j}")
            pptok[j] = tevk
            emit_rope_tm(krope[:], ktm[:], ckvtm[:, blk * 64:(blk + 1) * 64],
                         skvtm[:, blk * 64:(blk + 1) * 64], NKV, rtmp2[:, 0:256])
            trope = b.op("vector", lambda E: E.tensor_copy(out=krope_b[:], in_=krope[:]),
                         inc="RP")
            b.waits("tensor", [trope])
            for f in range(NKV):
                pb_ = f % 2
                b.waits("tensor", [pkb_tok[pb_]])
                ttp = b.op("tensor", lambda E, f=f, pb_=pb_: E.transpose(
                    pkb[pb_][:], krope_b[:, f * 128:(f + 1) * 128], idb[:]), inc=True, nop=False)
                b.wait("vector", ttp)
                tevt = b.op("vector", lambda E, f=f, blk=blk, pb_=pb_: E.tensor_copy(
                    out=kt[:, f * XKV + blk * 128:f * XKV + (blk + 1) * 128],
                    in_=pkb[pb_][:]), inc=f"PKB{pb_}")
                pkb_tok[pb_] = tevt
                last_use = tevt
        ntt = w // 128
        for tt in range(ntt):
            j = pidx % 3
            pidx += 1
            pwait(j, "tensor")
            tmm = None
            for dmc in range(16):
                tmm = b.op("tensor", lambda E, j=j, dmc=dmc, tt=tt, buf=buf: E.matmul(
                    pp[j][:, 0:512],
                    buf[:, dmc * 512 + tt * 128:dmc * 512 + (tt + 1) * 128],
                    wvb[:, dmc * 512:(dmc + 1) * 512],
                    start=(dmc == 0), stop=(dmc == 15)), nop=False,
                    inc=(True if dmc == 15 else None))
            blk = t0 // 128 + tt
            b.wait("scalar", tmm)
            tev = b.op("scalar", lambda E, j=j, blk=blk: E.copy(
                out=vt[:, blk * 512:(blk + 1) * 512], in_=pp[j][:, 0:512]), inc=f"PP{j}")
            pptok[j] = tev
            last_use = tev
        slice_done[si] = last_use

    if stage == "p1":
        b.barrier(ENGS)
        tl1 = b.dma("gpsimd", lambda E: E.dma_start(out=y_out[0:128, :], in_=kt[:, 0:2048]), ch="DBG")
        tl2 = b.dma("gpsimd", lambda E: E.dma_start(out=y_out[128:256, :], in_=vt[:, 0:2048]), ch="DBG")
        b.wait("sync", tl2)
        finish(nc, b)
        return nc

    pa0.close()
    wqb = SB(pa, "wqb", [128, 16 * 2048], BF16)
    b.barrier(("gpsimd",))
    for h in range(4):
        b.dma("gpsimd", lambda E, h=h: E.dma_start(
            out=wqb[:, h * 4 * 2048:(h + 1) * 4 * 2048].rearrange("p (c f) -> p c f", c=4),
            in_=wq_in[h * 512:(h + 1) * 512, :].rearrange("(c p) f -> p c f", p=128)),
            ch="WGU")
    b.barrier(("sync", "vector", "scalar", "tensor"))

    qst_tok = None
    p2_slfree = {}
    for si in range(4):
        t0 = 768 + si * 512
        buf = xsl[si % 2]
        b.waits("scalar", [p2_slfree.get(si % 2)])
        ttr = None
        for dmc in range(16):
            ttr = b.dma("scalar", lambda E, t0=t0, dmc=dmc, buf=buf: E.dma_start_transpose(
                out=buf[:, dmc * 512:(dmc + 1) * 512],
                in_=xkv_bf[t0:t0 + 512, dmc * 128:(dmc + 1) * 128]),
                ch="TRXa" if si % 2 == 0 else "TRXb")
        b.wait("tensor", ttr)
        pkb_tok = [None, None]
        for tt in range(4):
            tile = si * 4 + tt
            evtoks = []
            for nq in range(4):
                j = pidx % 3
                pidx += 1
                pwait(j, "tensor")
                tmm = None
                for dmc in range(16):
                    tmm = b.op("tensor", lambda E, j=j, dmc=dmc, nq=nq, tt=tt, buf=buf: E.matmul(
                        pp[j][:, 0:512],
                        buf[:, dmc * 512 + tt * 128:dmc * 512 + (tt + 1) * 128],
                        wqb[:, dmc * 2048 + nq * 512:dmc * 2048 + (nq + 1) * 512],
                        start=(dmc == 0), stop=(dmc == 15)), nop=False,
                        inc=(True if dmc == 15 else None))
                b.wait("vector", tmm)
                tcp = b.op("vector", lambda E, j=j, nq=nq: E.tensor_copy(
                    out=qtm[:, nq * 512:(nq + 1) * 512], in_=pp[j][:, 0:512]), inc=f"PP{j}")
                pptok[j] = tcp
                evtoks.append(tcp)
            emit_rope_tm(qrope[:], qtm[:], cqtm[:, tile * 64:(tile + 1) * 64],
                         sqtm[:, tile * 64:(tile + 1) * 64], NH, rtmp2[:])
            trope = b.op("vector", lambda E: E.tensor_copy(out=qrope_b[:], in_=qrope[:]),
                         inc="RP")
            b.waits("tensor", [trope])
            b.waits("vector", [qst_tok])
            for f in range(16):
                pb_ = f % 2
                b.waits("tensor", [pkb_tok[pb_]])
                ttp = b.op("tensor", lambda E, f=f, pb_=pb_: E.transpose(
                    pkb[pb_][:], qrope_b[:, f * 128:(f + 1) * 128], idb[:]), inc=True, nop=False)
                b.wait("vector", ttp)
                tevt = b.op("vector", lambda E, f=f, pb_=pb_: E.tensor_copy(
                    out=qtts[:, f * 128:(f + 1) * 128], in_=pkb[pb_][:]), inc=f"PKB{pb_}")
                pkb_tok[pb_] = tevt
            b.waits("sync", pkb_tok)
            qst_tok = b.dma("sync", lambda E, tile=tile: E.dma_start(
                out=qt_dram[tile].rearrange("f p c -> p f c"),
                in_=qtts[:].rearrange("p (f c) -> p f c", f=16)), ch="QST")
        p2_slfree[si % 2] = tmm

    b.barrier(ENGS)
    if stage == "p2":
        tl1 = None
        for i in range(4):
            tl1 = b.dma("gpsimd", lambda E, i=i: E.dma_start(
                out=y_out[i * 128:(i + 1) * 128, :].rearrange("p (f c) -> p f c", f=16),
                in_=qt_dram[i].rearrange("f p c -> p f c")), ch="DBG")
        b.wait("sync", tl1)
        finish(nc, b)
        return nc

    pa.close()

    pb = ExitStack()
    wob = SB(pb, "wob", [128, 16 * 2048], BF16)
    qt_t = SB(pb, "qt_t", [128, 16 * 128], BF16)
    mask_t = SB(pb, "mask_t", [128, 896], F32)
    x_t = SB(pb, "x_t", [128, D], F32)
    Pt = SB(pb, "Pt", [128, 896], BF16)
    PTs = SB(pb, "PTs", [128, 896], BF16)
    sm = SB(pb, "sm", [128, 896], F32)
    outT = SB(pb, "outT", [128, 16 * 128], BF16)
    xa_f = SB(pb, "xa_f", [128, D], F32)
    xab_t = SB(pb, "xab_t", [128, D], BF16)
    m8 = SB(pb, "m8", [128, 8], F32)
    zapt = SB(pb, "zapt", [128, NE], F32)
    Mm = SB(pb, "Mm", [128, NE], F32)
    ew = SB(pb, "ew", [128, NE], F32)
    Wd = SB(pb, "Wd", [128, NE], F32)
    Whi = SB(pb, "Whi", [128, NE], BF16)
    Wlo = SB(pb, "Wlo", [128, NE], F32)
    Wlob = SB(pb, "Wlob", [128, NE], BF16)
    rbT = SB(pb, "rbT", [32, 128], F32)
    cums = SB(pb, "cums", [32, 128], F32)
    zer32 = SB(pb, "zer32", [32, 128], F32)
    carry = SB(pb, "carry", [32, 1], F32)
    posT = SB(pb, "posT", [32, 128], F32)
    pos_tm = SB(pb, "pos_tm", [128, NE], F32)
    kva = SB(pb, "kva", [128, NE], F32)
    kvm = SB(pb, "kvm", [128, NE], F32)
    kv8 = SB(pb, "kv8", [128, 8], F32)
    e8 = SB(pb, "e8", [128, 8], F32)
    Em = SB(pb, "Em", [128, NE], F32)
    pos8 = SB(pb, "pos8", [128, 8], F32)
    slot8f = SB(pb, "slot8f", [128, 8], F32)
    ovf = SB(pb, "ovf", [128, 8], F32)
    logT_sb = SB(pb, "logT_sb", [32, 512], F32)
    payload = [SB(pb, f"pay{j}", [128, PAYW], BF16) for j in range(2)]
    S1 = PS(pb, "S1", [128, 512])
    S2 = PS(pb, "S2", [128, 384])
    PTp = PS(pb, "PTp", [128, 896], BF16)
    OHT = PS(pb, "OHT", [128, 128])
    misc1 = PS(pb, "misc1", [32, 512])
    misc2 = PS(pb, "misc2", [128, 32])

    b.op("vector", lambda E: E.memset(zer32[:], 0.0))
    b.op("vector", lambda E: E.memset(carry[:], 0.0))
    b.op("vector", lambda E: E.memset(payload[0][:, 2112:2176], 0.0))
    b.op("vector", lambda E: E.memset(payload[1][:, 2112:2176], 0.0))

    b.barrier(("gpsimd",))
    for h in range(4):
        b.dma("gpsimd", lambda E, h=h: E.dma_start(
            out=wob[:, h * 4 * 2048:(h + 1) * 4 * 2048].rearrange("p (c f) -> p c f", c=4),
            in_=wo_in[h * 512:(h + 1) * 512, :].rearrange("(c p) f -> p c f", p=128)),
            ch="WGU")
    b.barrier(("sync", "vector", "scalar", "tensor"))

    tile_tok = {}
    pay_tok = {}
    xab_store = {}
    xa_store = {}
    state = {
        "s_bank": None,
        "sm_free": None,
        "pt_bank": None,
        "pts_free": None,
        "pt_sb": None,
        "oht_bank": None,
        "outT_free": None,
        "y_bank": None,
        "xaf_free": None,
        "xab_free": None,
        "pay_free": {},
        "route_in": {},
        "rcol_guard": [None, None],
    }

    def emit_tile_loads(i, first=False):
        toks = []
        if not first:
            b.waits("sync", [state["y_bank"]])
        toks.append(b.dma("sync", lambda E, i=i: E.dma_start(
            out=qt_t[:].rearrange("p (f c) -> p f c", f=16),
            in_=qt_dram[i].rearrange("f p c -> p f c")), ch="TL"))
        toks.append(b.dma("sync", lambda E, i=i: E.dma_start(
            out=mask_t[:], in_=mask_in[i]), ch="TL"))
        toks.append(b.dma("sync", lambda E, i=i: E.dma_start(
            out=x_t[:], in_=xkv_in[768 + i * 128:768 + (i + 1) * 128, :]), ch="TL"))
        tile_tok[i] = toks

    def emit_gate_quarter(qtr):
        t0 = qtr * 512
        buf = xsl[0]
        b.waits("scalar", [xab_store[qtr * 4 + j] for j in range(4)]
                + [state.get("gate_sl_free")])
        ttr = None
        for dmc in range(16):
            ttr = b.dma("scalar", lambda E, t0=t0, dmc=dmc, buf=buf: E.dma_start_transpose(
                out=buf[:, dmc * 512:(dmc + 1) * 512],
                in_=xab_dram[t0:t0 + 512, dmc * 128:(dmc + 1) * 128]), ch="TRXG")
        b.waits("tensor", [ttr])
        tgm = None
        for dmc in range(16):
            tgm = b.op("tensor", lambda E, dmc=dmc, buf=buf: E.matmul(
                misc1[:, 0:512], gateb[:, dmc * NE:(dmc + 1) * NE],
                buf[:, dmc * 512:(dmc + 1) * 512],
                start=(dmc == 0), stop=(dmc == 15)), nop=False,
                inc=(True if dmc == 15 else None))
        state["gate_sl_free"] = tgm
        b.wait("vector", tgm)
        tge = b.op("vector", lambda E: E.tensor_copy(out=logT_sb[:], in_=misc1[:, 0:512]),
                   inc="GLG")
        tt2 = None
        for jj in range(4):
            i = qtr * 4 + jj
            b.waits("tensor", [tge, tt2])
            tt1 = b.op("tensor", lambda E, jj=jj: E.transpose(
                misc2[:], logT_sb[:, jj * 128:(jj + 1) * 128], idf[0:32, 0:32]),
                inc=True, nop=False)
            b.wait("vector", tt1)
            tt2 = b.op("vector", lambda E, i=i: E.tensor_copy(
                out=logits[:, i * NE:(i + 1) * NE], in_=misc2[:]), inc="GL2")
            state["route_in"][i] = tt2

    def emit_routing_tile(i):
        if rparts < 2:
            pay_tok[i] = None
            return
        lg = logits[:, i * NE:(i + 1) * NE]
        b.waits("vector", [state["route_in"][i]])
        b.op("vector", lambda E, lg=lg: E.tensor_copy(out=zapt[:], in_=lg))
        b.op("vector", lambda E: E.memset(Mm[:], 0.0))
        for kk in range(8):
            b.fop("vector", lambda E, kk=kk: E.reduce_max(m8[:, kk:kk + 1], zapt[:], axis=AX))
            b.fop("vector", lambda E, kk=kk: E.tensor_scalar(
                ew[:], zapt[:], m8[:, kk:kk + 1], None, op0=OP.is_equal))
            b.fop("vector", lambda E: E.tensor_tensor(Mm[:], Mm[:], ew[:], op=OP.add))
            b.fop("vector", lambda E: E.scalar_tensor_tensor(
                zapt[:], in0=ew[:], scalar=NEG, in1=zapt[:], op0=OP.mult, op1=OP.add))
        if i == 1:
            b.op("vector", lambda E, lg=lg: E.tensor_copy(out=rdbg[3][:], in_=lg))
        tneg = b.op("vector", lambda E: E.tensor_scalar(
            small[:, 14:15], m8[:, 0:1], -1.0, None, op0=OP.mult), inc="RNG")
        b.wait("scalar", tneg)
        texp = b.op("scalar", lambda E, lg=lg: E.activation(
            out=ew[:], in_=lg, func=AF.Exp, bias=small[:, 14:15]), inc="RXP")
        b.wait("vector", texp)
        b.fop("vector", lambda E: E.tensor_tensor(Wd[:], ew[:], Mm[:], op=OP.mult))
        b.fop("vector", lambda E: E.reduce_sum(small[:, 15:16], Wd[:], axis=AX))
        b.fop("vector", lambda E: E.reciprocal(small[:, 14:15], small[:, 15:16]))
        b.fop("vector", lambda E: E.tensor_scalar(
            Wd[:], Wd[:], small[:, 14:15], None, op0=OP.mult))
        b.fop("vector", lambda E: E.tensor_copy(out=Whi[:], in_=Wd[:]))
        b.fop("vector", lambda E: E.tensor_copy(out=ew[:], in_=Whi[:]))
        b.fop("vector", lambda E: E.tensor_tensor(Wlo[:], Wd[:], ew[:], op=OP.subtract))
        twv = b.fop("vector", lambda E: E.tensor_copy(out=Wlob[:], in_=Wlo[:]), inc="RWV")
        b.waits("tensor", [twv])
        tmt = b.op("tensor", lambda E: E.transpose(misc1[:, 0:128], Mm[:], idf[:]),
                   inc=True, nop=False)
        b.wait("vector", tmt)
        trb = b.fop("vector", lambda E: E.tensor_copy(out=rbT[:], in_=misc1[0:32, 0:128]),
                    inc="RCA")
        state["misc1_free"] = trb
        b.fop("vector", lambda E: E.tensor_tensor_scan(
            cums[:], data0=rbT[:], data1=zer32[:], initial=0.0,
            op0=OP.add, op1=OP.add))
        b.op("vector", lambda E: E.tensor_scalar(
            cums[:], cums[:], carry[:, 0:1], None, op0=OP.add))
        b.op("vector", lambda E: E.tensor_tensor(posT[:], cums[:], rbT[:], op=OP.subtract))
        if i == 1:
            b.op("vector", lambda E: E.tensor_copy(out=rdbg[0][:], in_=rbT[:]))
            b.op("vector", lambda E: E.tensor_copy(out=rdbg[1][:], in_=cums[:]))
        tcar = b.fop("vector", lambda E: E.tensor_copy(out=carry[:], in_=cums[:, 127:128]),
                     inc="RCA")
        b.wait("tensor", tcar)
        tpt = b.op("tensor", lambda E: E.transpose(misc2[:], posT[:], idf[0:32, 0:32]),
                   inc=True, nop=False)
        b.wait("vector", tpt)
        b.op("vector", lambda E: E.tensor_copy(out=pos_tm[:], in_=misc2[:]))
        if i == 1:
            b.op("vector", lambda E: E.tensor_copy(out=rdbg[2][:], in_=pos_tm[:]))
        b.op("vector", lambda E: E.tensor_tensor(kva[:], iota2048[:], pos_tm[:], op=OP.add))
        b.op("vector", lambda E: E.tensor_scalar(kva[:], kva[:], 1.0, None, op0=OP.add))
        b.op("vector", lambda E: E.tensor_tensor(kvm[:], kva[:], Mm[:], op=OP.mult))
        b.fop("vector", lambda E: E.tensor_scalar(kvm[:], kvm[:], 1.0, None, op0=OP.subtract))
        b.fop("vector", lambda E: E.max(out=kv8[:], in_=kvm[:]))
        b.op("vector", lambda E: E.tensor_scalar(Em[:], iota32[:], 1.0, None, op0=OP.add))
        b.op("vector", lambda E: E.tensor_tensor(Em[:], Em[:], Mm[:], op=OP.mult))
        b.fop("vector", lambda E: E.tensor_scalar(Em[:], Em[:], 1.0, None, op0=OP.subtract))
        b.fop("vector", lambda E: E.max(out=e8[:], in_=Em[:]))
        b.op("vector", lambda E: E.scalar_tensor_tensor(
            pos8[:], in0=e8[:], scalar=-2048.0, in1=kv8[:], op0=OP.mult, op1=OP.add))
        if i == 1:
            b.op("vector", lambda E: E.tensor_copy(out=rdbg[4][:], in_=kv8[:]))
            b.op("vector", lambda E: E.tensor_copy(out=rdbg[5][:], in_=e8[:]))
        b.op("vector", lambda E: E.scalar_tensor_tensor(
            slot8f[:], in0=e8[:], scalar=640.0, in1=pos8[:], op0=OP.mult, op1=OP.add))
        b.op("vector", lambda E: E.tensor_scalar(ovf[:], pos8[:], 639.5, None, op0=OP.is_gt))
        b.op("vector", lambda E: E.tensor_scalar(kv8[:], slot8f[:], float(ROWS), None,
                                                 op0=OP.subtract))
        b.op("vector", lambda E: E.tensor_tensor(kv8[:], kv8[:], ovf[:], op=OP.mult))
        b.op("vector", lambda E: E.tensor_tensor(slot8f[:], slot8f[:], kv8[:], op=OP.subtract))
        b.op("vector", lambda E: E.tensor_scalar(ovf[:], slot8f[:], 0.0, None, op0=OP.is_lt))
        b.op("vector", lambda E: E.tensor_scalar(kv8[:], slot8f[:], float(ROWS), None,
                                                 op0=OP.subtract))
        b.op("vector", lambda E: E.tensor_tensor(kv8[:], kv8[:], ovf[:], op=OP.mult))
        b.op("vector", lambda E: E.tensor_tensor(slot8f[:], slot8f[:], kv8[:], op=OP.subtract))
        tsl8 = b.op("vector", lambda E, i=i: E.tensor_copy(
            out=slot8i[:, i * 8:(i + 1) * 8], in_=slot8f[:]), inc="RS8")
        if rparts < 3:
            pay_tok[i] = None
            return
        bi = i % 2
        prevp = state["pay_free"].get(bi)
        if prevp is not None:
            b.waits("sync", [prevp])
        tpl = b.dma("sync", lambda E, i=i, bi=bi: E.dma_start(
            out=payload[bi][:, 0:2048], in_=xab_dram[i * 128:(i + 1) * 128, :]),
            ch="PLDa" if bi == 0 else "PLDb")
        b.waits("vector", [tpl] + ([prevp] if prevp else []))
        b.op("vector", lambda E, bi=bi: E.tensor_copy(
            out=payload[bi][:, 2048:2112].rearrange("p (c two) -> p c two", two=2)[:, :, 0],
            in_=Whi[:]))
        tpw = b.op("vector", lambda E, bi=bi: E.tensor_copy(
            out=payload[bi][:, 2048:2112].rearrange("p (c two) -> p c two", two=2)[:, :, 1],
            in_=Wlob[:]), inc="RPW")
        if rparts < 4:
            pay_tok[i] = None
            return
        b.waits("gpsimd", [tpw, tsl8])
        tsc = None
        for k in range(8):
            tsc = b.dma("gpsimd", lambda E, i=i, k=k, bi=bi: E.indirect_dma_start(
                out=xsorted[:, :],
                out_offset=IndirectOffsetOnAxis(slot8i[:, i * 8 + k:i * 8 + k + 1], 0),
                in_=payload[bi][:, :], in_offset=None),
                ch="SCTa" if bi == 0 else "SCTb")
        pay_tok[i] = tsc
        state["pay_free"][bi] = tsc

    emit_tile_loads(0, first=True)
    for i in range(NT):
        b.waits("tensor", tile_tok[i])
        head_BC = {}

        def emit_A(h, i=None):
            kvh = h // 4
            kb = kvh * XKV
            b.waits("tensor", [state["s_bank"]])
            b.op("tensor", lambda E, h=h, kb=kb: E.matmul(
                S1[:, 0:256], qt_t[:, h * 128:(h + 1) * 128],
                kt[:, kb:kb + 256], start=True, stop=True), nop=False)
            b.op("tensor", lambda E, h=h, kb=kb, i=i: E.matmul(
                S1[:, 256:512], qt_t[:, h * 128:(h + 1) * 128],
                kt[:, kb + 256 + 128 * i:kb + 512 + 128 * i], start=True, stop=True), nop=False)
            tsc_ = b.op("tensor", lambda E, h=h, kb=kb, i=i: E.matmul(
                S2[:, 0:384], qt_t[:, h * 128:(h + 1) * 128],
                kt[:, kb + 512 + 128 * i:kb + 896 + 128 * i], start=True, stop=True),
                nop=False, inc=True)
            b.wait("vector", tsc_)
            b.waits("vector", [state["sm_free"]])
            b.op("vector", lambda E: E.tensor_tensor(
                sm[:, 0:512], S1[:], mask_t[:, 0:512], op=OP.add))
            tmsk = b.op("vector", lambda E: E.tensor_tensor(
                sm[:, 512:896], S2[:], mask_t[:, 512:896], op=OP.add), inc="SMK")
            state["s_bank"] = tmsk
            return tmsk

        def emit_B(h, tmsk):
            rcol = 16 + (h % 2)
            b.waits("scalar", [tmsk, state["pt_sb"]])
            b.op("scalar", lambda E: E.activation(
                out=Pt[:, 0:512], in_=sm[:, 0:512], func=AF.Exp, accum_out=small[:, 4:5]))
            texp = b.op("scalar", lambda E: E.activation(
                out=Pt[:, 512:896], in_=sm[:, 512:896], func=AF.Exp,
                accum_out=small[:, 5:6]), inc="EXP")
            state["sm_free"] = texp
            b.wait("vector", texp)
            b.op("vector", lambda E: E.tensor_tensor(small[:, 6:7], small[:, 4:5],
                                                     small[:, 5:6], op=OP.add))
            b.waits("vector", [state["rcol_guard"][h % 2]])
            trec = b.fop("vector", lambda E, rcol=rcol: E.reciprocal(
                small[:, rcol:rcol + 1], small[:, 6:7]), inc="PNM")
            tn1 = b.op("vector", lambda E, rcol=rcol: E.tensor_scalar(
                Pt[:, 0:512], Pt[:, 0:512], small[:, rcol:rcol + 1], None,
                op0=OP.mult), inc="PNM")
            b.waits("scalar", [trec])
            tn2 = b.op("scalar", lambda E, rcol=rcol: E.activation(
                out=Pt[:, 512:896], in_=Pt[:, 512:896], func=AF.Copy,
                scale=small[:, rcol:rcol + 1]), inc="EXP")
            state["rcol_guard"][h % 2] = tn2
            head_BC[h] = (tn1, tn2)

        def emit_C(h, i=None):
            tn1, tn2 = head_BC.pop(h)
            kvh = h // 4
            b.waits("tensor", [tn1, tn2, state["pt_bank"]])
            ttr = None
            for jblk in range(7):
                ttr = b.op("tensor", lambda E, jblk=jblk: E.transpose(
                    PTp[:, jblk * 128:(jblk + 1) * 128], Pt[:, jblk * 128:(jblk + 1) * 128],
                    idb[:]), nop=False, inc=(True if jblk == 6 else None))
            state["pt_sb"] = ttr
            b.waits("scalar", [ttr, state["pts_free"]])
            tevp = b.op("scalar", lambda E: E.copy(out=PTs[:], in_=PTp[:]), inc="PTE")
            state["pt_bank"] = tevp
            b.waits("tensor", [tevp, state["oht_bank"]])
            tav = None
            for jblk in range(7):
                vblk = jblk if jblk < 2 else (i + jblk)
                tav = b.op("tensor", lambda E, jblk=jblk, vblk=vblk, kvh=kvh: E.matmul(
                    OHT[:], vt[:, vblk * 512 + kvh * 128:vblk * 512 + (kvh + 1) * 128],
                    PTs[:, jblk * 128:(jblk + 1) * 128],
                    start=(jblk == 0), stop=(jblk == 6)), nop=False,
                    inc=(True if jblk == 6 else None))
            state["pts_free"] = tav
            b.waits("scalar", [tav, state["outT_free"]])
            toh = b.op("scalar", lambda E, h=h: E.copy(
                out=outT[:, h * 128:(h + 1) * 128], in_=OHT[:]), inc="OHE")
            state["oht_bank"] = toh
            if dbg and i == 0 and h == 0:
                b.wait("gpsimd", toh)
                b.dma("gpsimd", lambda E: E.dma_start(out=dbg_out[1][:, 0:896], in_=Pt[:]), ch="DBG")
                b.dma("gpsimd", lambda E: E.dma_start(out=dbg_out[2][:, 0:20], in_=small[:, 0:20]), ch="DBG")
                b.dma("gpsimd", lambda E: E.dma_start(out=dbg_out[3][:, 0:896], in_=PTs[:]), ch="DBG")
                b.dma("gpsimd", lambda E: E.dma_start(out=dbg_out[4][:, 0:128], in_=outT[:, 0:128]), ch="DBG")
            return toh

        prev_h = None
        for h in range(NH):
            tmsk_h = emit_A(h, i=i)
            if prev_h is not None:
                emit_C(prev_h, i=i)
            emit_B(h, tmsk_h)
            prev_h = h
        toh = emit_C(prev_h, i=i)
        b.waits("tensor", [toh])
        b.waits("vector", [state["xaf_free"]])
        ybank_tok = state.get("ybank_tok", [None, None])
        tyev = None
        for nc4 in range(4):
            yb = nc4 % 2
            b.waits("tensor", [ybank_tok[yb]])
            two = None
            for f in range(16):
                two = b.op("tensor", lambda E, nc4=nc4, f=f: E.matmul(
                    Yp[nc4 % 2][:],
                    outT[:, f * 128:(f + 1) * 128],
                    wob[:, f * 2048 + nc4 * 512:f * 2048 + (nc4 + 1) * 512],
                    start=(f == 0), stop=(f == 15)), nop=False,
                    inc=(True if f == 15 else None))
            b.waits("vector", [two])
            tyev = b.op("vector", lambda E, nc4=nc4, yb=yb: E.scalar_tensor_tensor(
                x1[:, nc4 * 512:(nc4 + 1) * 512], in0=Yp[yb][:], scalar=0.0,
                in1=x_t[:, nc4 * 512:(nc4 + 1) * 512],
                op0=OP.add, op1=OP.add), inc=f"YB{yb}")
            ybank_tok[yb] = tyev
        state["ybank_tok"] = ybank_tok
        state["outT_free"] = two
        b.fop("vector", lambda E: E.reduce_sum(small[:, 8:9], x1[:], axis=AX))
        b.op("vector", lambda E: E.tensor_scalar(
            small[:, 10:11], small[:, 8:9], 1.0 / D, None, op0=OP.mult))
        b.wait("scalar", tyev)
        tsq = b.op("scalar", lambda E: E.activation(
            out=x_t[:], in_=x1[:], func=AF.Square, accum_out=small[:, 9:10]), inc="LSQ")
        state["y_bank"] = tsq
        if dbg and i == 0:
            b.wait("gpsimd", tsq)
            b.dma("gpsimd", lambda E: E.dma_start(out=dbg_out[5][:, :], in_=outT[:]), ch="DBG")
            b.dma("gpsimd", lambda E: E.dma_start(out=dbg_out[6][:, :], in_=x1[:]), ch="DBG")
        b.wait("vector", tsq)
        b.op("vector", lambda E: E.tensor_scalar(
            small[:, 11:12], small[:, 9:10], 1.0 / D, None, op0=OP.mult))
        b.op("vector", lambda E: E.tensor_tensor(
            small[:, 9:10], small[:, 10:11], small[:, 10:11], op=OP.mult))
        b.op("vector", lambda E: E.tensor_tensor(
            small[:, 11:12], small[:, 11:12], small[:, 9:10], op=OP.subtract))
        b.op("vector", lambda E: E.tensor_scalar(
            small[:, 11:12], small[:, 11:12], EPS, None, op0=OP.add))
        trec = b.fop("vector", lambda E: E.reciprocal(small[:, 12:13], small[:, 11:12]),
                     inc="LRC")
        b.wait("scalar", trec)
        tst_ = b.op("scalar", lambda E: E.activation(
            out=small[:, 13:14], in_=small[:, 12:13], func=AF.Sqrt), inc="LSR")
        b.wait("vector", tst_)
        txa = b.op("vector", lambda E: E.tensor_scalar(
            xa_f[:], x1[:], small[:, 10:11], small[:, 13:14],
            op0=OP.subtract, op1=OP.mult), inc="LXA")
        b.waits("scalar", [txa, state["xab_free"]])
        txab = b.op("scalar", lambda E: E.copy(out=xab_t[:], in_=xa_f[:]), inc="LXB")
        b.waits("sync", [txa, txab])
        xch = "XSTa" if (i // 4) % 2 == 0 else "XSTb"
        xa_store[i] = b.dma("sync", lambda E, i=i: E.dma_start(
            out=xa_dram[i * 128:(i + 1) * 128, :], in_=xa_f[:]), ch=xch)
        xab_store[i] = b.dma("sync", lambda E, i=i: E.dma_start(
            out=xab_dram[i * 128:(i + 1) * 128, :], in_=xab_t[:]), ch=xch)
        state["xaf_free"] = xa_store[i]
        state["xab_free"] = xab_store[i]
        if i + 1 < NT:
            emit_tile_loads(i + 1)

        if stage in ("full", "route", "p4") and i % 4 == 3:
            emit_gate_quarter(i // 4)
            for jj in range(4):
                emit_routing_tile(i - 3 + jj)

    if stage == "route":
        slotdump = dt("slotdump", [128, 128], I32, kind="ExternalOutput").ap()
        rdump = dt("rdump", [6, 128, 128], F32, kind="ExternalOutput").ap()
        b.waits("gpsimd", [b.last["vector"]])
        b.dma("gpsimd", lambda E: E.dma_start(out=rdump[0][0:32, :], in_=rdbg[0][:]), ch="DBG")
        b.dma("gpsimd", lambda E: E.dma_start(out=rdump[1][0:32, :], in_=rdbg[1][:]), ch="DBG")
        b.dma("gpsimd", lambda E: E.dma_start(out=rdump[2][:, 0:32], in_=rdbg[2][:]), ch="DBG")
        b.dma("gpsimd", lambda E: E.dma_start(out=rdump[3][:, 0:32], in_=rdbg[3][:]), ch="DBG")
        b.dma("gpsimd", lambda E: E.dma_start(out=rdump[4][:, 0:8], in_=rdbg[4][:]), ch="DBG")
        b.dma("gpsimd", lambda E: E.dma_start(out=rdump[5][:, 0:8], in_=rdbg[5][:]), ch="DBG")
        b.waits("gpsimd", [t for t in (pay_tok[i] for i in range(NT)) if t is not None])
        b.barrier(ENGS)
        tlast = b.dma("gpsimd", lambda E: E.dma_start(out=slotdump[:], in_=slot8i[:]), ch="DBG")
        for i in range(NT):
            tlast = b.dma("gpsimd", lambda E, i=i: E.dma_start(
                out=y_out[i * 128:(i + 1) * 128, :],
                in_=xa_dram[i * 128:(i + 1) * 128, :]), ch="DBG")
        b.wait("sync", tlast)
        finish(nc, b)
        pb.close()
        pab.close()
        top.close()
        return nc

    if stage == "attn":
        b.barrier(ENGS)
        tlast = None
        for i in range(NT):
            tlast = b.dma("gpsimd", lambda E, i=i: E.dma_start(
                out=y_out[i * 128:(i + 1) * 128, :],
                in_=xa_dram[i * 128:(i + 1) * 128, :]), ch="DBG")
        b.wait("sync", tlast)
        finish(nc, b)
        pb.close()
        pab.close()
        top.close()
        return nc

    b.barrier(ENGS)
    pb.close()
    pab.close()

    pc = ExitStack()
    wgb = SB(pc, "wgb", [128, 2 * 16 * INTER], BF16)
    WUB = 16 * INTER
    wdb = SB(pc, "wdb", [128, 6 * 2048], BF16)
    xgt = [SB(pc, f"xgt{j}", [128, 16 * CAP], BF16) for j in range(2)]
    xgrow = [SB(pc, f"xgrow{j}", [128, D], BF16) for j in range(2)]
    TRp = [PS(pc, f"TRp{j}", [128, 512], BF16) for j in range(2)]
    wc_b = [SB(pc, f"wc_b{j}", [128, 10], BF16) for j in range(2)]
    wcol = [SB(pc, f"wcol{j}", [128, 5], F32) for j in range(2)]
    hT = SB(pc, "hT", [128, 6 * CAP], BF16)
    hs = SB(pc, "hs", [128, CAP], F32)
    ysb = [SB(pc, f"ysb{j}", [128, D], BF16) for j in range(2)]
    Gp = PS(pc, "Gp", [128, CAP])
    Up = PS(pc, "Up", [128, CAP])

    wg_ready = [None] * NE
    wd_ready = [None] * NE
    ffn1_done = [None] * NE
    ffn2_done = [None] * NE
    xgt_ready = [None] * NE
    wcol_ok = [None] * NE
    ysb_tok = [None, None]
    sil_tok = None
    hmu_tok = None

    def emit_weight_load(e, kind):
        toks = []
        if kind == "gu":
            b.waits("gpsimd", [ffn1_done[e - 1]] if e > 0 else [])
            for src, base in ((wg_in, 0), (wu_in, WUB)):
                for h in range(2):
                    toks.append(b.dma("gpsimd", lambda E, src=src, base=base, h=h, e=e:
                        E.dma_start(
                            out=wgb[:, base + h * 8 * INTER:base + (h + 1) * 8 * INTER]
                                .rearrange("p (c f) -> p c f", c=8),
                            in_=src[e, h * 1024:(h + 1) * 1024, :]
                                .rearrange("(c p) f -> p c f", p=128)), ch="WGU"))
            wg_ready[e] = toks
        else:
            b.waits("gpsimd", [ffn2_done[e - 1]] if e > 0 else [])
            for h in range(2):
                toks.append(b.dma("gpsimd", lambda E, h=h, e=e: E.dma_start(
                    out=wdb[:, h * 3 * 2048:(h + 1) * 3 * 2048]
                        .rearrange("p (c f) -> p c f", c=3),
                    in_=wd_in[e, h * 384:(h + 1) * 384, :]
                        .rearrange("(c p) f -> p c f", p=128)), ch="WDD"))
            wd_ready[e] = toks

    xg_state = {"rowfree": [None, None], "trpfree": [None, None]}

    def emit_xgt_load(e):
        bj = e % 2
        tev = None
        for rc in range(5):
            rcj = rc % 2
            ldeng = "sync" if rcj == 0 else "scalar"
            b.waits(ldeng, [xg_state["rowfree"][rcj]])
            tld = b.dma(ldeng, lambda E, e=e, rc=rc, rcj=rcj: E.dma_start(
                out=xgrow[rcj][:],
                in_=xsorted[e * CAP + rc * 128:e * CAP + (rc + 1) * 128, 0:D]),
                ch="TRX4a" if rcj == 0 else "TRX4b")
            ttr = None
            for g in range(4):
                tb = (rc * 4 + g) % 2
                b.waits("tensor", [tld, xg_state["trpfree"][tb]])
                for q in range(4):
                    ttr = b.op("tensor", lambda E, g=g, q=q, tb=tb, rcj=rcj: E.transpose(
                        TRp[tb][:, q * 128:(q + 1) * 128],
                        xgrow[rcj][:, (g * 4 + q) * 128:(g * 4 + q + 1) * 128],
                        idb[:]), nop=False, inc=(True if q == 3 else None))
                b.waits("vector", [ttr] + (
                    [ffn1_done[e - 2]] if (e >= 2 and rc == 0 and g == 0) else []))
                tev = b.op("vector", lambda E, g=g, rc=rc, tb=tb, bj=bj: E.tensor_copy(
                    out=xgt[bj][:].rearrange("p (c t) -> p c t", c=16)
                        [:, g * 4:(g + 1) * 4, rc * 128:(rc + 1) * 128],
                    in_=TRp[tb][:].rearrange("p (c t) -> p c t", c=4)), inc="XEV")
                xg_state["trpfree"][tb] = tev
            xg_state["rowfree"][rcj] = ttr
        xgt_ready[e] = tev
        tw = []
        for tc5 in range(5):
            tw.append(b.dma("scalar", lambda E, e=e, tc5=tc5, bj=bj: E.dma_start(
                out=wc_b[bj][:, tc5 * 2:(tc5 + 1) * 2],
                in_=xsorted[e * CAP + tc5 * 128:e * CAP + (tc5 + 1) * 128,
                            WCOL + 2 * e:WCOL + 2 * e + 2]), ch="MISC"))
        b.waits("vector", tw + ([ffn2_done[e - 2]] if e >= 2 else []))
        tv = None
        for tc5 in range(5):
            tv = b.op("vector", lambda E, tc5=tc5, bj=bj: E.tensor_tensor(
                wcol[bj][:, tc5:tc5 + 1], wc_b[bj][:, tc5 * 2:tc5 * 2 + 1],
                wc_b[bj][:, tc5 * 2 + 1:tc5 * 2 + 2], op=OP.add),
                inc=("WCL" if tc5 == 4 else None))
        wcol_ok[e] = tv

    scat_all = [pay_tok[i] for i in range(NT)]
    b.waits("scalar", scat_all)
    b.waits("sync", scat_all)
    b.waits("gpsimd", scat_all)
    emit_weight_load(0, "gu")
    emit_xgt_load(0)
    emit_weight_load(0, "d")

    for e in range(NE):
        bj = e % 2
        b.waits("tensor", wg_ready[e] + [xgt_ready[e]])
        for ic in range(6):
            b.waits("tensor", [sil_tok])
            tgm = None
            for ncc, w in ((0, 512), (512, 128)):
                for dmc in range(16):
                    tgm = b.op("tensor", lambda E, ic=ic, ncc=ncc, w=w, dmc=dmc, bj=bj: E.matmul(
                        Gp[:, ncc:ncc + w],
                        wgb[:, dmc * INTER + ic * 128:dmc * INTER + (ic + 1) * 128],
                        xgt[bj][:, dmc * CAP + ncc:dmc * CAP + ncc + w],
                        start=(dmc == 0), stop=(dmc == 15)), nop=False,
                        inc=(True if (ncc, dmc) == (512, 15) else None))
            b.waits("tensor", [hmu_tok])
            tum = None
            for ncc, w in ((0, 512), (512, 128)):
                for dmc in range(16):
                    tum = b.op("tensor", lambda E, ic=ic, ncc=ncc, w=w, dmc=dmc, bj=bj: E.matmul(
                        Up[:, ncc:ncc + w],
                        wgb[:, WUB + dmc * INTER + ic * 128:WUB + dmc * INTER + (ic + 1) * 128],
                        xgt[bj][:, dmc * CAP + ncc:dmc * CAP + ncc + w],
                        start=(dmc == 0), stop=(dmc == 15)), nop=False,
                        inc=(True if (ncc, dmc) == (512, 15) else None))
            b.waits("scalar", [tgm, hmu_tok])
            tsl = b.op("scalar", lambda E: E.activation(
                out=hs[:], in_=Gp[:], func=AF.Silu), inc="SIL")
            sil_tok = tsl
            b.waits("vector", [tsl, tum])
            tmu = b.op("vector", lambda E, ic=ic: E.tensor_tensor(
                hT[:, ic * CAP:(ic + 1) * CAP], hs[:], Up[:], op=OP.mult), inc="HMU")
            hmu_tok = tmu
        ffn1_done[e] = hmu_tok
        if e + 1 < NE:
            emit_weight_load(e + 1, "gu")
            emit_xgt_load(e + 1)
        b.waits("tensor", wd_ready[e] + [ffn1_done[e], wcol_ok[e]])
        tym_last = None
        ychunk_tok = [None, None]
        for tc5 in range(5):
            yb2 = tc5 % 2
            b.waits("tensor", [ysb_tok[yb2]])
            tyv = None
            for nc4 in range(4):
                yq = nc4 % 2
                b.waits("tensor", [ychunk_tok[yq]])
                tym = None
                for ic in range(6):
                    tym = b.op("tensor", lambda E, nc4=nc4, ic=ic, tc5=tc5: E.matmul(
                        Yp[nc4 % 2][:],
                        hT[:, ic * CAP + tc5 * 128:ic * CAP + (tc5 + 1) * 128],
                        wdb[:, ic * 2048 + nc4 * 512:ic * 2048 + (nc4 + 1) * 512],
                        start=(ic == 0), stop=(ic == 5)), nop=False,
                        inc=(True if ic == 5 else None))
                tym_last = tym
                b.wait("vector", tym)
                tyv = b.op("vector", lambda E, yb2=yb2, yq=yq, nc4=nc4, tc5=tc5, bj=bj: E.tensor_scalar(
                    ysb[yb2][:, nc4 * 512:(nc4 + 1) * 512], Yp[yq][:],
                    wcol[bj][:, tc5:tc5 + 1], None, op0=OP.mult), inc=f"YC{yq}")
                ychunk_tok[yq] = tyv
            b.wait("sync", tyv)
            tst2 = b.dma("sync", lambda E, e=e, tc5=tc5, yb2=yb2: E.dma_start(
                out=ysorted[e * CAP + tc5 * 128:e * CAP + (tc5 + 1) * 128, :],
                in_=ysb[yb2][:]), ch="YSTa" if yb2 == 0 else "YSTb")
            ysb_tok[yb2] = tst2
        ffn2_done[e] = tym_last
        if e + 1 < NE:
            emit_weight_load(e + 1, "d")

    b.barrier(ENGS)
    if stage == "p4":
        tlast = None
        for i in range(NT):
            tlast = b.dma("gpsimd", lambda E, i=i: E.dma_start(
                out=y_out[i * 128:(i + 1) * 128, :].rearrange("p (h c) -> p h c", h=1),
                in_=ysorted[i * 128:(i + 1) * 128, :].rearrange("p (h c) -> p h c", h=1)), ch="DBG")
        b.wait("sync", tlast)
        finish(nc, b)
        pc.close()
        top.close()
        return nc
    pc.close()

    pd = ExitStack()
    gath = [SB(pd, f"gath{j}", [128, D], BF16) for j in range(8)]
    moe = SB(pd, "moe", [128, D], F32)
    t01 = SB(pd, "t01", [128, D], F32)
    t23 = SB(pd, "t23", [128, D], F32)
    t45 = SB(pd, "t45", [128, D], F32)
    t67 = SB(pd, "t67", [128, D], F32)
    xat = SB(pd, "xat", [128, D], F32)
    outf = SB(pd, "outf", [128, D], F32)

    gth_tok = {}

    def emit_gathers(i):
        toks = []
        for k in range(8):
            toks.append(b.dma("gpsimd", lambda E, i=i, k=k: E.indirect_dma_start(
                out=gath[k][:], out_offset=None,
                in_=ysorted[:, :],
                in_offset=IndirectOffsetOnAxis(slot8i[:, i * 8 + k:i * 8 + k + 1], 0)),
                ch="GTH"))
        gth_tok[i] = toks

    emit_gathers(0)
    out_last = None
    prev_xat = None
    prev_out = None
    for i in range(NT):
        b.waits("vector", gth_tok[i])
        if prev_xat is not None:
            b.waits("sync", [prev_xat])
        txl = b.dma("sync", lambda E, i=i: E.dma_start(
            out=xat[:], in_=xa_dram[i * 128:(i + 1) * 128, :]), ch="XAT")
        b.op("vector", lambda E: E.tensor_tensor(t01[:], gath[0][:], gath[1][:], op=OP.add))
        b.op("vector", lambda E: E.tensor_tensor(t23[:], gath[2][:], gath[3][:], op=OP.add))
        b.op("vector", lambda E: E.tensor_tensor(t45[:], gath[4][:], gath[5][:], op=OP.add))
        b.op("vector", lambda E: E.tensor_tensor(t67[:], gath[6][:], gath[7][:], op=OP.add))
        b.op("vector", lambda E: E.tensor_tensor(t01[:], t01[:], t23[:], op=OP.add))
        b.op("vector", lambda E: E.tensor_tensor(t45[:], t45[:], t67[:], op=OP.add))
        tgsum = b.op("vector", lambda E: E.tensor_tensor(moe[:], t01[:], t45[:], op=OP.add),
                     inc="GSM")
        if i + 1 < NT:
            b.wait("gpsimd", tgsum)
            emit_gathers(i + 1)
        b.wait("vector", txl)
        b.op("vector", lambda E: E.tensor_tensor(
            x1[:], moe[:], xat[:], op=OP.add))
        b.fop("vector", lambda E: E.reduce_sum(small[:, 8:9], x1[:], axis=AX))
        tv1 = b.op("vector", lambda E: E.tensor_scalar(
            small[:, 10:11], small[:, 8:9], 1.0 / D, None, op0=OP.mult), inc="LV1")
        b.wait("scalar", tv1)
        tsq = b.op("scalar", lambda E: E.activation(
            out=xat[:], in_=x1[:], func=AF.Square, accum_out=small[:, 9:10]), inc="LSQ")
        prev_xat = tsq
        b.wait("vector", tsq)
        b.op("vector", lambda E: E.tensor_scalar(
            small[:, 11:12], small[:, 9:10], 1.0 / D, None, op0=OP.mult))
        b.op("vector", lambda E: E.tensor_tensor(
            small[:, 9:10], small[:, 10:11], small[:, 10:11], op=OP.mult))
        b.op("vector", lambda E: E.tensor_tensor(
            small[:, 11:12], small[:, 11:12], small[:, 9:10], op=OP.subtract))
        b.op("vector", lambda E: E.tensor_scalar(
            small[:, 11:12], small[:, 11:12], EPS, None, op0=OP.add))
        trec = b.fop("vector", lambda E: E.reciprocal(small[:, 12:13], small[:, 11:12]),
                     inc="LRC")
        b.wait("scalar", trec)
        tst_ = b.op("scalar", lambda E: E.activation(
            out=small[:, 13:14], in_=small[:, 12:13], func=AF.Sqrt), inc="LSR")
        b.waits("vector", [tst_] + ([prev_out] if prev_out else []))
        tof = b.op("vector", lambda E: E.tensor_scalar(
            outf[:], x1[:], small[:, 10:11], small[:, 13:14],
            op0=OP.subtract, op1=OP.mult), inc="OF")
        b.wait("sync", tof)
        out_last = b.dma("sync", lambda E, i=i: E.dma_start(
            out=y_out[i * 128:(i + 1) * 128, :], in_=outf[:]), ch="OUT")
        prev_out = out_last

    b.wait("sync", out_last)
    b.wait("gpsimd", out_last)
    if dbg:
        tdb = b.dma("gpsimd", lambda E: E.dma_start(out=sdump[:], in_=slot8i[:]), ch="DBG")
        for ch_ in range(6):
            r0, r1 = ch_ * 3584, min((ch_ + 1) * 3584, ROWS + 1)
            tdb = b.dma("gpsimd", lambda E, r0=r0, r1=r1: E.dma_start(
                out=wdump[r0:r1, :], in_=xsorted[r0:r1, 2048:2112]), ch="DBG")
            tdb = b.dma("gpsimd", lambda E, r0=r0, r1=r1: E.dma_start(
                out=ydump[r0:r1, :], in_=ysorted[r0:r1, 0:8]), ch="DBG")
        b.wait("sync", tdb)
    finish(nc, b)
    pd.close()
    top.close()
    return nc


def finish(nc, b):
    with nc.Block() as block:
        for eng in ENGS:
            ops = list(b.q[eng])

            def runner(E, ops=ops):
                for fn in ops:
                    fn(E)
            getattr(block, eng)(runner)


def _rope_tables():
    inv = 1.0 / (10000.0 ** (np.arange(0, DQK, 2)[: DQK // 2].astype(np.float32) / DQK))
    t = np.arange(S, dtype=np.float32)
    ang = np.outer(t, inv)
    return np.cos(ang), np.sin(ang)


def _perm_cols(w, nheads):
    perm = np.concatenate([np.arange(0, 128, 2), np.arange(1, 128, 2)])
    w2 = w.reshape(w.shape[0], nheads, 128)[:, :, perm]
    return np.ascontiguousarray(w2.reshape(w.shape[0], nheads * 128))


def _mask_for_core(qo):
    m = np.full((NT, 128, 896), NEG, np.float32)
    for i in range(NT):
        q = qo + i * 128 + np.arange(128)[:, None]
        band_lo = qo - 512 + i * 128

        def allowed(j):
            band = (j <= q) & (j > q - WIN)
            front = (j < FRONT) & (j <= q - WIN)
            return band | front
        jf = np.arange(256)[None, :]
        in_band = (jf >= band_lo) & (jf < band_lo + 640)
        m[i, :, 0:256] = np.where(allowed(jf) & ~in_band, 0.0, NEG)
        jb = band_lo + np.arange(640)[None, :]
        m[i, :, 256:896] = np.where(allowed(jb) & (jb >= 0), 0.0, NEG)
    return m


def make_in_maps(L0_attn_x, L0_attn_wq, L0_attn_wk, L0_attn_wv, L0_attn_wo,
                 L0_ffn_gate_w, L0_ffn_w_gate, L0_ffn_w_up, L0_ffn_w_down):
    x = np.asarray(L0_attn_x, np.float32)
    wq = _perm_cols(np.asarray(L0_attn_wq, np.float32), NH)
    wk = _perm_cols(np.asarray(L0_attn_wk, np.float32), NKV)
    wv = np.ascontiguousarray(np.asarray(L0_attn_wv, np.float32))
    wo = np.ascontiguousarray(np.asarray(L0_attn_wo, np.float32))
    gw = np.ascontiguousarray(np.asarray(L0_ffn_gate_w, np.float32))
    wg = np.ascontiguousarray(np.asarray(L0_ffn_w_gate, np.float32))
    wu = np.ascontiguousarray(np.asarray(L0_ffn_w_up, np.float32))
    wdn = np.ascontiguousarray(np.asarray(L0_ffn_w_down, np.float32))
    cos, sin = _rope_tables()
    idf = np.eye(128, dtype=np.float32)
    idb = idf.astype(ml_dtypes.bfloat16)
    scale = np.float32(1.0 / np.sqrt(DQK))

    in_maps = []
    for c in range(8):
        bb, qo = c // 4, (c % 4) * T
        xb = x[bb]
        xkv = np.zeros((XKV, D), np.float32)
        xkv[0:FRONT] = xb[0:FRONT]
        lo = qo - WIN
        src0 = max(0, lo)
        xkv[FRONT + (src0 - lo):FRONT + WIN + T] = xb[src0:qo + T]
        qpos = np.arange(qo, qo + T)
        kvpos = np.concatenate([np.arange(FRONT),
                                np.clip(np.arange(lo, qo + T), 0, S - 1)])
        in_maps.append({
            "xkv_in": np.ascontiguousarray(xkv),
            "wq_in": wq, "wk_in": wk, "wv_in": wv, "wo_in": wo,
            "gate_in": gw, "wg_in": wg, "wu_in": wu, "wd_in": wdn,
            "cq_in": np.ascontiguousarray(cos[qpos] * scale),
            "sq_in": np.ascontiguousarray(sin[qpos] * scale),
            "ckv_in": np.ascontiguousarray(cos[kvpos]),
            "skv_in": np.ascontiguousarray(sin[kvpos]),
            "mask_in": _mask_for_core(qo),
            "idf_in": idf, "idb_in": idb,
        })
    return in_maps


def kernel(L0_attn_x, L0_attn_wq, L0_attn_wk, L0_attn_wv, L0_attn_wo,
           L0_ffn_gate_w, L0_ffn_w_gate, L0_ffn_w_up, L0_ffn_w_down):
    nc = build_nc(stage="full")
    in_maps = make_in_maps(L0_attn_x, L0_attn_wq, L0_attn_wk, L0_attn_wv, L0_attn_wo,
                           L0_ffn_gate_w, L0_ffn_w_gate, L0_ffn_w_up, L0_ffn_w_down)
    res = run_bass_kernel_spmd(nc, in_maps, core_ids=list(range(8)))
    out = np.empty((BATCH, S, D), np.float32)
    for c in range(8):
        bb, qo = c // 4, (c % 4) * T
        out[bb, qo:qo + T] = res.results[c]["y_out"]
    return out
